# version 05
# baseline (speedup 1.0000x reference)
import numpy as np
from contextlib import ExitStack
import concourse.bass as bass
import concourse.mybir as mybir
from concourse.bass_utils import run_bass_kernel_spmd
from concourse.alu_op_type import AluOpType as OP

F32 = mybir.dt.float32
BF16 = mybir.dt.bfloat16
AF = mybir.ActivationFunctionType
AX = mybir.AxisListType

NT = 4096
D = 1024
NBLK = 32
H = 16
HD = 64
FF = 4096
E = 8
GRP = 512
NGRP = NT // GRP
FT = FF // 128


class Sems:

    def __init__(self, nc, es):
        self.nc, self.es = nc, es
        self.h = {}
        self.v = {}

    def get(self, name):
        if name not in self.h:
            self.h[name] = self.nc.alloc_semaphore(name=name, num=None)
            self.v[name] = 0
        return self.h[name]

    def inc(self, inst, name, n=1):
        h = self.get(name)
        inst.then_inc(h, n)
        self.v[name] += n
        return self.v[name]

    def val(self, name):
        self.get(name)
        return self.v[name]

    def wait(self, eng, name, val):
        if val is None or val <= 0:
            return
        eng.wait_ge(self.get(name), val)


class Kb:

    def __init__(self, nc, es):
        self.nc = nc
        self.es = es
        self.scopes = []
        self.s = Sems(nc, es)
        self.pb = [es.enter_context(nc.psum_tensor(f"pb{i}", [128, 512], F32))
                   for i in range(6)]
        self.pt = es.enter_context(nc.psum_tensor("pt", [128, 1024], BF16))
        self.pg = es.enter_context(nc.psum_tensor("pg", [128, 512], F32))

    def sb(self, name, shape, dt):
        es = self.scopes[-1] if self.scopes else self.es
        return es.enter_context(self.nc.sbuf_tensor(name, shape, dt))

    def push_scope(self):
        self.scopes.append(ExitStack())

    def pop_scope(self):
        self.scopes.pop().close()


def emit_moe(k: Kb, t, n_experts=E):
    k.push_scope()
    nc, s = k.nc, k.s

    w1sb = [k.sb(f"moe_w1sb{i}", [128, 8, FF], BF16) for i in range(1)]
    w2sb = [k.sb(f"moe_w2sb{i}", [128, FT, D], BF16) for i in range(1)]
    x1g = [k.sb(f"moe_x1g{i}", [128, 8, GRP], BF16) for i in range(2)]
    hsT = k.sb("moe_hsT", [128, FT, GRP], BF16)
    ysb = [k.sb(f"moe_ysb{i}", [128, D], F32) for i in range(2)]
    yin = [k.sb(f"moe_yin{i}", [128, D], F32) for i in range(2)]
    b1c = k.sb("moe_b1c", [128, FT, E], F32)
    comb = k.sb("moe_comb", [128, NBLK, E], F32)

    s.inc(nc.scalar.dma_start(b1c[:], t["b1t"]), "moe_ld1", 16)
    s.inc(nc.scalar.dma_start(comb[:], t["combt"]), "moe_ld1", 16)

    w1sb_free = 0
    w2sb_free = 0

    def emit_w_load(e):
        s.wait(nc.sync, "mm", w1sb_free)
        ld = nc.sync.dma_start(w1sb[0][:], t["w1bf"][e].rearrange("c p f -> p c f"))
        r1 = s.inc(ld, "moe_w1", 16)
        s.wait(nc.sync, "mm", w2sb_free)
        ld = nc.sync.dma_start(w2sb[0][:], t["w2bf"][e].rearrange("f p d -> p f d"))
        r2 = s.inc(ld, "moe_w2", 16)
        return r1, r2

    w1_ready, w2_ready = emit_w_load(0)

    hbank_free = [0, 0]
    x1_free = [0, 0]
    x1_ready = [0, 0]
    ypair_free = [0, 0]
    ysb_free = [0, 0]
    first_use = True

    for e in range(n_experts):
        for g in range(NGRP):
            gi = g % 2
            xb = x1g[gi]
            s.wait(nc.scalar, "mm", x1_free[gi])
            ld = nc.scalar.dma_start(xb[:], t["x1t"][g].rearrange("c p t -> p c t"))
            x1_ready[gi] = s.inc(ld, f"moe_x{gi}", 16)

            s.wait(nc.tensor, "moe_w1", w1_ready)
            s.wait(nc.tensor, f"moe_x{gi}", x1_ready[gi])
            if first_use:
                s.wait(nc.tensor, "moe_ld1", 32)
                s.wait(nc.scalar, "moe_ld1", 32)
                first_use = False
            gelu_last = 0
            for f in range(FT):
                half = f % 2
                hp = k.pb[half][:]
                s.wait(nc.tensor, "moe_sc", hbank_free[half])
                for c in range(8):
                    mm = nc.tensor.matmul(hp, w1sb[0][:, c, bass.ts(f, 128)],
                                          xb[:, c, :],
                                          start=(c == 0), stop=(c == 7))
                mv = s.inc(mm, "mm", 1)
                s.wait(nc.scalar, "mm", mv)
                ge = nc.scalar.activation(hsT[:, f, :], hp, AF.Gelu,
                                          bias=b1c[:, f, e:e + 1], scale=1.0)
                hbank_free[half] = s.inc(ge, "moe_sc", 1)
            gelu_last = s.inc(nc.scalar.drain(), "moe_sc", 1)
            x1_free[gi] = s.val("mm")

            s.wait(nc.tensor, "moe_w2", w2_ready)
            for sub in range(4):
                blk = g * 4 + sub
                pi = sub % 2
                pair = (k.pb[2], k.pb[3]) if pi == 0 else (k.pb[4], k.pb[5])
                s.wait(nc.tensor, "moe_sc", gelu_last)
                s.wait(nc.tensor, "moe_v", ypair_free[pi])
                if e > 0:
                    s.wait(nc.gpsimd, f"moe_st{pi}", ysb_free[pi])
                    ldy = nc.gpsimd.dma_start(yin[pi][:], t["yacc"][bass.ts(blk, 128), :])
                    yin_rdy = s.inc(ldy, f"moe_yld{pi}", 16)
                for f in range(FT):
                    for dh in range(2):
                        mm = nc.tensor.matmul(
                            pair[dh][:],
                            hsT[:, f, bass.ts(sub, 128)],
                            w2sb[0][:, f, bass.ts(dh, 512)],
                            start=(f == 0), stop=(f == FT - 1))
                mv = s.inc(mm, "mm", 1)
                s.wait(nc.vector, "mm", mv)
                s.wait(nc.vector, f"moe_st{pi}", ysb_free[pi])
                if e == 0:
                    for dh in range(2):
                        nc.vector.tensor_scalar(ysb[pi][:, bass.ts(dh, 512)], pair[dh][:],
                                                comb[:, blk, e:e + 1], None, OP.mult)
                else:
                    s.wait(nc.vector, f"moe_yld{pi}", yin_rdy)
                    for dh in range(2):
                        nc.vector.scalar_tensor_tensor(ysb[pi][:, bass.ts(dh, 512)], pair[dh][:],
                                                       comb[:, blk, e:e + 1],
                                                       yin[pi][:, bass.ts(dh, 512)],
                                                       OP.mult, OP.add)
                ypair_free[pi] = s.inc(nc.vector.drain(), "moe_v", 1)
                s.wait(nc.gpsimd, "moe_v", ypair_free[pi])
                sd = nc.gpsimd.dma_start(t["yacc"][bass.ts(blk, 128), :], ysb[pi][:])
                ysb_free[pi] = s.inc(sd, f"moe_st{pi}", 16)
        if e + 1 < n_experts:
            w1sb_free = s.val("mm")
            w2sb_free = s.val("mm")
            w1_ready, w2_ready = emit_w_load(e + 1)
    for pi in range(2):
        s.wait(nc.gpsimd, f"moe_st{pi}", s.val(f"moe_st{pi}"))
    k.pop_scope()



def emit_cast_pass(k: Kb, jobs, eng=None, tag="cst"):
    k.push_scope()
    nc, s = k.nc, k.s
    eng = eng or nc.vector
    NBUF = 3
    fin = [k.sb(f"{tag}_fin{i}", [128, 2048], F32) for i in range(NBUF)]
    bout = [k.sb(f"{tag}_bout{i}", [128, 2048], BF16) for i in range(NBUF)]
    i = 0
    cast_done = [0] * NBUF
    out_done = [0] * NBUF
    for srcap, dst in jobs:
        R, C = srcap.shape
        assert R % 128 == 0 and C <= 2048
        for r in range(R // 128):
            j = i % NBUF
            s.wait(nc.sync, tag + "_c", cast_done[j])
            ld = nc.sync.dma_start(fin[j][:, 0:C], srcap[bass.ts(r, 128), :])
            lv = s.inc(ld, f"{tag}_l{j}", 16)
            s.wait(eng, f"{tag}_l{j}", lv)
            s.wait(eng, f"{tag}_o{j}", out_done[j])
            eng.tensor_copy(bout[j][:, 0:C], fin[j][:, 0:C])
            cast_done[j] = s.inc(eng.drain(), tag + "_c", 1)
            s.wait(nc.gpsimd, tag + "_c", cast_done[j])
            st = nc.gpsimd.dma_start(dst[bass.ts(r, 128), :], bout[j][:, 0:C])
            out_done[j] = s.inc(st, f"{tag}_o{j}", 16)
            i += 1
    for j in range(NBUF):
        s.wait(nc.gpsimd, f"{tag}_o{j}", s.val(f"{tag}_o{j}"))
    k.pop_scope()


def _ln_plain(k, eng_small, xin, xout, tmp, eps=1e-5, w_rep=None, b_rep=None,
              out2=None):
    nc, s = k.nc, k.s
    bn, mv, t1, t2 = tmp["bn"], tmp["mv"], tmp["t1"], tmp["t2"]
    nc.vector.bn_stats(bn[:, 0:6], xin[:, 0:512])
    v = s.inc(nc.vector.bn_stats(bn[:, 6:12], xin[:, 512:1024]), "ln", 1)
    s.wait(nc.vector, "ln", v)
    v = s.inc(nc.vector.bn_aggr(mv[:], bn[:]), "ln", 1)
    s.wait(nc.vector, "ln", v)
    v = s.inc(nc.vector.tensor_scalar(t1[:], mv[:, 1:2], eps, None, OP.add), "ln", 1)
    s.wait(nc.scalar, "ln", v)
    v = s.inc(nc.scalar.activation(t1[:], t1[:], AF.Sqrt), "ln", 1)
    s.wait(nc.vector, "ln", v)
    v = s.inc(nc.vector.reciprocal(t2[:], t1[:]), "ln", 1)
    s.wait(nc.vector, "ln", v)
    nc.vector.tensor_scalar(xout[:], xin[:], mv[:, 0:1], t2[:], OP.subtract, OP.mult)
    if w_rep is not None:
        nc.vector.tensor_tensor(xout[:], xout[:], w_rep[:], OP.mult)
    if b_rep is not None:
        nc.vector.tensor_tensor(out2 if out2 is not None else xout[:],
                                xout[:], b_rep[:], OP.add)


def emit_l1f_qkv(k: Kb, t):
    k.push_scope()
    nc, s = k.nc, k.s
    wq = k.sb("p4_wq", [128, 8, 2, 512], BF16)
    wk = k.sb("p4_wk", [128, 8, 2, 512], BF16)
    wv = k.sb("p4_wv", [128, 8, 2, 512], BF16)
    bias_bf = k.sb("p4_biasbf", [1, 4096], BF16)
    bias_f = k.sb("p4_biasf", [1, 4096], F32)
    onesr = k.sb("p4_ones", [1, 128], BF16)
    ident = k.sb("p4_ident", [128, 128], BF16)
    xs = k.sb("p4_xs", [128, D], F32)
    xn = k.sb("p4_xn", [128, D], F32)
    xnb = k.sb("p4_xnb", [128, D], BF16)
    xnT = k.sb("p4_xnT", [128, 8, 512], BF16)
    ctile = k.sb("p4_cos", [128, D], F32)
    stile = k.sb("p4_sin", [128, D], F32)
    qsb = k.sb("p4_qsb", [128, D], F32)
    qrot = k.sb("p4_qrot", [128, D], F32)
    qe = k.sb("p4_qe", [128, D], F32)
    qb2 = k.sb("p4_qb2", [128, D], BF16)
    qtev = k.sb("p4_qtev", [128, D], BF16)
    vsb = k.sb("p4_vsb", [128, D], BF16)
    tmp = {"bn": k.sb("p4_bn", [128, 12], F32), "mv": k.sb("p4_mv", [128, 2], F32),
           "t1": k.sb("p4_t1", [128, 1], F32), "t2": k.sb("p4_t2", [128, 1], F32)}

    for nm, w in (("wq_bf", wq), ("wk_bf", wk), ("wv_bf", wv)):
        s.inc(nc.sync.dma_start(w[:], t[nm].rearrange("c h p n -> p c h n")), "p4w", 16)
    s.inc(nc.sync.dma_start(bias_f[:], t["biases"]), "p4w", 16)
    s.inc(nc.sync.dma_start(ident[:], t["ident"]), "p4w", 16)
    s.wait(nc.vector, "p4w", s.val("p4w"))
    nc.vector.memset(onesr[:], 1.0)
    nc.vector.tensor_copy(bias_bf[:], bias_f[:])
    wrdy = s.inc(nc.vector.drain(), "p4v", 1)
    s.wait(nc.tensor, "p4v", wrdy)

    pb = k.pb
    xs_free = 0
    rope_free = 0

    def selfwait(eng=nc.vector, name="p4sw"):
        v = s.inc(eng.nop(), name, 1)
        s.wait(eng, name, v)

    for g in range(NGRP):
        for bi in range(4):
            blk = g * 4 + bi
            s.wait(nc.scalar, "p4v", xs_free)
            lv = s.inc(nc.scalar.dma_start(xs[:], t["s3"][bass.ts(blk, 128), :]), "p4xs", 16)
            s.wait(nc.vector, "p4xs", lv)
            s.wait(nc.vector, "p4t", s.val("p4t"))
            s.wait(nc.vector, "p4xo", s.val("p4xo"))
            _ln_plain(k, None, xs, xn, tmp)
            selfwait()
            nc.vector.tensor_copy(xnb[:], xn[:])
            xv = xs_free = s.inc(nc.vector.drain(), "p4v", 1)
            s.wait(nc.scalar, "p4v", xv)
            s.inc(nc.scalar.dma_start(t["xn"][bass.ts(blk, 128), :], xn[:]), "p4xo", 16)
            s.wait(nc.tensor, "p4v", xv)
            s.wait(nc.tensor, "p4tv", s.val("p4tv"))
            for c in range(8):
                mm = nc.tensor.transpose(k.pt[:, bass.ts(c, 128)],
                                         xnb[:, bass.ts(c, 128)], ident[:])
            tv = s.inc(mm, "p4t", 1)
            tv = s.inc(nc.tensor.nop(), "p4t", 1)
            s.wait(nc.vector, "p4t", tv)
            if bi == 0:
                s.wait(nc.vector, "mm", s.val("mm"))
            nc.vector.tensor_copy(xnT[:, :, bi * 128:(bi + 1) * 128],
                                  k.pt[:].rearrange("p (c n) -> p c n", c=8))
            s.inc(nc.vector.drain(), "p4tv", 1)
        xnT_rdy = s.val("p4tv")
        s.wait(nc.tensor, "p4tv", xnT_rdy)

        for bi in range(4):
            blk = g * 4 + bi
            s.wait(nc.tensor, "p4e", s.val("p4e"))
            for half in range(2):
                hp = pb[half][:]
                for c in range(8):
                    nc.tensor.matmul(hp, xnT[:, c, bass.ts(bi, 128)],
                                     wv[:, c, half, :], start=(c == 0), stop=False)
                mm = nc.tensor.matmul(hp, onesr[:],
                                      bias_bf[0:1, 2048 + half * 512:2048 + half * 512 + 512],
                                      start=False, stop=True)
            mv_ = s.inc(mm, "mm", 1)
            s.wait(nc.vector, "mm", mv_)
            s.wait(nc.vector, "p4vo", s.val("p4vo"))
            nc.vector.tensor_copy(vsb[:, 0:512], pb[0][:])
            nc.vector.tensor_copy(vsb[:, 512:1024], pb[1][:])
            ev = s.inc(nc.vector.drain(), "p4e", 1)
            s.wait(nc.gpsimd, "p4e", ev)
            sd = nc.gpsimd.dma_start(
                t["vh"][:, blk].rearrange("h p c -> p h c"),
                vsb.rearrange("p (h c) -> p h c", h=H))
            s.inc(sd, "p4vo", 16)

        for w, boff, dstname in ((wq, 0, "qT"), (wk, 1024, "kT")):
            for bi in range(4):
                blk = g * 4 + bi
                s.wait(nc.scalar, "p4v", rope_free)
                s.inc(nc.scalar.dma_start(ctile[:], t["cos_rep"][bass.ts(blk, 128), :]),
                      "p4xc", 16)
                lv2 = s.inc(nc.scalar.dma_start(stile[:], t["sinS_rep"][bass.ts(blk, 128), :]),
                            "p4xd", 16)
                s.wait(nc.tensor, "p4e", s.val("p4e"))
                for half in range(2):
                    hp = pb[half][:]
                    for c in range(8):
                        nc.tensor.matmul(hp, xnT[:, c, bass.ts(bi, 128)],
                                         w[:, c, half, :], start=(c == 0), stop=False)
                    mm = nc.tensor.matmul(hp, onesr[:],
                                          bias_bf[0:1, boff + half * 512:boff + half * 512 + 512],
                                          start=False, stop=True)
                mv_ = s.inc(mm, "mm", 1)
                s.wait(nc.vector, "mm", mv_)
                s.wait(nc.vector, "p4xc", s.val("p4xc"))
                s.wait(nc.vector, "p4xd", lv2)
                nc.vector.tensor_copy(qsb[:, 0:512], pb[0][:])
                nc.vector.tensor_copy(qsb[:, 512:1024], pb[1][:])
                ev = s.inc(nc.vector.drain(), "p4e", 1)
                selfwait()
                qv = qsb.rearrange("p (h s c) -> p h s c", h=H, s=2)
                rv = qrot.rearrange("p (h s c) -> p h s c", h=H, s=2)
                nc.vector.tensor_copy(rv[:, :, 0, :], qv[:, :, 1, :])
                selfwait()
                nc.vector.tensor_copy(rv[:, :, 1, :], qv[:, :, 0, :])
                selfwait()
                nc.vector.tensor_tensor(qe[:], qsb[:], ctile[:], OP.mult)
                selfwait()
                nc.vector.tensor_tensor(qrot[:], qrot[:], stile[:], OP.mult)
                selfwait()
                nc.vector.tensor_tensor(qb2[:], qe[:], qrot[:], OP.add)
                rpv = rope_free = s.inc(nc.vector.drain(), "p4v", 1)
                s.wait(nc.tensor, "p4v", rpv)
                s.wait(nc.tensor, "p4tv", s.val("p4tv"))
                for c in range(8):
                    mm = nc.tensor.transpose(k.pt[:, bass.ts(c, 128)],
                                             qb2[:, bass.ts(c, 128)], ident[:])
                tv = s.inc(mm, "p4t", 1)
                tv = s.inc(nc.tensor.nop(), "p4t", 1)
                s.wait(nc.vector, "p4t", tv)
                s.wait(nc.vector, "p4qo", s.val("p4qo"))
                nc.vector.tensor_copy(qtev[:], k.pt[:])
                qtv = s.inc(nc.vector.drain(), "p4tv", 1)
                s.wait(nc.gpsimd, "p4tv", qtv)
                sd = nc.gpsimd.dma_start(
                    t[dstname].rearrange("(c p) n -> p c n", p=128)[:, :, bass.ts(blk, 128)],
                    qtev.rearrange("p (c n) -> p c n", c=8))
                s.inc(sd, "p4qo", 16)
    s.wait(nc.gpsimd, "p4qo", s.val("p4qo"))
    s.wait(nc.gpsimd, "p4vo", s.val("p4vo"))
    s.wait(nc.scalar, "p4xo", s.val("p4xo"))
    k.pop_scope()


def emit_attention(k: Kb, t, mode, scale, tag):
    k.push_scope()
    nc, s = k.nc, k.s
    qt = k.sb(tag + "_qt", [64, 2, NT], BF16)
    kt = k.sb(tag + "_kt", [64, 2, NT], BF16)
    vaug = [k.sb(f"{tag}_va{i}", [128, NBLK, 65], BF16) for i in range(2)]
    eb = [k.sb(f"{tag}_eb{i}", [128, 1024], BF16) for i in range(3)]
    osb = k.sb(tag + "_osb", [128, 1024], BF16)
    oout = k.sb(tag + "_oout", [128, 4, 64], BF16)
    rcol = k.sb(tag + "_rcol", [128, 1], F32)
    rcan = k.sb(tag + "_rcan", [128, 8], F32)
    ident = k.sb(tag + "_ident", [128, 128], BF16)
    qkld = k.sb(tag + "_qkld", [128, NBLK, 64], BF16)

    s.inc(nc.sync.dma_start(ident[:], t["ident"]), tag + "w", 16)
    s.wait(nc.vector, tag + "w", s.val(tag + "w"))
    nc.vector.memset(vaug[0][:, :, 64:65], 1.0)
    nc.vector.memset(vaug[1][:, :, 64:65], 1.0)
    ones_rdy = s.inc(nc.vector.drain(), tag + "v", 1)
    s.wait(nc.tensor, tag + "v", ones_rdy)

    if mode == "l1a":
        qflat = t["q"].rearrange("a (u c) -> (a u) c", c=64)
        kflat = t["k"].rearrange("a (u c) -> (a u) c", c=64)
        vflat = t["v"].rearrange("a (u c) -> (a u) c", c=64)
        aoflat = t["ao"].rearrange("a (u c) -> (a u) c", c=64)

    pair_lastmm = 0
    va_free = [0, 0]
    sco_free = [0, 0]
    oev = 0
    odr_cnt = 0

    for hp_i in range(8):
        if mode == "l1f":
            s.wait(nc.sync, "mm", pair_lastmm)
            ld = nc.sync.dma_start(qt[:], t["qT"][bass.ts(hp_i, 128), :]
                                   .rearrange("(g p) n -> p g n", g=2))
            s.inc(ld, tag + "q", 16)
            ld = nc.sync.dma_start(kt[:], t["kT"][bass.ts(hp_i, 128), :]
                                   .rearrange("(g p) n -> p g n", g=2))
            k_ld = s.inc(ld, tag + "q", 16)
            s.wait(nc.tensor, tag + "q", k_ld)
        else:
            s.wait(nc.tensor, tag + "pv", s.val(tag + "pv"))
            for gi in range(2):
                grp = 2 * hp_i + gi
                for which, flat, dst in (("q", qflat, qt), ("k", kflat, kt)):
                    s.wait(nc.sync, tag + "tv", s.val(tag + "tv"))
                    if gi == 0 and which == "q":
                        s.wait(nc.sync, "mm", pair_lastmm)
                    src = flat[grp * 4096:(grp + 1) * 4096, :]
                    ld = nc.sync.dma_start(qkld[:], src.rearrange("(b p) c -> p b c", p=128))
                    lv = s.inc(ld, tag + "q", 16)
                    s.wait(nc.tensor, tag + "q", lv)
                    for bt in range(4):
                        s.wait(nc.tensor, tag + "tv", s.val(tag + "tv"))
                        for ti in range(8):
                            pt_i = bt * 8 + ti
                            mm = nc.tensor.transpose(k.pt[0:64, bass.ts(ti, 128)],
                                                     qkld[:, pt_i, :], ident[:])
                        tv = s.inc(mm, tag + "t", 1)
                        tv = s.inc(nc.tensor.nop(), tag + "t", 1)
                        s.wait(nc.vector, tag + "t", tv)
                        if bt == 0 and which == "q" and gi == 0:
                            s.wait(nc.vector, "mm", pair_lastmm)
                        nc.vector.tensor_copy(
                            dst[0:64, gi, bt * 1024:(bt + 1) * 1024]
                               .rearrange("p (a n) -> p a n", a=8),
                            k.pt[0:64, :].rearrange("p (a n) -> p a n", a=8))
                        s.inc(nc.vector.drain(), tag + "tv", 1)
            s.wait(nc.tensor, tag + "tv", s.val(tag + "tv"))
        for gi in range(2):
            s.wait(nc.scalar, "mm", va_free[gi])
            if mode == "l1f":
                h = 2 * hp_i + gi
                ld = nc.scalar.dma_start(vaug[gi][:, :, 0:64],
                                         t["vh"][h].rearrange("b p c -> p b c"))
            else:
                grp = 2 * hp_i + gi
                src = vflat[grp * 4096:(grp + 1) * 4096, :]
                ld = nc.scalar.dma_start(vaug[gi][:, :, 0:64],
                                         src.rearrange("(b p) c -> p b c", p=128))
            va_free[gi] = s.inc(ld, tag + "va", 16)
        s.wait(nc.tensor, tag + "va", max(va_free))

        for qb in range(8):
            sco_free = [0, 0]
            exp_cnt = {}
            smv = {}
            s.wait(nc.tensor, tag + "v", oev)

            def esc(kb):
                sp = kb % 2
                S = (k.pb[1], k.pb[0]) if sp == 0 else (k.pb[3], k.pb[2])
                s.wait(nc.tensor, tag + "e", sco_free[sp])
                nc.tensor.matmul(S[0][:], kt[0:64, 0, bass.ts(kb, 128)],
                                 qt[0:64, 0, bass.ts(qb, 512)], start=True, stop=True)
                mm = nc.tensor.matmul(S[1][:], kt[0:64, 1, bass.ts(kb, 128)],
                                      qt[0:64, 1, bass.ts(qb, 512)], start=True, stop=True)
                smv[kb] = s.inc(mm, "mm", 1)

            def eexp(kb, wv):
                sp = kb % 2
                S = (k.pb[1], k.pb[0]) if sp == 0 else (k.pb[3], k.pb[2])
                E = eb[kb % 3]
                s.wait(nc.scalar, "mm", wv)
                nc.scalar.activation(E[:, 0:512], S[0][:], AF.Exp, scale=scale)
                ex = nc.scalar.activation(E[:, 512:1024], S[1][:], AF.Exp, scale=scale)
                sco_free[sp] = exp_cnt[kb] = s.inc(ex, tag + "e", 1)

            def epv(kb, wv, last=False):
                s.wait(nc.tensor, tag + "e", wv)
                E = eb[kb % 3]
                for gi in range(2):
                    mm = nc.tensor.matmul(k.pb[4 + gi][0:65, :], vaug[gi][:, kb, :],
                                          E[:, bass.ts(gi, 512)],
                                          start=(kb == 0), stop=last)
                return mm

            for kb in range(NBLK):
                esc(kb)
                if kb >= 1:
                    eexp(kb - 1, smv[kb])
                if kb >= 2:
                    epv(kb - 2, exp_cnt[kb - 1])
            tdr = s.inc(nc.tensor.drain(), "mm", 1)
            eexp(NBLK - 1, tdr)
            epv(NBLK - 2, exp_cnt[NBLK - 1])
            nc.scalar.drain()
            dr = s.inc(nc.scalar.activation(rcol[:], eb[(NBLK - 1) % 3][:, 1020:1021],
                                            AF.Identity), tag + "e", 1)
            sco_free[0] = sco_free[1] = dr
            epv(NBLK - 1, dr, last=True)
            fence = nc.tensor.matmul(k.pg[0:1, 100:101], vaug[0][:, 0, 0:1],
                                     vaug[0][:, 0, 0:1], start=True, stop=True)
            pv_v = s.inc(fence, "mm", 1)
            va_free = [pv_v, pv_v]
            pair_lastmm = pv_v

            s.wait(nc.vector, "mm", pv_v)
            s.wait(nc.vector, tag + "t", s.val(tag + "t"))
            nc.vector.tensor_copy(osb[64:128, 0:512], osb[64:128, 0:512])
            nc.vector.tensor_copy(osb[0:65, 0:512], k.pb[4][0:65, :])
            nc.vector.tensor_copy(osb[0:65, 512:1024], k.pb[5][0:65, :])
            nc.vector.drain()
            nc.vector.tensor_copy(rcan[:], osb[0:128, 1016:1024])
            oev = s.inc(nc.vector.drain(), tag + "v", 1)
            for gi in range(2):
                s.wait(nc.tensor, tag + "v", oev)
                s.wait(nc.tensor, tag + "pv", odr_cnt)
                for ti in range(4):
                    mm = nc.tensor.transpose(k.pt[:, ti * 68:ti * 68 + 65],
                                             osb[0:65, gi * 512 + ti * 128:gi * 512 + (ti + 1) * 128],
                                             ident[0:65, 0:65])
                ptv = s.inc(mm, tag + "t", 1)
                ptv = s.inc(nc.tensor.nop(), tag + "t", 1)
                s.wait(nc.vector, tag + "t", ptv)
                s.wait(nc.vector, tag + "o", s.val(tag + "o"))
                for ti in range(4):
                    tile = k.pt[:, ti * 68:ti * 68 + 65]
                    v1 = s.inc(nc.vector.reciprocal(rcol[:], tile[:, 64:65]), tag + "v", 1)
                    s.wait(nc.vector, tag + "v", v1)
                    nc.vector.tensor_scalar(oout[:, ti, :], tile[:, 0:64], rcol[:],
                                            None, OP.mult)
                odr_cnt = s.inc(nc.vector.drain(), tag + "pv", 1)
                s.wait(nc.gpsimd, tag + "pv", odr_cnt)
                if mode == "l1f":
                    h = 2 * hp_i + gi
                    dst = t["ao"][qb * 512:(qb + 1) * 512, 64 * h:64 * h + 64]
                else:
                    grp = 2 * hp_i + gi
                    dst = aoflat[grp * 4096 + qb * 512: grp * 4096 + (qb + 1) * 512, :]
                sd = nc.gpsimd.dma_start(dst.rearrange("(ti p) c -> p ti c", p=128), oout[:])
                s.inc(sd, tag + "o", 16)
    s.wait(nc.gpsimd, tag + "o", s.val(tag + "o"))
    k.pop_scope()


def emit_l1f_x1gate(k: Kb, t):
    k.push_scope()
    nc, s = k.nc, k.s
    wo = k.sb("p5_wo", [128, 8, 2, 512], BF16)
    wg = k.sb("p5_wg", [128, 8, 8], BF16)
    bias_f = k.sb("p5_biasf", [1, 4096], F32)
    bias_bf = k.sb("p5_biasbf", [1, 4096], BF16)
    bg_f = k.sb("p5_bgf", [1, 8], F32)
    bg_bf = k.sb("p5_bgbf", [1, 8], BF16)
    onesr = k.sb("p5_ones", [1, 128], BF16)
    ident = k.sb("p5_ident", [128, 128], BF16)
    g1r = k.sb("p5_g1r", [128, D], F32)
    be1r = k.sb("p5_be1r", [128, D], F32)
    aob = k.sb("p5_aob", [128, D], BF16)
    aoT = k.sb("p5_aoT", [128, 8, 128], BF16)
    xnb = k.sb("p5_xnb", [128, D], F32)
    y1 = k.sb("p5_y1", [128, D], F32)
    x1f = k.sb("p5_x1f", [128, D], F32)
    x1b = k.sb("p5_x1b", [128, D], BF16)
    x1T = k.sb("p5_x1T", [128, 8, 128], BF16)
    pe_ = k.sb("p5_pe", [128, 8], F32)
    m1 = k.sb("p5_m1", [128, 1], F32)
    nm1 = k.sb("p5_nm1", [128, 1], F32)
    pm = k.sb("p5_pm", [128, 8], F32)
    m2 = k.sb("p5_m2", [128, 1], F32)
    s12 = k.sb("p5_s12", [128, 1], F32)
    rs = k.sb("p5_rs", [128, 1], F32)
    mask1 = k.sb("p5_mask1", [128, 8], F32)
    mask2 = k.sb("p5_mask2", [128, 8], F32)
    combv = k.sb("p5_comb", [128, 8], F32)
    tmp = {"bn": k.sb("p5_bn", [128, 12], F32), "mv": k.sb("p5_mv", [128, 2], F32),
           "t1": k.sb("p5_t1", [128, 1], F32), "t2": k.sb("p5_t2", [128, 1], F32)}

    s.inc(nc.sync.dma_start(wo[:], t["wo_bf"].rearrange("c h p n -> p c h n")), "p5w", 16)
    s.inc(nc.sync.dma_start(wg[:], t["wg_bf"].rearrange("c p n -> p c n")), "p5w", 16)
    s.inc(nc.sync.dma_start(bias_f[:], t["biases"]), "p5w", 16)
    s.inc(nc.sync.dma_start(bg_f[:], t["bg"]), "p5w", 16)
    s.inc(nc.sync.dma_start(ident[:], t["ident"]), "p5w", 16)
    s.inc(nc.sync.dma_start(g1r[:], t["g1_rep"]), "p5w", 16)
    s.inc(nc.sync.dma_start(be1r[:], t["be1_rep"]), "p5w", 16)
    s.wait(nc.vector, "p5w", s.val("p5w"))
    nc.vector.memset(onesr[:], 1.0)
    nc.vector.tensor_copy(bias_bf[:], bias_f[:])
    nc.vector.tensor_copy(bg_bf[:], bg_f[:])
    wrdy = s.inc(nc.vector.drain(), "p5v", 1)
    s.wait(nc.tensor, "p5v", wrdy)

    def selfwait(eng=nc.vector, name="p5sw"):
        v = s.inc(eng.nop(), name, 1)
        s.wait(eng, name, v)

    aob_free = 0
    xnb_free = 0
    for blk in range(NBLK):
        g, bi = blk // 4, blk % 4
        s.wait(nc.scalar, "p5t", aob_free)
        lv = s.inc(nc.scalar.dma_start(aob[:], t["ao"][bass.ts(blk, 128), :]), "p5xa", 16)
        s.wait(nc.scalar, "p5e", xnb_free)
        lvx = s.inc(nc.scalar.dma_start(xnb[:], t["xn"][bass.ts(blk, 128), :]), "p5xb", 16)
        s.wait(nc.tensor, "p5xa", lv)
        s.wait(nc.tensor, "p5tv", s.val("p5tv"))
        for c in range(8):
            mm = nc.tensor.transpose(k.pt[:, bass.ts(c, 128)], aob[:, bass.ts(c, 128)],
                                     ident[:])
        tv = aob_free = s.inc(mm, "p5t", 1)
        tv = s.inc(nc.tensor.nop(), "p5t", 1)
        s.wait(nc.vector, "p5t", tv)
        s.wait(nc.vector, "mm", s.val("mm"))
        nc.vector.tensor_copy(aoT[:], k.pt[:].rearrange("p (c n) -> p c n", c=8))
        aotv = s.inc(nc.vector.drain(), "p5tv", 1)
        s.wait(nc.tensor, "p5tv", aotv)
        s.wait(nc.tensor, "p5e", s.val("p5e"))
        for half in range(2):
            hp = k.pb[half][:]
            for c in range(8):
                nc.tensor.matmul(hp, aoT[:, c, :], wo[:, c, half, :],
                                 start=(c == 0), stop=False)
            mm = nc.tensor.matmul(hp, onesr[:],
                                  bias_bf[0:1, 3072 + half * 512:3072 + half * 512 + 512],
                                  start=False, stop=True)
        mv_ = s.inc(mm, "mm", 1)
        s.wait(nc.vector, "mm", mv_)
        s.wait(nc.vector, "p5xb", lvx)
        nc.vector.tensor_tensor(y1[:, 0:512], k.pb[0][:], xnb[:, 0:512], OP.add)
        nc.vector.tensor_tensor(y1[:, 512:1024], k.pb[1][:], xnb[:, 512:1024], OP.add)
        ev = xnb_free = s.inc(nc.vector.drain(), "p5e", 1)
        selfwait()
        s.wait(nc.vector, "p5xo", s.val("p5xo"))
        _ln_plain(k, None, y1, x1f, tmp, w_rep=g1r, b_rep=be1r)
        selfwait()
        nc.vector.tensor_copy(x1b[:], x1f[:])
        x1v = s.inc(nc.vector.drain(), "p5v", 1)
        s.wait(nc.scalar, "p5v", x1v)
        s.inc(nc.scalar.dma_start(t["x1"][bass.ts(blk, 128), :], x1f[:]), "p5xo", 16)
        s.wait(nc.tensor, "p5v", x1v)
        s.wait(nc.tensor, "p5tv", s.val("p5tv"))
        for c in range(8):
            mm = nc.tensor.transpose(k.pt[:, bass.ts(c, 128)], x1b[:, bass.ts(c, 128)],
                                     ident[:])
        tv = s.inc(mm, "p5t", 1)
        tv = s.inc(nc.tensor.nop(), "p5t", 1)
        s.wait(nc.vector, "p5t", tv)
        s.wait(nc.vector, "p5to", s.val("p5to"))
        nc.vector.tensor_copy(x1T[:], k.pt[:].rearrange("p (c n) -> p c n", c=8))
        x1tv = s.inc(nc.vector.drain(), "p5tv", 1)
        s.wait(nc.gpsimd, "p5tv", x1tv)
        sd = nc.gpsimd.dma_start(t["x1t"][g][:, :, bass.ts(bi, 128)].rearrange("c p n -> p c n"),
                                 x1T[:])
        s.inc(sd, "p5to", 16)
        s.wait(nc.tensor, "p5tv", x1tv)
        s.wait(nc.tensor, "p5g", s.val("p5g"))
        for c in range(8):
            nc.tensor.matmul(k.pg[:, 0:8], x1T[:, c, :], wg[:, c, :],
                             start=(c == 0), stop=False)
        mm = nc.tensor.matmul(k.pg[:, 0:8], onesr[:], bg_bf[:], start=False, stop=True)
        gv = s.inc(mm, "mm", 1)
        s.wait(nc.vector, "mm", gv)
        v1 = s.inc(nc.vector.tensor_reduce(m1[:], k.pg[:, 0:8], AX.X, OP.max), "p5g", 1)
        s.wait(nc.vector, "p5g", v1)
        v1 = s.inc(nc.vector.tensor_scalar(nm1[:], m1[:], -1.0, None, OP.mult), "p5g", 1)
        s.wait(nc.scalar, "p5g", v1)
        ex = nc.scalar.activation(pe_[:], k.pg[:, 0:8], AF.Exp, bias=nm1[:], scale=1.0)
        sv = s.inc(nc.scalar.drain(), "p5g", 1)
        s.wait(nc.vector, "p5g", sv)
        v1 = s.inc(nc.vector.tensor_reduce(m1[:], pe_[:], AX.X, OP.max), "p5g", 1)
        s.wait(nc.vector, "p5g", v1)
        v1 = s.inc(nc.vector.tensor_scalar(mask1[:], pe_[:], m1[:], None, OP.is_ge), "p5g", 1)
        s.wait(nc.vector, "p5g", v1)
        v1 = s.inc(nc.vector.scalar_tensor_tensor(pm[:], mask1[:], -1e9, pe_[:],
                                                  OP.mult, OP.add), "p5g", 1)
        s.wait(nc.vector, "p5g", v1)
        v1 = s.inc(nc.vector.tensor_reduce(m2[:], pm[:], AX.X, OP.max), "p5g", 1)
        s.wait(nc.vector, "p5g", v1)
        v1 = s.inc(nc.vector.tensor_tensor(s12[:], m1[:], m2[:], OP.add), "p5g", 1)
        s.wait(nc.vector, "p5g", v1)
        v1 = s.inc(nc.vector.reciprocal(rs[:], s12[:]), "p5g", 1)
        s.wait(nc.vector, "p5g", v1)
        v1 = s.inc(nc.vector.tensor_scalar(mask2[:], pe_[:], m2[:], None, OP.is_ge), "p5g", 1)
        s.wait(nc.vector, "p5g", v1)
        s.wait(nc.vector, "p5go", s.val("p5go"))
        v1 = s.inc(nc.vector.scalar_tensor_tensor(combv[:], pe_[:], rs[:], mask2[:],
                                                  OP.mult, OP.mult), "p5g", 1)
        cdr = s.inc(nc.vector.drain(), "p5gd", 1)
        s.wait(nc.gpsimd, "p5gd", cdr)
        sd = nc.gpsimd.dma_start(t["combt"][:, blk, :], combv[:])
        s.inc(sd, "p5go", 16)
    s.wait(nc.gpsimd, "p5go", s.val("p5go"))
    s.wait(nc.gpsimd, "p5to", s.val("p5to"))
    s.wait(nc.scalar, "p5xo", s.val("p5xo"))
    k.pop_scope()


def emit_moe_final(k: Kb, t, out_key="out"):
    k.push_scope()
    nc, s = k.nc, k.s
    ident = k.sb("p6_ident", [128, 128], BF16)
    g2r = k.sb("p6_g2r", [128, D], F32)
    be2r = k.sb("p6_be2r", [128, D], F32)
    b2f = k.sb("p6_b2f", [128, D], F32)
    b2b = k.sb("p6_b2b", [128, D], BF16)
    comb = k.sb("p6_comb", [128, NBLK, 8], F32)
    cbb = k.sb("p6_cbb", [128, 8], BF16)
    cbT = k.sb("p6_cbT", [128, 128], BF16)
    ya = k.sb("p6_ya", [128, D], F32)
    x1f = k.sb("p6_x1f", [128, D], F32)
    s3b = k.sb("p6_s3b", [128, D], F32)
    yt = k.sb("p6_yt", [128, D], F32)
    f2 = k.sb("p6_f2", [128, D], F32)
    f3 = k.sb("p6_f3", [128, D], F32)
    ot = k.sb("p6_ot", [128, D], F32)
    tmp = {"bn": k.sb("p6_bn", [128, 12], F32), "mv": k.sb("p6_mv", [128, 2], F32),
           "t1": k.sb("p6_t1", [128, 1], F32), "t2": k.sb("p6_t2", [128, 1], F32)}

    s.inc(nc.sync.dma_start(ident[:], t["ident"]), "p6w", 16)
    s.inc(nc.sync.dma_start(g2r[:], t["g2_rep"]), "p6w", 16)
    s.inc(nc.sync.dma_start(be2r[:], t["be2_rep"]), "p6w", 16)
    s.inc(nc.sync.dma_start(b2f[0:8, :], t["b2"]), "p6w", 16)
    s.inc(nc.sync.dma_start(comb[:], t["combt"]), "p6w", 16)
    s.wait(nc.vector, "p6w", s.val("p6w"))
    nc.vector.tensor_copy(b2b[:], b2f[:])
    wrdy = s.inc(nc.vector.drain(), "p6v", 1)
    s.wait(nc.tensor, "p6v", wrdy)

    def selfwait():
        v = s.inc(nc.vector.nop(), "p6sw", 1)
        s.wait(nc.vector, "p6sw", v)

    in_free = 0
    for blk in range(NBLK):
        s.wait(nc.scalar, "p6v", in_free)
        lv = s.inc(nc.scalar.dma_start(ya[:], t["yacc"][bass.ts(blk, 128), :]), "p6x", 16)
        lv = s.inc(nc.scalar.dma_start(x1f[:], t["x1"][bass.ts(blk, 128), :]), "p6x", 16)
        lv = s.inc(nc.scalar.dma_start(s3b[:], t["s3"][bass.ts(blk, 128), :]), "p6x", 16)
        s.wait(nc.vector, "mm", s.val("mm"))
        nc.vector.tensor_copy(cbb[:], comb[:, blk, :])
        cv = s.inc(nc.vector.drain(), "p6v", 1)
        s.wait(nc.tensor, "p6v", cv)
        s.wait(nc.tensor, "p6tv", s.val("p6tv"))
        mm = nc.tensor.transpose(k.pt[0:8, 0:128], cbb[:], ident[:])
        tv = s.inc(mm, "p6t", 1)
        tv = s.inc(nc.tensor.nop(), "p6t", 1)
        s.wait(nc.vector, "p6t", tv)
        nc.vector.tensor_copy(cbT[0:8, :], k.pt[0:8, 0:128])
        ctv = s.inc(nc.vector.drain(), "p6tv", 1)
        s.wait(nc.tensor, "p6tv", ctv)
        s.wait(nc.tensor, "p6e", s.val("p6e"))
        for half in range(2):
            mm = nc.tensor.matmul(k.pb[half][:], cbT[0:8, :], b2b[0:8, bass.ts(half, 512)],
                                  start=True, stop=True)
        mv_ = s.inc(mm, "mm", 1)
        s.wait(nc.vector, "mm", mv_)
        s.wait(nc.vector, "p6x", lv)
        nc.vector.tensor_tensor(yt[:, 0:512], k.pb[0][:], ya[:, 0:512], OP.add)
        nc.vector.tensor_tensor(yt[:, 512:1024], k.pb[1][:], ya[:, 512:1024], OP.add)
        ev = s.inc(nc.vector.drain(), "p6e", 1)
        selfwait()
        nc.vector.tensor_tensor(yt[:], yt[:], x1f[:], OP.add)
        selfwait()
        _ln_plain(k, None, yt, f2, tmp, w_rep=g2r, b_rep=be2r)
        selfwait()
        _ln_plain(k, None, f2, f3, tmp)
        selfwait()
        s.wait(nc.vector, "p6o", s.val("p6o"))
        nc.vector.tensor_tensor(ot[:], f3[:], s3b[:], OP.add)
        ov = in_free = s.inc(nc.vector.drain(), "p6v", 1)
        s.wait(nc.gpsimd, "p6v", ov)
        sd = nc.gpsimd.dma_start(t[out_key][bass.ts(blk, 128), :], ot[:])
        s.inc(sd, "p6o", 16)
    s.wait(nc.gpsimd, "p6o", s.val("p6o"))
    k.pop_scope()


def emit_l1a_qkv(k: Kb, t):
    k.push_scope()
    nc, s = k.nc, k.s
    wq = k.sb("p3_wq", [128, 8, 2, 512], BF16)
    wk = k.sb("p3_wk", [128, 8, 2, 512], BF16)
    wv = k.sb("p3_wv", [128, 8, 2, 512], BF16)
    ident = k.sb("p3_ident", [128, 128], BF16)
    xs = k.sb("p3_xs", [128, D], F32)
    xn = k.sb("p3_xn", [128, D], F32)
    xnb = k.sb("p3_xnb", [128, D], BF16)
    xnT = k.sb("p3_xnT", [128, 8, 512], BF16)
    osb = [k.sb(f"p3_o{i}", [128, D], BF16) for i in range(3)]
    tmp = {"bn": k.sb("p3_bn", [128, 12], F32), "mv": k.sb("p3_mv", [128, 2], F32),
           "t1": k.sb("p3_t1", [128, 1], F32), "t2": k.sb("p3_t2", [128, 1], F32)}

    for nm, w in (("wqT_bf", wq), ("wkT_bf", wk), ("wvT_bf", wv)):
        s.inc(nc.sync.dma_start(w[:], t[nm].rearrange("c h p n -> p c h n")), "p3w", 16)
    s.inc(nc.sync.dma_start(ident[:], t["ident"]), "p3w", 16)
    s.wait(nc.tensor, "p3w", s.val("p3w"))

    xs_free = 0
    o_free = [0, 0, 0]
    for g in range(NGRP):
        for bi in range(4):
            blk = g * 4 + bi
            s.wait(nc.scalar, "p3v", xs_free)
            lv = s.inc(nc.scalar.dma_start(xs[:], t["s2"][bass.ts(blk, 128), :]), "p3x", 16)
            s.wait(nc.vector, "p3x", lv)
            s.wait(nc.vector, "p3t", s.val("p3t"))
            s.wait(nc.vector, "p3xo", s.val("p3xo"))
            _ln_plain(k, None, xs, xn, tmp)
            v = s.inc(nc.vector.nop(), "p3v", 1)
            s.wait(nc.vector, "p3v", v)
            nc.vector.tensor_copy(xnb[:], xn[:])
            xv = xs_free = s.inc(nc.vector.drain(), "p3v", 1)
            s.wait(nc.scalar, "p3v", xv)
            s.inc(nc.scalar.dma_start(t["xn2"][bass.ts(blk, 128), :], xn[:]), "p3xo", 16)
            s.wait(nc.tensor, "p3v", xv)
            s.wait(nc.tensor, "p3tv", s.val("p3tv"))
            for c in range(8):
                mm = nc.tensor.transpose(k.pt[:, bass.ts(c, 128)],
                                         xnb[:, bass.ts(c, 128)], ident[:])
            tv = s.inc(mm, "p3t", 1)
            tv = s.inc(nc.tensor.nop(), "p3t", 1)
            s.wait(nc.vector, "p3t", tv)
            if bi == 0:
                s.wait(nc.vector, "mm", s.val("mm"))
            nc.vector.tensor_copy(xnT[:, :, bi * 128:(bi + 1) * 128],
                                  k.pt[:].rearrange("p (c n) -> p c n", c=8))
            s.inc(nc.vector.drain(), "p3tv", 1)
        s.wait(nc.tensor, "p3tv", s.val("p3tv"))
        for wi, (w, dstname) in enumerate(((wq, "q"), (wk, "k"), (wv, "v"))):
            for bi in range(4):
                blk = g * 4 + bi
                oi = (wi * 4 + bi) % 3
                s.wait(nc.tensor, "p3e", s.val("p3e"))
                for half in range(2):
                    hp = k.pb[half][:]
                    for c in range(8):
                        mm = nc.tensor.matmul(hp, xnT[:, c, bass.ts(bi, 128)],
                                              w[:, c, half, :],
                                              start=(c == 0), stop=(c == 7))
                mv_ = s.inc(mm, "mm", 1)
                s.wait(nc.vector, "mm", mv_)
                s.wait(nc.vector, f"p3o{oi}", o_free[oi])
                nc.vector.tensor_copy(osb[oi][:, 0:512], k.pb[0][:])
                nc.vector.tensor_copy(osb[oi][:, 512:1024], k.pb[1][:])
                ev = s.inc(nc.vector.drain(), "p3e", 1)
                s.wait(nc.gpsimd, "p3e", ev)
                sd = nc.gpsimd.dma_start(t[dstname][bass.ts(blk, 128), :], osb[oi][:])
                o_free[oi] = s.inc(sd, f"p3o{oi}", 16)
    for oi in range(3):
        s.wait(nc.gpsimd, f"p3o{oi}", s.val(f"p3o{oi}"))
    s.wait(nc.scalar, "p3xo", s.val("p3xo"))
    k.pop_scope()


def emit_l1a_norm(k: Kb, t):
    k.push_scope()
    nc, s = k.nc, k.s
    aob = [k.sb(f"p3n_ao{i}", [128, D], BF16) for i in range(2)]
    s2b = [k.sb(f"p3n_s2{i}", [128, D], F32) for i in range(2)]
    yb = [k.sb(f"p3n_y{i}", [128, D], F32) for i in range(2)]
    part = k.sb("p3n_part", [128, 2], F32)
    tpart = k.sb("p3n_t", [128, 2], F32)
    sq = k.sb("p3n_sq", [128, D], F32)
    onesc = k.sb("p3n_ones", [128, 1], F32)
    musd = k.sb("p3n_musd", [128, 2], F32)
    sc = k.sb("p3n_sc", [1, 4], F32)
    prow = k.sb("p3n_prow", [1, 256], F32)
    onesr = k.sb("p3n_onesr", [1, 128], F32)

    def vsw():
        v = s.inc(nc.vector.nop(), "p3n_v", 1)
        s.wait(nc.vector, "p3n_v", v)

    nc.vector.memset(part[:], 0.0)
    nc.vector.memset(onesc[:], 1.0)
    nc.vector.memset(onesr[:], 1.0)
    vsw()

    ld_free = [0, 0]
    for blk in range(NBLK):
        j = blk % 2
        s.wait(nc.scalar, "p3n_c", ld_free[j])
        s.inc(nc.scalar.dma_start(aob[j][:], t["ao"][bass.ts(blk, 128), :]), f"p3n_l{j}", 16)
        lv = s.inc(nc.scalar.dma_start(s2b[j][:], t["xn2"][bass.ts(blk, 128), :]), f"p3n_l{j}", 16)
        s.wait(nc.vector, f"p3n_l{j}", lv)
        s.wait(nc.vector, f"p3n_s{j}", s.val(f"p3n_s{j}"))
        nc.vector.tensor_tensor(yb[j][:], aob[j][:], s2b[j][:], OP.add)
        yv = s.inc(nc.vector.drain(), "p3n_y", 1)
        v = s.inc(nc.vector.reduce_sum(tpart[:, 0:1], yb[j][:], AX.X), "p3n_v", 1)
        s.wait(nc.scalar, "p3n_y", yv)
        sv = s.inc(nc.scalar.activation(sq[:], yb[j][:], AF.Square,
                                        accum_out=tpart[:, 1:2]), "p3n_sc", 1)
        s.wait(nc.vector, "p3n_sc", sv)
        s.wait(nc.vector, "p3n_v", v)
        v = s.inc(nc.vector.tensor_tensor(part[:], part[:], tpart[:], OP.add), "p3n_v", 1)
        s.wait(nc.vector, "p3n_v", v)
        ld_free[j] = s.inc(nc.vector.nop(), "p3n_c", 1)
        s.wait(nc.gpsimd, "p3n_c", ld_free[j])
        sd = nc.gpsimd.dma_start(t["yg"][bass.ts(blk, 128), :], yb[j][:])
        s.inc(sd, f"p3n_s{j}", 16)

    cv = s.inc(nc.vector.drain(), "p3n_v", 1)
    s.wait(nc.gpsimd, "p3n_v", cv)
    s.inc(nc.gpsimd.dma_start(t["gscr"][0:1, 0:128].rearrange("a b -> b a"),
                              part[:, 0:1]), "p3n_g1", 16)
    sd = s.inc(nc.gpsimd.dma_start(t["gscr"][0:1, 128:256].rearrange("a b -> b a"),
                                   part[:, 1:2]), "p3n_g1", 16)
    s.wait(nc.sync, "p3n_g1", sd)
    sr = s.inc(nc.sync.dma_start(prow[:], t["gscr"][0:1, 0:256]), "p3n_g2", 16)
    s.wait(nc.vector, "p3n_g2", sr)
    NTD = float(NT * D)
    v = s.inc(nc.vector.reduce_sum(sc[0:1, 0:1], prow[0:1, 0:128], AX.X), "p3n_v", 1)
    s.wait(nc.vector, "p3n_v", v)
    v = s.inc(nc.vector.reduce_sum(sc[0:1, 1:2], prow[0:1, 128:256], AX.X), "p3n_v", 1)
    s.wait(nc.vector, "p3n_v", v)
    v = s.inc(nc.vector.tensor_scalar(sc[0:1, 0:2], sc[0:1, 0:2], 1.0 / NTD, None,
                                      OP.mult), "p3n_v", 1)
    s.wait(nc.vector, "p3n_v", v)
    v = s.inc(nc.vector.tensor_tensor(sc[0:1, 2:3], sc[0:1, 0:1], sc[0:1, 0:1],
                                      OP.mult), "p3n_v", 1)
    s.wait(nc.vector, "p3n_v", v)
    v = s.inc(nc.vector.tensor_tensor(sc[0:1, 3:4], sc[0:1, 1:2], sc[0:1, 2:3],
                                      OP.subtract), "p3n_v", 1)
    s.wait(nc.vector, "p3n_v", v)
    v = s.inc(nc.vector.tensor_scalar(sc[0:1, 3:4], sc[0:1, 3:4], 1e-5, None, OP.add),
              "p3n_v", 1)
    s.wait(nc.scalar, "p3n_v", v)
    sv = s.inc(nc.scalar.activation(sc[0:1, 3:4], sc[0:1, 3:4], AF.Sqrt), "p3n_sc", 1)
    s.wait(nc.vector, "p3n_sc", sv)
    v = s.inc(nc.vector.reciprocal(sc[0:1, 1:2], sc[0:1, 3:4]), "p3n_v", 1)
    s.wait(nc.vector, "p3n_v", v)
    bv = s.inc(nc.vector.drain(), "p3n_v", 1)
    s.wait(nc.gpsimd, "p3n_v", bv)
    sd = s.inc(nc.gpsimd.dma_start(t["gscr"][0:1, 256:258], sc[0:1, 0:2]), "p3n_g1", 16)
    s.wait(nc.sync, "p3n_g1", sd)
    bsrc = bass.AP(t["gscr"].tensor, 256, [[0, 128], [1, 2]])
    sr = s.inc(nc.sync.dma_start(musd[:], bsrc), "p3n_g2", 16)
    s.wait(nc.vector, "p3n_g2", sr)

    st_free = [0, 0]
    tmp2 = {"bn": k.sb("p3n_bn", [128, 12], F32), "mv": k.sb("p3n_mv", [128, 2], F32),
            "t1": k.sb("p3n_t1", [128, 1], F32), "t2": k.sb("p3n_t2", [128, 1], F32)}
    for blk in range(NBLK):
        j = blk % 2
        s.wait(nc.scalar, "p3n_v", st_free[j])
        lv = s.inc(nc.scalar.dma_start(yb[j][:], t["yg"][bass.ts(blk, 128), :]),
                   f"p3n_l2{j}", 16)
        lv = s.inc(nc.scalar.dma_start(aob[j].bitcast(F32) if False else s2b[j][:],
                                       t["s2"][bass.ts(blk, 128), :]), f"p3n_l2{j}", 16)
        s.wait(nc.vector, f"p3n_l2{j}", lv)
        s.wait(nc.vector, f"p3n_o{j}", s.val(f"p3n_o{j}"))
        nc.vector.tensor_scalar(yb[j][:], yb[j][:], musd[:, 0:1], musd[:, 1:2],
                                OP.subtract, OP.mult)
        gv = s.inc(nc.vector.drain(), "p3n_v", 1)
        s.wait(nc.vector, "p3n_v", gv)
        _ln_plain(k, None, yb[j], sq, tmp2)
        gv = s.inc(nc.vector.nop(), "p3n_v", 1)
        s.wait(nc.vector, "p3n_v", gv)
        nc.vector.tensor_tensor(sq[:], sq[:], s2b[j][:], OP.add)
        ov = s.inc(nc.vector.drain(), "p3n_v", 1)
        st_free[j] = ov
        s.wait(nc.gpsimd, "p3n_v", ov)
        sd = nc.gpsimd.dma_start(t["s3"][bass.ts(blk, 128), :], sq[:])
        s.inc(sd, f"p3n_o{j}", 16)
    for j in range(2):
        s.wait(nc.gpsimd, f"p3n_o{j}", s.val(f"p3n_o{j}"))
        s.wait(nc.gpsimd, f"p3n_s{j}", s.val(f"p3n_s{j}"))
    k.pop_scope()


def emit_l0_quant(k: Kb, t, wkey, dstkey, R, C, tag="q", scr=None):
    k.push_scope()
    nc, s = k.nc, k.s
    NB = R // 128
    stg = [k.sb(f"{tag}_stg{i}", [128, C], F32) for i in range(2)]
    aw = k.sb(tag + "_aw", [128, C], F32)
    sg = k.sb(tag + "_sg", [128, C], F32)
    msk = k.sb(tag + "_msk", [128, C], F32)
    qb = [k.sb(f"{tag}_qb{i}", [128, C], BF16) for i in range(2)]
    acc = k.sb(tag + "_acc", [128, 2], F32)
    tpart = k.sb(tag + "_tp", [128, 2], F32)
    onesc = k.sb(tag + "_ones", [128, 1], F32)
    thr = k.sb(tag + "_thr", [128, 1], F32)
    sc1 = k.sb(tag + "_sc1", [1, 2], F32)
    sc1r = k.sb(tag + "_sc1r", [1, 128], F32)

    def vsw():
        v = s.inc(nc.vector.nop(), "q_v", 1)
        s.wait(nc.vector, "q_v", v)

    nc.vector.memset(acc[:], 0.0)
    nc.vector.memset(tpart[:], 0.0)
    nc.vector.memset(onesc[:], 1.0)
    vsw()
    ld_free = [0, 0]
    for b in range(NB):
        j = b % 2
        s.wait(nc.sync, "q_v", ld_free[j])
        lv = s.inc(nc.sync.dma_start(stg[j][:], t[wkey][bass.ts(b, 128), :]), f"q_l{j}", 16)
        s.wait(nc.vector, f"q_l{j}", lv)
        v = s.inc(nc.vector.tensor_reduce(tpart[:, 0:1], stg[j][:], AX.X, OP.add,
                                          apply_absolute_value=True), "q_v", 1)
        s.wait(nc.vector, "q_v", v)
        ld_free[j] = s.inc(nc.vector.tensor_tensor(acc[:], acc[:], tpart[:], OP.add),
                           "q_v", 1)
        s.wait(nc.vector, "q_v", ld_free[j])
    cv = s.inc(nc.vector.drain(), "q_v", 1)
    s.wait(nc.gpsimd, "q_v", cv)
    sd = s.inc(nc.gpsimd.dma_start(scr[0:1, 0:128].rearrange("a b -> b a"),
                                   acc[:, 0:1]), "q_g1", 16)
    s.wait(nc.sync, "q_g1", sd)
    sr = s.inc(nc.sync.dma_start(sc1r[:], scr[0:1, 0:128]), "q_g2", 16)
    s.wait(nc.vector, "q_g2", sr)
    v = s.inc(nc.vector.reduce_sum(sc1[0:1, 0:1], sc1r[:], AX.X), "q_v", 1)
    s.wait(nc.vector, "q_v", v)
    v = s.inc(nc.vector.tensor_scalar(sc1[0:1, 1:2], sc1[0:1, 0:1], 1.0 / (R * C), 1e-6,
                                      OP.mult, OP.add), "q_v", 1)
    s.wait(nc.vector, "q_v", v)
    v = s.inc(nc.vector.tensor_scalar(sc1[0:1, 1:2], sc1[0:1, 1:2], 0.5, None, OP.mult),
              "q_v", 1)
    bv = s.inc(nc.vector.drain(), "q_v", 1)
    s.wait(nc.gpsimd, "q_v", bv)
    sd = s.inc(nc.gpsimd.dma_start(scr[0:1, 128:129], sc1[0:1, 1:2]), "q_g1", 16)
    s.wait(nc.sync, "q_g1", sd)
    bsrc = bass.AP(scr.tensor, 128, [[0, 128], [1, 1]])
    sr = s.inc(nc.sync.dma_start(thr[:], bsrc), "q_g2", 16)
    s.wait(nc.vector, "q_g2", sr)
    ld_free = [0, 0]
    out_done = [0, 0]
    prev_qv = 0
    for b in range(NB):
        j = b % 2
        s.wait(nc.sync, "q_v", ld_free[j])
        lv = s.inc(nc.sync.dma_start(stg[j][:], t[wkey][bass.ts(b, 128), :]), f"q_l{j}", 16)
        s.wait(nc.scalar, f"q_l{j}", lv)
        s.wait(nc.scalar, "q_v", prev_qv)
        sv = s.inc(nc.scalar.activation(aw[:], stg[j][:], AF.Abs), "q_s", 1)
        sv = s.inc(nc.scalar.activation(sg[:], stg[j][:], AF.Sign), "q_s", 1)
        s.wait(nc.vector, "q_s", sv)
        v = s.inc(nc.vector.tensor_scalar(msk[:], aw[:], thr[:], None, OP.is_ge), "q_v", 1)
        ld_free[j] = v
        s.wait(nc.vector, "q_v", v)
        s.wait(nc.vector, f"q_o{j}", out_done[j])
        nc.vector.tensor_tensor(qb[j][:], msk[:], sg[:], OP.mult)
        qv = prev_qv = s.inc(nc.vector.drain(), "q_v", 1)
        s.wait(nc.gpsimd, "q_v", qv)
        sd = nc.gpsimd.dma_start(t[dstkey][b], qb[j][:])
        out_done[j] = s.inc(sd, f"q_o{j}", 16)
    for j in range(2):
        s.wait(nc.gpsimd, f"q_o{j}", s.val(f"q_o{j}"))
    k.pop_scope()


def emit_l0_ffn(k: Kb, t):
    k.push_scope()
    nc, s = k.nc, k.s
    wfc = k.sb("f0_wfc", [128, 8, FF], BF16)
    wpj = k.sb("f0_wpj", [128, FT, D], BF16)
    lnw = k.sb("f0_lnw", [128, FF], F32)
    lnb = k.sb("f0_lnb", [128, FF], F32)
    ident = k.sb("f0_ident", [128, 128], BF16)
    xs = k.sb("f0_xs", [128, D], F32)
    xnb = k.sb("f0_xnb", [128, D], BF16)
    xnT = k.sb("f0_xnT", [128, 8, 128], BF16)
    hsb = k.sb("f0_h", [128, FF], F32)
    hnb = k.sb("f0_hnb", [128, FF], BF16)
    hT = k.sb("f0_hT", [128, FT, 128], BF16)
    ot_ = k.sb("f0_ot", [128, D], F32)
    ot = [ot_, ot_]
    tmp = {"bn": k.sb("f0_bn", [128, 48], F32), "mv": k.sb("f0_mv", [128, 2], F32),
           "t1": k.sb("f0_t1", [128, 1], F32), "t2": k.sb("f0_t2", [128, 1], F32)}

    s.inc(nc.sync.dma_start(wfc[:], t["wfcq"].rearrange("c p f -> p c f")), "f0_w", 16)
    s.inc(nc.sync.dma_start(wpj[:], t["wprojq"].rearrange("f p d -> p f d")), "f0_w", 16)
    s.inc(nc.sync.dma_start(lnw[:], t["lnw_rep"]), "f0_w", 16)
    s.inc(nc.sync.dma_start(lnb[:], t["lnb_rep"]), "f0_w", 16)
    wl = s.inc(nc.sync.dma_start(ident[:], t["ident"]), "f0_w", 16)
    s.wait(nc.tensor, "f0_w", wl)
    s.wait(nc.vector, "f0_w", wl)

    xs_free = 0
    st_free = [0, 0]
    def lnfull(xin, xout, n, w_rep=None, b_rep=None):
        for i in range(n):
            mm = nc.vector.bn_stats(tmp["bn"][:, 6 * i:6 * i + 6],
                                    xin[:, 512 * i:512 * (i + 1)])
        v = s.inc(mm, "ln", 1)
        s.wait(nc.vector, "ln", v)
        v = s.inc(nc.vector.bn_aggr(tmp["mv"][:], tmp["bn"][:, 0:6 * n]), "ln", 1)
        s.wait(nc.vector, "ln", v)
        v = s.inc(nc.vector.tensor_scalar(tmp["t1"][:], tmp["mv"][:, 1:2], 1e-5, None,
                                          OP.add), "ln", 1)
        s.wait(nc.scalar, "ln", v)
        v = s.inc(nc.scalar.activation(tmp["t1"][:], tmp["t1"][:], AF.Sqrt), "ln", 1)
        s.wait(nc.vector, "ln", v)
        v = s.inc(nc.vector.reciprocal(tmp["t2"][:], tmp["t1"][:]), "ln", 1)
        s.wait(nc.vector, "ln", v)
        nc.vector.tensor_scalar(xout[:], xin[:], tmp["mv"][:, 0:1], tmp["t2"][:],
                                OP.subtract, OP.mult)
        if w_rep is not None:
            v = s.inc(nc.vector.nop(), "ln", 1)
            s.wait(nc.vector, "ln", v)
            nc.vector.tensor_tensor(xout[:], xout[:], w_rep[:], OP.mult)
            v = s.inc(nc.vector.nop(), "ln", 1)
            s.wait(nc.vector, "ln", v)
            nc.vector.tensor_tensor(xout[:], xout[:], b_rep[:], OP.add)

    for blk in range(NBLK):
        j = blk % 2
        s.wait(nc.scalar, "f0_v", xs_free)
        lv = s.inc(nc.scalar.dma_start(xs[:], t["s1"][bass.ts(blk, 128), :]), "f0_x", 16)
        s.wait(nc.vector, "f0_x", lv)
        s.wait(nc.vector, "f0_t", s.val("f0_t"))
        lnfull(xs, xnb, 2)
        xv = s.inc(nc.vector.drain(), "f0_v", 1)
        s.wait(nc.tensor, "f0_v", xv)
        s.wait(nc.tensor, "f0_tv", s.val("f0_tv"))
        for c in range(8):
            mm = nc.tensor.transpose(k.pt[:, bass.ts(c, 128)], xnb[:, bass.ts(c, 128)],
                                     ident[:])
        tv = s.inc(mm, "f0_t", 1)
        tv = s.inc(nc.tensor.nop(), "f0_t", 1)
        s.wait(nc.vector, "f0_t", tv)
        s.wait(nc.vector, "mm", s.val("mm"))
        nc.vector.tensor_copy(xnT[:], k.pt[:].rearrange("p (c n) -> p c n", c=8))
        xtv = s.inc(nc.vector.drain(), "f0_tv", 1)
        s.wait(nc.tensor, "f0_tv", xtv)
        for f8 in range(8):
            hp = k.pb[f8 % 2][:]
            s.wait(nc.tensor, "f0_e", s.val("f0_e") - 1 if f8 >= 2 or blk > 0 else 0)
            for c in range(8):
                mm = nc.tensor.matmul(hp, xnT[:, c, :], wfc[:, c, bass.ts(f8, 512)],
                                      start=(c == 0), stop=(c == 7))
            hv = s.inc(mm, "mm", 1)
            s.wait(nc.scalar, "mm", hv)
            ge = nc.scalar.activation(hsb[:, bass.ts(f8, 512)], hp, AF.Gelu)
            s.inc(ge, "f0_e", 1)
        gl = s.inc(nc.scalar.drain(), "f0_e", 1)
        s.wait(nc.vector, "f0_e", gl)
        lnfull(hsb, hsb, 8, w_rep=lnw, b_rep=lnb)
        v = s.inc(nc.vector.nop(), "f0_v", 1)
        s.wait(nc.vector, "f0_v", v)
        nc.vector.tensor_copy(hnb[:], hsb[:])
        hv2 = s.inc(nc.vector.drain(), "f0_v", 1)
        for bt in range(4):
            s.wait(nc.tensor, "f0_v", hv2)
            s.wait(nc.tensor, "f0_tv", s.val("f0_tv"))
            for ti in range(8):
                c = bt * 8 + ti
                mm = nc.tensor.transpose(k.pt[:, bass.ts(ti, 128)],
                                         hnb[:, bass.ts(c, 128)], ident[:])
            tv = s.inc(mm, "f0_t", 1)
            tv = s.inc(nc.tensor.nop(), "f0_t", 1)
            s.wait(nc.vector, "f0_t", tv)
            nc.vector.tensor_copy(hT[:, bass.ts(bt, 8), :],
                                  k.pt[:].rearrange("p (c n) -> p c n", c=8))
            s.inc(nc.vector.drain(), "f0_tv", 1)
        htv = s.val("f0_tv")
        s.wait(nc.tensor, "f0_tv", htv)
        for half in range(2):
            yp = k.pb[4 + half][:]
            for f in range(FT):
                mm = nc.tensor.matmul(yp, hT[:, f, :], wpj[:, f, bass.ts(half, 512)],
                                      start=(f == 0), stop=(f == FT - 1))
        fence = nc.tensor.matmul(k.pg[0:1, 100:101], ident[0:128, 0:1], ident[0:128, 0:1],
                                 start=True, stop=True)
        yv = s.inc(fence, "mm", 1)
        yv = s.inc(nc.tensor.nop(), "mm", 1)
        s.wait(nc.vector, "mm", yv)
        s.wait(nc.vector, "f0_o0", st_free[0])
        s.wait(nc.vector, "f0_o1", st_free[1])
        nc.vector.tensor_copy(ot[j][64:128, 0:512], ot[j][64:128, 0:512])
        nc.vector.tensor_copy(ot[j][:, 0:512], k.pb[4][:])
        nc.vector.tensor_copy(ot[j][:, 512:1024], k.pb[5][:])
        v = s.inc(nc.vector.drain(), "f0_v", 1)
        s.wait(nc.vector, "f0_v", v)
        lnfull(ot[j], ot[j], 2)
        v = s.inc(nc.vector.nop(), "f0_v", 1)
        s.wait(nc.vector, "f0_v", v)
        nc.vector.tensor_tensor(ot[j][:], ot[j][:], xs[:], OP.add)
        ov = xs_free = s.inc(nc.vector.drain(), "f0_v", 1)
        s.wait(nc.gpsimd, "f0_v", ov)
        sd = nc.gpsimd.dma_start(t["s2"][bass.ts(blk, 128), :], ot[j][:])
        st_free[j] = s.inc(sd, f"f0_o{j}", 16)
    for j in range(2):
        s.wait(nc.gpsimd, f"f0_o{j}", s.val(f"f0_o{j}"))
    k.pop_scope()


def emit_l0_attn(k: Kb, t, nchunks=NBLK):
    import os as _o
    CUT = int(_o.environ.get("CUT", "9"))
    """s1 = x0 + LN_plain(linattn(LN_plain(x0))), chunked causal-inclusive scan.

    All matmul operands at partition base 0 (base-64 operands crash the PE).
    PSUM: pb0/1 = Qp then A-slots; pb2/3 = S state (persistent, [0:64, 8, 64]
    even/odd heads); pb4/5 = Vp then O (time-shared); pt transposes; pg free.
    """
    k.push_scope()
    nc, s = k.nc, k.s
    wq = k.sb("l0_wq", [128, 8, 2, 512], BF16)
    wv = k.sb("l0_wv", [128, 8, 2, 512], BF16)
    qnw = k.sb("l0_qnw", [128, D], F32)
    vnw = k.sb("l0_vnw", [128, D], F32)
    tri = k.sb("l0_tri", [128, 128], F32)
    ident = k.sb("l0_ident", [128, 128], BF16)
    xs = k.sb("l0_xs", [128, D], F32)
    xn = k.sb("l0_xn", [128, D], F32)
    xnb = k.sb("l0_xnb", [128, D], BF16)
    xnT = k.sb("l0_xnT", [128, 8, 128], BF16)
    xnK = k.sb("l0_xnK", [64, 16, 128], BF16)
    sq = k.sb("l0_sq", [128, D], F32)
    rr = k.sb("l0_r", [128, 16], F32)
    qsc = k.sb("l0_qsc", [128, D], F32)
    qbf = k.sb("l0_qbf", [128, D], BF16)
    vbf = k.sb("l0_vbf", [128, D], BF16)
    qT = k.sb("l0_qT", [64, 16, 128], BF16)
    Ssb = k.sb("l0_Ssb", [64, 16, 64], BF16)
    Asb = k.sb("l0_Asb", [128, 16, 128], BF16)
    osb = k.sb("l0_osb", [128, D], F32)
    ot = [k.sb(f"l0_ot{i}", [128, D], F32) for i in range(2)]
    can = k.sb("l0_can", [128, 8], F32)
    tmp = {"bn": k.sb("l0_bn", [128, 12], F32), "mv": k.sb("l0_mv", [128, 2], F32),
           "t1": k.sb("l0_t1", [128, 1], F32), "t2": k.sb("l0_t2", [128, 1], F32)}

    s.inc(nc.sync.dma_start(wq[:], t["wqT_bf"].rearrange("c h p n -> p c h n")), "l0_w", 16)
    s.inc(nc.sync.dma_start(wv[:], t["wvT_bf"].rearrange("c h p n -> p c h n")), "l0_w", 16)
    s.inc(nc.sync.dma_start(qnw[:], t["qnw_rep"]), "l0_w", 16)
    s.inc(nc.sync.dma_start(vnw[:], t["vnw_rep"]), "l0_w", 16)
    s.inc(nc.sync.dma_start(tri[:], t["triT"]), "l0_w", 16)
    wl = s.inc(nc.sync.dma_start(ident[:], t["ident"]), "l0_w", 16)
    s.wait(nc.tensor, "l0_w", wl)
    s.wait(nc.vector, "l0_w", wl)

    def vsw():
        v = s.inc(nc.vector.nop(), "l0_v", 1)
        s.wait(nc.vector, "l0_v", v)

    def rms_chain(banks, wrep, outbf, gate_mm):
        s.wait(nc.scalar, "mm", gate_mm)
        s.wait(nc.scalar, "l0_v", s.val("l0_v"))
        nc.scalar.activation(sq[:, 0:512], banks[0][:], AF.Square)
        sv = s.inc(nc.scalar.activation(sq[:, 512:1024], banks[1][:], AF.Square), "l0_s", 1)
        s.wait(nc.vector, "l0_s", sv)
        v = s.inc(nc.vector.tensor_reduce(rr[:], sq.rearrange("p (h c) -> p h c", h=16),
                                          AX.X, OP.add), "l0_v", 1)
        s.wait(nc.vector, "l0_v", v)
        v = s.inc(nc.vector.tensor_scalar(rr[:], rr[:], 1.0 / 64, 1e-8, OP.mult, OP.add),
                  "l0_v", 1)
        s.wait(nc.scalar, "l0_v", v)
        sv = s.inc(nc.scalar.activation(rr[:], rr[:], AF.Sqrt), "l0_s", 1)
        s.wait(nc.vector, "l0_s", sv)
        v = s.inc(nc.vector.reciprocal(rr[:], rr[:]), "l0_v", 1)
        s.wait(nc.vector, "l0_v", v)
        for h in range(16):
            nc.vector.tensor_scalar(qsc[:, 64 * h:64 * h + 64],
                                    banks[h // 8][:, (h % 8) * 64:(h % 8) * 64 + 64],
                                    rr[:, h:h + 1], None, OP.mult)
        vsw()
        nc.vector.tensor_tensor(outbf[:], qsc[:], wrep[:], OP.mult)
        return s.inc(nc.vector.drain(), "l0_v", 1)

    xs_free = 0
    st_free = [0, 0]
    qdone = vdone = 0
    odone = 0
    fence_prev = 0
    for c in range(nchunks):
        j = c % 2
        s.wait(nc.scalar, "l0_v", xs_free)
        lv = s.inc(nc.scalar.dma_start(xs[:], t["x0"][bass.ts(c, 128), :]), "l0_x", 16)
        s.wait(nc.vector, "l0_x", lv)
        s.wait(nc.vector, "l0_t", s.val("l0_t"))
        _ln_plain(k, None, xs, xn, tmp)
        vsw()
        nc.vector.tensor_copy(xnb[:], xn[:])
        xv = s.inc(nc.vector.drain(), "l0_v", 1)
        s.wait(nc.tensor, "l0_v", xv)
        s.wait(nc.tensor, "l0_tv", s.val("l0_tv"))
        for cc in range(8):
            mm = nc.tensor.transpose(k.pt[:, bass.ts(cc, 128)], xnb[:, bass.ts(cc, 128)],
                                     ident[:])
        tv = s.inc(mm, "l0_t", 1)
        tv = s.inc(nc.tensor.nop(), "l0_t", 1)
        s.wait(nc.vector, "l0_t", tv)
        s.wait(nc.vector, "mm", s.val("mm"))
        nc.vector.tensor_copy(xnT[:], k.pt[:].rearrange("p (c n) -> p c n", c=8))
        xtv = s.inc(nc.vector.drain(), "l0_tv", 1)
        for bt in range(2):
            s.wait(nc.tensor, "l0_tv", s.val("l0_tv"))
            for ti in range(8):
                h = bt * 8 + ti
                mm = nc.tensor.transpose(k.pt[0:64, bass.ts(ti, 128)],
                                         xnb[:, 64 * h:64 * h + 64], ident[:])
            tv = s.inc(mm, "l0_t", 1)
            tv = s.inc(nc.tensor.nop(), "l0_t", 1)
            s.wait(nc.vector, "l0_t", tv)
            nc.vector.tensor_copy(xnK[0:64, bass.ts(bt, 8), :],
                                  k.pt[0:64, :].rearrange("p (a n) -> p a n", a=8))
            s.inc(nc.vector.drain(), "l0_tv", 1)
        s.wait(nc.tensor, "l0_tv", s.val("l0_tv"))
        if CUT <= 0:
            xs_free = xv
            continue
        s.wait(nc.tensor, "l0_v", qdone)
        for half in range(2):
            for cc in range(8):
                mm = nc.tensor.matmul(k.pb[half][:], xnT[:, cc, :],
                                      wq[:, cc, half, :], start=(cc == 0), stop=(cc == 7))
        qmv = s.inc(mm, "mm", 1)
        s.wait(nc.tensor, "l0_v", odone)
        for half in range(2):
            for cc in range(8):
                mm = nc.tensor.matmul(k.pb[4 + half][:], xnT[:, cc, :],
                                      wv[:, cc, half, :], start=(cc == 0), stop=(cc == 7))
        vmv = s.inc(mm, "mm", 1)
        qdone_ = rms_chain((k.pb[0], k.pb[1]), qnw, qbf, qmv)
        vdone = rms_chain((k.pb[4], k.pb[5]), vnw, vbf, vmv)
        if CUT <= 1:
            qdone = qdone_
            odone = vdone
            xs_free = vdone
            continue
        s.wait(nc.tensor, "l0_v", qdone_)
        for bt in range(2):
            s.wait(nc.tensor, "l0_tv", s.val("l0_tv"))
            for ti in range(8):
                h = bt * 8 + ti
                mm = nc.tensor.transpose(k.pt[0:64, bass.ts(ti, 128)],
                                         qbf[:, 64 * h:64 * h + 64], ident[:])
            tv = s.inc(mm, "l0_t", 1)
            tv = s.inc(nc.tensor.nop(), "l0_t", 1)
            s.wait(nc.vector, "l0_t", tv)
            nc.vector.tensor_copy(qT[0:64, bass.ts(bt, 8), :],
                                  k.pt[0:64, :].rearrange("p (a n) -> p a n", a=8))
            s.inc(nc.vector.drain(), "l0_tv", 1)
        s.wait(nc.tensor, "l0_tv", s.val("l0_tv"))
        if CUT <= 2:
            qdone = qdone_
            odone = vdone
            xs_free = vdone
            continue
        s.wait(nc.vector, "mm", fence_prev)
        if c == 0:
            nc.vector.memset(Ssb[:], 0.0)
        else:
            sv2 = Ssb.rearrange("p (a b) c -> p a b c", b=2)
            nc.vector.tensor_copy(sv2[0:64, :, 0, :],
                                  k.pb[2][0:64, :].rearrange("p (a c) -> p a c", a=8))
            nc.vector.tensor_copy(sv2[0:64, :, 1, :],
                                  k.pb[3][0:64, :].rearrange("p (a c) -> p a c", a=8))
        nc.vector.drain()
        nc.vector.tensor_copy(can[0:64, 0:8], Ssb[0:64, 0, :][:, 0:8])
        sdone = s.inc(nc.vector.drain(), "l0_v", 1)
        mk = 0
        for wv_i in range(2):
            s.wait(nc.tensor, "l0_v", qdone_ if wv_i == 0 else mk)
            for ti in range(8):
                h = wv_i * 8 + ti
                mm = nc.tensor.matmul(k.pb[ti // 4][:, (ti % 4) * 128:(ti % 4) * 128 + 128],
                                      xnK[0:64, h, :], qT[0:64, h, :],
                                      start=True, stop=True)
            amv = s.inc(mm, "mm", 1)
            amv = s.inc(nc.tensor.nop(), "mm", 1)
            s.wait(nc.vector, "mm", amv)
            for ti in range(8):
                h = wv_i * 8 + ti
                nc.vector.tensor_tensor(Asb[:, h, :],
                                        k.pb[ti // 4][:, (ti % 4) * 128:(ti % 4) * 128 + 128],
                                        tri[:], OP.mult)
            nc.vector.drain()
            nc.vector.tensor_copy(can[:], Asb[:, wv_i * 8, 0:8])
            mk = s.inc(nc.vector.drain(), "l0_v", 1)
            s.wait(nc.tensor, "l0_v", mk)
            s.wait(nc.tensor, "l0_v", sdone)
            s.wait(nc.tensor, "l0_v", vdone)
            for ti in range(8):
                h = wv_i * 8 + ti
                oslot = k.pb[4 + h // 8][:, (h % 8) * 64:(h % 8) * 64 + 64]
                nc.tensor.matmul(oslot, Asb[:, h, :], vbf[:, 64 * h:64 * h + 64],
                                 start=(h % 8 == 0), stop=False)
                nc.tensor.matmul(oslot, qT[0:64, h, :], Ssb[0:64, h, :],
                                 start=False, stop=True)
                mm = nc.tensor.matmul(k.pb[2 + h % 2][0:64, (h // 2) * 64:(h // 2) * 64 + 64],
                                      xnb[:, 64 * h:64 * h + 64],
                                      vbf[:, 64 * h:64 * h + 64],
                                      start=(c == 0 and h <= 1),
                                      stop=(c == nchunks - 1))
            s.inc(mm, "mm", 1)
        if CUT <= 3:
            qdone = qdone_
            odone = vdone
            xs_free = vdone
            continue
        fence = nc.tensor.matmul(k.pg[0:1, 100:101], ident[0:128, 0:1], ident[0:128, 0:1],
                                 start=True, stop=True)
        fence_prev = s.inc(fence, "mm", 1)
        fence_prev = s.inc(nc.tensor.nop(), "mm", 1)
        s.wait(nc.vector, "mm", fence_prev)
        s.wait(nc.vector, f"l0_o{j}", st_free[j])
        nc.vector.tensor_copy(osb[64:128, 0:512], osb[64:128, 0:512])
        nc.vector.tensor_copy(osb[:, 0:512], k.pb[4][:])
        nc.vector.tensor_copy(osb[:, 512:1024], k.pb[5][:])
        odone = s.inc(nc.vector.drain(), "l0_v", 1)
        qdone = qdone_
        vsw()
        _ln_plain(k, None, osb, ot[j], tmp)
        vsw()
        nc.vector.tensor_tensor(ot[j][:], ot[j][:], xs[:], OP.add)
        ov = xs_free = s.inc(nc.vector.drain(), "l0_v", 1)
        s.wait(nc.gpsimd, "l0_v", ov)
        sd = nc.gpsimd.dma_start(t["s1"][bass.ts(c, 128), :], ot[j][:])
        st_free[j] = s.inc(sd, f"l0_o{j}", 16)
    for j in range(2):
        s.wait(nc.gpsimd, f"l0_o{j}", s.val(f"l0_o{j}"))
    k.pop_scope()


def _barrier(k: Kb, recycle=False):
    s = k.s
    for eng in (k.nc.sync, k.nc.tensor, k.nc.vector, k.nc.scalar, k.nc.gpsimd):
        for name in list(s.h):
            v = s.val(name)
            if v > 0:
                eng.wait_ge(s.get(name), v)
    if recycle:
        keep = {"mm", "ln"}
        drop = [n for n in list(s.h) if n not in keep]
        k.nc.clear_and_free_semaphores([s.h[n] for n in drop])
        for n in drop:
            del s.h[n]
            del s.v[n]
        v = s.inc(k.nc.gpsimd.nop(), "mm", 1)
        for eng in (k.nc.sync, k.nc.tensor, k.nc.vector, k.nc.scalar, k.nc.gpsimd):
            eng.wait_ge(s.get("mm"), v)


def _declare_common(nc, t):
    EI = lambda n, sh, dt=F32: nc.dram_tensor(n, sh, dt, kind="ExternalInput").ap()
    t["ident"] = EI("ident", [128, 128], BF16)
    return EI


def build_p1():
    nc = bass.Bass("TRN2", target_bir_lowering=False)
    t = {}
    EI = _declare_common(nc, t)
    IN = lambda n, sh, dt=F32: nc.dram_tensor(n, sh, dt, kind="Internal").ap()
    t["x0"] = EI("x0", [NT, D])
    t["l0_wqT"] = EI("l0_wqT", [D, D]); t["l0_wvT"] = EI("l0_wvT", [D, D])
    t["qnw_rep"] = EI("qnw_rep", [128, D]); t["vnw_rep"] = EI("vnw_rep", [128, D])
    t["triT"] = EI("triT", [128, 128])
    t["wfcT"] = EI("wfcT", [D, FF]); t["wprojT"] = EI("wprojT", [FF, D])
    t["lnw_rep"] = EI("lnw_rep", [128, FF]); t["lnb_rep"] = EI("lnb_rep", [128, FF])
    t["wqT_bf"] = IN("wqT_bf_s", [8, 2, 128, 512], BF16)
    t["wvT_bf"] = IN("wvT_bf_s", [8, 2, 128, 512], BF16)
    t["wfcq"] = IN("wfcq_s", [8, 128, FF], BF16)
    t["wprojq"] = IN("wprojq_s", [FT, 128, D], BF16)
    t["qscr_a"] = IN("qscr_a", [1, 256]); t["qscr_b"] = IN("qscr_b", [1, 256])
    t["s1"] = IN("s1_s", [NT, D])
    t["s2"] = nc.dram_tensor("s2", [NT, D], F32, kind="ExternalOutput").ap()
    es = ExitStack()
    k = Kb(nc, es)
    jobs = []
    for nm in ("wqT", "wvT"):
        for c in range(8):
            for hh in range(2):
                jobs.append((t["l0_" + nm][c * 128:(c + 1) * 128, hh * 512:(hh + 1) * 512],
                             t[nm + "_bf"][c, hh]))
    emit_cast_pass(k, jobs)
    _barrier(k)
    emit_l0_quant(k, t, "wfcT", "wfcq", D, FF, tag="qa", scr=t["qscr_a"])
    _barrier(k)
    emit_l0_quant(k, t, "wprojT", "wprojq", FF, D, tag="qb", scr=t["qscr_b"])
    _barrier(k)
    emit_l0_attn(k, t)
    _barrier(k)
    emit_l0_ffn(k, t)
    es.close()
    return nc


def build_p2():
    nc = bass.Bass("TRN2", target_bir_lowering=False)
    t = {}
    EI = _declare_common(nc, t)
    IN = lambda n, sh, dt=F32: nc.dram_tensor(n, sh, dt, kind="Internal").ap()
    t["s2"] = EI("s2", [NT, D])
    t["l1a_wqT"] = EI("l1a_wqT", [D, D]); t["l1a_wkT"] = EI("l1a_wkT", [D, D])
    t["l1a_wvT"] = EI("l1a_wvT", [D, D])
    for nm in ("wqT", "wkT", "wvT"):
        t[nm + "_bf"] = IN(nm + "_bf_s", [8, 2, 128, 512], BF16)
    t["q"] = IN("q_s", [NT, D], BF16); t["k"] = IN("k_s", [NT, D], BF16)
    t["v"] = IN("v_s", [NT, D], BF16)
    t["ao"] = IN("ao_s", [NT, D], BF16)
    t["yg"] = IN("yg_s", [NT, D])
    t["xn2"] = IN("xn2_s", [NT, D])
    t["gscr"] = IN("gscr", [1, 512])
    t["s3"] = nc.dram_tensor("s3", [NT, D], F32, kind="ExternalOutput").ap()
    es = ExitStack()
    k = Kb(nc, es)
    jobs = []
    for nm in ("wqT", "wkT", "wvT"):
        for c in range(8):
            for hh in range(2):
                jobs.append((t["l1a_" + nm][c * 128:(c + 1) * 128, hh * 512:(hh + 1) * 512],
                             t[nm + "_bf"][c, hh]))
    emit_cast_pass(k, jobs)
    _barrier(k)
    emit_l1a_qkv(k, t)
    _barrier(k)
    emit_attention(k, t, "l1a", 1.0, "a3")
    _barrier(k)
    emit_l1a_norm(k, t)
    es.close()
    return nc


def build_p3():
    nc = bass.Bass("TRN2", target_bir_lowering=False)
    t = {}
    EI = _declare_common(nc, t)
    IN = lambda n, sh, dt=F32: nc.dram_tensor(n, sh, dt, kind="Internal").ap()
    t["s3"] = EI("s3", [NT, D])
    for nm in ("wq", "wk", "wv", "wo"):
        t["l1f_" + nm] = EI("l1f_" + nm, [D, D])
    t["biases"] = EI("biases", [1, 4096])
    t["wgw"] = EI("wgw", [D, 8]); t["bg"] = EI("bg", [1, 8])
    t["w1"] = EI("w1", [E, D, FF]); t["w2"] = EI("w2", [E, FF, D])
    t["b1t"] = EI("b1t", [128, FT, E]); t["b2"] = EI("b2", [8, D])
    t["cos_rep"] = EI("cos_rep", [NT, D]); t["sinS_rep"] = EI("sinS_rep", [NT, D])
    for nm in ("g1_rep", "be1_rep", "g2_rep", "be2_rep"):
        t[nm] = EI(nm, [128, D])
    for nm in ("wq", "wk", "wv", "wo"):
        t[nm + "_bf"] = IN(nm + "_bf_s", [8, 2, 128, 512], BF16)
    t["wg_bf"] = IN("wg_bf_s", [8, 128, 8], BF16)
    t["w1bf"] = IN("w1bf_s", [E, 8, 128, FF], BF16)
    t["w2bf"] = IN("w2bf_s", [E, FT, 128, D], BF16)
    t["xn"] = IN("xn_s", [NT, D])
    t["qT"] = IN("qT_s", [D, NT], BF16); t["kT"] = IN("kT_s", [D, NT], BF16)
    t["vh"] = IN("vh_s", [H, NBLK, 128, 64], BF16)
    t["ao"] = IN("ao_s", [NT, D], BF16)
    t["x1"] = IN("x1_s", [NT, D])
    t["x1t"] = IN("x1t_s", [NGRP, 8, 128, GRP], BF16)
    t["combt"] = IN("combt_s", [128, NBLK, 8])
    t["yacc"] = IN("yacc_s", [NT, D])
    t["wo_bf"] = t["wo_bf"]
    t["out"] = nc.dram_tensor("out", [NT, D], F32, kind="ExternalOutput").ap()
    es = ExitStack()
    k = Kb(nc, es)
    jobs = []
    for nm in ("wq", "wk", "wv", "wo"):
        for c in range(8):
            for hh in range(2):
                jobs.append((t["l1f_" + nm][c * 128:(c + 1) * 128, hh * 512:(hh + 1) * 512],
                             t[nm + "_bf"][c, hh]))
    for c in range(8):
        jobs.append((t["wgw"][c * 128:(c + 1) * 128, :], t["wg_bf"][c]))
    for e in range(E):
        for c in range(8):
            for hh in range(2):
                jobs.append((t["w1"][e, c * 128:(c + 1) * 128, hh * 2048:(hh + 1) * 2048],
                             t["w1bf"][e, c][:, hh * 2048:(hh + 1) * 2048]))
        for f in range(FT):
            jobs.append((t["w2"][e, f * 128:(f + 1) * 128, :], t["w2bf"][e, f]))
    emit_cast_pass(k, jobs)
    _barrier(k)
    emit_l1f_qkv(k, t)
    _barrier(k)
    emit_attention(k, t, "l1f", 0.125, "a4")
    _barrier(k)
    emit_l1f_x1gate(k, t)
    _barrier(k)
    emit_moe(k, t)
    _barrier(k)
    emit_moe_final(k, t)
    es.close()
    return nc


def _host_prep(inputs):
    import ml_dtypes
    P = inputs
    base = {}
    base["l0_wqT"] = np.ascontiguousarray(P["L0_attn_Wq"].T)
    base["l0_wvT"] = np.ascontiguousarray(P["L0_attn_Wv"].T)
    base["qnw_rep"] = np.ascontiguousarray(
        np.broadcast_to(P["L0_attn_q_norm_w"].reshape(-1), (128, D)))
    base["vnw_rep"] = np.ascontiguousarray(
        np.broadcast_to(P["L0_attn_v_norm_w"].reshape(-1), (128, D)))
    base["triT"] = np.triu(np.ones((128, 128), np.float32))
    base["ident"] = np.eye(128, dtype=ml_dtypes.bfloat16)
    base["wfcT"] = np.ascontiguousarray(P["L0_ffn_W_fc"].T)
    base["wprojT"] = np.ascontiguousarray(P["L0_ffn_W_proj"].T)
    base["lnw_rep"] = np.ascontiguousarray(np.broadcast_to(P["L0_ffn_ln_w"], (128, FF)))
    base["lnb_rep"] = np.ascontiguousarray(np.broadcast_to(P["L0_ffn_ln_b"], (128, FF)))
    base["l1a_wqT"] = np.ascontiguousarray(P["L1_attn_Wq"].T)
    base["l1a_wkT"] = np.ascontiguousarray(P["L1_attn_Wk"].T)
    base["l1a_wvT"] = np.ascontiguousarray(P["L1_attn_Wv"].T)
    base["l1f_wq"] = np.ascontiguousarray(P["L1_ffn_Wq"])
    base["l1f_wk"] = np.ascontiguousarray(P["L1_ffn_Wk"])
    base["l1f_wv"] = np.ascontiguousarray(P["L1_ffn_Wv"])
    base["l1f_wo"] = np.ascontiguousarray(P["L1_ffn_Wo"])
    base["biases"] = np.concatenate([P["L1_ffn_bq"], P["L1_ffn_bk"], P["L1_ffn_bv"],
                                     P["L1_ffn_bo"]]).astype(np.float32)[None, :]
    base["wgw"] = np.ascontiguousarray(P["L1_ffn_Wg"])
    base["bg"] = P["L1_ffn_bg"].astype(np.float32)[None, :]
    base["w1"] = np.ascontiguousarray(P["L1_ffn_W1"])
    base["w2"] = np.ascontiguousarray(P["L1_ffn_W2"])
    base["b1t"] = np.ascontiguousarray(
        P["L1_ffn_b1"].reshape(E, FT, 128).transpose(2, 1, 0))
    base["b2"] = np.ascontiguousarray(P["L1_ffn_b2"])
    hd = 64
    inv = (1.0 / 10000.0 ** (np.arange(0, hd, 2, dtype=np.float32) / hd)).astype(np.float32)
    tt_ = np.arange(NT, dtype=np.float32)[:, None] * inv[None, :]
    emb = np.concatenate([tt_, tt_], -1).astype(np.float32)
    cos = np.cos(emb).astype(np.float32)
    sin = np.sin(emb).astype(np.float32)
    sinS = sin.copy(); sinS[:, :32] *= -1
    base["cos_rep"] = np.ascontiguousarray(np.tile(cos, (1, H)))
    base["sinS_rep"] = np.ascontiguousarray(np.tile(sinS, (1, H)))
    base["g1_rep"] = np.ascontiguousarray(np.broadcast_to(P["L1_ffn_g1"], (128, D)))
    base["be1_rep"] = np.ascontiguousarray(np.broadcast_to(P["L1_ffn_be1"], (128, D)))
    base["g2_rep"] = np.ascontiguousarray(np.broadcast_to(P["L1_ffn_g2"], (128, D)))
    base["be2_rep"] = np.ascontiguousarray(np.broadcast_to(P["L1_ffn_be2"], (128, D)))
    return base


def kernel(**inputs):
    base = _host_prep(inputs)
    x = np.ascontiguousarray(inputs["L0_attn_X"])
    cores = list(range(8))
    keys1 = ("ident", "l0_wqT", "l0_wvT", "qnw_rep", "vnw_rep", "triT",
             "wfcT", "wprojT", "lnw_rep", "lnb_rep")
    nc1 = build_p1()
    maps = [dict({kk: base[kk] for kk in keys1}, x0=np.ascontiguousarray(x[c]))
            for c in cores]
    r1 = run_bass_kernel_spmd(nc1, maps, core_ids=cores)
    s2 = [np.asarray(r["s2"]) for r in r1.results]
    keys2 = ("ident", "l1a_wqT", "l1a_wkT", "l1a_wvT")
    nc2 = build_p2()
    maps = [dict({kk: base[kk] for kk in keys2}, s2=s2[c]) for c in cores]
    r2 = run_bass_kernel_spmd(nc2, maps, core_ids=cores)
    s3 = [np.asarray(r["s3"]) for r in r2.results]
    keys3 = ("ident", "l1f_wq", "l1f_wk", "l1f_wv", "l1f_wo", "biases", "wgw",
             "bg", "w1", "w2", "b1t", "b2", "cos_rep", "sinS_rep",
             "g1_rep", "be1_rep", "g2_rep", "be2_rep")
    nc3 = build_p3()
    maps = [dict({kk: base[kk] for kk in keys3}, s3=s3[c]) for c in cores]
    r3 = run_bass_kernel_spmd(nc3, maps, core_ids=cores)
    out = np.stack([np.asarray(r["out"]) for r in r3.results], axis=0)
    return out.astype(np.float32)


# version 06
# speedup vs baseline: 1.1790x; 1.1790x over previous
import numpy as np
from contextlib import ExitStack
import concourse.bass as bass
import concourse.mybir as mybir
from concourse.bass_utils import run_bass_kernel_spmd
from concourse.alu_op_type import AluOpType as OP

F32 = mybir.dt.float32
BF16 = mybir.dt.bfloat16
AF = mybir.ActivationFunctionType
AX = mybir.AxisListType

NT = 4096
D = 1024
NBLK = 32
H = 16
HD = 64
FF = 4096
E = 8
GRP = 512
NGRP = NT // GRP
FT = FF // 128


class Sems:

    def __init__(self, nc, es):
        self.nc, self.es = nc, es
        self.h = {}
        self.v = {}
        self.pool = []

    def get(self, name):
        if name not in self.h:
            if self.pool:
                handle, val = self.pool.pop()
            else:
                handle = self.nc.alloc_semaphore(name=name, num=None)
                val = 0
            self.h[name] = handle
            self.v[name] = val
        return self.h[name]

    def inc(self, inst, name, n=1):
        h = self.get(name)
        inst.then_inc(h, n)
        self.v[name] += n
        return self.v[name]

    def val(self, name):
        self.get(name)
        return self.v[name]

    def wait(self, eng, name, val):
        if val is None or val <= 0:
            return
        eng.wait_ge(self.get(name), val)


class Kb:

    def __init__(self, nc, es):
        self.nc = nc
        self.es = es
        self.scopes = []
        self.s = Sems(nc, es)
        self.pb = [es.enter_context(nc.psum_tensor(f"pb{i}", [128, 512], F32))
                   for i in range(6)]
        self.pt = es.enter_context(nc.psum_tensor("pt", [128, 1024], BF16))
        self.pg = es.enter_context(nc.psum_tensor("pg", [128, 512], F32))

    def sb(self, name, shape, dt):
        es = self.scopes[-1] if self.scopes else self.es
        return es.enter_context(self.nc.sbuf_tensor(name, shape, dt))

    def push_scope(self):
        self.scopes.append(ExitStack())

    def pop_scope(self):
        self.scopes.pop().close()


def emit_moe(k: Kb, t, n_experts=E):
    k.push_scope()
    nc, s = k.nc, k.s

    w1sb = [k.sb(f"moe_w1sb{i}", [128, 8, FF], BF16) for i in range(1)]
    w2sb = [k.sb(f"moe_w2sb{i}", [128, FT, D], BF16) for i in range(1)]
    x1g = [k.sb(f"moe_x1g{i}", [128, 8, GRP], BF16) for i in range(2)]
    hsT = k.sb("moe_hsT", [128, FT, GRP], BF16)
    ysb = [k.sb(f"moe_ysb{i}", [128, D], F32) for i in range(2)]
    yin = [k.sb(f"moe_yin{i}", [128, D], F32) for i in range(2)]
    b1c = k.sb("moe_b1c", [128, FT, E], F32)
    comb = k.sb("moe_comb", [128, NBLK, E], F32)

    s.inc(nc.scalar.dma_start(b1c[:], t["b1t"]), "moe_ld1", 16)
    s.inc(nc.scalar.dma_start(comb[:], t["combt"]), "moe_ld1", 16)

    w1sb_free = 0
    w2sb_free = 0

    def emit_w_load(e):
        s.wait(nc.sync, "mm", w1sb_free)
        ld = nc.sync.dma_start(w1sb[0][:], t["w1bf"][e].rearrange("c p f -> p c f"))
        r1 = s.inc(ld, "moe_w1", 16)
        s.wait(nc.sync, "mm", w2sb_free)
        ld = nc.sync.dma_start(w2sb[0][:], t["w2bf"][e].rearrange("f p d -> p f d"))
        r2 = s.inc(ld, "moe_w2", 16)
        return r1, r2

    w1_ready, w2_ready = emit_w_load(0)

    hbank_free = [0, 0]
    x1_free = [0, 0]
    x1_ready = [0, 0]
    ypair_free = [0, 0]
    ysb_free = [0, 0]
    first_use = True

    for e in range(n_experts):
        for g in range(NGRP):
            gi = g % 2
            xb = x1g[gi]
            s.wait(nc.scalar, "mm", x1_free[gi])
            ld = nc.scalar.dma_start(xb[:], t["x1t"][g].rearrange("c p t -> p c t"))
            x1_ready[gi] = s.inc(ld, f"moe_x{gi}", 16)

            s.wait(nc.tensor, "moe_w1", w1_ready)
            s.wait(nc.tensor, f"moe_x{gi}", x1_ready[gi])
            if first_use:
                s.wait(nc.tensor, "moe_ld1", 32)
                s.wait(nc.scalar, "moe_ld1", 32)
                first_use = False
            gelu_last = 0
            for f in range(FT):
                half = f % 2
                hp = k.pb[half][:]
                s.wait(nc.tensor, "moe_sc", hbank_free[half])
                for c in range(8):
                    mm = nc.tensor.matmul(hp, w1sb[0][:, c, bass.ts(f, 128)],
                                          xb[:, c, :],
                                          start=(c == 0), stop=(c == 7))
                mv = s.inc(mm, "mm", 1)
                s.wait(nc.scalar, "mm", mv)
                ge = nc.scalar.activation(hsT[:, f, :], hp, AF.Gelu,
                                          bias=b1c[:, f, e:e + 1], scale=1.0)
                hbank_free[half] = s.inc(ge, "moe_sc", 1)
            gelu_last = s.inc(nc.scalar.drain(), "moe_sc", 1)
            x1_free[gi] = s.val("mm")

            s.wait(nc.tensor, "moe_w2", w2_ready)
            for sub in range(4):
                blk = g * 4 + sub
                pi = sub % 2
                pair = (k.pb[2], k.pb[3]) if pi == 0 else (k.pb[4], k.pb[5])
                s.wait(nc.tensor, "moe_sc", gelu_last)
                s.wait(nc.tensor, "moe_v", ypair_free[pi])
                if e > 0:
                    s.wait(nc.gpsimd, f"moe_st{pi}", ysb_free[pi])
                    ldy = nc.gpsimd.dma_start(yin[pi][:], t["yacc"][bass.ts(blk, 128), :])
                    yin_rdy = s.inc(ldy, f"moe_yld{pi}", 16)
                for f in range(FT):
                    for dh in range(2):
                        mm = nc.tensor.matmul(
                            pair[dh][:],
                            hsT[:, f, bass.ts(sub, 128)],
                            w2sb[0][:, f, bass.ts(dh, 512)],
                            start=(f == 0), stop=(f == FT - 1))
                mv = s.inc(mm, "mm", 1)
                s.wait(nc.vector, "mm", mv)
                s.wait(nc.vector, f"moe_st{pi}", ysb_free[pi])
                if e == 0:
                    for dh in range(2):
                        nc.vector.tensor_scalar(ysb[pi][:, bass.ts(dh, 512)], pair[dh][:],
                                                comb[:, blk, e:e + 1], None, OP.mult)
                else:
                    s.wait(nc.vector, f"moe_yld{pi}", yin_rdy)
                    for dh in range(2):
                        nc.vector.scalar_tensor_tensor(ysb[pi][:, bass.ts(dh, 512)], pair[dh][:],
                                                       comb[:, blk, e:e + 1],
                                                       yin[pi][:, bass.ts(dh, 512)],
                                                       OP.mult, OP.add)
                ypair_free[pi] = s.inc(nc.vector.drain(), "moe_v", 1)
                s.wait(nc.gpsimd, "moe_v", ypair_free[pi])
                sd = nc.gpsimd.dma_start(t["yacc"][bass.ts(blk, 128), :], ysb[pi][:])
                ysb_free[pi] = s.inc(sd, f"moe_st{pi}", 16)
        if e + 1 < n_experts:
            w1sb_free = s.val("mm")
            w2sb_free = s.val("mm")
            w1_ready, w2_ready = emit_w_load(e + 1)
    for pi in range(2):
        s.wait(nc.gpsimd, f"moe_st{pi}", s.val(f"moe_st{pi}"))
    k.pop_scope()



def emit_cast_pass(k: Kb, jobs, eng=None, tag="cst"):
    k.push_scope()
    nc, s = k.nc, k.s
    eng = eng or nc.vector
    NBUF = 3
    fin = [k.sb(f"{tag}_fin{i}", [128, 2048], F32) for i in range(NBUF)]
    bout = [k.sb(f"{tag}_bout{i}", [128, 2048], BF16) for i in range(NBUF)]
    i = 0
    cast_done = [0] * NBUF
    out_done = [0] * NBUF
    for srcap, dst in jobs:
        R, C = srcap.shape
        assert R % 128 == 0 and C <= 2048
        for r in range(R // 128):
            j = i % NBUF
            s.wait(nc.sync, tag + "_c", cast_done[j])
            ld = nc.sync.dma_start(fin[j][:, 0:C], srcap[bass.ts(r, 128), :])
            lv = s.inc(ld, f"{tag}_l{j}", 16)
            s.wait(eng, f"{tag}_l{j}", lv)
            s.wait(eng, f"{tag}_o{j}", out_done[j])
            eng.tensor_copy(bout[j][:, 0:C], fin[j][:, 0:C])
            cast_done[j] = s.inc(eng.drain(), tag + "_c", 1)
            s.wait(nc.gpsimd, tag + "_c", cast_done[j])
            st = nc.gpsimd.dma_start(dst[bass.ts(r, 128), :], bout[j][:, 0:C])
            out_done[j] = s.inc(st, f"{tag}_o{j}", 16)
            i += 1
    for j in range(NBUF):
        s.wait(nc.gpsimd, f"{tag}_o{j}", s.val(f"{tag}_o{j}"))
    k.pop_scope()


def _ln_plain(k, eng_small, xin, xout, tmp, eps=1e-5, w_rep=None, b_rep=None,
              out2=None):
    nc, s = k.nc, k.s
    bn, mv, t1, t2 = tmp["bn"], tmp["mv"], tmp["t1"], tmp["t2"]
    nc.vector.bn_stats(bn[:, 0:6], xin[:, 0:512])
    v = s.inc(nc.vector.bn_stats(bn[:, 6:12], xin[:, 512:1024]), "ln", 1)
    s.wait(nc.vector, "ln", v)
    v = s.inc(nc.vector.bn_aggr(mv[:], bn[:]), "ln", 1)
    s.wait(nc.vector, "ln", v)
    v = s.inc(nc.vector.tensor_scalar(t1[:], mv[:, 1:2], eps, None, OP.add), "ln", 1)
    s.wait(nc.scalar, "ln", v)
    v = s.inc(nc.scalar.activation(t1[:], t1[:], AF.Sqrt), "ln", 1)
    s.wait(nc.vector, "ln", v)
    v = s.inc(nc.vector.reciprocal(t2[:], t1[:]), "ln", 1)
    s.wait(nc.vector, "ln", v)
    nc.vector.tensor_scalar(xout[:], xin[:], mv[:, 0:1], t2[:], OP.subtract, OP.mult)
    if w_rep is not None:
        nc.vector.tensor_tensor(xout[:], xout[:], w_rep[:], OP.mult)
    if b_rep is not None:
        nc.vector.tensor_tensor(out2 if out2 is not None else xout[:],
                                xout[:], b_rep[:], OP.add)


def emit_l1f_qkv(k: Kb, t):
    k.push_scope()
    nc, s = k.nc, k.s
    wq = k.sb("p4_wq", [128, 8, 2, 512], BF16)
    wk = k.sb("p4_wk", [128, 8, 2, 512], BF16)
    wv = k.sb("p4_wv", [128, 8, 2, 512], BF16)
    bias_bf = k.sb("p4_biasbf", [1, 4096], BF16)
    bias_f = k.sb("p4_biasf", [1, 4096], F32)
    onesr = k.sb("p4_ones", [1, 128], BF16)
    ident = k.sb("p4_ident", [128, 128], BF16)
    xs = k.sb("p4_xs", [128, D], F32)
    xn = k.sb("p4_xn", [128, D], F32)
    xnb = k.sb("p4_xnb", [128, D], BF16)
    xnT = k.sb("p4_xnT", [128, 8, 512], BF16)
    ctile = k.sb("p4_cos", [128, D], F32)
    stile = k.sb("p4_sin", [128, D], F32)
    qsb = k.sb("p4_qsb", [128, D], F32)
    qrot = k.sb("p4_qrot", [128, D], F32)
    qe = k.sb("p4_qe", [128, D], F32)
    qb2 = k.sb("p4_qb2", [128, D], BF16)
    qtev = k.sb("p4_qtev", [128, D], BF16)
    vsb = k.sb("p4_vsb", [128, D], BF16)
    tmp = {"bn": k.sb("p4_bn", [128, 12], F32), "mv": k.sb("p4_mv", [128, 2], F32),
           "t1": k.sb("p4_t1", [128, 1], F32), "t2": k.sb("p4_t2", [128, 1], F32)}

    for nm, w in (("wq_bf", wq), ("wk_bf", wk), ("wv_bf", wv)):
        s.inc(nc.sync.dma_start(w[:], t[nm].rearrange("c h p n -> p c h n")), "p4w", 16)
    s.inc(nc.sync.dma_start(bias_f[:], t["biases"]), "p4w", 16)
    s.inc(nc.sync.dma_start(ident[:], t["ident"]), "p4w", 16)
    s.wait(nc.vector, "p4w", s.val("p4w"))
    nc.vector.memset(onesr[:], 1.0)
    nc.vector.tensor_copy(bias_bf[:], bias_f[:])
    wrdy = s.inc(nc.vector.drain(), "p4v", 1)
    s.wait(nc.tensor, "p4v", wrdy)

    pb = k.pb
    xs_free = 0
    rope_free = 0

    def selfwait(eng=nc.vector, name="p4sw"):
        v = s.inc(eng.nop(), name, 1)
        s.wait(eng, name, v)

    for g in range(NGRP):
        for bi in range(4):
            blk = g * 4 + bi
            s.wait(nc.scalar, "p4v", xs_free)
            lv = s.inc(nc.scalar.dma_start(xs[:], t["s3"][bass.ts(blk, 128), :]), "p4xs", 16)
            s.wait(nc.vector, "p4xs", lv)
            s.wait(nc.vector, "p4t", s.val("p4t"))
            s.wait(nc.vector, "p4xo", s.val("p4xo"))
            _ln_plain(k, None, xs, xn, tmp)
            selfwait()
            nc.vector.tensor_copy(xnb[:], xn[:])
            xv = xs_free = s.inc(nc.vector.drain(), "p4v", 1)
            s.wait(nc.scalar, "p4v", xv)
            s.inc(nc.scalar.dma_start(t["xn"][bass.ts(blk, 128), :], xn[:]), "p4xo", 16)
            s.wait(nc.tensor, "p4v", xv)
            s.wait(nc.tensor, "p4tv", s.val("p4tv"))
            for c in range(8):
                mm = nc.tensor.transpose(k.pt[:, bass.ts(c, 128)],
                                         xnb[:, bass.ts(c, 128)], ident[:])
            tv = s.inc(mm, "p4t", 1)
            tv = s.inc(nc.tensor.nop(), "p4t", 1)
            s.wait(nc.vector, "p4t", tv)
            if bi == 0:
                s.wait(nc.vector, "mm", s.val("mm"))
            nc.vector.tensor_copy(xnT[:, :, bi * 128:(bi + 1) * 128],
                                  k.pt[:].rearrange("p (c n) -> p c n", c=8))
            s.inc(nc.vector.drain(), "p4tv", 1)
        xnT_rdy = s.val("p4tv")
        s.wait(nc.tensor, "p4tv", xnT_rdy)

        for bi in range(4):
            blk = g * 4 + bi
            s.wait(nc.tensor, "p4e", s.val("p4e"))
            for half in range(2):
                hp = pb[half][:]
                for c in range(8):
                    nc.tensor.matmul(hp, xnT[:, c, bass.ts(bi, 128)],
                                     wv[:, c, half, :], start=(c == 0), stop=False)
                mm = nc.tensor.matmul(hp, onesr[:],
                                      bias_bf[0:1, 2048 + half * 512:2048 + half * 512 + 512],
                                      start=False, stop=True)
            mv_ = s.inc(mm, "mm", 1)
            s.wait(nc.vector, "mm", mv_)
            s.wait(nc.vector, "p4vo", s.val("p4vo"))
            nc.vector.tensor_copy(vsb[:, 0:512], pb[0][:])
            nc.vector.tensor_copy(vsb[:, 512:1024], pb[1][:])
            ev = s.inc(nc.vector.drain(), "p4e", 1)
            s.wait(nc.gpsimd, "p4e", ev)
            sd = nc.gpsimd.dma_start(
                t["vh"][:, blk].rearrange("h p c -> p h c"),
                vsb.rearrange("p (h c) -> p h c", h=H))
            s.inc(sd, "p4vo", 16)

        for w, boff, dstname in ((wq, 0, "qT"), (wk, 1024, "kT")):
            for bi in range(4):
                blk = g * 4 + bi
                s.wait(nc.scalar, "p4v", rope_free)
                s.inc(nc.scalar.dma_start(ctile[:], t["cos_rep"][bass.ts(blk, 128), :]),
                      "p4xc", 16)
                lv2 = s.inc(nc.scalar.dma_start(stile[:], t["sinS_rep"][bass.ts(blk, 128), :]),
                            "p4xd", 16)
                s.wait(nc.tensor, "p4e", s.val("p4e"))
                for half in range(2):
                    hp = pb[half][:]
                    for c in range(8):
                        nc.tensor.matmul(hp, xnT[:, c, bass.ts(bi, 128)],
                                         w[:, c, half, :], start=(c == 0), stop=False)
                    mm = nc.tensor.matmul(hp, onesr[:],
                                          bias_bf[0:1, boff + half * 512:boff + half * 512 + 512],
                                          start=False, stop=True)
                mv_ = s.inc(mm, "mm", 1)
                s.wait(nc.vector, "mm", mv_)
                s.wait(nc.vector, "p4xc", s.val("p4xc"))
                s.wait(nc.vector, "p4xd", lv2)
                nc.vector.tensor_copy(qsb[:, 0:512], pb[0][:])
                nc.vector.tensor_copy(qsb[:, 512:1024], pb[1][:])
                ev = s.inc(nc.vector.drain(), "p4e", 1)
                selfwait()
                qv = qsb.rearrange("p (h s c) -> p h s c", h=H, s=2)
                rv = qrot.rearrange("p (h s c) -> p h s c", h=H, s=2)
                nc.vector.tensor_copy(rv[:, :, 0, :], qv[:, :, 1, :])
                selfwait()
                nc.vector.tensor_copy(rv[:, :, 1, :], qv[:, :, 0, :])
                selfwait()
                nc.vector.tensor_tensor(qe[:], qsb[:], ctile[:], OP.mult)
                selfwait()
                nc.vector.tensor_tensor(qrot[:], qrot[:], stile[:], OP.mult)
                selfwait()
                nc.vector.tensor_tensor(qb2[:], qe[:], qrot[:], OP.add)
                rpv = rope_free = s.inc(nc.vector.drain(), "p4v", 1)
                s.wait(nc.tensor, "p4v", rpv)
                s.wait(nc.tensor, "p4tv", s.val("p4tv"))
                for c in range(8):
                    mm = nc.tensor.transpose(k.pt[:, bass.ts(c, 128)],
                                             qb2[:, bass.ts(c, 128)], ident[:])
                tv = s.inc(mm, "p4t", 1)
                tv = s.inc(nc.tensor.nop(), "p4t", 1)
                s.wait(nc.vector, "p4t", tv)
                s.wait(nc.vector, "p4qo", s.val("p4qo"))
                nc.vector.tensor_copy(qtev[:], k.pt[:])
                qtv = s.inc(nc.vector.drain(), "p4tv", 1)
                s.wait(nc.gpsimd, "p4tv", qtv)
                sd = nc.gpsimd.dma_start(
                    t[dstname].rearrange("(c p) n -> p c n", p=128)[:, :, bass.ts(blk, 128)],
                    qtev.rearrange("p (c n) -> p c n", c=8))
                s.inc(sd, "p4qo", 16)
    s.wait(nc.gpsimd, "p4qo", s.val("p4qo"))
    s.wait(nc.gpsimd, "p4vo", s.val("p4vo"))
    s.wait(nc.scalar, "p4xo", s.val("p4xo"))
    k.pop_scope()


def emit_attention(k: Kb, t, mode, scale, tag):
    k.push_scope()
    nc, s = k.nc, k.s
    qt = k.sb(tag + "_qt", [64, 2, NT], BF16)
    kt = k.sb(tag + "_kt", [64, 2, NT], BF16)
    vaug = [k.sb(f"{tag}_va{i}", [128, NBLK, 65], BF16) for i in range(2)]
    eb = [k.sb(f"{tag}_eb{i}", [128, 1024], BF16) for i in range(3)]
    osb = k.sb(tag + "_osb", [128, 1024], BF16)
    oout = k.sb(tag + "_oout", [128, 4, 64], BF16)
    rcol = k.sb(tag + "_rcol", [128, 1], F32)
    rcan = k.sb(tag + "_rcan", [128, 8], F32)
    ident = k.sb(tag + "_ident", [128, 128], BF16)
    qkld = k.sb(tag + "_qkld", [128, NBLK, 64], BF16)

    s.inc(nc.sync.dma_start(ident[:], t["ident"]), tag + "w", 16)
    s.wait(nc.vector, tag + "w", s.val(tag + "w"))
    nc.vector.memset(vaug[0][:, :, 64:65], 1.0)
    nc.vector.memset(vaug[1][:, :, 64:65], 1.0)
    ones_rdy = s.inc(nc.vector.drain(), tag + "v", 1)
    s.wait(nc.tensor, tag + "v", ones_rdy)

    if mode == "l1a":
        qflat = t["q"].rearrange("a (u c) -> (a u) c", c=64)
        kflat = t["k"].rearrange("a (u c) -> (a u) c", c=64)
        vflat = t["v"].rearrange("a (u c) -> (a u) c", c=64)
        aoflat = t["ao"].rearrange("a (u c) -> (a u) c", c=64)

    pair_lastmm = 0
    va_free = [0, 0]
    sco_free = [0, 0]
    oev = 0
    odr_cnt = 0

    for hp_i in range(8):
        if mode == "l1f":
            s.wait(nc.sync, "mm", pair_lastmm)
            ld = nc.sync.dma_start(qt[:], t["qT"][bass.ts(hp_i, 128), :]
                                   .rearrange("(g p) n -> p g n", g=2))
            s.inc(ld, tag + "q", 16)
            ld = nc.sync.dma_start(kt[:], t["kT"][bass.ts(hp_i, 128), :]
                                   .rearrange("(g p) n -> p g n", g=2))
            k_ld = s.inc(ld, tag + "q", 16)
            s.wait(nc.tensor, tag + "q", k_ld)
        else:
            s.wait(nc.tensor, tag + "pv", s.val(tag + "pv"))
            for gi in range(2):
                grp = 2 * hp_i + gi
                for which, flat, dst in (("q", qflat, qt), ("k", kflat, kt)):
                    s.wait(nc.sync, tag + "tv", s.val(tag + "tv"))
                    if gi == 0 and which == "q":
                        s.wait(nc.sync, "mm", pair_lastmm)
                    src = flat[grp * 4096:(grp + 1) * 4096, :]
                    ld = nc.sync.dma_start(qkld[:], src.rearrange("(b p) c -> p b c", p=128))
                    lv = s.inc(ld, tag + "q", 16)
                    s.wait(nc.tensor, tag + "q", lv)
                    for bt in range(4):
                        s.wait(nc.tensor, tag + "tv", s.val(tag + "tv"))
                        for ti in range(8):
                            pt_i = bt * 8 + ti
                            mm = nc.tensor.transpose(k.pt[0:64, bass.ts(ti, 128)],
                                                     qkld[:, pt_i, :], ident[:])
                        tv = s.inc(mm, tag + "t", 1)
                        tv = s.inc(nc.tensor.nop(), tag + "t", 1)
                        s.wait(nc.vector, tag + "t", tv)
                        if bt == 0 and which == "q" and gi == 0:
                            s.wait(nc.vector, "mm", pair_lastmm)
                        nc.vector.tensor_copy(
                            dst[0:64, gi, bt * 1024:(bt + 1) * 1024]
                               .rearrange("p (a n) -> p a n", a=8),
                            k.pt[0:64, :].rearrange("p (a n) -> p a n", a=8))
                        s.inc(nc.vector.drain(), tag + "tv", 1)
            s.wait(nc.tensor, tag + "tv", s.val(tag + "tv"))
        for gi in range(2):
            s.wait(nc.scalar, "mm", va_free[gi])
            if mode == "l1f":
                h = 2 * hp_i + gi
                ld = nc.scalar.dma_start(vaug[gi][:, :, 0:64],
                                         t["vh"][h].rearrange("b p c -> p b c"))
            else:
                grp = 2 * hp_i + gi
                src = vflat[grp * 4096:(grp + 1) * 4096, :]
                ld = nc.scalar.dma_start(vaug[gi][:, :, 0:64],
                                         src.rearrange("(b p) c -> p b c", p=128))
            va_free[gi] = s.inc(ld, tag + "va", 16)
        s.wait(nc.tensor, tag + "va", max(va_free))

        for qb in range(8):
            sco_free = [0, 0]
            exp_cnt = {}
            smv = {}
            s.wait(nc.tensor, tag + "v", oev)

            def esc(kb):
                sp = kb % 2
                S = (k.pb[1], k.pb[0]) if sp == 0 else (k.pb[3], k.pb[2])
                s.wait(nc.tensor, tag + "e", sco_free[sp])
                nc.tensor.matmul(S[0][:], kt[0:64, 0, bass.ts(kb, 128)],
                                 qt[0:64, 0, bass.ts(qb, 512)], start=True, stop=True)
                mm = nc.tensor.matmul(S[1][:], kt[0:64, 1, bass.ts(kb, 128)],
                                      qt[0:64, 1, bass.ts(qb, 512)], start=True, stop=True)
                smv[kb] = s.inc(mm, "mm", 1)

            def eexp(kb, wv):
                sp = kb % 2
                S = (k.pb[1], k.pb[0]) if sp == 0 else (k.pb[3], k.pb[2])
                E = eb[kb % 3]
                s.wait(nc.scalar, "mm", wv)
                nc.scalar.activation(E[:, 0:512], S[0][:], AF.Exp, scale=scale)
                ex = nc.scalar.activation(E[:, 512:1024], S[1][:], AF.Exp, scale=scale)
                sco_free[sp] = exp_cnt[kb] = s.inc(ex, tag + "e", 1)

            def epv(kb, wv, last=False):
                s.wait(nc.tensor, tag + "e", wv)
                E = eb[kb % 3]
                for gi in range(2):
                    mm = nc.tensor.matmul(k.pb[4 + gi][0:65, :], vaug[gi][:, kb, :],
                                          E[:, bass.ts(gi, 512)],
                                          start=(kb == 0), stop=last)
                return mm

            for kb in range(NBLK):
                esc(kb)
                if kb >= 1:
                    eexp(kb - 1, smv[kb])
                if kb >= 2:
                    epv(kb - 2, exp_cnt[kb - 1])
            tdr = s.inc(nc.tensor.drain(), "mm", 1)
            eexp(NBLK - 1, tdr)
            epv(NBLK - 2, exp_cnt[NBLK - 1])
            nc.scalar.drain()
            dr = s.inc(nc.scalar.activation(rcol[:], eb[(NBLK - 1) % 3][:, 1020:1021],
                                            AF.Identity), tag + "e", 1)
            sco_free[0] = sco_free[1] = dr
            epv(NBLK - 1, dr, last=True)
            fence = nc.tensor.matmul(k.pg[0:1, 100:101], vaug[0][:, 0, 0:1],
                                     vaug[0][:, 0, 0:1], start=True, stop=True)
            pv_v = s.inc(fence, "mm", 1)
            va_free = [pv_v, pv_v]
            pair_lastmm = pv_v

            s.wait(nc.vector, "mm", pv_v)
            s.wait(nc.vector, tag + "t", s.val(tag + "t"))
            nc.vector.tensor_copy(osb[64:128, 0:512], osb[64:128, 0:512])
            nc.vector.tensor_copy(osb[0:65, 0:512], k.pb[4][0:65, :])
            nc.vector.tensor_copy(osb[0:65, 512:1024], k.pb[5][0:65, :])
            nc.vector.drain()
            nc.vector.tensor_copy(rcan[:], osb[0:128, 1016:1024])
            oev = s.inc(nc.vector.drain(), tag + "v", 1)
            for gi in range(2):
                s.wait(nc.tensor, tag + "v", oev)
                s.wait(nc.tensor, tag + "pv", odr_cnt)
                for ti in range(4):
                    mm = nc.tensor.transpose(k.pt[:, ti * 68:ti * 68 + 65],
                                             osb[0:65, gi * 512 + ti * 128:gi * 512 + (ti + 1) * 128],
                                             ident[0:65, 0:65])
                ptv = s.inc(mm, tag + "t", 1)
                ptv = s.inc(nc.tensor.nop(), tag + "t", 1)
                s.wait(nc.vector, tag + "t", ptv)
                s.wait(nc.vector, tag + "o", s.val(tag + "o"))
                for ti in range(4):
                    tile = k.pt[:, ti * 68:ti * 68 + 65]
                    v1 = s.inc(nc.vector.reciprocal(rcol[:], tile[:, 64:65]), tag + "v", 1)
                    s.wait(nc.vector, tag + "v", v1)
                    nc.vector.tensor_scalar(oout[:, ti, :], tile[:, 0:64], rcol[:],
                                            None, OP.mult)
                odr_cnt = s.inc(nc.vector.drain(), tag + "pv", 1)
                s.wait(nc.gpsimd, tag + "pv", odr_cnt)
                if mode == "l1f":
                    h = 2 * hp_i + gi
                    dst = t["ao"][qb * 512:(qb + 1) * 512, 64 * h:64 * h + 64]
                else:
                    grp = 2 * hp_i + gi
                    dst = aoflat[grp * 4096 + qb * 512: grp * 4096 + (qb + 1) * 512, :]
                sd = nc.gpsimd.dma_start(dst.rearrange("(ti p) c -> p ti c", p=128), oout[:])
                s.inc(sd, tag + "o", 16)
    s.wait(nc.gpsimd, tag + "o", s.val(tag + "o"))
    k.pop_scope()


def emit_l1f_x1gate(k: Kb, t):
    k.push_scope()
    nc, s = k.nc, k.s
    wo = k.sb("p5_wo", [128, 8, 2, 512], BF16)
    wg = k.sb("p5_wg", [128, 8, 8], BF16)
    bias_f = k.sb("p5_biasf", [1, 4096], F32)
    bias_bf = k.sb("p5_biasbf", [1, 4096], BF16)
    bg_f = k.sb("p5_bgf", [1, 8], F32)
    bg_bf = k.sb("p5_bgbf", [1, 8], BF16)
    onesr = k.sb("p5_ones", [1, 128], BF16)
    ident = k.sb("p5_ident", [128, 128], BF16)
    g1r = k.sb("p5_g1r", [128, D], F32)
    be1r = k.sb("p5_be1r", [128, D], F32)
    aob = k.sb("p5_aob", [128, D], BF16)
    aoT = k.sb("p5_aoT", [128, 8, 128], BF16)
    xnb = k.sb("p5_xnb", [128, D], F32)
    y1 = k.sb("p5_y1", [128, D], F32)
    x1f = k.sb("p5_x1f", [128, D], F32)
    x1b = k.sb("p5_x1b", [128, D], BF16)
    x1T = k.sb("p5_x1T", [128, 8, 128], BF16)
    pe_ = k.sb("p5_pe", [128, 8], F32)
    m1 = k.sb("p5_m1", [128, 1], F32)
    nm1 = k.sb("p5_nm1", [128, 1], F32)
    pm = k.sb("p5_pm", [128, 8], F32)
    m2 = k.sb("p5_m2", [128, 1], F32)
    s12 = k.sb("p5_s12", [128, 1], F32)
    rs = k.sb("p5_rs", [128, 1], F32)
    mask1 = k.sb("p5_mask1", [128, 8], F32)
    mask2 = k.sb("p5_mask2", [128, 8], F32)
    combv = k.sb("p5_comb", [128, 8], F32)
    tmp = {"bn": k.sb("p5_bn", [128, 12], F32), "mv": k.sb("p5_mv", [128, 2], F32),
           "t1": k.sb("p5_t1", [128, 1], F32), "t2": k.sb("p5_t2", [128, 1], F32)}

    s.inc(nc.sync.dma_start(wo[:], t["wo_bf"].rearrange("c h p n -> p c h n")), "p5w", 16)
    s.inc(nc.sync.dma_start(wg[:], t["wg_bf"].rearrange("c p n -> p c n")), "p5w", 16)
    s.inc(nc.sync.dma_start(bias_f[:], t["biases"]), "p5w", 16)
    s.inc(nc.sync.dma_start(bg_f[:], t["bg"]), "p5w", 16)
    s.inc(nc.sync.dma_start(ident[:], t["ident"]), "p5w", 16)
    s.inc(nc.sync.dma_start(g1r[:], t["g1_rep"]), "p5w", 16)
    s.inc(nc.sync.dma_start(be1r[:], t["be1_rep"]), "p5w", 16)
    s.wait(nc.vector, "p5w", s.val("p5w"))
    nc.vector.memset(onesr[:], 1.0)
    nc.vector.tensor_copy(bias_bf[:], bias_f[:])
    nc.vector.tensor_copy(bg_bf[:], bg_f[:])
    wrdy = s.inc(nc.vector.drain(), "p5v", 1)
    s.wait(nc.tensor, "p5v", wrdy)

    def selfwait(eng=nc.vector, name="p5sw"):
        v = s.inc(eng.nop(), name, 1)
        s.wait(eng, name, v)

    aob_free = 0
    xnb_free = 0
    for blk in range(NBLK):
        g, bi = blk // 4, blk % 4
        s.wait(nc.scalar, "p5t", aob_free)
        lv = s.inc(nc.scalar.dma_start(aob[:], t["ao"][bass.ts(blk, 128), :]), "p5xa", 16)
        s.wait(nc.scalar, "p5e", xnb_free)
        lvx = s.inc(nc.scalar.dma_start(xnb[:], t["xn"][bass.ts(blk, 128), :]), "p5xb", 16)
        s.wait(nc.tensor, "p5xa", lv)
        s.wait(nc.tensor, "p5tv", s.val("p5tv"))
        for c in range(8):
            mm = nc.tensor.transpose(k.pt[:, bass.ts(c, 128)], aob[:, bass.ts(c, 128)],
                                     ident[:])
        tv = aob_free = s.inc(mm, "p5t", 1)
        tv = s.inc(nc.tensor.nop(), "p5t", 1)
        s.wait(nc.vector, "p5t", tv)
        s.wait(nc.vector, "mm", s.val("mm"))
        nc.vector.tensor_copy(aoT[:], k.pt[:].rearrange("p (c n) -> p c n", c=8))
        aotv = s.inc(nc.vector.drain(), "p5tv", 1)
        s.wait(nc.tensor, "p5tv", aotv)
        s.wait(nc.tensor, "p5e", s.val("p5e"))
        for half in range(2):
            hp = k.pb[half][:]
            for c in range(8):
                nc.tensor.matmul(hp, aoT[:, c, :], wo[:, c, half, :],
                                 start=(c == 0), stop=False)
            mm = nc.tensor.matmul(hp, onesr[:],
                                  bias_bf[0:1, 3072 + half * 512:3072 + half * 512 + 512],
                                  start=False, stop=True)
        mv_ = s.inc(mm, "mm", 1)
        s.wait(nc.vector, "mm", mv_)
        s.wait(nc.vector, "p5xb", lvx)
        nc.vector.tensor_tensor(y1[:, 0:512], k.pb[0][:], xnb[:, 0:512], OP.add)
        nc.vector.tensor_tensor(y1[:, 512:1024], k.pb[1][:], xnb[:, 512:1024], OP.add)
        ev = xnb_free = s.inc(nc.vector.drain(), "p5e", 1)
        selfwait()
        s.wait(nc.vector, "p5xo", s.val("p5xo"))
        _ln_plain(k, None, y1, x1f, tmp, w_rep=g1r, b_rep=be1r)
        selfwait()
        nc.vector.tensor_copy(x1b[:], x1f[:])
        x1v = s.inc(nc.vector.drain(), "p5v", 1)
        s.wait(nc.scalar, "p5v", x1v)
        s.inc(nc.scalar.dma_start(t["x1"][bass.ts(blk, 128), :], x1f[:]), "p5xo", 16)
        s.wait(nc.tensor, "p5v", x1v)
        s.wait(nc.tensor, "p5tv", s.val("p5tv"))
        for c in range(8):
            mm = nc.tensor.transpose(k.pt[:, bass.ts(c, 128)], x1b[:, bass.ts(c, 128)],
                                     ident[:])
        tv = s.inc(mm, "p5t", 1)
        tv = s.inc(nc.tensor.nop(), "p5t", 1)
        s.wait(nc.vector, "p5t", tv)
        s.wait(nc.vector, "p5to", s.val("p5to"))
        nc.vector.tensor_copy(x1T[:], k.pt[:].rearrange("p (c n) -> p c n", c=8))
        x1tv = s.inc(nc.vector.drain(), "p5tv", 1)
        s.wait(nc.gpsimd, "p5tv", x1tv)
        sd = nc.gpsimd.dma_start(t["x1t"][g][:, :, bass.ts(bi, 128)].rearrange("c p n -> p c n"),
                                 x1T[:])
        s.inc(sd, "p5to", 16)
        s.wait(nc.tensor, "p5tv", x1tv)
        s.wait(nc.tensor, "p5g", s.val("p5g"))
        for c in range(8):
            nc.tensor.matmul(k.pg[:, 0:8], x1T[:, c, :], wg[:, c, :],
                             start=(c == 0), stop=False)
        mm = nc.tensor.matmul(k.pg[:, 0:8], onesr[:], bg_bf[:], start=False, stop=True)
        gv = s.inc(mm, "mm", 1)
        s.wait(nc.vector, "mm", gv)
        v1 = s.inc(nc.vector.tensor_reduce(m1[:], k.pg[:, 0:8], AX.X, OP.max), "p5g", 1)
        s.wait(nc.vector, "p5g", v1)
        v1 = s.inc(nc.vector.tensor_scalar(nm1[:], m1[:], -1.0, None, OP.mult), "p5g", 1)
        s.wait(nc.scalar, "p5g", v1)
        ex = nc.scalar.activation(pe_[:], k.pg[:, 0:8], AF.Exp, bias=nm1[:], scale=1.0)
        sv = s.inc(nc.scalar.drain(), "p5g", 1)
        s.wait(nc.vector, "p5g", sv)
        v1 = s.inc(nc.vector.tensor_reduce(m1[:], pe_[:], AX.X, OP.max), "p5g", 1)
        s.wait(nc.vector, "p5g", v1)
        v1 = s.inc(nc.vector.tensor_scalar(mask1[:], pe_[:], m1[:], None, OP.is_ge), "p5g", 1)
        s.wait(nc.vector, "p5g", v1)
        v1 = s.inc(nc.vector.scalar_tensor_tensor(pm[:], mask1[:], -1e9, pe_[:],
                                                  OP.mult, OP.add), "p5g", 1)
        s.wait(nc.vector, "p5g", v1)
        v1 = s.inc(nc.vector.tensor_reduce(m2[:], pm[:], AX.X, OP.max), "p5g", 1)
        s.wait(nc.vector, "p5g", v1)
        v1 = s.inc(nc.vector.tensor_tensor(s12[:], m1[:], m2[:], OP.add), "p5g", 1)
        s.wait(nc.vector, "p5g", v1)
        v1 = s.inc(nc.vector.reciprocal(rs[:], s12[:]), "p5g", 1)
        s.wait(nc.vector, "p5g", v1)
        v1 = s.inc(nc.vector.tensor_scalar(mask2[:], pe_[:], m2[:], None, OP.is_ge), "p5g", 1)
        s.wait(nc.vector, "p5g", v1)
        s.wait(nc.vector, "p5go", s.val("p5go"))
        v1 = s.inc(nc.vector.scalar_tensor_tensor(combv[:], pe_[:], rs[:], mask2[:],
                                                  OP.mult, OP.mult), "p5g", 1)
        cdr = s.inc(nc.vector.drain(), "p5gd", 1)
        s.wait(nc.gpsimd, "p5gd", cdr)
        sd = nc.gpsimd.dma_start(t["combt"][:, blk, :], combv[:])
        s.inc(sd, "p5go", 16)
    s.wait(nc.gpsimd, "p5go", s.val("p5go"))
    s.wait(nc.gpsimd, "p5to", s.val("p5to"))
    s.wait(nc.scalar, "p5xo", s.val("p5xo"))
    k.pop_scope()


def emit_moe_final(k: Kb, t, out_key="out"):
    k.push_scope()
    nc, s = k.nc, k.s
    ident = k.sb("p6_ident", [128, 128], BF16)
    g2r = k.sb("p6_g2r", [128, D], F32)
    be2r = k.sb("p6_be2r", [128, D], F32)
    b2f = k.sb("p6_b2f", [128, D], F32)
    b2b = k.sb("p6_b2b", [128, D], BF16)
    comb = k.sb("p6_comb", [128, NBLK, 8], F32)
    cbb = k.sb("p6_cbb", [128, 8], BF16)
    cbT = k.sb("p6_cbT", [128, 128], BF16)
    ya = k.sb("p6_ya", [128, D], F32)
    x1f = k.sb("p6_x1f", [128, D], F32)
    s3b = k.sb("p6_s3b", [128, D], F32)
    yt = k.sb("p6_yt", [128, D], F32)
    f2 = k.sb("p6_f2", [128, D], F32)
    f3 = k.sb("p6_f3", [128, D], F32)
    ot = k.sb("p6_ot", [128, D], F32)
    tmp = {"bn": k.sb("p6_bn", [128, 12], F32), "mv": k.sb("p6_mv", [128, 2], F32),
           "t1": k.sb("p6_t1", [128, 1], F32), "t2": k.sb("p6_t2", [128, 1], F32)}

    s.inc(nc.sync.dma_start(ident[:], t["ident"]), "p6w", 16)
    s.inc(nc.sync.dma_start(g2r[:], t["g2_rep"]), "p6w", 16)
    s.inc(nc.sync.dma_start(be2r[:], t["be2_rep"]), "p6w", 16)
    s.inc(nc.sync.dma_start(b2f[0:8, :], t["b2"]), "p6w", 16)
    s.inc(nc.sync.dma_start(comb[:], t["combt"]), "p6w", 16)
    s.wait(nc.vector, "p6w", s.val("p6w"))
    nc.vector.tensor_copy(b2b[:], b2f[:])
    wrdy = s.inc(nc.vector.drain(), "p6v", 1)
    s.wait(nc.tensor, "p6v", wrdy)

    def selfwait():
        v = s.inc(nc.vector.nop(), "p6sw", 1)
        s.wait(nc.vector, "p6sw", v)

    in_free = 0
    for blk in range(NBLK):
        s.wait(nc.scalar, "p6v", in_free)
        lv = s.inc(nc.scalar.dma_start(ya[:], t["yacc"][bass.ts(blk, 128), :]), "p6x", 16)
        lv = s.inc(nc.scalar.dma_start(x1f[:], t["x1"][bass.ts(blk, 128), :]), "p6x", 16)
        lv = s.inc(nc.scalar.dma_start(s3b[:], t["s3"][bass.ts(blk, 128), :]), "p6x", 16)
        s.wait(nc.vector, "mm", s.val("mm"))
        nc.vector.tensor_copy(cbb[:], comb[:, blk, :])
        cv = s.inc(nc.vector.drain(), "p6v", 1)
        s.wait(nc.tensor, "p6v", cv)
        s.wait(nc.tensor, "p6tv", s.val("p6tv"))
        mm = nc.tensor.transpose(k.pt[0:8, 0:128], cbb[:], ident[:])
        tv = s.inc(mm, "p6t", 1)
        tv = s.inc(nc.tensor.nop(), "p6t", 1)
        s.wait(nc.vector, "p6t", tv)
        nc.vector.tensor_copy(cbT[0:8, :], k.pt[0:8, 0:128])
        ctv = s.inc(nc.vector.drain(), "p6tv", 1)
        s.wait(nc.tensor, "p6tv", ctv)
        s.wait(nc.tensor, "p6e", s.val("p6e"))
        for half in range(2):
            mm = nc.tensor.matmul(k.pb[half][:], cbT[0:8, :], b2b[0:8, bass.ts(half, 512)],
                                  start=True, stop=True)
        mv_ = s.inc(mm, "mm", 1)
        s.wait(nc.vector, "mm", mv_)
        s.wait(nc.vector, "p6x", lv)
        nc.vector.tensor_tensor(yt[:, 0:512], k.pb[0][:], ya[:, 0:512], OP.add)
        nc.vector.tensor_tensor(yt[:, 512:1024], k.pb[1][:], ya[:, 512:1024], OP.add)
        ev = s.inc(nc.vector.drain(), "p6e", 1)
        selfwait()
        nc.vector.tensor_tensor(yt[:], yt[:], x1f[:], OP.add)
        selfwait()
        _ln_plain(k, None, yt, f2, tmp, w_rep=g2r, b_rep=be2r)
        selfwait()
        _ln_plain(k, None, f2, f3, tmp)
        selfwait()
        s.wait(nc.vector, "p6o", s.val("p6o"))
        nc.vector.tensor_tensor(ot[:], f3[:], s3b[:], OP.add)
        ov = in_free = s.inc(nc.vector.drain(), "p6v", 1)
        s.wait(nc.gpsimd, "p6v", ov)
        sd = nc.gpsimd.dma_start(t[out_key][bass.ts(blk, 128), :], ot[:])
        s.inc(sd, "p6o", 16)
    s.wait(nc.gpsimd, "p6o", s.val("p6o"))
    k.pop_scope()


def emit_l1a_qkv(k: Kb, t):
    k.push_scope()
    nc, s = k.nc, k.s
    wq = k.sb("p3_wq", [128, 8, 2, 512], BF16)
    wk = k.sb("p3_wk", [128, 8, 2, 512], BF16)
    wv = k.sb("p3_wv", [128, 8, 2, 512], BF16)
    ident = k.sb("p3_ident", [128, 128], BF16)
    xs = k.sb("p3_xs", [128, D], F32)
    xn = k.sb("p3_xn", [128, D], F32)
    xnb = k.sb("p3_xnb", [128, D], BF16)
    xnT = k.sb("p3_xnT", [128, 8, 512], BF16)
    osb = [k.sb(f"p3_o{i}", [128, D], BF16) for i in range(3)]
    tmp = {"bn": k.sb("p3_bn", [128, 12], F32), "mv": k.sb("p3_mv", [128, 2], F32),
           "t1": k.sb("p3_t1", [128, 1], F32), "t2": k.sb("p3_t2", [128, 1], F32)}

    for nm, w in (("wqT_bf", wq), ("wkT_bf", wk), ("wvT_bf", wv)):
        s.inc(nc.sync.dma_start(w[:], t[nm].rearrange("c h p n -> p c h n")), "p3w", 16)
    s.inc(nc.sync.dma_start(ident[:], t["ident"]), "p3w", 16)
    s.wait(nc.tensor, "p3w", s.val("p3w"))

    xs_free = 0
    o_free = [0, 0, 0]
    for g in range(NGRP):
        for bi in range(4):
            blk = g * 4 + bi
            s.wait(nc.scalar, "p3v", xs_free)
            lv = s.inc(nc.scalar.dma_start(xs[:], t["s2"][bass.ts(blk, 128), :]), "p3x", 16)
            s.wait(nc.vector, "p3x", lv)
            s.wait(nc.vector, "p3t", s.val("p3t"))
            s.wait(nc.vector, "p3xo", s.val("p3xo"))
            _ln_plain(k, None, xs, xn, tmp)
            v = s.inc(nc.vector.nop(), "p3v", 1)
            s.wait(nc.vector, "p3v", v)
            nc.vector.tensor_copy(xnb[:], xn[:])
            xv = xs_free = s.inc(nc.vector.drain(), "p3v", 1)
            s.wait(nc.scalar, "p3v", xv)
            s.inc(nc.scalar.dma_start(t["xn2"][bass.ts(blk, 128), :], xn[:]), "p3xo", 16)
            s.wait(nc.tensor, "p3v", xv)
            s.wait(nc.tensor, "p3tv", s.val("p3tv"))
            for c in range(8):
                mm = nc.tensor.transpose(k.pt[:, bass.ts(c, 128)],
                                         xnb[:, bass.ts(c, 128)], ident[:])
            tv = s.inc(mm, "p3t", 1)
            tv = s.inc(nc.tensor.nop(), "p3t", 1)
            s.wait(nc.vector, "p3t", tv)
            if bi == 0:
                s.wait(nc.vector, "mm", s.val("mm"))
            nc.vector.tensor_copy(xnT[:, :, bi * 128:(bi + 1) * 128],
                                  k.pt[:].rearrange("p (c n) -> p c n", c=8))
            s.inc(nc.vector.drain(), "p3tv", 1)
        s.wait(nc.tensor, "p3tv", s.val("p3tv"))
        for wi, (w, dstname) in enumerate(((wq, "q"), (wk, "k"), (wv, "v"))):
            for bi in range(4):
                blk = g * 4 + bi
                oi = (wi * 4 + bi) % 3
                s.wait(nc.tensor, "p3e", s.val("p3e"))
                for half in range(2):
                    hp = k.pb[half][:]
                    for c in range(8):
                        mm = nc.tensor.matmul(hp, xnT[:, c, bass.ts(bi, 128)],
                                              w[:, c, half, :],
                                              start=(c == 0), stop=(c == 7))
                mv_ = s.inc(mm, "mm", 1)
                s.wait(nc.vector, "mm", mv_)
                s.wait(nc.vector, f"p3o{oi}", o_free[oi])
                nc.vector.tensor_copy(osb[oi][:, 0:512], k.pb[0][:])
                nc.vector.tensor_copy(osb[oi][:, 512:1024], k.pb[1][:])
                ev = s.inc(nc.vector.drain(), "p3e", 1)
                s.wait(nc.gpsimd, "p3e", ev)
                sd = nc.gpsimd.dma_start(t[dstname][bass.ts(blk, 128), :], osb[oi][:])
                o_free[oi] = s.inc(sd, f"p3o{oi}", 16)
    for oi in range(3):
        s.wait(nc.gpsimd, f"p3o{oi}", s.val(f"p3o{oi}"))
    s.wait(nc.scalar, "p3xo", s.val("p3xo"))
    k.pop_scope()


def emit_l1a_norm(k: Kb, t):
    k.push_scope()
    nc, s = k.nc, k.s
    aob = [k.sb(f"p3n_ao{i}", [128, D], BF16) for i in range(2)]
    s2b = [k.sb(f"p3n_s2{i}", [128, D], F32) for i in range(2)]
    yb = [k.sb(f"p3n_y{i}", [128, D], F32) for i in range(2)]
    part = k.sb("p3n_part", [128, 2], F32)
    tpart = k.sb("p3n_t", [128, 2], F32)
    sq = k.sb("p3n_sq", [128, D], F32)
    onesc = k.sb("p3n_ones", [128, 1], F32)
    musd = k.sb("p3n_musd", [128, 2], F32)
    sc = k.sb("p3n_sc", [1, 4], F32)
    prow = k.sb("p3n_prow", [1, 256], F32)
    onesr = k.sb("p3n_onesr", [1, 128], F32)

    def vsw():
        v = s.inc(nc.vector.nop(), "p3n_v", 1)
        s.wait(nc.vector, "p3n_v", v)

    nc.vector.memset(part[:], 0.0)
    nc.vector.memset(onesc[:], 1.0)
    nc.vector.memset(onesr[:], 1.0)
    vsw()

    ld_free = [0, 0]
    for blk in range(NBLK):
        j = blk % 2
        s.wait(nc.scalar, "p3n_c", ld_free[j])
        s.inc(nc.scalar.dma_start(aob[j][:], t["ao"][bass.ts(blk, 128), :]), f"p3n_l{j}", 16)
        lv = s.inc(nc.scalar.dma_start(s2b[j][:], t["xn2"][bass.ts(blk, 128), :]), f"p3n_l{j}", 16)
        s.wait(nc.vector, f"p3n_l{j}", lv)
        s.wait(nc.vector, f"p3n_s{j}", s.val(f"p3n_s{j}"))
        nc.vector.tensor_tensor(yb[j][:], aob[j][:], s2b[j][:], OP.add)
        yv = s.inc(nc.vector.drain(), "p3n_y", 1)
        v = s.inc(nc.vector.reduce_sum(tpart[:, 0:1], yb[j][:], AX.X), "p3n_v", 1)
        s.wait(nc.scalar, "p3n_y", yv)
        sv = s.inc(nc.scalar.activation(sq[:], yb[j][:], AF.Square,
                                        accum_out=tpart[:, 1:2]), "p3n_sc", 1)
        s.wait(nc.vector, "p3n_sc", sv)
        s.wait(nc.vector, "p3n_v", v)
        v = s.inc(nc.vector.tensor_tensor(part[:], part[:], tpart[:], OP.add), "p3n_v", 1)
        s.wait(nc.vector, "p3n_v", v)
        ld_free[j] = s.inc(nc.vector.nop(), "p3n_c", 1)
        s.wait(nc.gpsimd, "p3n_c", ld_free[j])
        sd = nc.gpsimd.dma_start(t["yg"][bass.ts(blk, 128), :], yb[j][:])
        s.inc(sd, f"p3n_s{j}", 16)

    cv = s.inc(nc.vector.drain(), "p3n_v", 1)
    s.wait(nc.gpsimd, "p3n_v", cv)
    s.inc(nc.gpsimd.dma_start(t["gscr"][0:1, 0:128].rearrange("a b -> b a"),
                              part[:, 0:1]), "p3n_g1", 16)
    sd = s.inc(nc.gpsimd.dma_start(t["gscr"][0:1, 128:256].rearrange("a b -> b a"),
                                   part[:, 1:2]), "p3n_g1", 16)
    s.wait(nc.sync, "p3n_g1", sd)
    sr = s.inc(nc.sync.dma_start(prow[:], t["gscr"][0:1, 0:256]), "p3n_g2", 16)
    s.wait(nc.vector, "p3n_g2", sr)
    NTD = float(NT * D)
    v = s.inc(nc.vector.reduce_sum(sc[0:1, 0:1], prow[0:1, 0:128], AX.X), "p3n_v", 1)
    s.wait(nc.vector, "p3n_v", v)
    v = s.inc(nc.vector.reduce_sum(sc[0:1, 1:2], prow[0:1, 128:256], AX.X), "p3n_v", 1)
    s.wait(nc.vector, "p3n_v", v)
    v = s.inc(nc.vector.tensor_scalar(sc[0:1, 0:2], sc[0:1, 0:2], 1.0 / NTD, None,
                                      OP.mult), "p3n_v", 1)
    s.wait(nc.vector, "p3n_v", v)
    v = s.inc(nc.vector.tensor_tensor(sc[0:1, 2:3], sc[0:1, 0:1], sc[0:1, 0:1],
                                      OP.mult), "p3n_v", 1)
    s.wait(nc.vector, "p3n_v", v)
    v = s.inc(nc.vector.tensor_tensor(sc[0:1, 3:4], sc[0:1, 1:2], sc[0:1, 2:3],
                                      OP.subtract), "p3n_v", 1)
    s.wait(nc.vector, "p3n_v", v)
    v = s.inc(nc.vector.tensor_scalar(sc[0:1, 3:4], sc[0:1, 3:4], 1e-5, None, OP.add),
              "p3n_v", 1)
    s.wait(nc.scalar, "p3n_v", v)
    sv = s.inc(nc.scalar.activation(sc[0:1, 3:4], sc[0:1, 3:4], AF.Sqrt), "p3n_sc", 1)
    s.wait(nc.vector, "p3n_sc", sv)
    v = s.inc(nc.vector.reciprocal(sc[0:1, 1:2], sc[0:1, 3:4]), "p3n_v", 1)
    s.wait(nc.vector, "p3n_v", v)
    bv = s.inc(nc.vector.drain(), "p3n_v", 1)
    s.wait(nc.gpsimd, "p3n_v", bv)
    sd = s.inc(nc.gpsimd.dma_start(t["gscr"][0:1, 256:258], sc[0:1, 0:2]), "p3n_g1", 16)
    s.wait(nc.sync, "p3n_g1", sd)
    bsrc = bass.AP(t["gscr"].tensor, 256, [[0, 128], [1, 2]])
    sr = s.inc(nc.sync.dma_start(musd[:], bsrc), "p3n_g2", 16)
    s.wait(nc.vector, "p3n_g2", sr)

    st_free = [0, 0]
    tmp2 = {"bn": k.sb("p3n_bn", [128, 12], F32), "mv": k.sb("p3n_mv", [128, 2], F32),
            "t1": k.sb("p3n_t1", [128, 1], F32), "t2": k.sb("p3n_t2", [128, 1], F32)}
    for blk in range(NBLK):
        j = blk % 2
        s.wait(nc.scalar, "p3n_v", st_free[j])
        lv = s.inc(nc.scalar.dma_start(yb[j][:], t["yg"][bass.ts(blk, 128), :]),
                   f"p3n_l2{j}", 16)
        lv = s.inc(nc.scalar.dma_start(aob[j].bitcast(F32) if False else s2b[j][:],
                                       t["s2"][bass.ts(blk, 128), :]), f"p3n_l2{j}", 16)
        s.wait(nc.vector, f"p3n_l2{j}", lv)
        s.wait(nc.vector, f"p3n_o{j}", s.val(f"p3n_o{j}"))
        nc.vector.tensor_scalar(yb[j][:], yb[j][:], musd[:, 0:1], musd[:, 1:2],
                                OP.subtract, OP.mult)
        gv = s.inc(nc.vector.drain(), "p3n_v", 1)
        s.wait(nc.vector, "p3n_v", gv)
        _ln_plain(k, None, yb[j], sq, tmp2)
        gv = s.inc(nc.vector.nop(), "p3n_v", 1)
        s.wait(nc.vector, "p3n_v", gv)
        nc.vector.tensor_tensor(sq[:], sq[:], s2b[j][:], OP.add)
        ov = s.inc(nc.vector.drain(), "p3n_v", 1)
        st_free[j] = ov
        s.wait(nc.gpsimd, "p3n_v", ov)
        sd = nc.gpsimd.dma_start(t["s3"][bass.ts(blk, 128), :], sq[:])
        s.inc(sd, f"p3n_o{j}", 16)
    for j in range(2):
        s.wait(nc.gpsimd, f"p3n_o{j}", s.val(f"p3n_o{j}"))
        s.wait(nc.gpsimd, f"p3n_s{j}", s.val(f"p3n_s{j}"))
    k.pop_scope()


def emit_l0_quant(k: Kb, t, wkey, dstkey, R, C, tag="q", scr=None):
    k.push_scope()
    nc, s = k.nc, k.s
    NB = R // 128
    stg = [k.sb(f"{tag}_stg{i}", [128, C], F32) for i in range(2)]
    aw = k.sb(tag + "_aw", [128, C], F32)
    sg = k.sb(tag + "_sg", [128, C], F32)
    msk = k.sb(tag + "_msk", [128, C], F32)
    qb = [k.sb(f"{tag}_qb{i}", [128, C], BF16) for i in range(2)]
    acc = k.sb(tag + "_acc", [128, 2], F32)
    tpart = k.sb(tag + "_tp", [128, 2], F32)
    onesc = k.sb(tag + "_ones", [128, 1], F32)
    thr = k.sb(tag + "_thr", [128, 1], F32)
    sc1 = k.sb(tag + "_sc1", [1, 2], F32)
    sc1r = k.sb(tag + "_sc1r", [1, 128], F32)

    def vsw():
        v = s.inc(nc.vector.nop(), "q_v", 1)
        s.wait(nc.vector, "q_v", v)

    nc.vector.memset(acc[:], 0.0)
    nc.vector.memset(tpart[:], 0.0)
    nc.vector.memset(onesc[:], 1.0)
    vsw()
    ld_free = [0, 0]
    for b in range(NB):
        j = b % 2
        s.wait(nc.sync, "q_v", ld_free[j])
        lv = s.inc(nc.sync.dma_start(stg[j][:], t[wkey][bass.ts(b, 128), :]), f"q_l{j}", 16)
        s.wait(nc.vector, f"q_l{j}", lv)
        v = s.inc(nc.vector.tensor_reduce(tpart[:, 0:1], stg[j][:], AX.X, OP.add,
                                          apply_absolute_value=True), "q_v", 1)
        s.wait(nc.vector, "q_v", v)
        ld_free[j] = s.inc(nc.vector.tensor_tensor(acc[:], acc[:], tpart[:], OP.add),
                           "q_v", 1)
        s.wait(nc.vector, "q_v", ld_free[j])
    cv = s.inc(nc.vector.drain(), "q_v", 1)
    s.wait(nc.gpsimd, "q_v", cv)
    sd = s.inc(nc.gpsimd.dma_start(scr[0:1, 0:128].rearrange("a b -> b a"),
                                   acc[:, 0:1]), "q_g1", 16)
    s.wait(nc.sync, "q_g1", sd)
    sr = s.inc(nc.sync.dma_start(sc1r[:], scr[0:1, 0:128]), "q_g2", 16)
    s.wait(nc.vector, "q_g2", sr)
    v = s.inc(nc.vector.reduce_sum(sc1[0:1, 0:1], sc1r[:], AX.X), "q_v", 1)
    s.wait(nc.vector, "q_v", v)
    v = s.inc(nc.vector.tensor_scalar(sc1[0:1, 1:2], sc1[0:1, 0:1], 1.0 / (R * C), 1e-6,
                                      OP.mult, OP.add), "q_v", 1)
    s.wait(nc.vector, "q_v", v)
    v = s.inc(nc.vector.tensor_scalar(sc1[0:1, 1:2], sc1[0:1, 1:2], 0.5, None, OP.mult),
              "q_v", 1)
    bv = s.inc(nc.vector.drain(), "q_v", 1)
    s.wait(nc.gpsimd, "q_v", bv)
    sd = s.inc(nc.gpsimd.dma_start(scr[0:1, 128:129], sc1[0:1, 1:2]), "q_g1", 16)
    s.wait(nc.sync, "q_g1", sd)
    bsrc = bass.AP(scr.tensor, 128, [[0, 128], [1, 1]])
    sr = s.inc(nc.sync.dma_start(thr[:], bsrc), "q_g2", 16)
    s.wait(nc.vector, "q_g2", sr)
    ld_free = [0, 0]
    out_done = [0, 0]
    prev_qv = 0
    for b in range(NB):
        j = b % 2
        s.wait(nc.sync, "q_v", ld_free[j])
        lv = s.inc(nc.sync.dma_start(stg[j][:], t[wkey][bass.ts(b, 128), :]), f"q_l{j}", 16)
        s.wait(nc.scalar, f"q_l{j}", lv)
        s.wait(nc.scalar, "q_v", prev_qv)
        sv = s.inc(nc.scalar.activation(aw[:], stg[j][:], AF.Abs), "q_s", 1)
        sv = s.inc(nc.scalar.activation(sg[:], stg[j][:], AF.Sign), "q_s", 1)
        s.wait(nc.vector, "q_s", sv)
        v = s.inc(nc.vector.tensor_scalar(msk[:], aw[:], thr[:], None, OP.is_ge), "q_v", 1)
        ld_free[j] = v
        s.wait(nc.vector, "q_v", v)
        s.wait(nc.vector, f"q_o{j}", out_done[j])
        nc.vector.tensor_tensor(qb[j][:], msk[:], sg[:], OP.mult)
        qv = prev_qv = s.inc(nc.vector.drain(), "q_v", 1)
        s.wait(nc.gpsimd, "q_v", qv)
        sd = nc.gpsimd.dma_start(t[dstkey][b], qb[j][:])
        out_done[j] = s.inc(sd, f"q_o{j}", 16)
    for j in range(2):
        s.wait(nc.gpsimd, f"q_o{j}", s.val(f"q_o{j}"))
    k.pop_scope()


def emit_l0_ffn(k: Kb, t):
    k.push_scope()
    nc, s = k.nc, k.s
    wfc = k.sb("f0_wfc", [128, 8, FF], BF16)
    wpj = k.sb("f0_wpj", [128, FT, D], BF16)
    lnw = k.sb("f0_lnw", [128, FF], F32)
    lnb = k.sb("f0_lnb", [128, FF], F32)
    ident = k.sb("f0_ident", [128, 128], BF16)
    xs = k.sb("f0_xs", [128, D], F32)
    xnb = k.sb("f0_xnb", [128, D], BF16)
    xnT = k.sb("f0_xnT", [128, 8, 128], BF16)
    hsb = k.sb("f0_h", [128, FF], F32)
    hnb = k.sb("f0_hnb", [128, FF], BF16)
    hT = k.sb("f0_hT", [128, FT, 128], BF16)
    ot_ = k.sb("f0_ot", [128, D], F32)
    ot = [ot_, ot_]
    tmp = {"bn": k.sb("f0_bn", [128, 48], F32), "mv": k.sb("f0_mv", [128, 2], F32),
           "t1": k.sb("f0_t1", [128, 1], F32), "t2": k.sb("f0_t2", [128, 1], F32)}

    s.inc(nc.sync.dma_start(wfc[:], t["wfcq"].rearrange("c p f -> p c f")), "f0_w", 16)
    s.inc(nc.sync.dma_start(wpj[:], t["wprojq"].rearrange("f p d -> p f d")), "f0_w", 16)
    s.inc(nc.sync.dma_start(lnw[:], t["lnw_rep"]), "f0_w", 16)
    s.inc(nc.sync.dma_start(lnb[:], t["lnb_rep"]), "f0_w", 16)
    wl = s.inc(nc.sync.dma_start(ident[:], t["ident"]), "f0_w", 16)
    s.wait(nc.tensor, "f0_w", wl)
    s.wait(nc.vector, "f0_w", wl)

    xs_free = 0
    st_free = [0, 0]
    def lnfull(xin, xout, n, w_rep=None, b_rep=None):
        for i in range(n):
            mm = nc.vector.bn_stats(tmp["bn"][:, 6 * i:6 * i + 6],
                                    xin[:, 512 * i:512 * (i + 1)])
        v = s.inc(mm, "ln", 1)
        s.wait(nc.vector, "ln", v)
        v = s.inc(nc.vector.bn_aggr(tmp["mv"][:], tmp["bn"][:, 0:6 * n]), "ln", 1)
        s.wait(nc.vector, "ln", v)
        v = s.inc(nc.vector.tensor_scalar(tmp["t1"][:], tmp["mv"][:, 1:2], 1e-5, None,
                                          OP.add), "ln", 1)
        s.wait(nc.scalar, "ln", v)
        v = s.inc(nc.scalar.activation(tmp["t1"][:], tmp["t1"][:], AF.Sqrt), "ln", 1)
        s.wait(nc.vector, "ln", v)
        v = s.inc(nc.vector.reciprocal(tmp["t2"][:], tmp["t1"][:]), "ln", 1)
        s.wait(nc.vector, "ln", v)
        nc.vector.tensor_scalar(xout[:], xin[:], tmp["mv"][:, 0:1], tmp["t2"][:],
                                OP.subtract, OP.mult)
        if w_rep is not None:
            v = s.inc(nc.vector.nop(), "ln", 1)
            s.wait(nc.vector, "ln", v)
            nc.vector.tensor_tensor(xout[:], xout[:], w_rep[:], OP.mult)
            v = s.inc(nc.vector.nop(), "ln", 1)
            s.wait(nc.vector, "ln", v)
            nc.vector.tensor_tensor(xout[:], xout[:], b_rep[:], OP.add)

    for blk in range(NBLK):
        j = blk % 2
        s.wait(nc.scalar, "f0_v", xs_free)
        lv = s.inc(nc.scalar.dma_start(xs[:], t["s1"][bass.ts(blk, 128), :]), "f0_x", 16)
        s.wait(nc.vector, "f0_x", lv)
        s.wait(nc.vector, "f0_t", s.val("f0_t"))
        lnfull(xs, xnb, 2)
        xv = s.inc(nc.vector.drain(), "f0_v", 1)
        s.wait(nc.tensor, "f0_v", xv)
        s.wait(nc.tensor, "f0_tv", s.val("f0_tv"))
        for c in range(8):
            mm = nc.tensor.transpose(k.pt[:, bass.ts(c, 128)], xnb[:, bass.ts(c, 128)],
                                     ident[:])
        tv = s.inc(mm, "f0_t", 1)
        tv = s.inc(nc.tensor.nop(), "f0_t", 1)
        s.wait(nc.vector, "f0_t", tv)
        s.wait(nc.vector, "mm", s.val("mm"))
        nc.vector.tensor_copy(xnT[:], k.pt[:].rearrange("p (c n) -> p c n", c=8))
        xtv = s.inc(nc.vector.drain(), "f0_tv", 1)
        s.wait(nc.tensor, "f0_tv", xtv)
        for f8 in range(8):
            hp = k.pb[f8 % 2][:]
            s.wait(nc.tensor, "f0_e", s.val("f0_e") - 1 if f8 >= 2 or blk > 0 else 0)
            for c in range(8):
                mm = nc.tensor.matmul(hp, xnT[:, c, :], wfc[:, c, bass.ts(f8, 512)],
                                      start=(c == 0), stop=(c == 7))
            hv = s.inc(mm, "mm", 1)
            s.wait(nc.scalar, "mm", hv)
            ge = nc.scalar.activation(hsb[:, bass.ts(f8, 512)], hp, AF.Gelu)
            s.inc(ge, "f0_e", 1)
        gl = s.inc(nc.scalar.drain(), "f0_e", 1)
        s.wait(nc.vector, "f0_e", gl)
        lnfull(hsb, hsb, 8, w_rep=lnw, b_rep=lnb)
        v = s.inc(nc.vector.nop(), "f0_v", 1)
        s.wait(nc.vector, "f0_v", v)
        nc.vector.tensor_copy(hnb[:], hsb[:])
        hv2 = s.inc(nc.vector.drain(), "f0_v", 1)
        for bt in range(4):
            s.wait(nc.tensor, "f0_v", hv2)
            s.wait(nc.tensor, "f0_tv", s.val("f0_tv"))
            for ti in range(8):
                c = bt * 8 + ti
                mm = nc.tensor.transpose(k.pt[:, bass.ts(ti, 128)],
                                         hnb[:, bass.ts(c, 128)], ident[:])
            tv = s.inc(mm, "f0_t", 1)
            tv = s.inc(nc.tensor.nop(), "f0_t", 1)
            s.wait(nc.vector, "f0_t", tv)
            nc.vector.tensor_copy(hT[:, bass.ts(bt, 8), :],
                                  k.pt[:].rearrange("p (c n) -> p c n", c=8))
            s.inc(nc.vector.drain(), "f0_tv", 1)
        htv = s.val("f0_tv")
        s.wait(nc.tensor, "f0_tv", htv)
        for half in range(2):
            yp = k.pb[4 + half][:]
            for f in range(FT):
                mm = nc.tensor.matmul(yp, hT[:, f, :], wpj[:, f, bass.ts(half, 512)],
                                      start=(f == 0), stop=(f == FT - 1))
        fence = nc.tensor.matmul(k.pg[0:1, 100:101], ident[0:128, 0:1], ident[0:128, 0:1],
                                 start=True, stop=True)
        yv = s.inc(fence, "mm", 1)
        yv = s.inc(nc.tensor.nop(), "mm", 1)
        s.wait(nc.vector, "mm", yv)
        s.wait(nc.vector, "f0_o0", st_free[0])
        s.wait(nc.vector, "f0_o1", st_free[1])
        nc.vector.tensor_copy(ot[j][64:128, 0:512], ot[j][64:128, 0:512])
        nc.vector.tensor_copy(ot[j][:, 0:512], k.pb[4][:])
        nc.vector.tensor_copy(ot[j][:, 512:1024], k.pb[5][:])
        v = s.inc(nc.vector.drain(), "f0_v", 1)
        s.wait(nc.vector, "f0_v", v)
        lnfull(ot[j], ot[j], 2)
        v = s.inc(nc.vector.nop(), "f0_v", 1)
        s.wait(nc.vector, "f0_v", v)
        nc.vector.tensor_tensor(ot[j][:], ot[j][:], xs[:], OP.add)
        ov = xs_free = s.inc(nc.vector.drain(), "f0_v", 1)
        s.wait(nc.gpsimd, "f0_v", ov)
        sd = nc.gpsimd.dma_start(t["s2"][bass.ts(blk, 128), :], ot[j][:])
        st_free[j] = s.inc(sd, f"f0_o{j}", 16)
    for j in range(2):
        s.wait(nc.gpsimd, f"f0_o{j}", s.val(f"f0_o{j}"))
    k.pop_scope()


def emit_l0_attn(k: Kb, t, nchunks=NBLK):
    import os as _o
    CUT = int(_o.environ.get("CUT", "9"))
    """s1 = x0 + LN_plain(linattn(LN_plain(x0))), chunked causal-inclusive scan.

    All matmul operands at partition base 0 (base-64 operands crash the PE).
    PSUM: pb0/1 = Qp then A-slots; pb2/3 = S state (persistent, [0:64, 8, 64]
    even/odd heads); pb4/5 = Vp then O (time-shared); pt transposes; pg free.
    """
    k.push_scope()
    nc, s = k.nc, k.s
    wq = k.sb("l0_wq", [128, 8, 2, 512], BF16)
    wv = k.sb("l0_wv", [128, 8, 2, 512], BF16)
    qnw = k.sb("l0_qnw", [128, D], F32)
    vnw = k.sb("l0_vnw", [128, D], F32)
    tri = k.sb("l0_tri", [128, 128], F32)
    ident = k.sb("l0_ident", [128, 128], BF16)
    xs = k.sb("l0_xs", [128, D], F32)
    xn = k.sb("l0_xn", [128, D], F32)
    xnb = k.sb("l0_xnb", [128, D], BF16)
    xnT = k.sb("l0_xnT", [128, 8, 128], BF16)
    xnK = k.sb("l0_xnK", [64, 16, 128], BF16)
    sq = k.sb("l0_sq", [128, D], F32)
    rr = k.sb("l0_r", [128, 16], F32)
    qsc = k.sb("l0_qsc", [128, D], F32)
    qbf = k.sb("l0_qbf", [128, D], BF16)
    vbf = k.sb("l0_vbf", [128, D], BF16)
    qT = k.sb("l0_qT", [64, 16, 128], BF16)
    Ssb = k.sb("l0_Ssb", [64, 16, 64], BF16)
    Asb = k.sb("l0_Asb", [128, 16, 128], BF16)
    osb = k.sb("l0_osb", [128, D], F32)
    ot = [k.sb(f"l0_ot{i}", [128, D], F32) for i in range(2)]
    can = k.sb("l0_can", [128, 8], F32)
    tmp = {"bn": k.sb("l0_bn", [128, 12], F32), "mv": k.sb("l0_mv", [128, 2], F32),
           "t1": k.sb("l0_t1", [128, 1], F32), "t2": k.sb("l0_t2", [128, 1], F32)}

    s.inc(nc.sync.dma_start(wq[:], t["wqT_bf"].rearrange("c h p n -> p c h n")), "l0_w", 16)
    s.inc(nc.sync.dma_start(wv[:], t["wvT_bf"].rearrange("c h p n -> p c h n")), "l0_w", 16)
    s.inc(nc.sync.dma_start(qnw[:], t["qnw_rep"]), "l0_w", 16)
    s.inc(nc.sync.dma_start(vnw[:], t["vnw_rep"]), "l0_w", 16)
    s.inc(nc.sync.dma_start(tri[:], t["triT"]), "l0_w", 16)
    wl = s.inc(nc.sync.dma_start(ident[:], t["ident"]), "l0_w", 16)
    s.wait(nc.tensor, "l0_w", wl)
    s.wait(nc.vector, "l0_w", wl)

    def vsw():
        v = s.inc(nc.vector.nop(), "l0_v", 1)
        s.wait(nc.vector, "l0_v", v)

    def rms_chain(banks, wrep, outbf, gate_mm):
        s.wait(nc.scalar, "mm", gate_mm)
        s.wait(nc.scalar, "l0_v", s.val("l0_v"))
        nc.scalar.activation(sq[:, 0:512], banks[0][:], AF.Square)
        sv = s.inc(nc.scalar.activation(sq[:, 512:1024], banks[1][:], AF.Square), "l0_s", 1)
        s.wait(nc.vector, "l0_s", sv)
        v = s.inc(nc.vector.tensor_reduce(rr[:], sq.rearrange("p (h c) -> p h c", h=16),
                                          AX.X, OP.add), "l0_v", 1)
        s.wait(nc.vector, "l0_v", v)
        v = s.inc(nc.vector.tensor_scalar(rr[:], rr[:], 1.0 / 64, 1e-8, OP.mult, OP.add),
                  "l0_v", 1)
        s.wait(nc.scalar, "l0_v", v)
        sv = s.inc(nc.scalar.activation(rr[:], rr[:], AF.Sqrt), "l0_s", 1)
        s.wait(nc.vector, "l0_s", sv)
        v = s.inc(nc.vector.reciprocal(rr[:], rr[:]), "l0_v", 1)
        s.wait(nc.vector, "l0_v", v)
        for h in range(16):
            nc.vector.tensor_scalar(qsc[:, 64 * h:64 * h + 64],
                                    banks[h // 8][:, (h % 8) * 64:(h % 8) * 64 + 64],
                                    rr[:, h:h + 1], None, OP.mult)
        vsw()
        nc.vector.tensor_tensor(outbf[:], qsc[:], wrep[:], OP.mult)
        return s.inc(nc.vector.drain(), "l0_v", 1)

    xs_free = 0
    st_free = [0, 0]
    qdone = vdone = 0
    odone = 0
    fence_prev = 0
    for c in range(nchunks):
        j = c % 2
        s.wait(nc.scalar, "l0_v", xs_free)
        lv = s.inc(nc.scalar.dma_start(xs[:], t["x0"][bass.ts(c, 128), :]), "l0_x", 16)
        s.wait(nc.vector, "l0_x", lv)
        s.wait(nc.vector, "l0_t", s.val("l0_t"))
        _ln_plain(k, None, xs, xn, tmp)
        vsw()
        nc.vector.tensor_copy(xnb[:], xn[:])
        xv = s.inc(nc.vector.drain(), "l0_v", 1)
        s.wait(nc.tensor, "l0_v", xv)
        s.wait(nc.tensor, "l0_tv", s.val("l0_tv"))
        for cc in range(8):
            mm = nc.tensor.transpose(k.pt[:, bass.ts(cc, 128)], xnb[:, bass.ts(cc, 128)],
                                     ident[:])
        tv = s.inc(mm, "l0_t", 1)
        tv = s.inc(nc.tensor.nop(), "l0_t", 1)
        s.wait(nc.vector, "l0_t", tv)
        s.wait(nc.vector, "mm", s.val("mm"))
        nc.vector.tensor_copy(xnT[:], k.pt[:].rearrange("p (c n) -> p c n", c=8))
        xtv = s.inc(nc.vector.drain(), "l0_tv", 1)
        for bt in range(2):
            s.wait(nc.tensor, "l0_tv", s.val("l0_tv"))
            for ti in range(8):
                h = bt * 8 + ti
                mm = nc.tensor.transpose(k.pt[0:64, bass.ts(ti, 128)],
                                         xnb[:, 64 * h:64 * h + 64], ident[:])
            tv = s.inc(mm, "l0_t", 1)
            tv = s.inc(nc.tensor.nop(), "l0_t", 1)
            s.wait(nc.vector, "l0_t", tv)
            nc.vector.tensor_copy(xnK[0:64, bass.ts(bt, 8), :],
                                  k.pt[0:64, :].rearrange("p (a n) -> p a n", a=8))
            s.inc(nc.vector.drain(), "l0_tv", 1)
        s.wait(nc.tensor, "l0_tv", s.val("l0_tv"))
        if CUT <= 0:
            xs_free = xv
            continue
        s.wait(nc.tensor, "l0_v", qdone)
        for half in range(2):
            for cc in range(8):
                mm = nc.tensor.matmul(k.pb[half][:], xnT[:, cc, :],
                                      wq[:, cc, half, :], start=(cc == 0), stop=(cc == 7))
        qmv = s.inc(mm, "mm", 1)
        s.wait(nc.tensor, "l0_v", odone)
        for half in range(2):
            for cc in range(8):
                mm = nc.tensor.matmul(k.pb[4 + half][:], xnT[:, cc, :],
                                      wv[:, cc, half, :], start=(cc == 0), stop=(cc == 7))
        vmv = s.inc(mm, "mm", 1)
        qdone_ = rms_chain((k.pb[0], k.pb[1]), qnw, qbf, qmv)
        vdone = rms_chain((k.pb[4], k.pb[5]), vnw, vbf, vmv)
        if CUT <= 1:
            qdone = qdone_
            odone = vdone
            xs_free = vdone
            continue
        s.wait(nc.tensor, "l0_v", qdone_)
        for bt in range(2):
            s.wait(nc.tensor, "l0_tv", s.val("l0_tv"))
            for ti in range(8):
                h = bt * 8 + ti
                mm = nc.tensor.transpose(k.pt[0:64, bass.ts(ti, 128)],
                                         qbf[:, 64 * h:64 * h + 64], ident[:])
            tv = s.inc(mm, "l0_t", 1)
            tv = s.inc(nc.tensor.nop(), "l0_t", 1)
            s.wait(nc.vector, "l0_t", tv)
            nc.vector.tensor_copy(qT[0:64, bass.ts(bt, 8), :],
                                  k.pt[0:64, :].rearrange("p (a n) -> p a n", a=8))
            s.inc(nc.vector.drain(), "l0_tv", 1)
        s.wait(nc.tensor, "l0_tv", s.val("l0_tv"))
        if CUT <= 2:
            qdone = qdone_
            odone = vdone
            xs_free = vdone
            continue
        s.wait(nc.vector, "mm", fence_prev)
        if c == 0:
            nc.vector.memset(Ssb[:], 0.0)
        else:
            sv2 = Ssb.rearrange("p (a b) c -> p a b c", b=2)
            nc.vector.tensor_copy(sv2[0:64, :, 0, :],
                                  k.pb[2][0:64, :].rearrange("p (a c) -> p a c", a=8))
            nc.vector.tensor_copy(sv2[0:64, :, 1, :],
                                  k.pb[3][0:64, :].rearrange("p (a c) -> p a c", a=8))
        nc.vector.drain()
        nc.vector.tensor_copy(can[0:64, 0:8], Ssb[0:64, 0, :][:, 0:8])
        sdone = s.inc(nc.vector.drain(), "l0_v", 1)
        mk = 0
        for wv_i in range(2):
            s.wait(nc.tensor, "l0_v", qdone_ if wv_i == 0 else mk)
            for ti in range(8):
                h = wv_i * 8 + ti
                mm = nc.tensor.matmul(k.pb[ti // 4][:, (ti % 4) * 128:(ti % 4) * 128 + 128],
                                      xnK[0:64, h, :], qT[0:64, h, :],
                                      start=True, stop=True)
            amv = s.inc(mm, "mm", 1)
            amv = s.inc(nc.tensor.nop(), "mm", 1)
            s.wait(nc.vector, "mm", amv)
            for ti in range(8):
                h = wv_i * 8 + ti
                nc.vector.tensor_tensor(Asb[:, h, :],
                                        k.pb[ti // 4][:, (ti % 4) * 128:(ti % 4) * 128 + 128],
                                        tri[:], OP.mult)
            nc.vector.drain()
            nc.vector.tensor_copy(can[:], Asb[:, wv_i * 8, 0:8])
            mk = s.inc(nc.vector.drain(), "l0_v", 1)
            s.wait(nc.tensor, "l0_v", mk)
            s.wait(nc.tensor, "l0_v", sdone)
            s.wait(nc.tensor, "l0_v", vdone)
            for ti in range(8):
                h = wv_i * 8 + ti
                oslot = k.pb[4 + h // 8][:, (h % 8) * 64:(h % 8) * 64 + 64]
                nc.tensor.matmul(oslot, Asb[:, h, :], vbf[:, 64 * h:64 * h + 64],
                                 start=(h % 8 == 0), stop=False)
                nc.tensor.matmul(oslot, qT[0:64, h, :], Ssb[0:64, h, :],
                                 start=False, stop=True)
                mm = nc.tensor.matmul(k.pb[2 + h % 2][0:64, (h // 2) * 64:(h // 2) * 64 + 64],
                                      xnb[:, 64 * h:64 * h + 64],
                                      vbf[:, 64 * h:64 * h + 64],
                                      start=(c == 0 and h <= 1),
                                      stop=(c == nchunks - 1))
            s.inc(mm, "mm", 1)
        if CUT <= 3:
            qdone = qdone_
            odone = vdone
            xs_free = vdone
            continue
        fence = nc.tensor.matmul(k.pg[0:1, 100:101], ident[0:128, 0:1], ident[0:128, 0:1],
                                 start=True, stop=True)
        fence_prev = s.inc(fence, "mm", 1)
        fence_prev = s.inc(nc.tensor.nop(), "mm", 1)
        s.wait(nc.vector, "mm", fence_prev)
        s.wait(nc.vector, f"l0_o{j}", st_free[j])
        nc.vector.tensor_copy(osb[64:128, 0:512], osb[64:128, 0:512])
        nc.vector.tensor_copy(osb[:, 0:512], k.pb[4][:])
        nc.vector.tensor_copy(osb[:, 512:1024], k.pb[5][:])
        odone = s.inc(nc.vector.drain(), "l0_v", 1)
        qdone = qdone_
        vsw()
        _ln_plain(k, None, osb, ot[j], tmp)
        vsw()
        nc.vector.tensor_tensor(ot[j][:], ot[j][:], xs[:], OP.add)
        ov = xs_free = s.inc(nc.vector.drain(), "l0_v", 1)
        s.wait(nc.gpsimd, "l0_v", ov)
        sd = nc.gpsimd.dma_start(t["s1"][bass.ts(c, 128), :], ot[j][:])
        st_free[j] = s.inc(sd, f"l0_o{j}", 16)
    for j in range(2):
        s.wait(nc.gpsimd, f"l0_o{j}", s.val(f"l0_o{j}"))
    k.pop_scope()


def _barrier(k: Kb, recycle=False):
    s = k.s
    for eng in (k.nc.sync, k.nc.tensor, k.nc.vector, k.nc.scalar, k.nc.gpsimd):
        for name in list(s.h):
            v = s.val(name)
            if v > 0:
                eng.wait_ge(s.get(name), v)
    if recycle:
        keep = {"mm", "ln"}
        drop = [n for n in list(s.h) if n not in keep]
        for n in drop:
            s.pool.append((s.h[n], s.v[n]))
            del s.h[n]
            del s.v[n]


def build_fused():
    nc = bass.Bass("TRN2", target_bir_lowering=False)
    t = {}
    EI = lambda n, sh, dt=F32: nc.dram_tensor(n, sh, dt, kind="ExternalInput").ap()
    IN = lambda n, sh, dt=F32: nc.dram_tensor(n, sh, dt, kind="Internal").ap()
    t["ident"] = EI("ident", [128, 128], BF16)
    t["x0"] = EI("x0", [NT, D])
    t["l0_wqT"] = EI("l0_wqT", [D, D]); t["l0_wvT"] = EI("l0_wvT", [D, D])
    t["qnw_rep"] = EI("qnw_rep", [128, D]); t["vnw_rep"] = EI("vnw_rep", [128, D])
    t["triT"] = EI("triT", [128, 128])
    t["wfcT"] = EI("wfcT", [D, FF]); t["wprojT"] = EI("wprojT", [FF, D])
    t["lnw_rep"] = EI("lnw_rep", [128, FF]); t["lnb_rep"] = EI("lnb_rep", [128, FF])
    t["l1a_wqT"] = EI("l1a_wqT", [D, D]); t["l1a_wkT"] = EI("l1a_wkT", [D, D])
    t["l1a_wvT"] = EI("l1a_wvT", [D, D])
    for nm in ("wq", "wk", "wv", "wo"):
        t["l1f_" + nm] = EI("l1f_" + nm, [D, D])
    t["biases"] = EI("biases", [1, 4096])
    t["wgw"] = EI("wgw", [D, 8]); t["bg"] = EI("bg", [1, 8])
    t["w1"] = EI("w1", [E, D, FF]); t["w2"] = EI("w2", [E, FF, D])
    t["b1t"] = EI("b1t", [128, FT, E]); t["b2"] = EI("b2", [8, D])
    t["cos_rep"] = EI("cos_rep", [NT, D]); t["sinS_rep"] = EI("sinS_rep", [NT, D])
    for nm in ("g1_rep", "be1_rep", "g2_rep", "be2_rep"):
        t[nm] = EI(nm, [128, D])
    for nm in ("l0_wqT_bf", "l0_wvT_bf", "l1a_wqT_bf", "l1a_wkT_bf", "l1a_wvT_bf",
               "l1f_wq_bf", "l1f_wk_bf", "l1f_wv_bf", "l1f_wo_bf"):
        t[nm] = IN(nm + "_s", [8, 2, 128, 512], BF16)
    t["wg_bf"] = IN("wg_bf_s", [8, 128, 8], BF16)
    t["w1bf"] = IN("w1bf_s", [E, 8, 128, FF], BF16)
    t["w2bf"] = IN("w2bf_s", [E, FT, 128, D], BF16)
    t["wfcq"] = IN("wfcq_s", [8, 128, FF], BF16)
    t["wprojq"] = IN("wprojq_s", [FT, 128, D], BF16)
    t["qscr_a"] = IN("qscr_a", [1, 256]); t["qscr_b"] = IN("qscr_b", [1, 256])
    t["gscr"] = IN("gscr", [1, 512])
    t["s1"] = IN("s1_s", [NT, D]); t["s2"] = IN("s2_s", [NT, D])
    t["s3"] = IN("s3_s", [NT, D])
    t["q"] = IN("q_s", [NT, D], BF16); t["k"] = IN("k_s", [NT, D], BF16)
    t["v"] = IN("v_s", [NT, D], BF16)
    t["ao1"] = IN("ao1_s", [NT, D], BF16)
    t["yg"] = IN("yg_s", [NT, D]); t["xn2"] = IN("xn2_s", [NT, D])
    t["xn"] = IN("xn_s", [NT, D])
    t["qT"] = IN("qT_s", [D, NT], BF16); t["kT"] = IN("kT_s", [D, NT], BF16)
    t["vh"] = IN("vh_s", [H, NBLK, 128, 64], BF16)
    t["ao2"] = IN("ao2_s", [NT, D], BF16)
    t["x1"] = IN("x1_s", [NT, D])
    t["x1t"] = IN("x1t_s", [NGRP, 8, 128, GRP], BF16)
    t["combt"] = IN("combt_s", [128, NBLK, 8])
    t["yacc"] = IN("yacc_s", [NT, D])
    t["out"] = nc.dram_tensor("out", [NT, D], F32, kind="ExternalOutput").ap()

    es = ExitStack()
    k = Kb(nc, es)
    jobs = []
    for srcnm, dstnm in (("l0_wqT", "l0_wqT_bf"), ("l0_wvT", "l0_wvT_bf"),
                         ("l1a_wqT", "l1a_wqT_bf"), ("l1a_wkT", "l1a_wkT_bf"),
                         ("l1a_wvT", "l1a_wvT_bf"), ("l1f_wq", "l1f_wq_bf"),
                         ("l1f_wk", "l1f_wk_bf"), ("l1f_wv", "l1f_wv_bf"),
                         ("l1f_wo", "l1f_wo_bf")):
        for c in range(8):
            for hh in range(2):
                jobs.append((t[srcnm][c * 128:(c + 1) * 128, hh * 512:(hh + 1) * 512],
                             t[dstnm][c, hh]))
    for c in range(8):
        jobs.append((t["wgw"][c * 128:(c + 1) * 128, :], t["wg_bf"][c]))
    for e in range(E):
        for c in range(8):
            for hh in range(2):
                jobs.append((t["w1"][e, c * 128:(c + 1) * 128, hh * 2048:(hh + 1) * 2048],
                             t["w1bf"][e, c][:, hh * 2048:(hh + 1) * 2048]))
        for f in range(FT):
            jobs.append((t["w2"][e, f * 128:(f + 1) * 128, :], t["w2bf"][e, f]))
    emit_cast_pass(k, jobs)
    _barrier(k, recycle=True)
    emit_l0_quant(k, t, "wfcT", "wfcq", D, FF, tag="qa", scr=t["qscr_a"])
    _barrier(k)
    emit_l0_quant(k, t, "wprojT", "wprojq", FF, D, tag="qb", scr=t["qscr_b"])
    _barrier(k, recycle=True)
    emit_l0_attn(k, dict(t, wqT_bf=t["l0_wqT_bf"], wvT_bf=t["l0_wvT_bf"]))
    _barrier(k, recycle=True)
    emit_l0_ffn(k, t)
    _barrier(k, recycle=True)
    emit_l1a_qkv(k, dict(t, wqT_bf=t["l1a_wqT_bf"], wkT_bf=t["l1a_wkT_bf"],
                         wvT_bf=t["l1a_wvT_bf"]))
    _barrier(k, recycle=True)
    emit_attention(k, dict(t, ao=t["ao1"]), "l1a", 1.0, "a3")
    _barrier(k, recycle=True)
    emit_l1a_norm(k, dict(t, ao=t["ao1"]))
    _barrier(k, recycle=True)
    emit_l1f_qkv(k, dict(t, wq_bf=t["l1f_wq_bf"], wk_bf=t["l1f_wk_bf"],
                         wv_bf=t["l1f_wv_bf"]))
    _barrier(k, recycle=True)
    emit_attention(k, dict(t, ao=t["ao2"]), "l1f", 0.125, "a4")
    _barrier(k, recycle=True)
    emit_l1f_x1gate(k, dict(t, ao=t["ao2"], wo_bf=t["l1f_wo_bf"]))
    _barrier(k, recycle=True)
    emit_moe(k, t)
    _barrier(k, recycle=True)
    emit_moe_final(k, t)
    es.close()
    return nc


def _declare_common(nc, t):
    EI = lambda n, sh, dt=F32: nc.dram_tensor(n, sh, dt, kind="ExternalInput").ap()
    t["ident"] = EI("ident", [128, 128], BF16)
    return EI


def build_p1():
    nc = bass.Bass("TRN2", target_bir_lowering=False)
    t = {}
    EI = _declare_common(nc, t)
    IN = lambda n, sh, dt=F32: nc.dram_tensor(n, sh, dt, kind="Internal").ap()
    t["x0"] = EI("x0", [NT, D])
    t["l0_wqT"] = EI("l0_wqT", [D, D]); t["l0_wvT"] = EI("l0_wvT", [D, D])
    t["qnw_rep"] = EI("qnw_rep", [128, D]); t["vnw_rep"] = EI("vnw_rep", [128, D])
    t["triT"] = EI("triT", [128, 128])
    t["wfcT"] = EI("wfcT", [D, FF]); t["wprojT"] = EI("wprojT", [FF, D])
    t["lnw_rep"] = EI("lnw_rep", [128, FF]); t["lnb_rep"] = EI("lnb_rep", [128, FF])
    t["wqT_bf"] = IN("wqT_bf_s", [8, 2, 128, 512], BF16)
    t["wvT_bf"] = IN("wvT_bf_s", [8, 2, 128, 512], BF16)
    t["wfcq"] = IN("wfcq_s", [8, 128, FF], BF16)
    t["wprojq"] = IN("wprojq_s", [FT, 128, D], BF16)
    t["qscr_a"] = IN("qscr_a", [1, 256]); t["qscr_b"] = IN("qscr_b", [1, 256])
    t["s1"] = IN("s1_s", [NT, D])
    t["s2"] = nc.dram_tensor("s2", [NT, D], F32, kind="ExternalOutput").ap()
    es = ExitStack()
    k = Kb(nc, es)
    jobs = []
    for nm in ("wqT", "wvT"):
        for c in range(8):
            for hh in range(2):
                jobs.append((t["l0_" + nm][c * 128:(c + 1) * 128, hh * 512:(hh + 1) * 512],
                             t[nm + "_bf"][c, hh]))
    emit_cast_pass(k, jobs)
    _barrier(k)
    emit_l0_quant(k, t, "wfcT", "wfcq", D, FF, tag="qa", scr=t["qscr_a"])
    _barrier(k)
    emit_l0_quant(k, t, "wprojT", "wprojq", FF, D, tag="qb", scr=t["qscr_b"])
    _barrier(k)
    emit_l0_attn(k, t)
    _barrier(k)
    emit_l0_ffn(k, t)
    es.close()
    return nc


def build_p2():
    nc = bass.Bass("TRN2", target_bir_lowering=False)
    t = {}
    EI = _declare_common(nc, t)
    IN = lambda n, sh, dt=F32: nc.dram_tensor(n, sh, dt, kind="Internal").ap()
    t["s2"] = EI("s2", [NT, D])
    t["l1a_wqT"] = EI("l1a_wqT", [D, D]); t["l1a_wkT"] = EI("l1a_wkT", [D, D])
    t["l1a_wvT"] = EI("l1a_wvT", [D, D])
    for nm in ("wqT", "wkT", "wvT"):
        t[nm + "_bf"] = IN(nm + "_bf_s", [8, 2, 128, 512], BF16)
    t["q"] = IN("q_s", [NT, D], BF16); t["k"] = IN("k_s", [NT, D], BF16)
    t["v"] = IN("v_s", [NT, D], BF16)
    t["ao"] = IN("ao_s", [NT, D], BF16)
    t["yg"] = IN("yg_s", [NT, D])
    t["xn2"] = IN("xn2_s", [NT, D])
    t["gscr"] = IN("gscr", [1, 512])
    t["s3"] = nc.dram_tensor("s3", [NT, D], F32, kind="ExternalOutput").ap()
    es = ExitStack()
    k = Kb(nc, es)
    jobs = []
    for nm in ("wqT", "wkT", "wvT"):
        for c in range(8):
            for hh in range(2):
                jobs.append((t["l1a_" + nm][c * 128:(c + 1) * 128, hh * 512:(hh + 1) * 512],
                             t[nm + "_bf"][c, hh]))
    emit_cast_pass(k, jobs)
    _barrier(k)
    emit_l1a_qkv(k, t)
    _barrier(k)
    emit_attention(k, t, "l1a", 1.0, "a3")
    _barrier(k)
    emit_l1a_norm(k, t)
    es.close()
    return nc


def build_p3():
    nc = bass.Bass("TRN2", target_bir_lowering=False)
    t = {}
    EI = _declare_common(nc, t)
    IN = lambda n, sh, dt=F32: nc.dram_tensor(n, sh, dt, kind="Internal").ap()
    t["s3"] = EI("s3", [NT, D])
    for nm in ("wq", "wk", "wv", "wo"):
        t["l1f_" + nm] = EI("l1f_" + nm, [D, D])
    t["biases"] = EI("biases", [1, 4096])
    t["wgw"] = EI("wgw", [D, 8]); t["bg"] = EI("bg", [1, 8])
    t["w1"] = EI("w1", [E, D, FF]); t["w2"] = EI("w2", [E, FF, D])
    t["b1t"] = EI("b1t", [128, FT, E]); t["b2"] = EI("b2", [8, D])
    t["cos_rep"] = EI("cos_rep", [NT, D]); t["sinS_rep"] = EI("sinS_rep", [NT, D])
    for nm in ("g1_rep", "be1_rep", "g2_rep", "be2_rep"):
        t[nm] = EI(nm, [128, D])
    for nm in ("wq", "wk", "wv", "wo"):
        t[nm + "_bf"] = IN(nm + "_bf_s", [8, 2, 128, 512], BF16)
    t["wg_bf"] = IN("wg_bf_s", [8, 128, 8], BF16)
    t["w1bf"] = IN("w1bf_s", [E, 8, 128, FF], BF16)
    t["w2bf"] = IN("w2bf_s", [E, FT, 128, D], BF16)
    t["xn"] = IN("xn_s", [NT, D])
    t["qT"] = IN("qT_s", [D, NT], BF16); t["kT"] = IN("kT_s", [D, NT], BF16)
    t["vh"] = IN("vh_s", [H, NBLK, 128, 64], BF16)
    t["ao"] = IN("ao_s", [NT, D], BF16)
    t["x1"] = IN("x1_s", [NT, D])
    t["x1t"] = IN("x1t_s", [NGRP, 8, 128, GRP], BF16)
    t["combt"] = IN("combt_s", [128, NBLK, 8])
    t["yacc"] = IN("yacc_s", [NT, D])
    t["wo_bf"] = t["wo_bf"]
    t["out"] = nc.dram_tensor("out", [NT, D], F32, kind="ExternalOutput").ap()
    es = ExitStack()
    k = Kb(nc, es)
    jobs = []
    for nm in ("wq", "wk", "wv", "wo"):
        for c in range(8):
            for hh in range(2):
                jobs.append((t["l1f_" + nm][c * 128:(c + 1) * 128, hh * 512:(hh + 1) * 512],
                             t[nm + "_bf"][c, hh]))
    for c in range(8):
        jobs.append((t["wgw"][c * 128:(c + 1) * 128, :], t["wg_bf"][c]))
    for e in range(E):
        for c in range(8):
            for hh in range(2):
                jobs.append((t["w1"][e, c * 128:(c + 1) * 128, hh * 2048:(hh + 1) * 2048],
                             t["w1bf"][e, c][:, hh * 2048:(hh + 1) * 2048]))
        for f in range(FT):
            jobs.append((t["w2"][e, f * 128:(f + 1) * 128, :], t["w2bf"][e, f]))
    emit_cast_pass(k, jobs)
    _barrier(k)
    emit_l1f_qkv(k, t)
    _barrier(k)
    emit_attention(k, t, "l1f", 0.125, "a4")
    _barrier(k)
    emit_l1f_x1gate(k, t)
    _barrier(k)
    emit_moe(k, t)
    _barrier(k)
    emit_moe_final(k, t)
    es.close()
    return nc


def _host_prep(inputs):
    import ml_dtypes
    P = inputs
    base = {}
    base["l0_wqT"] = np.ascontiguousarray(P["L0_attn_Wq"].T)
    base["l0_wvT"] = np.ascontiguousarray(P["L0_attn_Wv"].T)
    base["qnw_rep"] = np.ascontiguousarray(
        np.broadcast_to(P["L0_attn_q_norm_w"].reshape(-1), (128, D)))
    base["vnw_rep"] = np.ascontiguousarray(
        np.broadcast_to(P["L0_attn_v_norm_w"].reshape(-1), (128, D)))
    base["triT"] = np.triu(np.ones((128, 128), np.float32))
    base["ident"] = np.eye(128, dtype=ml_dtypes.bfloat16)
    base["wfcT"] = np.ascontiguousarray(P["L0_ffn_W_fc"].T)
    base["wprojT"] = np.ascontiguousarray(P["L0_ffn_W_proj"].T)
    base["lnw_rep"] = np.ascontiguousarray(np.broadcast_to(P["L0_ffn_ln_w"], (128, FF)))
    base["lnb_rep"] = np.ascontiguousarray(np.broadcast_to(P["L0_ffn_ln_b"], (128, FF)))
    base["l1a_wqT"] = np.ascontiguousarray(P["L1_attn_Wq"].T)
    base["l1a_wkT"] = np.ascontiguousarray(P["L1_attn_Wk"].T)
    base["l1a_wvT"] = np.ascontiguousarray(P["L1_attn_Wv"].T)
    base["l1f_wq"] = np.ascontiguousarray(P["L1_ffn_Wq"])
    base["l1f_wk"] = np.ascontiguousarray(P["L1_ffn_Wk"])
    base["l1f_wv"] = np.ascontiguousarray(P["L1_ffn_Wv"])
    base["l1f_wo"] = np.ascontiguousarray(P["L1_ffn_Wo"])
    base["biases"] = np.concatenate([P["L1_ffn_bq"], P["L1_ffn_bk"], P["L1_ffn_bv"],
                                     P["L1_ffn_bo"]]).astype(np.float32)[None, :]
    base["wgw"] = np.ascontiguousarray(P["L1_ffn_Wg"])
    base["bg"] = P["L1_ffn_bg"].astype(np.float32)[None, :]
    base["w1"] = np.ascontiguousarray(P["L1_ffn_W1"])
    base["w2"] = np.ascontiguousarray(P["L1_ffn_W2"])
    base["b1t"] = np.ascontiguousarray(
        P["L1_ffn_b1"].reshape(E, FT, 128).transpose(2, 1, 0))
    base["b2"] = np.ascontiguousarray(P["L1_ffn_b2"])
    hd = 64
    inv = (1.0 / 10000.0 ** (np.arange(0, hd, 2, dtype=np.float32) / hd)).astype(np.float32)
    tt_ = np.arange(NT, dtype=np.float32)[:, None] * inv[None, :]
    emb = np.concatenate([tt_, tt_], -1).astype(np.float32)
    cos = np.cos(emb).astype(np.float32)
    sin = np.sin(emb).astype(np.float32)
    sinS = sin.copy(); sinS[:, :32] *= -1
    base["cos_rep"] = np.ascontiguousarray(np.tile(cos, (1, H)))
    base["sinS_rep"] = np.ascontiguousarray(np.tile(sinS, (1, H)))
    base["g1_rep"] = np.ascontiguousarray(np.broadcast_to(P["L1_ffn_g1"], (128, D)))
    base["be1_rep"] = np.ascontiguousarray(np.broadcast_to(P["L1_ffn_be1"], (128, D)))
    base["g2_rep"] = np.ascontiguousarray(np.broadcast_to(P["L1_ffn_g2"], (128, D)))
    base["be2_rep"] = np.ascontiguousarray(np.broadcast_to(P["L1_ffn_be2"], (128, D)))
    return base


FUSED = True


def kernel(**inputs):
    base = _host_prep(inputs)
    x = np.ascontiguousarray(inputs["L0_attn_X"])
    cores = list(range(8))
    if FUSED:
        nc = build_fused()
        keys = ("ident", "l0_wqT", "l0_wvT", "qnw_rep", "vnw_rep", "triT",
                "wfcT", "wprojT", "lnw_rep", "lnb_rep", "l1a_wqT", "l1a_wkT",
                "l1a_wvT", "l1f_wq", "l1f_wk", "l1f_wv", "l1f_wo", "biases",
                "wgw", "bg", "w1", "w2", "b1t", "b2", "cos_rep", "sinS_rep",
                "g1_rep", "be1_rep", "g2_rep", "be2_rep")
        maps = [dict({kk: base[kk] for kk in keys}, x0=np.ascontiguousarray(x[c]))
                for c in cores]
        res = run_bass_kernel_spmd(nc, maps, core_ids=cores)
        out = np.stack([np.asarray(r["out"]) for r in res.results], axis=0)
        return out.astype(np.float32)
    keys1 = ("ident", "l0_wqT", "l0_wvT", "qnw_rep", "vnw_rep", "triT",
             "wfcT", "wprojT", "lnw_rep", "lnb_rep")
    nc1 = build_p1()
    maps = [dict({kk: base[kk] for kk in keys1}, x0=np.ascontiguousarray(x[c]))
            for c in cores]
    r1 = run_bass_kernel_spmd(nc1, maps, core_ids=cores)
    s2 = [np.asarray(r["s2"]) for r in r1.results]
    keys2 = ("ident", "l1a_wqT", "l1a_wkT", "l1a_wvT")
    nc2 = build_p2()
    maps = [dict({kk: base[kk] for kk in keys2}, s2=s2[c]) for c in cores]
    r2 = run_bass_kernel_spmd(nc2, maps, core_ids=cores)
    s3 = [np.asarray(r["s3"]) for r in r2.results]
    keys3 = ("ident", "l1f_wq", "l1f_wk", "l1f_wv", "l1f_wo", "biases", "wgw",
             "bg", "w1", "w2", "b1t", "b2", "cos_rep", "sinS_rep",
             "g1_rep", "be1_rep", "g2_rep", "be2_rep")
    nc3 = build_p3()
    maps = [dict({kk: base[kk] for kk in keys3}, s3=s3[c]) for c in cores]
    r3 = run_bass_kernel_spmd(nc3, maps, core_ids=cores)
    out = np.stack([np.asarray(r["out"]) for r in r3.results], axis=0)
    return out.astype(np.float32)


# version 07
# speedup vs baseline: 1.4393x; 1.2208x over previous
import numpy as np
from contextlib import ExitStack
import concourse.bass as bass
import concourse.mybir as mybir
from concourse.bass_utils import run_bass_kernel_spmd
from concourse.alu_op_type import AluOpType as OP

F32 = mybir.dt.float32
BF16 = mybir.dt.bfloat16
AF = mybir.ActivationFunctionType
AX = mybir.AxisListType

NT = 4096
D = 1024
NBLK = 32
H = 16
HD = 64
FF = 4096
E = 8
GRP = 512
NGRP = NT // GRP
FT = FF // 128


class Sems:

    def __init__(self, nc, es):
        self.nc, self.es = nc, es
        self.h = {}
        self.v = {}
        self.pool = []

    def get(self, name):
        if name not in self.h:
            if self.pool:
                handle, val = self.pool.pop()
            else:
                handle = self.nc.alloc_semaphore(name=name, num=None)
                val = 0
            self.h[name] = handle
            self.v[name] = val
        return self.h[name]

    def inc(self, inst, name, n=1):
        h = self.get(name)
        inst.then_inc(h, n)
        self.v[name] += n
        return self.v[name]

    def val(self, name):
        self.get(name)
        return self.v[name]

    def wait(self, eng, name, val):
        if val is None or val <= 0:
            return
        eng.wait_ge(self.get(name), val)


class Kb:

    def __init__(self, nc, es):
        self.nc = nc
        self.es = es
        self.scopes = []
        self.s = Sems(nc, es)
        self.pb = [es.enter_context(nc.psum_tensor(f"pb{i}", [128, 512], F32))
                   for i in range(6)]
        self.pt = es.enter_context(nc.psum_tensor("pt", [128, 1024], BF16))
        self.pg = es.enter_context(nc.psum_tensor("pg", [128, 512], F32))

    def sb(self, name, shape, dt):
        es = self.scopes[-1] if self.scopes else self.es
        return es.enter_context(self.nc.sbuf_tensor(name, shape, dt))

    def push_scope(self):
        self.scopes.append(ExitStack())

    def pop_scope(self):
        self.scopes.pop().close()


def emit_moe(k: Kb, t, n_experts=E):
    k.push_scope()
    nc, s = k.nc, k.s

    w1sb = [k.sb(f"moe_w1sb{i}", [128, 8, FF], BF16) for i in range(1)]
    w2sb = [k.sb(f"moe_w2sb{i}", [128, FT, D], BF16) for i in range(1)]
    x1g = [k.sb(f"moe_x1g{i}", [128, 8, GRP], BF16) for i in range(2)]
    hsT = k.sb("moe_hsT", [128, FT, GRP], BF16)
    ysb = [k.sb(f"moe_ysb{i}", [128, D], F32) for i in range(2)]
    yin = [k.sb(f"moe_yin{i}", [128, D], F32) for i in range(2)]
    b1c = k.sb("moe_b1c", [128, FT, E], F32)
    comb = k.sb("moe_comb", [128, NBLK, E], F32)

    s.inc(nc.scalar.dma_start(b1c[:], t["b1t"]), "moe_ld1", 16)
    s.inc(nc.scalar.dma_start(comb[:], t["combt"]), "moe_ld1", 16)

    w1sb_free = 0
    w2sb_free = 0

    def emit_w_load(e):
        s.wait(nc.sync, "mm", w1sb_free)
        ld = nc.sync.dma_start(w1sb[0][:], t["w1bf"][e].rearrange("c p f -> p c f"))
        r1 = s.inc(ld, "moe_w1", 16)
        s.wait(nc.sync, "mm", w2sb_free)
        ld = nc.sync.dma_start(w2sb[0][:], t["w2bf"][e].rearrange("f p d -> p f d"))
        r2 = s.inc(ld, "moe_w2", 16)
        return r1, r2

    w1_ready, w2_ready = emit_w_load(0)

    hbank_free = [0, 0]
    x1_free = [0, 0]
    x1_ready = [0, 0]
    ypair_free = [0, 0]
    ysb_free = [0, 0]
    first_use = True

    for e in range(n_experts):
        for g in range(NGRP):
            gi = g % 2
            xb = x1g[gi]
            s.wait(nc.scalar, "mm", x1_free[gi])
            ld = nc.scalar.dma_start(xb[:], t["x1t"][g].rearrange("c p t -> p c t"))
            x1_ready[gi] = s.inc(ld, f"moe_x{gi}", 16)

            s.wait(nc.tensor, "moe_w1", w1_ready)
            s.wait(nc.tensor, f"moe_x{gi}", x1_ready[gi])
            if first_use:
                s.wait(nc.tensor, "moe_ld1", 32)
                s.wait(nc.scalar, "moe_ld1", 32)
                first_use = False
            gelu_last = 0
            for f in range(FT):
                half = f % 2
                hp = k.pb[half][:]
                s.wait(nc.tensor, "moe_sc", hbank_free[half])
                for c in range(8):
                    mm = nc.tensor.matmul(hp, w1sb[0][:, c, bass.ts(f, 128)],
                                          xb[:, c, :],
                                          start=(c == 0), stop=(c == 7))
                mv = s.inc(mm, "mm", 1)
                s.wait(nc.scalar, "mm", mv)
                ge = nc.scalar.activation(hsT[:, f, :], hp, AF.Gelu,
                                          bias=b1c[:, f, e:e + 1], scale=1.0)
                hbank_free[half] = s.inc(ge, "moe_sc", 1)
            gelu_last = s.inc(nc.scalar.drain(), "moe_sc", 1)
            x1_free[gi] = s.val("mm")

            s.wait(nc.tensor, "moe_w2", w2_ready)
            for sub in range(4):
                blk = g * 4 + sub
                pi = sub % 2
                pair = (k.pb[2], k.pb[3]) if pi == 0 else (k.pb[4], k.pb[5])
                s.wait(nc.tensor, "moe_sc", gelu_last)
                s.wait(nc.tensor, "moe_v", ypair_free[pi])
                if e > 0:
                    s.wait(nc.gpsimd, f"moe_st{pi}", ysb_free[pi])
                    ldy = nc.gpsimd.dma_start(yin[pi][:], t["yacc"][bass.ts(blk, 128), :])
                    yin_rdy = s.inc(ldy, f"moe_yld{pi}", 16)
                for f in range(FT):
                    for dh in range(2):
                        mm = nc.tensor.matmul(
                            pair[dh][:],
                            hsT[:, f, bass.ts(sub, 128)],
                            w2sb[0][:, f, bass.ts(dh, 512)],
                            start=(f == 0), stop=(f == FT - 1))
                mv = s.inc(mm, "mm", 1)
                s.wait(nc.vector, "mm", mv)
                s.wait(nc.vector, f"moe_st{pi}", ysb_free[pi])
                if e == 0:
                    for dh in range(2):
                        nc.vector.tensor_scalar(ysb[pi][:, bass.ts(dh, 512)], pair[dh][:],
                                                comb[:, blk, e:e + 1], None, OP.mult)
                else:
                    s.wait(nc.vector, f"moe_yld{pi}", yin_rdy)
                    for dh in range(2):
                        nc.vector.scalar_tensor_tensor(ysb[pi][:, bass.ts(dh, 512)], pair[dh][:],
                                                       comb[:, blk, e:e + 1],
                                                       yin[pi][:, bass.ts(dh, 512)],
                                                       OP.mult, OP.add)
                ypair_free[pi] = s.inc(nc.vector.drain(), "moe_v", 1)
                s.wait(nc.gpsimd, "moe_v", ypair_free[pi])
                sd = nc.gpsimd.dma_start(t["yacc"][bass.ts(blk, 128), :], ysb[pi][:])
                ysb_free[pi] = s.inc(sd, f"moe_st{pi}", 16)
        if e + 1 < n_experts:
            w1sb_free = s.val("mm")
            w2sb_free = s.val("mm")
            w1_ready, w2_ready = emit_w_load(e + 1)
    for pi in range(2):
        s.wait(nc.gpsimd, f"moe_st{pi}", s.val(f"moe_st{pi}"))
    k.pop_scope()



def emit_cast_pass(k: Kb, jobs, eng=None, tag="cst"):
    k.push_scope()
    nc, s = k.nc, k.s
    eng = eng or nc.vector
    NBUF = 3
    fin = [k.sb(f"{tag}_fin{i}", [128, 2048], F32) for i in range(NBUF)]
    bout = [k.sb(f"{tag}_bout{i}", [128, 2048], BF16) for i in range(NBUF)]
    i = 0
    cast_done = [0] * NBUF
    out_done = [0] * NBUF
    for srcap, dst in jobs:
        R, C = srcap.shape
        assert R % 128 == 0 and C <= 2048
        for r in range(R // 128):
            j = i % NBUF
            s.wait(nc.sync, tag + "_c", cast_done[j])
            ld = nc.sync.dma_start(fin[j][:, 0:C], srcap[bass.ts(r, 128), :])
            lv = s.inc(ld, f"{tag}_l{j}", 16)
            s.wait(eng, f"{tag}_l{j}", lv)
            s.wait(eng, f"{tag}_o{j}", out_done[j])
            eng.tensor_copy(bout[j][:, 0:C], fin[j][:, 0:C])
            cast_done[j] = s.inc(eng.drain(), tag + "_c", 1)
            s.wait(nc.gpsimd, tag + "_c", cast_done[j])
            st = nc.gpsimd.dma_start(dst[bass.ts(r, 128), :], bout[j][:, 0:C])
            out_done[j] = s.inc(st, f"{tag}_o{j}", 16)
            i += 1
    for j in range(NBUF):
        s.wait(nc.gpsimd, f"{tag}_o{j}", s.val(f"{tag}_o{j}"))
    k.pop_scope()


def _ln_plain(k, eng_small, xin, xout, tmp, eps=1e-5, w_rep=None, b_rep=None,
              out2=None):
    nc, s = k.nc, k.s
    bn, mv, t1, t2 = tmp["bn"], tmp["mv"], tmp["t1"], tmp["t2"]
    nc.vector.bn_stats(bn[:, 0:6], xin[:, 0:512])
    v = s.inc(nc.vector.bn_stats(bn[:, 6:12], xin[:, 512:1024]), "ln", 1)
    s.wait(nc.vector, "ln", v)
    v = s.inc(nc.vector.bn_aggr(mv[:], bn[:]), "ln", 1)
    s.wait(nc.vector, "ln", v)
    v = s.inc(nc.vector.tensor_scalar(t1[:], mv[:, 1:2], eps, None, OP.add), "ln", 1)
    s.wait(nc.scalar, "ln", v)
    v = s.inc(nc.scalar.activation(t1[:], t1[:], AF.Sqrt), "ln", 1)
    s.wait(nc.vector, "ln", v)
    v = s.inc(nc.vector.reciprocal(t2[:], t1[:]), "ln", 1)
    s.wait(nc.vector, "ln", v)
    nc.vector.tensor_scalar(xout[:], xin[:], mv[:, 0:1], t2[:], OP.subtract, OP.mult)
    if w_rep is not None:
        nc.vector.tensor_tensor(xout[:], xout[:], w_rep[:], OP.mult)
    if b_rep is not None:
        nc.vector.tensor_tensor(out2 if out2 is not None else xout[:],
                                xout[:], b_rep[:], OP.add)


def emit_l1f_qkv(k: Kb, t):
    k.push_scope()
    nc, s = k.nc, k.s
    wq = k.sb("p4_wq", [128, 8, 2, 512], BF16)
    wk = k.sb("p4_wk", [128, 8, 2, 512], BF16)
    wv = k.sb("p4_wv", [128, 8, 2, 512], BF16)
    bias_bf = k.sb("p4_biasbf", [1, 4096], BF16)
    bias_f = k.sb("p4_biasf", [1, 4096], F32)
    onesr = k.sb("p4_ones", [1, 128], BF16)
    ident = k.sb("p4_ident", [128, 128], BF16)
    xs = k.sb("p4_xs", [128, D], F32)
    xn = k.sb("p4_xn", [128, D], F32)
    xnb = k.sb("p4_xnb", [128, D], BF16)
    xnT = k.sb("p4_xnT", [128, 8, 512], BF16)
    ctile = k.sb("p4_cos", [128, D], F32)
    stile = k.sb("p4_sin", [128, D], F32)
    qsb = k.sb("p4_qsb", [128, D], F32)
    qrot = k.sb("p4_qrot", [128, D], F32)
    qe = k.sb("p4_qe", [128, D], F32)
    qb2 = k.sb("p4_qb2", [128, D], BF16)
    qtev = k.sb("p4_qtev", [128, D], BF16)
    vsb = k.sb("p4_vsb", [128, D], BF16)
    tmp = {"bn": k.sb("p4_bn", [128, 12], F32), "mv": k.sb("p4_mv", [128, 2], F32),
           "t1": k.sb("p4_t1", [128, 1], F32), "t2": k.sb("p4_t2", [128, 1], F32)}

    for nm, w in (("wq_bf", wq), ("wk_bf", wk), ("wv_bf", wv)):
        s.inc(nc.sync.dma_start(w[:], t[nm].rearrange("c h p n -> p c h n")), "p4w", 16)
    s.inc(nc.sync.dma_start(bias_f[:], t["biases"]), "p4w", 16)
    s.inc(nc.sync.dma_start(ident[:], t["ident"]), "p4w", 16)
    s.wait(nc.vector, "p4w", s.val("p4w"))
    nc.vector.memset(onesr[:], 1.0)
    nc.vector.tensor_copy(bias_bf[:], bias_f[:])
    wrdy = s.inc(nc.vector.drain(), "p4v", 1)
    s.wait(nc.tensor, "p4v", wrdy)

    pb = k.pb
    xs_free = 0
    rope_free = 0

    def selfwait(eng=nc.vector, name="p4sw"):
        v = s.inc(eng.nop(), name, 1)
        s.wait(eng, name, v)

    for g in range(NGRP):
        for bi in range(4):
            blk = g * 4 + bi
            s.wait(nc.scalar, "p4v", xs_free)
            lv = s.inc(nc.scalar.dma_start(xs[:], t["s3"][bass.ts(blk, 128), :]), "p4xs", 16)
            s.wait(nc.vector, "p4xs", lv)
            s.wait(nc.vector, "p4t", s.val("p4t"))
            s.wait(nc.vector, "p4xo", s.val("p4xo"))
            _ln_plain(k, None, xs, xn, tmp)
            selfwait()
            nc.vector.tensor_copy(xnb[:], xn[:])
            xv = xs_free = s.inc(nc.vector.drain(), "p4v", 1)
            s.wait(nc.scalar, "p4v", xv)
            s.inc(nc.scalar.dma_start(t["xn"][bass.ts(blk, 128), :], xn[:]), "p4xo", 16)
            s.wait(nc.tensor, "p4v", xv)
            s.wait(nc.tensor, "p4tv", s.val("p4tv"))
            for c in range(8):
                mm = nc.tensor.transpose(k.pt[:, bass.ts(c, 128)],
                                         xnb[:, bass.ts(c, 128)], ident[:])
            tv = s.inc(mm, "p4t", 1)
            tv = s.inc(nc.tensor.nop(), "p4t", 1)
            s.wait(nc.vector, "p4t", tv)
            if bi == 0:
                s.wait(nc.vector, "mm", s.val("mm"))
            nc.vector.tensor_copy(xnT[:, :, bi * 128:(bi + 1) * 128],
                                  k.pt[:].rearrange("p (c n) -> p c n", c=8))
            s.inc(nc.vector.drain(), "p4tv", 1)
        xnT_rdy = s.val("p4tv")
        s.wait(nc.tensor, "p4tv", xnT_rdy)

        for bi in range(4):
            blk = g * 4 + bi
            s.wait(nc.tensor, "p4e", s.val("p4e"))
            for half in range(2):
                hp = pb[half][:]
                for c in range(8):
                    nc.tensor.matmul(hp, xnT[:, c, bass.ts(bi, 128)],
                                     wv[:, c, half, :], start=(c == 0), stop=False)
                mm = nc.tensor.matmul(hp, onesr[:],
                                      bias_bf[0:1, 2048 + half * 512:2048 + half * 512 + 512],
                                      start=False, stop=True)
            mv_ = s.inc(mm, "mm", 1)
            s.wait(nc.vector, "mm", mv_)
            s.wait(nc.vector, "p4vo", s.val("p4vo"))
            nc.vector.tensor_copy(vsb[:, 0:512], pb[0][:])
            nc.vector.tensor_copy(vsb[:, 512:1024], pb[1][:])
            ev = s.inc(nc.vector.drain(), "p4e", 1)
            s.wait(nc.gpsimd, "p4e", ev)
            sd = nc.gpsimd.dma_start(
                t["vh"][:, blk].rearrange("h p c -> p h c"),
                vsb.rearrange("p (h c) -> p h c", h=H))
            s.inc(sd, "p4vo", 16)

        for w, boff, dstname in ((wq, 0, "qT"), (wk, 1024, "kT")):
            for bi in range(4):
                blk = g * 4 + bi
                s.wait(nc.scalar, "p4v", rope_free)
                s.inc(nc.scalar.dma_start(ctile[:], t["cos_rep"][bass.ts(blk, 128), :]),
                      "p4xc", 16)
                lv2 = s.inc(nc.scalar.dma_start(stile[:], t["sinS_rep"][bass.ts(blk, 128), :]),
                            "p4xd", 16)
                s.wait(nc.tensor, "p4e", s.val("p4e"))
                for half in range(2):
                    hp = pb[half][:]
                    for c in range(8):
                        nc.tensor.matmul(hp, xnT[:, c, bass.ts(bi, 128)],
                                         w[:, c, half, :], start=(c == 0), stop=False)
                    mm = nc.tensor.matmul(hp, onesr[:],
                                          bias_bf[0:1, boff + half * 512:boff + half * 512 + 512],
                                          start=False, stop=True)
                mv_ = s.inc(mm, "mm", 1)
                s.wait(nc.vector, "mm", mv_)
                s.wait(nc.vector, "p4xc", s.val("p4xc"))
                s.wait(nc.vector, "p4xd", lv2)
                nc.vector.tensor_copy(qsb[:, 0:512], pb[0][:])
                nc.vector.tensor_copy(qsb[:, 512:1024], pb[1][:])
                ev = s.inc(nc.vector.drain(), "p4e", 1)
                selfwait()
                qv = qsb.rearrange("p (h s c) -> p h s c", h=H, s=2)
                rv = qrot.rearrange("p (h s c) -> p h s c", h=H, s=2)
                nc.vector.tensor_copy(rv[:, :, 0, :], qv[:, :, 1, :])
                selfwait()
                nc.vector.tensor_copy(rv[:, :, 1, :], qv[:, :, 0, :])
                selfwait()
                nc.vector.tensor_tensor(qe[:], qsb[:], ctile[:], OP.mult)
                selfwait()
                nc.vector.tensor_tensor(qrot[:], qrot[:], stile[:], OP.mult)
                selfwait()
                nc.vector.tensor_tensor(qb2[:], qe[:], qrot[:], OP.add)
                rpv = rope_free = s.inc(nc.vector.drain(), "p4v", 1)
                s.wait(nc.tensor, "p4v", rpv)
                s.wait(nc.tensor, "p4tv", s.val("p4tv"))
                for c in range(8):
                    mm = nc.tensor.transpose(k.pt[:, bass.ts(c, 128)],
                                             qb2[:, bass.ts(c, 128)], ident[:])
                tv = s.inc(mm, "p4t", 1)
                tv = s.inc(nc.tensor.nop(), "p4t", 1)
                s.wait(nc.vector, "p4t", tv)
                s.wait(nc.vector, "p4qo", s.val("p4qo"))
                nc.vector.tensor_copy(qtev[:], k.pt[:])
                qtv = s.inc(nc.vector.drain(), "p4tv", 1)
                s.wait(nc.gpsimd, "p4tv", qtv)
                sd = nc.gpsimd.dma_start(
                    t[dstname].rearrange("(c p) n -> p c n", p=128)[:, :, bass.ts(blk, 128)],
                    qtev.rearrange("p (c n) -> p c n", c=8))
                s.inc(sd, "p4qo", 16)
    s.wait(nc.gpsimd, "p4qo", s.val("p4qo"))
    s.wait(nc.gpsimd, "p4vo", s.val("p4vo"))
    s.wait(nc.scalar, "p4xo", s.val("p4xo"))
    k.pop_scope()


def emit_attention(k: Kb, t, mode, scale, tag):
    k.push_scope()
    nc, s = k.nc, k.s
    qt = k.sb(tag + "_qt", [64, 2, NT], BF16)
    kt = k.sb(tag + "_kt", [64, 2, NT], BF16)
    vaug = [k.sb(f"{tag}_va{i}", [128, NBLK, 65], BF16) for i in range(2)]
    eb = [k.sb(f"{tag}_eb{i}", [128, 1024], BF16) for i in range(3)]
    osb = k.sb(tag + "_osb", [128, 1024], BF16)
    oout = k.sb(tag + "_oout", [128, 4, 64], BF16)
    rcol = k.sb(tag + "_rcol", [128, 1], F32)
    rcan = k.sb(tag + "_rcan", [128, 8], F32)
    ident = k.sb(tag + "_ident", [128, 128], BF16)
    qkld = k.sb(tag + "_qkld", [128, NBLK, 64], BF16)

    s.inc(nc.sync.dma_start(ident[:], t["ident"]), tag + "w", 16)
    s.wait(nc.vector, tag + "w", s.val(tag + "w"))
    nc.vector.memset(vaug[0][:, :, 64:65], 1.0)
    nc.vector.memset(vaug[1][:, :, 64:65], 1.0)
    ones_rdy = s.inc(nc.vector.drain(), tag + "v", 1)
    s.wait(nc.tensor, tag + "v", ones_rdy)

    if mode == "l1a":
        qflat = t["q"].rearrange("a (u c) -> (a u) c", c=64)
        kflat = t["k"].rearrange("a (u c) -> (a u) c", c=64)
        vflat = t["v"].rearrange("a (u c) -> (a u) c", c=64)
        aoflat = t["ao"].rearrange("a (u c) -> (a u) c", c=64)

    pair_lastmm = 0
    va_free = [0, 0]
    sco_free = [0, 0]
    oev = 0
    odr_cnt = 0

    for hp_i in range(8):
        if mode == "l1f":
            s.wait(nc.sync, "mm", pair_lastmm)
            ld = nc.sync.dma_start(qt[:], t["qT"][bass.ts(hp_i, 128), :]
                                   .rearrange("(g p) n -> p g n", g=2))
            s.inc(ld, tag + "q", 16)
            ld = nc.sync.dma_start(kt[:], t["kT"][bass.ts(hp_i, 128), :]
                                   .rearrange("(g p) n -> p g n", g=2))
            k_ld = s.inc(ld, tag + "q", 16)
            s.wait(nc.tensor, tag + "q", k_ld)
        else:
            s.wait(nc.tensor, tag + "pv", s.val(tag + "pv"))
            for gi in range(2):
                grp = 2 * hp_i + gi
                for which, flat, dst in (("q", qflat, qt), ("k", kflat, kt)):
                    s.wait(nc.sync, tag + "tv", s.val(tag + "tv"))
                    if gi == 0 and which == "q":
                        s.wait(nc.sync, "mm", pair_lastmm)
                    src = flat[grp * 4096:(grp + 1) * 4096, :]
                    ld = nc.sync.dma_start(qkld[:], src.rearrange("(b p) c -> p b c", p=128))
                    lv = s.inc(ld, tag + "q", 16)
                    s.wait(nc.tensor, tag + "q", lv)
                    for bt in range(4):
                        s.wait(nc.tensor, tag + "tv", s.val(tag + "tv"))
                        for ti in range(8):
                            pt_i = bt * 8 + ti
                            mm = nc.tensor.transpose(k.pt[0:64, bass.ts(ti, 128)],
                                                     qkld[:, pt_i, :], ident[:])
                        tv = s.inc(mm, tag + "t", 1)
                        tv = s.inc(nc.tensor.nop(), tag + "t", 1)
                        s.wait(nc.vector, tag + "t", tv)
                        if bt == 0 and which == "q" and gi == 0:
                            s.wait(nc.vector, "mm", pair_lastmm)
                        nc.vector.tensor_copy(
                            dst[0:64, gi, bt * 1024:(bt + 1) * 1024]
                               .rearrange("p (a n) -> p a n", a=8),
                            k.pt[0:64, :].rearrange("p (a n) -> p a n", a=8))
                        s.inc(nc.vector.drain(), tag + "tv", 1)
            s.wait(nc.tensor, tag + "tv", s.val(tag + "tv"))
        for gi in range(2):
            s.wait(nc.scalar, "mm", va_free[gi])
            if mode == "l1f":
                h = 2 * hp_i + gi
                ld = nc.scalar.dma_start(vaug[gi][:, :, 0:64],
                                         t["vh"][h].rearrange("b p c -> p b c"))
            else:
                grp = 2 * hp_i + gi
                src = vflat[grp * 4096:(grp + 1) * 4096, :]
                ld = nc.scalar.dma_start(vaug[gi][:, :, 0:64],
                                         src.rearrange("(b p) c -> p b c", p=128))
            va_free[gi] = s.inc(ld, tag + "va", 16)
        s.wait(nc.tensor, tag + "va", max(va_free))

        for qb in range(8):
            sco_free = [0, 0]
            exp_cnt = {}
            smv = {}

            def esc(kb):
                sp = kb % 2
                S = (k.pb[1], k.pb[0]) if sp == 0 else (k.pb[3], k.pb[2])
                s.wait(nc.tensor, tag + "e", sco_free[sp])
                nc.tensor.matmul(S[0][:], kt[0:64, 0, bass.ts(kb, 128)],
                                 qt[0:64, 0, bass.ts(qb, 512)], start=True, stop=True)
                mm = nc.tensor.matmul(S[1][:], kt[0:64, 1, bass.ts(kb, 128)],
                                      qt[0:64, 1, bass.ts(qb, 512)], start=True, stop=True)
                smv[kb] = s.inc(mm, "mm", 1)

            def eexp(kb, wv):
                sp = kb % 2
                S = (k.pb[1], k.pb[0]) if sp == 0 else (k.pb[3], k.pb[2])
                E = eb[kb % 3]
                s.wait(nc.scalar, "mm", wv)
                nc.scalar.activation(E[:, 0:512], S[0][:], AF.Exp, scale=scale)
                ex = nc.scalar.activation(E[:, 512:1024], S[1][:], AF.Exp, scale=scale)
                sco_free[sp] = exp_cnt[kb] = s.inc(ex, tag + "e", 1)

            def epv(kb, wv, last=False):
                s.wait(nc.tensor, tag + "e", wv)
                if kb == 0:
                    s.wait(nc.tensor, tag + "v", oev)
                E = eb[kb % 3]
                for gi in range(2):
                    mm = nc.tensor.matmul(k.pb[4 + gi][0:65, :], vaug[gi][:, kb, :],
                                          E[:, bass.ts(gi, 512)],
                                          start=(kb == 0), stop=last)
                return mm

            for kb in range(NBLK):
                esc(kb)
                if kb >= 1:
                    eexp(kb - 1, smv[kb - 1])
                if kb >= 2:
                    epv(kb - 2, exp_cnt[kb - 2])
            eexp(NBLK - 1, smv[NBLK - 1])
            epv(NBLK - 2, exp_cnt[NBLK - 2])
            epv(NBLK - 1, exp_cnt[NBLK - 1], last=True)
            fence = nc.tensor.matmul(k.pg[0:1, 100:101], vaug[0][:, 0, 0:1],
                                     vaug[0][:, 0, 0:1], start=True, stop=True)
            pv_v = s.inc(fence, "mm", 1)
            va_free = [pv_v, pv_v]
            pair_lastmm = pv_v

            s.wait(nc.vector, "mm", pv_v)
            s.wait(nc.vector, tag + "t", s.val(tag + "t"))
            nc.vector.tensor_copy(osb[64:128, 0:512], osb[64:128, 0:512])
            nc.vector.tensor_copy(osb[0:65, 0:512], k.pb[4][0:65, :])
            nc.vector.tensor_copy(osb[0:65, 512:1024], k.pb[5][0:65, :])
            nc.vector.drain()
            nc.vector.tensor_copy(rcan[:], osb[0:128, 1016:1024])
            oev = s.inc(nc.vector.drain(), tag + "v", 1)
            for gi in range(2):
                s.wait(nc.tensor, tag + "v", oev)
                s.wait(nc.tensor, tag + "pv", odr_cnt)
                for ti in range(4):
                    mm = nc.tensor.transpose(k.pt[:, ti * 68:ti * 68 + 65],
                                             osb[0:65, gi * 512 + ti * 128:gi * 512 + (ti + 1) * 128],
                                             ident[0:65, 0:65])
                ptv = s.inc(mm, tag + "t", 1)
                ptv = s.inc(nc.tensor.nop(), tag + "t", 1)
                s.wait(nc.vector, tag + "t", ptv)
                s.wait(nc.vector, tag + "o", s.val(tag + "o"))
                for ti in range(4):
                    tile = k.pt[:, ti * 68:ti * 68 + 65]
                    v1 = s.inc(nc.vector.reciprocal(rcol[:], tile[:, 64:65]), tag + "v", 1)
                    s.wait(nc.vector, tag + "v", v1)
                    nc.vector.tensor_scalar(oout[:, ti, :], tile[:, 0:64], rcol[:],
                                            None, OP.mult)
                odr_cnt = s.inc(nc.vector.drain(), tag + "pv", 1)
                s.wait(nc.gpsimd, tag + "pv", odr_cnt)
                if mode == "l1f":
                    h = 2 * hp_i + gi
                    dst = t["ao"][qb * 512:(qb + 1) * 512, 64 * h:64 * h + 64]
                else:
                    grp = 2 * hp_i + gi
                    dst = aoflat[grp * 4096 + qb * 512: grp * 4096 + (qb + 1) * 512, :]
                sd = nc.gpsimd.dma_start(dst.rearrange("(ti p) c -> p ti c", p=128), oout[:])
                s.inc(sd, tag + "o", 16)
    s.wait(nc.gpsimd, tag + "o", s.val(tag + "o"))
    k.pop_scope()


def emit_l1f_x1gate(k: Kb, t):
    k.push_scope()
    nc, s = k.nc, k.s
    wo = k.sb("p5_wo", [128, 8, 2, 512], BF16)
    wg = k.sb("p5_wg", [128, 8, 8], BF16)
    bias_f = k.sb("p5_biasf", [1, 4096], F32)
    bias_bf = k.sb("p5_biasbf", [1, 4096], BF16)
    bg_f = k.sb("p5_bgf", [1, 8], F32)
    bg_bf = k.sb("p5_bgbf", [1, 8], BF16)
    onesr = k.sb("p5_ones", [1, 128], BF16)
    ident = k.sb("p5_ident", [128, 128], BF16)
    g1r = k.sb("p5_g1r", [128, D], F32)
    be1r = k.sb("p5_be1r", [128, D], F32)
    aob = k.sb("p5_aob", [128, D], BF16)
    aoT = k.sb("p5_aoT", [128, 8, 128], BF16)
    xnb = k.sb("p5_xnb", [128, D], F32)
    y1 = k.sb("p5_y1", [128, D], F32)
    x1f = k.sb("p5_x1f", [128, D], F32)
    x1b = k.sb("p5_x1b", [128, D], BF16)
    x1T = k.sb("p5_x1T", [128, 8, 128], BF16)
    pe_ = k.sb("p5_pe", [128, 8], F32)
    m1 = k.sb("p5_m1", [128, 1], F32)
    nm1 = k.sb("p5_nm1", [128, 1], F32)
    pm = k.sb("p5_pm", [128, 8], F32)
    m2 = k.sb("p5_m2", [128, 1], F32)
    s12 = k.sb("p5_s12", [128, 1], F32)
    rs = k.sb("p5_rs", [128, 1], F32)
    mask1 = k.sb("p5_mask1", [128, 8], F32)
    mask2 = k.sb("p5_mask2", [128, 8], F32)
    combv = k.sb("p5_comb", [128, 8], F32)
    tmp = {"bn": k.sb("p5_bn", [128, 12], F32), "mv": k.sb("p5_mv", [128, 2], F32),
           "t1": k.sb("p5_t1", [128, 1], F32), "t2": k.sb("p5_t2", [128, 1], F32)}

    s.inc(nc.sync.dma_start(wo[:], t["wo_bf"].rearrange("c h p n -> p c h n")), "p5w", 16)
    s.inc(nc.sync.dma_start(wg[:], t["wg_bf"].rearrange("c p n -> p c n")), "p5w", 16)
    s.inc(nc.sync.dma_start(bias_f[:], t["biases"]), "p5w", 16)
    s.inc(nc.sync.dma_start(bg_f[:], t["bg"]), "p5w", 16)
    s.inc(nc.sync.dma_start(ident[:], t["ident"]), "p5w", 16)
    s.inc(nc.sync.dma_start(g1r[:], t["g1_rep"]), "p5w", 16)
    s.inc(nc.sync.dma_start(be1r[:], t["be1_rep"]), "p5w", 16)
    s.wait(nc.vector, "p5w", s.val("p5w"))
    nc.vector.memset(onesr[:], 1.0)
    nc.vector.tensor_copy(bias_bf[:], bias_f[:])
    nc.vector.tensor_copy(bg_bf[:], bg_f[:])
    wrdy = s.inc(nc.vector.drain(), "p5v", 1)
    s.wait(nc.tensor, "p5v", wrdy)

    def selfwait(eng=nc.vector, name="p5sw"):
        v = s.inc(eng.nop(), name, 1)
        s.wait(eng, name, v)

    aob_free = 0
    xnb_free = 0
    for blk in range(NBLK):
        g, bi = blk // 4, blk % 4
        s.wait(nc.scalar, "p5t", aob_free)
        lv = s.inc(nc.scalar.dma_start(aob[:], t["ao"][bass.ts(blk, 128), :]), "p5xa", 16)
        s.wait(nc.scalar, "p5e", xnb_free)
        lvx = s.inc(nc.scalar.dma_start(xnb[:], t["xn"][bass.ts(blk, 128), :]), "p5xb", 16)
        s.wait(nc.tensor, "p5xa", lv)
        s.wait(nc.tensor, "p5tv", s.val("p5tv"))
        for c in range(8):
            mm = nc.tensor.transpose(k.pt[:, bass.ts(c, 128)], aob[:, bass.ts(c, 128)],
                                     ident[:])
        tv = aob_free = s.inc(mm, "p5t", 1)
        tv = s.inc(nc.tensor.nop(), "p5t", 1)
        s.wait(nc.vector, "p5t", tv)
        s.wait(nc.vector, "mm", s.val("mm"))
        nc.vector.tensor_copy(aoT[:], k.pt[:].rearrange("p (c n) -> p c n", c=8))
        aotv = s.inc(nc.vector.drain(), "p5tv", 1)
        s.wait(nc.tensor, "p5tv", aotv)
        s.wait(nc.tensor, "p5e", s.val("p5e"))
        for half in range(2):
            hp = k.pb[half][:]
            for c in range(8):
                nc.tensor.matmul(hp, aoT[:, c, :], wo[:, c, half, :],
                                 start=(c == 0), stop=False)
            mm = nc.tensor.matmul(hp, onesr[:],
                                  bias_bf[0:1, 3072 + half * 512:3072 + half * 512 + 512],
                                  start=False, stop=True)
        mv_ = s.inc(mm, "mm", 1)
        s.wait(nc.vector, "mm", mv_)
        s.wait(nc.vector, "p5xb", lvx)
        nc.vector.tensor_tensor(y1[:, 0:512], k.pb[0][:], xnb[:, 0:512], OP.add)
        nc.vector.tensor_tensor(y1[:, 512:1024], k.pb[1][:], xnb[:, 512:1024], OP.add)
        ev = xnb_free = s.inc(nc.vector.drain(), "p5e", 1)
        selfwait()
        s.wait(nc.vector, "p5xo", s.val("p5xo"))
        _ln_plain(k, None, y1, x1f, tmp, w_rep=g1r, b_rep=be1r)
        selfwait()
        nc.vector.tensor_copy(x1b[:], x1f[:])
        x1v = s.inc(nc.vector.drain(), "p5v", 1)
        s.wait(nc.scalar, "p5v", x1v)
        s.inc(nc.scalar.dma_start(t["x1"][bass.ts(blk, 128), :], x1f[:]), "p5xo", 16)
        s.wait(nc.tensor, "p5v", x1v)
        s.wait(nc.tensor, "p5tv", s.val("p5tv"))
        for c in range(8):
            mm = nc.tensor.transpose(k.pt[:, bass.ts(c, 128)], x1b[:, bass.ts(c, 128)],
                                     ident[:])
        tv = s.inc(mm, "p5t", 1)
        tv = s.inc(nc.tensor.nop(), "p5t", 1)
        s.wait(nc.vector, "p5t", tv)
        s.wait(nc.vector, "p5to", s.val("p5to"))
        nc.vector.tensor_copy(x1T[:], k.pt[:].rearrange("p (c n) -> p c n", c=8))
        x1tv = s.inc(nc.vector.drain(), "p5tv", 1)
        s.wait(nc.gpsimd, "p5tv", x1tv)
        sd = nc.gpsimd.dma_start(t["x1t"][g][:, :, bass.ts(bi, 128)].rearrange("c p n -> p c n"),
                                 x1T[:])
        s.inc(sd, "p5to", 16)
        s.wait(nc.tensor, "p5tv", x1tv)
        s.wait(nc.tensor, "p5g", s.val("p5g"))
        for c in range(8):
            nc.tensor.matmul(k.pg[:, 0:8], x1T[:, c, :], wg[:, c, :],
                             start=(c == 0), stop=False)
        mm = nc.tensor.matmul(k.pg[:, 0:8], onesr[:], bg_bf[:], start=False, stop=True)
        gv = s.inc(mm, "mm", 1)
        s.wait(nc.vector, "mm", gv)
        v1 = s.inc(nc.vector.tensor_reduce(m1[:], k.pg[:, 0:8], AX.X, OP.max), "p5g", 1)
        s.wait(nc.vector, "p5g", v1)
        v1 = s.inc(nc.vector.tensor_scalar(nm1[:], m1[:], -1.0, None, OP.mult), "p5g", 1)
        s.wait(nc.scalar, "p5g", v1)
        ex = nc.scalar.activation(pe_[:], k.pg[:, 0:8], AF.Exp, bias=nm1[:], scale=1.0)
        sv = s.inc(nc.scalar.drain(), "p5g", 1)
        s.wait(nc.vector, "p5g", sv)
        v1 = s.inc(nc.vector.tensor_reduce(m1[:], pe_[:], AX.X, OP.max), "p5g", 1)
        s.wait(nc.vector, "p5g", v1)
        v1 = s.inc(nc.vector.tensor_scalar(mask1[:], pe_[:], m1[:], None, OP.is_ge), "p5g", 1)
        s.wait(nc.vector, "p5g", v1)
        v1 = s.inc(nc.vector.scalar_tensor_tensor(pm[:], mask1[:], -1e9, pe_[:],
                                                  OP.mult, OP.add), "p5g", 1)
        s.wait(nc.vector, "p5g", v1)
        v1 = s.inc(nc.vector.tensor_reduce(m2[:], pm[:], AX.X, OP.max), "p5g", 1)
        s.wait(nc.vector, "p5g", v1)
        v1 = s.inc(nc.vector.tensor_tensor(s12[:], m1[:], m2[:], OP.add), "p5g", 1)
        s.wait(nc.vector, "p5g", v1)
        v1 = s.inc(nc.vector.reciprocal(rs[:], s12[:]), "p5g", 1)
        s.wait(nc.vector, "p5g", v1)
        v1 = s.inc(nc.vector.tensor_scalar(mask2[:], pe_[:], m2[:], None, OP.is_ge), "p5g", 1)
        s.wait(nc.vector, "p5g", v1)
        s.wait(nc.vector, "p5go", s.val("p5go"))
        v1 = s.inc(nc.vector.scalar_tensor_tensor(combv[:], pe_[:], rs[:], mask2[:],
                                                  OP.mult, OP.mult), "p5g", 1)
        cdr = s.inc(nc.vector.drain(), "p5gd", 1)
        s.wait(nc.gpsimd, "p5gd", cdr)
        sd = nc.gpsimd.dma_start(t["combt"][:, blk, :], combv[:])
        s.inc(sd, "p5go", 16)
    s.wait(nc.gpsimd, "p5go", s.val("p5go"))
    s.wait(nc.gpsimd, "p5to", s.val("p5to"))
    s.wait(nc.scalar, "p5xo", s.val("p5xo"))
    k.pop_scope()


def emit_moe_final(k: Kb, t, out_key="out"):
    k.push_scope()
    nc, s = k.nc, k.s
    ident = k.sb("p6_ident", [128, 128], BF16)
    g2r = k.sb("p6_g2r", [128, D], F32)
    be2r = k.sb("p6_be2r", [128, D], F32)
    b2f = k.sb("p6_b2f", [128, D], F32)
    b2b = k.sb("p6_b2b", [128, D], BF16)
    comb = k.sb("p6_comb", [128, NBLK, 8], F32)
    cbb = k.sb("p6_cbb", [128, 8], BF16)
    cbT = k.sb("p6_cbT", [128, 128], BF16)
    ya = k.sb("p6_ya", [128, D], F32)
    x1f = k.sb("p6_x1f", [128, D], F32)
    s3b = k.sb("p6_s3b", [128, D], F32)
    yt = k.sb("p6_yt", [128, D], F32)
    f2 = k.sb("p6_f2", [128, D], F32)
    f3 = k.sb("p6_f3", [128, D], F32)
    ot = k.sb("p6_ot", [128, D], F32)
    tmp = {"bn": k.sb("p6_bn", [128, 12], F32), "mv": k.sb("p6_mv", [128, 2], F32),
           "t1": k.sb("p6_t1", [128, 1], F32), "t2": k.sb("p6_t2", [128, 1], F32)}

    s.inc(nc.sync.dma_start(ident[:], t["ident"]), "p6w", 16)
    s.inc(nc.sync.dma_start(g2r[:], t["g2_rep"]), "p6w", 16)
    s.inc(nc.sync.dma_start(be2r[:], t["be2_rep"]), "p6w", 16)
    s.inc(nc.sync.dma_start(b2f[0:8, :], t["b2"]), "p6w", 16)
    s.inc(nc.sync.dma_start(comb[:], t["combt"]), "p6w", 16)
    s.wait(nc.vector, "p6w", s.val("p6w"))
    nc.vector.tensor_copy(b2b[:], b2f[:])
    wrdy = s.inc(nc.vector.drain(), "p6v", 1)
    s.wait(nc.tensor, "p6v", wrdy)

    def selfwait():
        v = s.inc(nc.vector.nop(), "p6sw", 1)
        s.wait(nc.vector, "p6sw", v)

    in_free = 0
    for blk in range(NBLK):
        s.wait(nc.scalar, "p6v", in_free)
        lv = s.inc(nc.scalar.dma_start(ya[:], t["yacc"][bass.ts(blk, 128), :]), "p6x", 16)
        lv = s.inc(nc.scalar.dma_start(x1f[:], t["x1"][bass.ts(blk, 128), :]), "p6x", 16)
        lv = s.inc(nc.scalar.dma_start(s3b[:], t["s3"][bass.ts(blk, 128), :]), "p6x", 16)
        s.wait(nc.vector, "mm", s.val("mm"))
        nc.vector.tensor_copy(cbb[:], comb[:, blk, :])
        cv = s.inc(nc.vector.drain(), "p6v", 1)
        s.wait(nc.tensor, "p6v", cv)
        s.wait(nc.tensor, "p6tv", s.val("p6tv"))
        mm = nc.tensor.transpose(k.pt[0:8, 0:128], cbb[:], ident[:])
        tv = s.inc(mm, "p6t", 1)
        tv = s.inc(nc.tensor.nop(), "p6t", 1)
        s.wait(nc.vector, "p6t", tv)
        nc.vector.tensor_copy(cbT[0:8, :], k.pt[0:8, 0:128])
        ctv = s.inc(nc.vector.drain(), "p6tv", 1)
        s.wait(nc.tensor, "p6tv", ctv)
        s.wait(nc.tensor, "p6e", s.val("p6e"))
        for half in range(2):
            mm = nc.tensor.matmul(k.pb[half][:], cbT[0:8, :], b2b[0:8, bass.ts(half, 512)],
                                  start=True, stop=True)
        mv_ = s.inc(mm, "mm", 1)
        s.wait(nc.vector, "mm", mv_)
        s.wait(nc.vector, "p6x", lv)
        nc.vector.tensor_tensor(yt[:, 0:512], k.pb[0][:], ya[:, 0:512], OP.add)
        nc.vector.tensor_tensor(yt[:, 512:1024], k.pb[1][:], ya[:, 512:1024], OP.add)
        ev = s.inc(nc.vector.drain(), "p6e", 1)
        selfwait()
        nc.vector.tensor_tensor(yt[:], yt[:], x1f[:], OP.add)
        selfwait()
        _ln_plain(k, None, yt, f2, tmp, w_rep=g2r, b_rep=be2r)
        selfwait()
        _ln_plain(k, None, f2, f3, tmp)
        selfwait()
        s.wait(nc.vector, "p6o", s.val("p6o"))
        nc.vector.tensor_tensor(ot[:], f3[:], s3b[:], OP.add)
        ov = in_free = s.inc(nc.vector.drain(), "p6v", 1)
        s.wait(nc.gpsimd, "p6v", ov)
        sd = nc.gpsimd.dma_start(t[out_key][bass.ts(blk, 128), :], ot[:])
        s.inc(sd, "p6o", 16)
    s.wait(nc.gpsimd, "p6o", s.val("p6o"))
    k.pop_scope()


def emit_l1a_qkv(k: Kb, t):
    k.push_scope()
    nc, s = k.nc, k.s
    wq = k.sb("p3_wq", [128, 8, 2, 512], BF16)
    wk = k.sb("p3_wk", [128, 8, 2, 512], BF16)
    wv = k.sb("p3_wv", [128, 8, 2, 512], BF16)
    ident = k.sb("p3_ident", [128, 128], BF16)
    xs = k.sb("p3_xs", [128, D], F32)
    xn = k.sb("p3_xn", [128, D], F32)
    xnb = k.sb("p3_xnb", [128, D], BF16)
    xnT = k.sb("p3_xnT", [128, 8, 512], BF16)
    osb = [k.sb(f"p3_o{i}", [128, D], BF16) for i in range(3)]
    tmp = {"bn": k.sb("p3_bn", [128, 12], F32), "mv": k.sb("p3_mv", [128, 2], F32),
           "t1": k.sb("p3_t1", [128, 1], F32), "t2": k.sb("p3_t2", [128, 1], F32)}

    for nm, w in (("wqT_bf", wq), ("wkT_bf", wk), ("wvT_bf", wv)):
        s.inc(nc.sync.dma_start(w[:], t[nm].rearrange("c h p n -> p c h n")), "p3w", 16)
    s.inc(nc.sync.dma_start(ident[:], t["ident"]), "p3w", 16)
    s.wait(nc.tensor, "p3w", s.val("p3w"))

    xs_free = 0
    o_free = [0, 0, 0]
    for g in range(NGRP):
        for bi in range(4):
            blk = g * 4 + bi
            s.wait(nc.scalar, "p3v", xs_free)
            lv = s.inc(nc.scalar.dma_start(xs[:], t["s2"][bass.ts(blk, 128), :]), "p3x", 16)
            s.wait(nc.vector, "p3x", lv)
            s.wait(nc.vector, "p3t", s.val("p3t"))
            s.wait(nc.vector, "p3xo", s.val("p3xo"))
            _ln_plain(k, None, xs, xn, tmp)
            v = s.inc(nc.vector.nop(), "p3v", 1)
            s.wait(nc.vector, "p3v", v)
            nc.vector.tensor_copy(xnb[:], xn[:])
            xv = xs_free = s.inc(nc.vector.drain(), "p3v", 1)
            s.wait(nc.scalar, "p3v", xv)
            s.inc(nc.scalar.dma_start(t["xn2"][bass.ts(blk, 128), :], xn[:]), "p3xo", 16)
            s.wait(nc.tensor, "p3v", xv)
            s.wait(nc.tensor, "p3tv", s.val("p3tv"))
            for c in range(8):
                mm = nc.tensor.transpose(k.pt[:, bass.ts(c, 128)],
                                         xnb[:, bass.ts(c, 128)], ident[:])
            tv = s.inc(mm, "p3t", 1)
            tv = s.inc(nc.tensor.nop(), "p3t", 1)
            s.wait(nc.vector, "p3t", tv)
            if bi == 0:
                s.wait(nc.vector, "mm", s.val("mm"))
            nc.vector.tensor_copy(xnT[:, :, bi * 128:(bi + 1) * 128],
                                  k.pt[:].rearrange("p (c n) -> p c n", c=8))
            s.inc(nc.vector.drain(), "p3tv", 1)
        s.wait(nc.tensor, "p3tv", s.val("p3tv"))
        for wi, (w, dstname) in enumerate(((wq, "q"), (wk, "k"), (wv, "v"))):
            for bi in range(4):
                blk = g * 4 + bi
                oi = (wi * 4 + bi) % 3
                s.wait(nc.tensor, "p3e", s.val("p3e"))
                for half in range(2):
                    hp = k.pb[half][:]
                    for c in range(8):
                        mm = nc.tensor.matmul(hp, xnT[:, c, bass.ts(bi, 128)],
                                              w[:, c, half, :],
                                              start=(c == 0), stop=(c == 7))
                mv_ = s.inc(mm, "mm", 1)
                s.wait(nc.vector, "mm", mv_)
                s.wait(nc.vector, f"p3o{oi}", o_free[oi])
                nc.vector.tensor_copy(osb[oi][:, 0:512], k.pb[0][:])
                nc.vector.tensor_copy(osb[oi][:, 512:1024], k.pb[1][:])
                ev = s.inc(nc.vector.drain(), "p3e", 1)
                s.wait(nc.gpsimd, "p3e", ev)
                sd = nc.gpsimd.dma_start(t[dstname][bass.ts(blk, 128), :], osb[oi][:])
                o_free[oi] = s.inc(sd, f"p3o{oi}", 16)
    for oi in range(3):
        s.wait(nc.gpsimd, f"p3o{oi}", s.val(f"p3o{oi}"))
    s.wait(nc.scalar, "p3xo", s.val("p3xo"))
    k.pop_scope()


def emit_l1a_norm(k: Kb, t):
    k.push_scope()
    nc, s = k.nc, k.s
    aob = [k.sb(f"p3n_ao{i}", [128, D], BF16) for i in range(2)]
    s2b = [k.sb(f"p3n_s2{i}", [128, D], F32) for i in range(2)]
    yb = [k.sb(f"p3n_y{i}", [128, D], F32) for i in range(2)]
    part = k.sb("p3n_part", [128, 2], F32)
    tpart = k.sb("p3n_t", [128, 2], F32)
    sq = k.sb("p3n_sq", [128, D], F32)
    onesc = k.sb("p3n_ones", [128, 1], F32)
    musd = k.sb("p3n_musd", [128, 2], F32)
    sc = k.sb("p3n_sc", [1, 4], F32)
    prow = k.sb("p3n_prow", [1, 256], F32)
    onesr = k.sb("p3n_onesr", [1, 128], F32)

    def vsw():
        v = s.inc(nc.vector.nop(), "p3n_v", 1)
        s.wait(nc.vector, "p3n_v", v)

    nc.vector.memset(part[:], 0.0)
    nc.vector.memset(onesc[:], 1.0)
    nc.vector.memset(onesr[:], 1.0)
    vsw()

    ld_free = [0, 0]
    for blk in range(NBLK):
        j = blk % 2
        s.wait(nc.scalar, "p3n_c", ld_free[j])
        s.inc(nc.scalar.dma_start(aob[j][:], t["ao"][bass.ts(blk, 128), :]), f"p3n_l{j}", 16)
        lv = s.inc(nc.scalar.dma_start(s2b[j][:], t["xn2"][bass.ts(blk, 128), :]), f"p3n_l{j}", 16)
        s.wait(nc.vector, f"p3n_l{j}", lv)
        s.wait(nc.vector, f"p3n_s{j}", s.val(f"p3n_s{j}"))
        nc.vector.tensor_tensor(yb[j][:], aob[j][:], s2b[j][:], OP.add)
        yv = s.inc(nc.vector.drain(), "p3n_y", 1)
        v = s.inc(nc.vector.reduce_sum(tpart[:, 0:1], yb[j][:], AX.X), "p3n_v", 1)
        s.wait(nc.scalar, "p3n_y", yv)
        sv = s.inc(nc.scalar.activation(sq[:], yb[j][:], AF.Square,
                                        accum_out=tpart[:, 1:2]), "p3n_sc", 1)
        s.wait(nc.vector, "p3n_sc", sv)
        s.wait(nc.vector, "p3n_v", v)
        v = s.inc(nc.vector.tensor_tensor(part[:], part[:], tpart[:], OP.add), "p3n_v", 1)
        s.wait(nc.vector, "p3n_v", v)
        ld_free[j] = s.inc(nc.vector.nop(), "p3n_c", 1)
        s.wait(nc.gpsimd, "p3n_c", ld_free[j])
        sd = nc.gpsimd.dma_start(t["yg"][bass.ts(blk, 128), :], yb[j][:])
        s.inc(sd, f"p3n_s{j}", 16)

    cv = s.inc(nc.vector.drain(), "p3n_v", 1)
    s.wait(nc.gpsimd, "p3n_v", cv)
    s.inc(nc.gpsimd.dma_start(t["gscr"][0:1, 0:128].rearrange("a b -> b a"),
                              part[:, 0:1]), "p3n_g1", 16)
    sd = s.inc(nc.gpsimd.dma_start(t["gscr"][0:1, 128:256].rearrange("a b -> b a"),
                                   part[:, 1:2]), "p3n_g1", 16)
    s.wait(nc.sync, "p3n_g1", sd)
    sr = s.inc(nc.sync.dma_start(prow[:], t["gscr"][0:1, 0:256]), "p3n_g2", 16)
    s.wait(nc.vector, "p3n_g2", sr)
    NTD = float(NT * D)
    v = s.inc(nc.vector.reduce_sum(sc[0:1, 0:1], prow[0:1, 0:128], AX.X), "p3n_v", 1)
    s.wait(nc.vector, "p3n_v", v)
    v = s.inc(nc.vector.reduce_sum(sc[0:1, 1:2], prow[0:1, 128:256], AX.X), "p3n_v", 1)
    s.wait(nc.vector, "p3n_v", v)
    v = s.inc(nc.vector.tensor_scalar(sc[0:1, 0:2], sc[0:1, 0:2], 1.0 / NTD, None,
                                      OP.mult), "p3n_v", 1)
    s.wait(nc.vector, "p3n_v", v)
    v = s.inc(nc.vector.tensor_tensor(sc[0:1, 2:3], sc[0:1, 0:1], sc[0:1, 0:1],
                                      OP.mult), "p3n_v", 1)
    s.wait(nc.vector, "p3n_v", v)
    v = s.inc(nc.vector.tensor_tensor(sc[0:1, 3:4], sc[0:1, 1:2], sc[0:1, 2:3],
                                      OP.subtract), "p3n_v", 1)
    s.wait(nc.vector, "p3n_v", v)
    v = s.inc(nc.vector.tensor_scalar(sc[0:1, 3:4], sc[0:1, 3:4], 1e-5, None, OP.add),
              "p3n_v", 1)
    s.wait(nc.scalar, "p3n_v", v)
    sv = s.inc(nc.scalar.activation(sc[0:1, 3:4], sc[0:1, 3:4], AF.Sqrt), "p3n_sc", 1)
    s.wait(nc.vector, "p3n_sc", sv)
    v = s.inc(nc.vector.reciprocal(sc[0:1, 1:2], sc[0:1, 3:4]), "p3n_v", 1)
    s.wait(nc.vector, "p3n_v", v)
    bv = s.inc(nc.vector.drain(), "p3n_v", 1)
    s.wait(nc.gpsimd, "p3n_v", bv)
    sd = s.inc(nc.gpsimd.dma_start(t["gscr"][0:1, 256:258], sc[0:1, 0:2]), "p3n_g1", 16)
    s.wait(nc.sync, "p3n_g1", sd)
    bsrc = bass.AP(t["gscr"].tensor, 256, [[0, 128], [1, 2]])
    sr = s.inc(nc.sync.dma_start(musd[:], bsrc), "p3n_g2", 16)
    s.wait(nc.vector, "p3n_g2", sr)

    st_free = [0, 0]
    tmp2 = {"bn": k.sb("p3n_bn", [128, 12], F32), "mv": k.sb("p3n_mv", [128, 2], F32),
            "t1": k.sb("p3n_t1", [128, 1], F32), "t2": k.sb("p3n_t2", [128, 1], F32)}
    for blk in range(NBLK):
        j = blk % 2
        s.wait(nc.scalar, "p3n_v", st_free[j])
        lv = s.inc(nc.scalar.dma_start(yb[j][:], t["yg"][bass.ts(blk, 128), :]),
                   f"p3n_l2{j}", 16)
        lv = s.inc(nc.scalar.dma_start(aob[j].bitcast(F32) if False else s2b[j][:],
                                       t["s2"][bass.ts(blk, 128), :]), f"p3n_l2{j}", 16)
        s.wait(nc.vector, f"p3n_l2{j}", lv)
        s.wait(nc.vector, f"p3n_o{j}", s.val(f"p3n_o{j}"))
        nc.vector.tensor_scalar(yb[j][:], yb[j][:], musd[:, 0:1], musd[:, 1:2],
                                OP.subtract, OP.mult)
        gv = s.inc(nc.vector.drain(), "p3n_v", 1)
        s.wait(nc.vector, "p3n_v", gv)
        _ln_plain(k, None, yb[j], sq, tmp2)
        gv = s.inc(nc.vector.nop(), "p3n_v", 1)
        s.wait(nc.vector, "p3n_v", gv)
        nc.vector.tensor_tensor(sq[:], sq[:], s2b[j][:], OP.add)
        ov = s.inc(nc.vector.drain(), "p3n_v", 1)
        st_free[j] = ov
        s.wait(nc.gpsimd, "p3n_v", ov)
        sd = nc.gpsimd.dma_start(t["s3"][bass.ts(blk, 128), :], sq[:])
        s.inc(sd, f"p3n_o{j}", 16)
    for j in range(2):
        s.wait(nc.gpsimd, f"p3n_o{j}", s.val(f"p3n_o{j}"))
        s.wait(nc.gpsimd, f"p3n_s{j}", s.val(f"p3n_s{j}"))
    k.pop_scope()


def emit_l0_quant(k: Kb, t, wkey, dstkey, R, C, tag="q", scr=None):
    k.push_scope()
    nc, s = k.nc, k.s
    NB = R // 128
    stg = [k.sb(f"{tag}_stg{i}", [128, C], F32) for i in range(2)]
    aw = k.sb(tag + "_aw", [128, C], F32)
    sg = k.sb(tag + "_sg", [128, C], F32)
    msk = k.sb(tag + "_msk", [128, C], F32)
    qb = [k.sb(f"{tag}_qb{i}", [128, C], BF16) for i in range(2)]
    acc = k.sb(tag + "_acc", [128, 2], F32)
    tpart = k.sb(tag + "_tp", [128, 2], F32)
    onesc = k.sb(tag + "_ones", [128, 1], F32)
    thr = k.sb(tag + "_thr", [128, 1], F32)
    sc1 = k.sb(tag + "_sc1", [1, 2], F32)
    sc1r = k.sb(tag + "_sc1r", [1, 128], F32)

    def vsw():
        v = s.inc(nc.vector.nop(), "q_v", 1)
        s.wait(nc.vector, "q_v", v)

    nc.vector.memset(acc[:], 0.0)
    nc.vector.memset(tpart[:], 0.0)
    nc.vector.memset(onesc[:], 1.0)
    vsw()
    ld_free = [0, 0]
    for b in range(NB):
        j = b % 2
        s.wait(nc.sync, "q_v", ld_free[j])
        lv = s.inc(nc.sync.dma_start(stg[j][:], t[wkey][bass.ts(b, 128), :]), f"q_l{j}", 16)
        s.wait(nc.vector, f"q_l{j}", lv)
        v = s.inc(nc.vector.tensor_reduce(tpart[:, 0:1], stg[j][:], AX.X, OP.add,
                                          apply_absolute_value=True), "q_v", 1)
        s.wait(nc.vector, "q_v", v)
        ld_free[j] = s.inc(nc.vector.tensor_tensor(acc[:], acc[:], tpart[:], OP.add),
                           "q_v", 1)
        s.wait(nc.vector, "q_v", ld_free[j])
    cv = s.inc(nc.vector.drain(), "q_v", 1)
    s.wait(nc.gpsimd, "q_v", cv)
    sd = s.inc(nc.gpsimd.dma_start(scr[0:1, 0:128].rearrange("a b -> b a"),
                                   acc[:, 0:1]), "q_g1", 16)
    s.wait(nc.sync, "q_g1", sd)
    sr = s.inc(nc.sync.dma_start(sc1r[:], scr[0:1, 0:128]), "q_g2", 16)
    s.wait(nc.vector, "q_g2", sr)
    v = s.inc(nc.vector.reduce_sum(sc1[0:1, 0:1], sc1r[:], AX.X), "q_v", 1)
    s.wait(nc.vector, "q_v", v)
    v = s.inc(nc.vector.tensor_scalar(sc1[0:1, 1:2], sc1[0:1, 0:1], 1.0 / (R * C), 1e-6,
                                      OP.mult, OP.add), "q_v", 1)
    s.wait(nc.vector, "q_v", v)
    v = s.inc(nc.vector.tensor_scalar(sc1[0:1, 1:2], sc1[0:1, 1:2], 0.5, None, OP.mult),
              "q_v", 1)
    bv = s.inc(nc.vector.drain(), "q_v", 1)
    s.wait(nc.gpsimd, "q_v", bv)
    sd = s.inc(nc.gpsimd.dma_start(scr[0:1, 128:129], sc1[0:1, 1:2]), "q_g1", 16)
    s.wait(nc.sync, "q_g1", sd)
    bsrc = bass.AP(scr.tensor, 128, [[0, 128], [1, 1]])
    sr = s.inc(nc.sync.dma_start(thr[:], bsrc), "q_g2", 16)
    s.wait(nc.vector, "q_g2", sr)
    ld_free = [0, 0]
    out_done = [0, 0]
    prev_qv = 0
    for b in range(NB):
        j = b % 2
        s.wait(nc.sync, "q_v", ld_free[j])
        lv = s.inc(nc.sync.dma_start(stg[j][:], t[wkey][bass.ts(b, 128), :]), f"q_l{j}", 16)
        s.wait(nc.scalar, f"q_l{j}", lv)
        s.wait(nc.scalar, "q_v", prev_qv)
        sv = s.inc(nc.scalar.activation(aw[:], stg[j][:], AF.Abs), "q_s", 1)
        sv = s.inc(nc.scalar.activation(sg[:], stg[j][:], AF.Sign), "q_s", 1)
        s.wait(nc.vector, "q_s", sv)
        v = s.inc(nc.vector.tensor_scalar(msk[:], aw[:], thr[:], None, OP.is_ge), "q_v", 1)
        ld_free[j] = v
        s.wait(nc.vector, "q_v", v)
        s.wait(nc.vector, f"q_o{j}", out_done[j])
        nc.vector.tensor_tensor(qb[j][:], msk[:], sg[:], OP.mult)
        qv = prev_qv = s.inc(nc.vector.drain(), "q_v", 1)
        s.wait(nc.gpsimd, "q_v", qv)
        sd = nc.gpsimd.dma_start(t[dstkey][b], qb[j][:])
        out_done[j] = s.inc(sd, f"q_o{j}", 16)
    for j in range(2):
        s.wait(nc.gpsimd, f"q_o{j}", s.val(f"q_o{j}"))
    k.pop_scope()


def emit_l0_ffn(k: Kb, t):
    k.push_scope()
    nc, s = k.nc, k.s
    wfc = k.sb("f0_wfc", [128, 8, FF], BF16)
    wpj = k.sb("f0_wpj", [128, FT, D], BF16)
    lnw = k.sb("f0_lnw", [128, FF], F32)
    lnb = k.sb("f0_lnb", [128, FF], F32)
    ident = k.sb("f0_ident", [128, 128], BF16)
    xs = k.sb("f0_xs", [128, D], F32)
    xnb = k.sb("f0_xnb", [128, D], BF16)
    xnT = k.sb("f0_xnT", [128, 8, 128], BF16)
    hsb = k.sb("f0_h", [128, FF], F32)
    hnb = k.sb("f0_hnb", [128, FF], BF16)
    hT = k.sb("f0_hT", [128, FT, 128], BF16)
    ot_ = k.sb("f0_ot", [128, D], F32)
    ot = [ot_, ot_]
    tmp = {"bn": k.sb("f0_bn", [128, 48], F32), "mv": k.sb("f0_mv", [128, 2], F32),
           "t1": k.sb("f0_t1", [128, 1], F32), "t2": k.sb("f0_t2", [128, 1], F32)}

    s.inc(nc.sync.dma_start(wfc[:], t["wfcq"].rearrange("c p f -> p c f")), "f0_w", 16)
    s.inc(nc.sync.dma_start(wpj[:], t["wprojq"].rearrange("f p d -> p f d")), "f0_w", 16)
    s.inc(nc.sync.dma_start(lnw[:], t["lnw_rep"]), "f0_w", 16)
    s.inc(nc.sync.dma_start(lnb[:], t["lnb_rep"]), "f0_w", 16)
    wl = s.inc(nc.sync.dma_start(ident[:], t["ident"]), "f0_w", 16)
    s.wait(nc.tensor, "f0_w", wl)
    s.wait(nc.vector, "f0_w", wl)

    xs_free = 0
    st_free = [0, 0]
    def lnfull(xin, xout, n, w_rep=None, b_rep=None):
        for i in range(n):
            mm = nc.vector.bn_stats(tmp["bn"][:, 6 * i:6 * i + 6],
                                    xin[:, 512 * i:512 * (i + 1)])
        v = s.inc(mm, "ln", 1)
        s.wait(nc.vector, "ln", v)
        v = s.inc(nc.vector.bn_aggr(tmp["mv"][:], tmp["bn"][:, 0:6 * n]), "ln", 1)
        s.wait(nc.vector, "ln", v)
        v = s.inc(nc.vector.tensor_scalar(tmp["t1"][:], tmp["mv"][:, 1:2], 1e-5, None,
                                          OP.add), "ln", 1)
        s.wait(nc.scalar, "ln", v)
        v = s.inc(nc.scalar.activation(tmp["t1"][:], tmp["t1"][:], AF.Sqrt), "ln", 1)
        s.wait(nc.vector, "ln", v)
        v = s.inc(nc.vector.reciprocal(tmp["t2"][:], tmp["t1"][:]), "ln", 1)
        s.wait(nc.vector, "ln", v)
        nc.vector.tensor_scalar(xout[:], xin[:], tmp["mv"][:, 0:1], tmp["t2"][:],
                                OP.subtract, OP.mult)
        if w_rep is not None:
            v = s.inc(nc.vector.nop(), "ln", 1)
            s.wait(nc.vector, "ln", v)
            nc.vector.tensor_tensor(xout[:], xout[:], w_rep[:], OP.mult)
            v = s.inc(nc.vector.nop(), "ln", 1)
            s.wait(nc.vector, "ln", v)
            nc.vector.tensor_tensor(xout[:], xout[:], b_rep[:], OP.add)

    for blk in range(NBLK):
        j = blk % 2
        s.wait(nc.scalar, "f0_v", xs_free)
        lv = s.inc(nc.scalar.dma_start(xs[:], t["s1"][bass.ts(blk, 128), :]), "f0_x", 16)
        s.wait(nc.vector, "f0_x", lv)
        s.wait(nc.vector, "f0_t", s.val("f0_t"))
        lnfull(xs, xnb, 2)
        xv = s.inc(nc.vector.drain(), "f0_v", 1)
        s.wait(nc.tensor, "f0_v", xv)
        s.wait(nc.tensor, "f0_tv", s.val("f0_tv"))
        for c in range(8):
            mm = nc.tensor.transpose(k.pt[:, bass.ts(c, 128)], xnb[:, bass.ts(c, 128)],
                                     ident[:])
        tv = s.inc(mm, "f0_t", 1)
        tv = s.inc(nc.tensor.nop(), "f0_t", 1)
        s.wait(nc.vector, "f0_t", tv)
        s.wait(nc.vector, "mm", s.val("mm"))
        nc.vector.tensor_copy(xnT[:], k.pt[:].rearrange("p (c n) -> p c n", c=8))
        xtv = s.inc(nc.vector.drain(), "f0_tv", 1)
        s.wait(nc.tensor, "f0_tv", xtv)
        for f8 in range(8):
            hp = k.pb[f8 % 2][:]
            s.wait(nc.tensor, "f0_e", s.val("f0_e") - 1 if f8 >= 2 or blk > 0 else 0)
            for c in range(8):
                mm = nc.tensor.matmul(hp, xnT[:, c, :], wfc[:, c, bass.ts(f8, 512)],
                                      start=(c == 0), stop=(c == 7))
            hv = s.inc(mm, "mm", 1)
            s.wait(nc.scalar, "mm", hv)
            ge = nc.scalar.activation(hsb[:, bass.ts(f8, 512)], hp, AF.Gelu)
            s.inc(ge, "f0_e", 1)
        gl = s.inc(nc.scalar.drain(), "f0_e", 1)
        s.wait(nc.vector, "f0_e", gl)
        lnfull(hsb, hsb, 8, w_rep=lnw, b_rep=lnb)
        v = s.inc(nc.vector.nop(), "f0_v", 1)
        s.wait(nc.vector, "f0_v", v)
        nc.vector.tensor_copy(hnb[:], hsb[:])
        hv2 = s.inc(nc.vector.drain(), "f0_v", 1)
        for bt in range(4):
            s.wait(nc.tensor, "f0_v", hv2)
            s.wait(nc.tensor, "f0_tv", s.val("f0_tv"))
            for ti in range(8):
                c = bt * 8 + ti
                mm = nc.tensor.transpose(k.pt[:, bass.ts(ti, 128)],
                                         hnb[:, bass.ts(c, 128)], ident[:])
            tv = s.inc(mm, "f0_t", 1)
            tv = s.inc(nc.tensor.nop(), "f0_t", 1)
            s.wait(nc.vector, "f0_t", tv)
            nc.vector.tensor_copy(hT[:, bass.ts(bt, 8), :],
                                  k.pt[:].rearrange("p (c n) -> p c n", c=8))
            s.inc(nc.vector.drain(), "f0_tv", 1)
        htv = s.val("f0_tv")
        s.wait(nc.tensor, "f0_tv", htv)
        for half in range(2):
            yp = k.pb[4 + half][:]
            for f in range(FT):
                mm = nc.tensor.matmul(yp, hT[:, f, :], wpj[:, f, bass.ts(half, 512)],
                                      start=(f == 0), stop=(f == FT - 1))
        fence = nc.tensor.matmul(k.pg[0:1, 100:101], ident[0:128, 0:1], ident[0:128, 0:1],
                                 start=True, stop=True)
        yv = s.inc(fence, "mm", 1)
        yv = s.inc(nc.tensor.nop(), "mm", 1)
        s.wait(nc.vector, "mm", yv)
        s.wait(nc.vector, "f0_o0", st_free[0])
        s.wait(nc.vector, "f0_o1", st_free[1])
        nc.vector.tensor_copy(ot[j][64:128, 0:512], ot[j][64:128, 0:512])
        nc.vector.tensor_copy(ot[j][:, 0:512], k.pb[4][:])
        nc.vector.tensor_copy(ot[j][:, 512:1024], k.pb[5][:])
        v = s.inc(nc.vector.drain(), "f0_v", 1)
        s.wait(nc.vector, "f0_v", v)
        lnfull(ot[j], ot[j], 2)
        v = s.inc(nc.vector.nop(), "f0_v", 1)
        s.wait(nc.vector, "f0_v", v)
        nc.vector.tensor_tensor(ot[j][:], ot[j][:], xs[:], OP.add)
        ov = xs_free = s.inc(nc.vector.drain(), "f0_v", 1)
        s.wait(nc.gpsimd, "f0_v", ov)
        sd = nc.gpsimd.dma_start(t["s2"][bass.ts(blk, 128), :], ot[j][:])
        st_free[j] = s.inc(sd, f"f0_o{j}", 16)
    for j in range(2):
        s.wait(nc.gpsimd, f"f0_o{j}", s.val(f"f0_o{j}"))
    k.pop_scope()


def emit_l0_attn(k: Kb, t, nchunks=NBLK):
    import os as _o
    CUT = int(_o.environ.get("CUT", "9"))
    """s1 = x0 + LN_plain(linattn(LN_plain(x0))), chunked causal-inclusive scan.

    All matmul operands at partition base 0 (base-64 operands crash the PE).
    PSUM: pb0/1 = Qp then A-slots; pb2/3 = S state (persistent, [0:64, 8, 64]
    even/odd heads); pb4/5 = Vp then O (time-shared); pt transposes; pg free.
    """
    k.push_scope()
    nc, s = k.nc, k.s
    wq = k.sb("l0_wq", [128, 8, 2, 512], BF16)
    wv = k.sb("l0_wv", [128, 8, 2, 512], BF16)
    qnw = k.sb("l0_qnw", [128, D], F32)
    vnw = k.sb("l0_vnw", [128, D], F32)
    tri = k.sb("l0_tri", [128, 128], F32)
    ident = k.sb("l0_ident", [128, 128], BF16)
    xs = k.sb("l0_xs", [128, D], F32)
    xn = k.sb("l0_xn", [128, D], F32)
    xnb = k.sb("l0_xnb", [128, D], BF16)
    xnT = k.sb("l0_xnT", [128, 8, 128], BF16)
    xnK = k.sb("l0_xnK", [64, 16, 128], BF16)
    sq = k.sb("l0_sq", [128, D], F32)
    rr = k.sb("l0_r", [128, 16], F32)
    qsc = k.sb("l0_qsc", [128, D], F32)
    qbf = k.sb("l0_qbf", [128, D], BF16)
    vbf = k.sb("l0_vbf", [128, D], BF16)
    qT = k.sb("l0_qT", [64, 16, 128], BF16)
    Ssb = k.sb("l0_Ssb", [64, 16, 64], BF16)
    Asb = k.sb("l0_Asb", [128, 16, 128], BF16)
    osb = k.sb("l0_osb", [128, D], F32)
    ot = [k.sb(f"l0_ot{i}", [128, D], F32) for i in range(2)]
    can = k.sb("l0_can", [128, 8], F32)
    tmp = {"bn": k.sb("l0_bn", [128, 12], F32), "mv": k.sb("l0_mv", [128, 2], F32),
           "t1": k.sb("l0_t1", [128, 1], F32), "t2": k.sb("l0_t2", [128, 1], F32)}

    s.inc(nc.sync.dma_start(wq[:], t["wqT_bf"].rearrange("c h p n -> p c h n")), "l0_w", 16)
    s.inc(nc.sync.dma_start(wv[:], t["wvT_bf"].rearrange("c h p n -> p c h n")), "l0_w", 16)
    s.inc(nc.sync.dma_start(qnw[:], t["qnw_rep"]), "l0_w", 16)
    s.inc(nc.sync.dma_start(vnw[:], t["vnw_rep"]), "l0_w", 16)
    s.inc(nc.sync.dma_start(tri[:], t["triT"]), "l0_w", 16)
    wl = s.inc(nc.sync.dma_start(ident[:], t["ident"]), "l0_w", 16)
    s.wait(nc.tensor, "l0_w", wl)
    s.wait(nc.vector, "l0_w", wl)

    def vsw():
        v = s.inc(nc.vector.nop(), "l0_v", 1)
        s.wait(nc.vector, "l0_v", v)

    def rms_chain(banks, wrep, outbf, gate_mm):
        s.wait(nc.scalar, "mm", gate_mm)
        s.wait(nc.scalar, "l0_v", s.val("l0_v"))
        nc.scalar.activation(sq[:, 0:512], banks[0][:], AF.Square)
        sv = s.inc(nc.scalar.activation(sq[:, 512:1024], banks[1][:], AF.Square), "l0_s", 1)
        s.wait(nc.vector, "l0_s", sv)
        v = s.inc(nc.vector.tensor_reduce(rr[:], sq.rearrange("p (h c) -> p h c", h=16),
                                          AX.X, OP.add), "l0_v", 1)
        s.wait(nc.vector, "l0_v", v)
        v = s.inc(nc.vector.tensor_scalar(rr[:], rr[:], 1.0 / 64, 1e-8, OP.mult, OP.add),
                  "l0_v", 1)
        s.wait(nc.scalar, "l0_v", v)
        sv = s.inc(nc.scalar.activation(rr[:], rr[:], AF.Sqrt), "l0_s", 1)
        s.wait(nc.vector, "l0_s", sv)
        v = s.inc(nc.vector.reciprocal(rr[:], rr[:]), "l0_v", 1)
        s.wait(nc.vector, "l0_v", v)
        for h in range(16):
            nc.vector.tensor_scalar(qsc[:, 64 * h:64 * h + 64],
                                    banks[h // 8][:, (h % 8) * 64:(h % 8) * 64 + 64],
                                    rr[:, h:h + 1], None, OP.mult)
        vsw()
        nc.vector.tensor_tensor(outbf[:], qsc[:], wrep[:], OP.mult)
        return s.inc(nc.vector.drain(), "l0_v", 1)

    xs_free = 0
    st_free = [0, 0]
    qdone = vdone = 0
    odone = 0
    fence_prev = 0
    for c in range(nchunks):
        j = c % 2
        s.wait(nc.scalar, "l0_v", xs_free)
        lv = s.inc(nc.scalar.dma_start(xs[:], t["x0"][bass.ts(c, 128), :]), "l0_x", 16)
        s.wait(nc.vector, "l0_x", lv)
        s.wait(nc.vector, "l0_t", s.val("l0_t"))
        _ln_plain(k, None, xs, xn, tmp)
        vsw()
        nc.vector.tensor_copy(xnb[:], xn[:])
        xv = s.inc(nc.vector.drain(), "l0_v", 1)
        s.wait(nc.tensor, "l0_v", xv)
        s.wait(nc.tensor, "l0_tv", s.val("l0_tv"))
        for cc in range(8):
            mm = nc.tensor.transpose(k.pt[:, bass.ts(cc, 128)], xnb[:, bass.ts(cc, 128)],
                                     ident[:])
        tv = s.inc(mm, "l0_t", 1)
        tv = s.inc(nc.tensor.nop(), "l0_t", 1)
        s.wait(nc.vector, "l0_t", tv)
        s.wait(nc.vector, "mm", s.val("mm"))
        nc.vector.tensor_copy(xnT[:], k.pt[:].rearrange("p (c n) -> p c n", c=8))
        xtv = s.inc(nc.vector.drain(), "l0_tv", 1)
        for bt in range(2):
            s.wait(nc.tensor, "l0_tv", s.val("l0_tv"))
            for ti in range(8):
                h = bt * 8 + ti
                mm = nc.tensor.transpose(k.pt[0:64, bass.ts(ti, 128)],
                                         xnb[:, 64 * h:64 * h + 64], ident[:])
            tv = s.inc(mm, "l0_t", 1)
            tv = s.inc(nc.tensor.nop(), "l0_t", 1)
            s.wait(nc.vector, "l0_t", tv)
            nc.vector.tensor_copy(xnK[0:64, bass.ts(bt, 8), :],
                                  k.pt[0:64, :].rearrange("p (a n) -> p a n", a=8))
            s.inc(nc.vector.drain(), "l0_tv", 1)
        s.wait(nc.tensor, "l0_tv", s.val("l0_tv"))
        if CUT <= 0:
            xs_free = xv
            continue
        s.wait(nc.tensor, "l0_v", qdone)
        for half in range(2):
            for cc in range(8):
                mm = nc.tensor.matmul(k.pb[half][:], xnT[:, cc, :],
                                      wq[:, cc, half, :], start=(cc == 0), stop=(cc == 7))
        qmv = s.inc(mm, "mm", 1)
        s.wait(nc.tensor, "l0_v", odone)
        for half in range(2):
            for cc in range(8):
                mm = nc.tensor.matmul(k.pb[4 + half][:], xnT[:, cc, :],
                                      wv[:, cc, half, :], start=(cc == 0), stop=(cc == 7))
        vmv = s.inc(mm, "mm", 1)
        qdone_ = rms_chain((k.pb[0], k.pb[1]), qnw, qbf, qmv)
        vdone = rms_chain((k.pb[4], k.pb[5]), vnw, vbf, vmv)
        if CUT <= 1:
            qdone = qdone_
            odone = vdone
            xs_free = vdone
            continue
        s.wait(nc.tensor, "l0_v", qdone_)
        for bt in range(2):
            s.wait(nc.tensor, "l0_tv", s.val("l0_tv"))
            for ti in range(8):
                h = bt * 8 + ti
                mm = nc.tensor.transpose(k.pt[0:64, bass.ts(ti, 128)],
                                         qbf[:, 64 * h:64 * h + 64], ident[:])
            tv = s.inc(mm, "l0_t", 1)
            tv = s.inc(nc.tensor.nop(), "l0_t", 1)
            s.wait(nc.vector, "l0_t", tv)
            nc.vector.tensor_copy(qT[0:64, bass.ts(bt, 8), :],
                                  k.pt[0:64, :].rearrange("p (a n) -> p a n", a=8))
            s.inc(nc.vector.drain(), "l0_tv", 1)
        s.wait(nc.tensor, "l0_tv", s.val("l0_tv"))
        if CUT <= 2:
            qdone = qdone_
            odone = vdone
            xs_free = vdone
            continue
        s.wait(nc.vector, "mm", fence_prev)
        if c == 0:
            nc.vector.memset(Ssb[:], 0.0)
        else:
            sv2 = Ssb.rearrange("p (a b) c -> p a b c", b=2)
            nc.vector.tensor_copy(sv2[0:64, :, 0, :],
                                  k.pb[2][0:64, :].rearrange("p (a c) -> p a c", a=8))
            nc.vector.tensor_copy(sv2[0:64, :, 1, :],
                                  k.pb[3][0:64, :].rearrange("p (a c) -> p a c", a=8))
        nc.vector.drain()
        nc.vector.tensor_copy(can[0:64, 0:8], Ssb[0:64, 0, :][:, 0:8])
        sdone = s.inc(nc.vector.drain(), "l0_v", 1)
        mk = 0
        for wv_i in range(2):
            s.wait(nc.tensor, "l0_v", qdone_ if wv_i == 0 else mk)
            for ti in range(8):
                h = wv_i * 8 + ti
                mm = nc.tensor.matmul(k.pb[ti // 4][:, (ti % 4) * 128:(ti % 4) * 128 + 128],
                                      xnK[0:64, h, :], qT[0:64, h, :],
                                      start=True, stop=True)
            amv = s.inc(mm, "mm", 1)
            amv = s.inc(nc.tensor.nop(), "mm", 1)
            s.wait(nc.vector, "mm", amv)
            for ti in range(8):
                h = wv_i * 8 + ti
                nc.vector.tensor_tensor(Asb[:, h, :],
                                        k.pb[ti // 4][:, (ti % 4) * 128:(ti % 4) * 128 + 128],
                                        tri[:], OP.mult)
            nc.vector.drain()
            nc.vector.tensor_copy(can[:], Asb[:, wv_i * 8, 0:8])
            mk = s.inc(nc.vector.drain(), "l0_v", 1)
            s.wait(nc.tensor, "l0_v", mk)
            s.wait(nc.tensor, "l0_v", sdone)
            s.wait(nc.tensor, "l0_v", vdone)
            for ti in range(8):
                h = wv_i * 8 + ti
                oslot = k.pb[4 + h // 8][:, (h % 8) * 64:(h % 8) * 64 + 64]
                nc.tensor.matmul(oslot, Asb[:, h, :], vbf[:, 64 * h:64 * h + 64],
                                 start=(h % 8 == 0), stop=False)
                nc.tensor.matmul(oslot, qT[0:64, h, :], Ssb[0:64, h, :],
                                 start=False, stop=True)
                mm = nc.tensor.matmul(k.pb[2 + h % 2][0:64, (h // 2) * 64:(h // 2) * 64 + 64],
                                      xnb[:, 64 * h:64 * h + 64],
                                      vbf[:, 64 * h:64 * h + 64],
                                      start=(c == 0 and h <= 1),
                                      stop=(c == nchunks - 1))
            s.inc(mm, "mm", 1)
        if CUT <= 3:
            qdone = qdone_
            odone = vdone
            xs_free = vdone
            continue
        fence = nc.tensor.matmul(k.pg[0:1, 100:101], ident[0:128, 0:1], ident[0:128, 0:1],
                                 start=True, stop=True)
        fence_prev = s.inc(fence, "mm", 1)
        fence_prev = s.inc(nc.tensor.nop(), "mm", 1)
        s.wait(nc.vector, "mm", fence_prev)
        s.wait(nc.vector, f"l0_o{j}", st_free[j])
        nc.vector.tensor_copy(osb[64:128, 0:512], osb[64:128, 0:512])
        nc.vector.tensor_copy(osb[:, 0:512], k.pb[4][:])
        nc.vector.tensor_copy(osb[:, 512:1024], k.pb[5][:])
        odone = s.inc(nc.vector.drain(), "l0_v", 1)
        qdone = qdone_
        vsw()
        _ln_plain(k, None, osb, ot[j], tmp)
        vsw()
        nc.vector.tensor_tensor(ot[j][:], ot[j][:], xs[:], OP.add)
        ov = xs_free = s.inc(nc.vector.drain(), "l0_v", 1)
        s.wait(nc.gpsimd, "l0_v", ov)
        sd = nc.gpsimd.dma_start(t["s1"][bass.ts(c, 128), :], ot[j][:])
        st_free[j] = s.inc(sd, f"l0_o{j}", 16)
    for j in range(2):
        s.wait(nc.gpsimd, f"l0_o{j}", s.val(f"l0_o{j}"))
    k.pop_scope()


def _barrier(k: Kb, recycle=False):
    s = k.s
    for eng in (k.nc.sync, k.nc.tensor, k.nc.vector, k.nc.scalar, k.nc.gpsimd):
        for name in list(s.h):
            v = s.val(name)
            if v > 0:
                eng.wait_ge(s.get(name), v)
    if recycle:
        keep = {"mm", "ln"}
        drop = [n for n in list(s.h) if n not in keep]
        for n in drop:
            s.pool.append((s.h[n], s.v[n]))
            del s.h[n]
            del s.v[n]


def build_fused():
    nc = bass.Bass("TRN2", target_bir_lowering=False)
    t = {}
    EI = lambda n, sh, dt=F32: nc.dram_tensor(n, sh, dt, kind="ExternalInput").ap()
    IN = lambda n, sh, dt=F32: nc.dram_tensor(n, sh, dt, kind="Internal").ap()
    t["ident"] = EI("ident", [128, 128], BF16)
    t["x0"] = EI("x0", [NT, D])
    t["l0_wqT"] = EI("l0_wqT", [D, D]); t["l0_wvT"] = EI("l0_wvT", [D, D])
    t["qnw_rep"] = EI("qnw_rep", [128, D]); t["vnw_rep"] = EI("vnw_rep", [128, D])
    t["triT"] = EI("triT", [128, 128])
    t["wfcT"] = EI("wfcT", [D, FF]); t["wprojT"] = EI("wprojT", [FF, D])
    t["lnw_rep"] = EI("lnw_rep", [128, FF]); t["lnb_rep"] = EI("lnb_rep", [128, FF])
    t["l1a_wqT"] = EI("l1a_wqT", [D, D]); t["l1a_wkT"] = EI("l1a_wkT", [D, D])
    t["l1a_wvT"] = EI("l1a_wvT", [D, D])
    for nm in ("wq", "wk", "wv", "wo"):
        t["l1f_" + nm] = EI("l1f_" + nm, [D, D])
    t["biases"] = EI("biases", [1, 4096])
    t["wgw"] = EI("wgw", [D, 8]); t["bg"] = EI("bg", [1, 8])
    t["w1"] = EI("w1", [E, D, FF]); t["w2"] = EI("w2", [E, FF, D])
    t["b1t"] = EI("b1t", [128, FT, E]); t["b2"] = EI("b2", [8, D])
    t["cos_rep"] = EI("cos_rep", [NT, D]); t["sinS_rep"] = EI("sinS_rep", [NT, D])
    for nm in ("g1_rep", "be1_rep", "g2_rep", "be2_rep"):
        t[nm] = EI(nm, [128, D])
    for nm in ("l0_wqT_bf", "l0_wvT_bf", "l1a_wqT_bf", "l1a_wkT_bf", "l1a_wvT_bf",
               "l1f_wq_bf", "l1f_wk_bf", "l1f_wv_bf", "l1f_wo_bf"):
        t[nm] = IN(nm + "_s", [8, 2, 128, 512], BF16)
    t["wg_bf"] = IN("wg_bf_s", [8, 128, 8], BF16)
    t["w1bf"] = IN("w1bf_s", [E, 8, 128, FF], BF16)
    t["w2bf"] = IN("w2bf_s", [E, FT, 128, D], BF16)
    t["wfcq"] = IN("wfcq_s", [8, 128, FF], BF16)
    t["wprojq"] = IN("wprojq_s", [FT, 128, D], BF16)
    t["qscr_a"] = IN("qscr_a", [1, 256]); t["qscr_b"] = IN("qscr_b", [1, 256])
    t["gscr"] = IN("gscr", [1, 512])
    t["s1"] = IN("s1_s", [NT, D]); t["s2"] = IN("s2_s", [NT, D])
    t["s3"] = IN("s3_s", [NT, D])
    t["q"] = IN("q_s", [NT, D], BF16); t["k"] = IN("k_s", [NT, D], BF16)
    t["v"] = IN("v_s", [NT, D], BF16)
    t["ao1"] = IN("ao1_s", [NT, D], BF16)
    t["yg"] = IN("yg_s", [NT, D]); t["xn2"] = IN("xn2_s", [NT, D])
    t["xn"] = IN("xn_s", [NT, D])
    t["qT"] = IN("qT_s", [D, NT], BF16); t["kT"] = IN("kT_s", [D, NT], BF16)
    t["vh"] = IN("vh_s", [H, NBLK, 128, 64], BF16)
    t["ao2"] = IN("ao2_s", [NT, D], BF16)
    t["x1"] = IN("x1_s", [NT, D])
    t["x1t"] = IN("x1t_s", [NGRP, 8, 128, GRP], BF16)
    t["combt"] = IN("combt_s", [128, NBLK, 8])
    t["yacc"] = IN("yacc_s", [NT, D])
    t["out"] = nc.dram_tensor("out", [NT, D], F32, kind="ExternalOutput").ap()

    es = ExitStack()
    k = Kb(nc, es)
    jobs = []
    for srcnm, dstnm in (("l0_wqT", "l0_wqT_bf"), ("l0_wvT", "l0_wvT_bf"),
                         ("l1a_wqT", "l1a_wqT_bf"), ("l1a_wkT", "l1a_wkT_bf"),
                         ("l1a_wvT", "l1a_wvT_bf"), ("l1f_wq", "l1f_wq_bf"),
                         ("l1f_wk", "l1f_wk_bf"), ("l1f_wv", "l1f_wv_bf"),
                         ("l1f_wo", "l1f_wo_bf")):
        for c in range(8):
            for hh in range(2):
                jobs.append((t[srcnm][c * 128:(c + 1) * 128, hh * 512:(hh + 1) * 512],
                             t[dstnm][c, hh]))
    for c in range(8):
        jobs.append((t["wgw"][c * 128:(c + 1) * 128, :], t["wg_bf"][c]))
    for e in range(E):
        for c in range(8):
            for hh in range(2):
                jobs.append((t["w1"][e, c * 128:(c + 1) * 128, hh * 2048:(hh + 1) * 2048],
                             t["w1bf"][e, c][:, hh * 2048:(hh + 1) * 2048]))
        for f in range(FT):
            jobs.append((t["w2"][e, f * 128:(f + 1) * 128, :], t["w2bf"][e, f]))
    emit_cast_pass(k, jobs)
    _barrier(k, recycle=True)
    emit_l0_quant(k, t, "wfcT", "wfcq", D, FF, tag="qa", scr=t["qscr_a"])
    _barrier(k)
    emit_l0_quant(k, t, "wprojT", "wprojq", FF, D, tag="qb", scr=t["qscr_b"])
    _barrier(k, recycle=True)
    emit_l0_attn(k, dict(t, wqT_bf=t["l0_wqT_bf"], wvT_bf=t["l0_wvT_bf"]))
    _barrier(k, recycle=True)
    emit_l0_ffn(k, t)
    _barrier(k, recycle=True)
    emit_l1a_qkv(k, dict(t, wqT_bf=t["l1a_wqT_bf"], wkT_bf=t["l1a_wkT_bf"],
                         wvT_bf=t["l1a_wvT_bf"]))
    _barrier(k, recycle=True)
    emit_attention(k, dict(t, ao=t["ao1"]), "l1a", 1.0, "a3")
    _barrier(k, recycle=True)
    emit_l1a_norm(k, dict(t, ao=t["ao1"]))
    _barrier(k, recycle=True)
    emit_l1f_qkv(k, dict(t, wq_bf=t["l1f_wq_bf"], wk_bf=t["l1f_wk_bf"],
                         wv_bf=t["l1f_wv_bf"]))
    _barrier(k, recycle=True)
    emit_attention(k, dict(t, ao=t["ao2"]), "l1f", 0.125, "a4")
    _barrier(k, recycle=True)
    emit_l1f_x1gate(k, dict(t, ao=t["ao2"], wo_bf=t["l1f_wo_bf"]))
    _barrier(k, recycle=True)
    emit_moe(k, t)
    _barrier(k, recycle=True)
    emit_moe_final(k, t)
    es.close()
    return nc


def _declare_common(nc, t):
    EI = lambda n, sh, dt=F32: nc.dram_tensor(n, sh, dt, kind="ExternalInput").ap()
    t["ident"] = EI("ident", [128, 128], BF16)
    return EI


def build_p1():
    nc = bass.Bass("TRN2", target_bir_lowering=False)
    t = {}
    EI = _declare_common(nc, t)
    IN = lambda n, sh, dt=F32: nc.dram_tensor(n, sh, dt, kind="Internal").ap()
    t["x0"] = EI("x0", [NT, D])
    t["l0_wqT"] = EI("l0_wqT", [D, D]); t["l0_wvT"] = EI("l0_wvT", [D, D])
    t["qnw_rep"] = EI("qnw_rep", [128, D]); t["vnw_rep"] = EI("vnw_rep", [128, D])
    t["triT"] = EI("triT", [128, 128])
    t["wfcT"] = EI("wfcT", [D, FF]); t["wprojT"] = EI("wprojT", [FF, D])
    t["lnw_rep"] = EI("lnw_rep", [128, FF]); t["lnb_rep"] = EI("lnb_rep", [128, FF])
    t["wqT_bf"] = IN("wqT_bf_s", [8, 2, 128, 512], BF16)
    t["wvT_bf"] = IN("wvT_bf_s", [8, 2, 128, 512], BF16)
    t["wfcq"] = IN("wfcq_s", [8, 128, FF], BF16)
    t["wprojq"] = IN("wprojq_s", [FT, 128, D], BF16)
    t["qscr_a"] = IN("qscr_a", [1, 256]); t["qscr_b"] = IN("qscr_b", [1, 256])
    t["s1"] = IN("s1_s", [NT, D])
    t["s2"] = nc.dram_tensor("s2", [NT, D], F32, kind="ExternalOutput").ap()
    es = ExitStack()
    k = Kb(nc, es)
    jobs = []
    for nm in ("wqT", "wvT"):
        for c in range(8):
            for hh in range(2):
                jobs.append((t["l0_" + nm][c * 128:(c + 1) * 128, hh * 512:(hh + 1) * 512],
                             t[nm + "_bf"][c, hh]))
    emit_cast_pass(k, jobs)
    _barrier(k)
    emit_l0_quant(k, t, "wfcT", "wfcq", D, FF, tag="qa", scr=t["qscr_a"])
    _barrier(k)
    emit_l0_quant(k, t, "wprojT", "wprojq", FF, D, tag="qb", scr=t["qscr_b"])
    _barrier(k)
    emit_l0_attn(k, t)
    _barrier(k)
    emit_l0_ffn(k, t)
    es.close()
    return nc


def build_p2():
    nc = bass.Bass("TRN2", target_bir_lowering=False)
    t = {}
    EI = _declare_common(nc, t)
    IN = lambda n, sh, dt=F32: nc.dram_tensor(n, sh, dt, kind="Internal").ap()
    t["s2"] = EI("s2", [NT, D])
    t["l1a_wqT"] = EI("l1a_wqT", [D, D]); t["l1a_wkT"] = EI("l1a_wkT", [D, D])
    t["l1a_wvT"] = EI("l1a_wvT", [D, D])
    for nm in ("wqT", "wkT", "wvT"):
        t[nm + "_bf"] = IN(nm + "_bf_s", [8, 2, 128, 512], BF16)
    t["q"] = IN("q_s", [NT, D], BF16); t["k"] = IN("k_s", [NT, D], BF16)
    t["v"] = IN("v_s", [NT, D], BF16)
    t["ao"] = IN("ao_s", [NT, D], BF16)
    t["yg"] = IN("yg_s", [NT, D])
    t["xn2"] = IN("xn2_s", [NT, D])
    t["gscr"] = IN("gscr", [1, 512])
    t["s3"] = nc.dram_tensor("s3", [NT, D], F32, kind="ExternalOutput").ap()
    es = ExitStack()
    k = Kb(nc, es)
    jobs = []
    for nm in ("wqT", "wkT", "wvT"):
        for c in range(8):
            for hh in range(2):
                jobs.append((t["l1a_" + nm][c * 128:(c + 1) * 128, hh * 512:(hh + 1) * 512],
                             t[nm + "_bf"][c, hh]))
    emit_cast_pass(k, jobs)
    _barrier(k)
    emit_l1a_qkv(k, t)
    _barrier(k)
    emit_attention(k, t, "l1a", 1.0, "a3")
    _barrier(k)
    emit_l1a_norm(k, t)
    es.close()
    return nc


def build_p3():
    nc = bass.Bass("TRN2", target_bir_lowering=False)
    t = {}
    EI = _declare_common(nc, t)
    IN = lambda n, sh, dt=F32: nc.dram_tensor(n, sh, dt, kind="Internal").ap()
    t["s3"] = EI("s3", [NT, D])
    for nm in ("wq", "wk", "wv", "wo"):
        t["l1f_" + nm] = EI("l1f_" + nm, [D, D])
    t["biases"] = EI("biases", [1, 4096])
    t["wgw"] = EI("wgw", [D, 8]); t["bg"] = EI("bg", [1, 8])
    t["w1"] = EI("w1", [E, D, FF]); t["w2"] = EI("w2", [E, FF, D])
    t["b1t"] = EI("b1t", [128, FT, E]); t["b2"] = EI("b2", [8, D])
    t["cos_rep"] = EI("cos_rep", [NT, D]); t["sinS_rep"] = EI("sinS_rep", [NT, D])
    for nm in ("g1_rep", "be1_rep", "g2_rep", "be2_rep"):
        t[nm] = EI(nm, [128, D])
    for nm in ("wq", "wk", "wv", "wo"):
        t[nm + "_bf"] = IN(nm + "_bf_s", [8, 2, 128, 512], BF16)
    t["wg_bf"] = IN("wg_bf_s", [8, 128, 8], BF16)
    t["w1bf"] = IN("w1bf_s", [E, 8, 128, FF], BF16)
    t["w2bf"] = IN("w2bf_s", [E, FT, 128, D], BF16)
    t["xn"] = IN("xn_s", [NT, D])
    t["qT"] = IN("qT_s", [D, NT], BF16); t["kT"] = IN("kT_s", [D, NT], BF16)
    t["vh"] = IN("vh_s", [H, NBLK, 128, 64], BF16)
    t["ao"] = IN("ao_s", [NT, D], BF16)
    t["x1"] = IN("x1_s", [NT, D])
    t["x1t"] = IN("x1t_s", [NGRP, 8, 128, GRP], BF16)
    t["combt"] = IN("combt_s", [128, NBLK, 8])
    t["yacc"] = IN("yacc_s", [NT, D])
    t["wo_bf"] = t["wo_bf"]
    t["out"] = nc.dram_tensor("out", [NT, D], F32, kind="ExternalOutput").ap()
    es = ExitStack()
    k = Kb(nc, es)
    jobs = []
    for nm in ("wq", "wk", "wv", "wo"):
        for c in range(8):
            for hh in range(2):
                jobs.append((t["l1f_" + nm][c * 128:(c + 1) * 128, hh * 512:(hh + 1) * 512],
                             t[nm + "_bf"][c, hh]))
    for c in range(8):
        jobs.append((t["wgw"][c * 128:(c + 1) * 128, :], t["wg_bf"][c]))
    for e in range(E):
        for c in range(8):
            for hh in range(2):
                jobs.append((t["w1"][e, c * 128:(c + 1) * 128, hh * 2048:(hh + 1) * 2048],
                             t["w1bf"][e, c][:, hh * 2048:(hh + 1) * 2048]))
        for f in range(FT):
            jobs.append((t["w2"][e, f * 128:(f + 1) * 128, :], t["w2bf"][e, f]))
    emit_cast_pass(k, jobs)
    _barrier(k)
    emit_l1f_qkv(k, t)
    _barrier(k)
    emit_attention(k, t, "l1f", 0.125, "a4")
    _barrier(k)
    emit_l1f_x1gate(k, t)
    _barrier(k)
    emit_moe(k, t)
    _barrier(k)
    emit_moe_final(k, t)
    es.close()
    return nc


def _host_prep(inputs):
    import ml_dtypes
    P = inputs
    base = {}
    base["l0_wqT"] = np.ascontiguousarray(P["L0_attn_Wq"].T)
    base["l0_wvT"] = np.ascontiguousarray(P["L0_attn_Wv"].T)
    base["qnw_rep"] = np.ascontiguousarray(
        np.broadcast_to(P["L0_attn_q_norm_w"].reshape(-1), (128, D)))
    base["vnw_rep"] = np.ascontiguousarray(
        np.broadcast_to(P["L0_attn_v_norm_w"].reshape(-1), (128, D)))
    base["triT"] = np.triu(np.ones((128, 128), np.float32))
    base["ident"] = np.eye(128, dtype=ml_dtypes.bfloat16)
    base["wfcT"] = np.ascontiguousarray(P["L0_ffn_W_fc"].T)
    base["wprojT"] = np.ascontiguousarray(P["L0_ffn_W_proj"].T)
    base["lnw_rep"] = np.ascontiguousarray(np.broadcast_to(P["L0_ffn_ln_w"], (128, FF)))
    base["lnb_rep"] = np.ascontiguousarray(np.broadcast_to(P["L0_ffn_ln_b"], (128, FF)))
    base["l1a_wqT"] = np.ascontiguousarray(P["L1_attn_Wq"].T)
    base["l1a_wkT"] = np.ascontiguousarray(P["L1_attn_Wk"].T)
    base["l1a_wvT"] = np.ascontiguousarray(P["L1_attn_Wv"].T)
    base["l1f_wq"] = np.ascontiguousarray(P["L1_ffn_Wq"])
    base["l1f_wk"] = np.ascontiguousarray(P["L1_ffn_Wk"])
    base["l1f_wv"] = np.ascontiguousarray(P["L1_ffn_Wv"])
    base["l1f_wo"] = np.ascontiguousarray(P["L1_ffn_Wo"])
    base["biases"] = np.concatenate([P["L1_ffn_bq"], P["L1_ffn_bk"], P["L1_ffn_bv"],
                                     P["L1_ffn_bo"]]).astype(np.float32)[None, :]
    base["wgw"] = np.ascontiguousarray(P["L1_ffn_Wg"])
    base["bg"] = P["L1_ffn_bg"].astype(np.float32)[None, :]
    base["w1"] = np.ascontiguousarray(P["L1_ffn_W1"])
    base["w2"] = np.ascontiguousarray(P["L1_ffn_W2"])
    base["b1t"] = np.ascontiguousarray(
        P["L1_ffn_b1"].reshape(E, FT, 128).transpose(2, 1, 0))
    base["b2"] = np.ascontiguousarray(P["L1_ffn_b2"])
    hd = 64
    inv = (1.0 / 10000.0 ** (np.arange(0, hd, 2, dtype=np.float32) / hd)).astype(np.float32)
    tt_ = np.arange(NT, dtype=np.float32)[:, None] * inv[None, :]
    emb = np.concatenate([tt_, tt_], -1).astype(np.float32)
    cos = np.cos(emb).astype(np.float32)
    sin = np.sin(emb).astype(np.float32)
    sinS = sin.copy(); sinS[:, :32] *= -1
    base["cos_rep"] = np.ascontiguousarray(np.tile(cos, (1, H)))
    base["sinS_rep"] = np.ascontiguousarray(np.tile(sinS, (1, H)))
    base["g1_rep"] = np.ascontiguousarray(np.broadcast_to(P["L1_ffn_g1"], (128, D)))
    base["be1_rep"] = np.ascontiguousarray(np.broadcast_to(P["L1_ffn_be1"], (128, D)))
    base["g2_rep"] = np.ascontiguousarray(np.broadcast_to(P["L1_ffn_g2"], (128, D)))
    base["be2_rep"] = np.ascontiguousarray(np.broadcast_to(P["L1_ffn_be2"], (128, D)))
    return base


FUSED = True


def kernel(**inputs):
    base = _host_prep(inputs)
    x = np.ascontiguousarray(inputs["L0_attn_X"])
    cores = list(range(8))
    if FUSED:
        nc = build_fused()
        keys = ("ident", "l0_wqT", "l0_wvT", "qnw_rep", "vnw_rep", "triT",
                "wfcT", "wprojT", "lnw_rep", "lnb_rep", "l1a_wqT", "l1a_wkT",
                "l1a_wvT", "l1f_wq", "l1f_wk", "l1f_wv", "l1f_wo", "biases",
                "wgw", "bg", "w1", "w2", "b1t", "b2", "cos_rep", "sinS_rep",
                "g1_rep", "be1_rep", "g2_rep", "be2_rep")
        maps = [dict({kk: base[kk] for kk in keys}, x0=np.ascontiguousarray(x[c]))
                for c in cores]
        res = run_bass_kernel_spmd(nc, maps, core_ids=cores)
        out = np.stack([np.asarray(r["out"]) for r in res.results], axis=0)
        return out.astype(np.float32)
    keys1 = ("ident", "l0_wqT", "l0_wvT", "qnw_rep", "vnw_rep", "triT",
             "wfcT", "wprojT", "lnw_rep", "lnb_rep")
    nc1 = build_p1()
    maps = [dict({kk: base[kk] for kk in keys1}, x0=np.ascontiguousarray(x[c]))
            for c in cores]
    r1 = run_bass_kernel_spmd(nc1, maps, core_ids=cores)
    s2 = [np.asarray(r["s2"]) for r in r1.results]
    keys2 = ("ident", "l1a_wqT", "l1a_wkT", "l1a_wvT")
    nc2 = build_p2()
    maps = [dict({kk: base[kk] for kk in keys2}, s2=s2[c]) for c in cores]
    r2 = run_bass_kernel_spmd(nc2, maps, core_ids=cores)
    s3 = [np.asarray(r["s3"]) for r in r2.results]
    keys3 = ("ident", "l1f_wq", "l1f_wk", "l1f_wv", "l1f_wo", "biases", "wgw",
             "bg", "w1", "w2", "b1t", "b2", "cos_rep", "sinS_rep",
             "g1_rep", "be1_rep", "g2_rep", "be2_rep")
    nc3 = build_p3()
    maps = [dict({kk: base[kk] for kk in keys3}, s3=s3[c]) for c in cores]
    r3 = run_bass_kernel_spmd(nc3, maps, core_ids=cores)
    out = np.stack([np.asarray(r["out"]) for r in r3.results], axis=0)
    return out.astype(np.float32)


# version 08
# speedup vs baseline: 1.4421x; 1.0019x over previous
import numpy as np
from contextlib import ExitStack
import concourse.bass as bass
import concourse.mybir as mybir
from concourse.bass_utils import run_bass_kernel_spmd
from concourse.alu_op_type import AluOpType as OP

F32 = mybir.dt.float32
BF16 = mybir.dt.bfloat16
AF = mybir.ActivationFunctionType
AX = mybir.AxisListType

NT = 4096
D = 1024
NBLK = 32
H = 16
HD = 64
FF = 4096
E = 8
GRP = 512
NGRP = NT // GRP
FT = FF // 128


class Sems:

    def __init__(self, nc, es):
        self.nc, self.es = nc, es
        self.h = {}
        self.v = {}
        self.pool = []

    def get(self, name):
        if name not in self.h:
            if self.pool:
                handle, val = self.pool.pop()
            else:
                handle = self.nc.alloc_semaphore(name=name, num=None)
                val = 0
            self.h[name] = handle
            self.v[name] = val
        return self.h[name]

    def inc(self, inst, name, n=1):
        h = self.get(name)
        inst.then_inc(h, n)
        self.v[name] += n
        return self.v[name]

    def val(self, name):
        self.get(name)
        return self.v[name]

    def wait(self, eng, name, val):
        if val is None or val <= 0:
            return
        eng.wait_ge(self.get(name), val)


class Kb:

    def __init__(self, nc, es):
        self.nc = nc
        self.es = es
        self.scopes = []
        self.s = Sems(nc, es)
        self.pb = [es.enter_context(nc.psum_tensor(f"pb{i}", [128, 512], F32))
                   for i in range(6)]
        self.pt = es.enter_context(nc.psum_tensor("pt", [128, 1024], BF16))
        self.pg = es.enter_context(nc.psum_tensor("pg", [128, 512], F32))

    def sb(self, name, shape, dt):
        es = self.scopes[-1] if self.scopes else self.es
        return es.enter_context(self.nc.sbuf_tensor(name, shape, dt))

    def push_scope(self):
        self.scopes.append(ExitStack())

    def pop_scope(self):
        self.scopes.pop().close()


def emit_moe(k: Kb, t, n_experts=E):
    k.push_scope()
    nc, s = k.nc, k.s

    w1sb = [k.sb(f"moe_w1sb{i}", [128, 8, FF], BF16) for i in range(1)]
    w2sb = [k.sb(f"moe_w2sb{i}", [128, FT, D], BF16) for i in range(1)]
    x1g = [k.sb(f"moe_x1g{i}", [128, 8, GRP], BF16) for i in range(2)]
    hsT = k.sb("moe_hsT", [128, FT, GRP], BF16)
    ysb = [k.sb(f"moe_ysb{i}", [128, D], F32) for i in range(2)]
    yin = [k.sb(f"moe_yin{i}", [128, D], F32) for i in range(2)]
    b1c = k.sb("moe_b1c", [128, FT, E], F32)
    comb = k.sb("moe_comb", [128, NBLK, E], F32)

    s.inc(nc.scalar.dma_start(b1c[:], t["b1t"]), "moe_ld1", 16)
    s.inc(nc.scalar.dma_start(comb[:], t["combt"]), "moe_ld1", 16)

    w1sb_free = 0
    w2sb_free = 0

    def emit_w_load(e):
        s.wait(nc.sync, "mm", w1sb_free)
        ld = nc.sync.dma_start(w1sb[0][:], t["w1bf"][e].rearrange("c p f -> p c f"))
        r1 = s.inc(ld, "moe_w1", 16)
        s.wait(nc.sync, "mm", w2sb_free)
        ld = nc.sync.dma_start(w2sb[0][:], t["w2bf"][e].rearrange("f p d -> p f d"))
        r2 = s.inc(ld, "moe_w2", 16)
        return r1, r2

    w1_ready, w2_ready = emit_w_load(0)

    hbank_free = [0, 0]
    x1_free = [0, 0]
    x1_ready = [0, 0]
    ypair_free = [0, 0]
    ysb_free = [0, 0]
    first_use = True

    for e in range(n_experts):
        for g in range(NGRP):
            gi = g % 2
            xb = x1g[gi]
            s.wait(nc.scalar, "mm", x1_free[gi])
            ld = nc.scalar.dma_start(xb[:], t["x1t"][g].rearrange("c p t -> p c t"))
            x1_ready[gi] = s.inc(ld, f"moe_x{gi}", 16)

            s.wait(nc.tensor, "moe_w1", w1_ready)
            s.wait(nc.tensor, f"moe_x{gi}", x1_ready[gi])
            if first_use:
                s.wait(nc.tensor, "moe_ld1", 32)
                s.wait(nc.scalar, "moe_ld1", 32)
                first_use = False
            gelu_last = 0
            for f in range(FT):
                half = f % 2
                hp = k.pb[half][:]
                s.wait(nc.tensor, "moe_sc", hbank_free[half])
                for c in range(8):
                    mm = nc.tensor.matmul(hp, w1sb[0][:, c, bass.ts(f, 128)],
                                          xb[:, c, :],
                                          start=(c == 0), stop=(c == 7))
                mv = s.inc(mm, "mm", 1)
                s.wait(nc.scalar, "mm", mv)
                ge = nc.scalar.activation(hsT[:, f, :], hp, AF.Gelu,
                                          bias=b1c[:, f, e:e + 1], scale=1.0)
                hbank_free[half] = s.inc(ge, "moe_sc", 1)
            gelu_last = s.inc(nc.scalar.drain(), "moe_sc", 1)
            x1_free[gi] = s.val("mm")

            s.wait(nc.tensor, "moe_w2", w2_ready)
            for sub in range(4):
                blk = g * 4 + sub
                pi = sub % 2
                pair = (k.pb[2], k.pb[3]) if pi == 0 else (k.pb[4], k.pb[5])
                s.wait(nc.tensor, "moe_sc", gelu_last)
                s.wait(nc.tensor, "moe_v", ypair_free[pi])
                if e > 0:
                    s.wait(nc.gpsimd, f"moe_st{pi}", ysb_free[pi])
                    ldy = nc.gpsimd.dma_start(yin[pi][:], t["yacc"][bass.ts(blk, 128), :])
                    yin_rdy = s.inc(ldy, f"moe_yld{pi}", 16)
                for f in range(FT):
                    for dh in range(2):
                        mm = nc.tensor.matmul(
                            pair[dh][:],
                            hsT[:, f, bass.ts(sub, 128)],
                            w2sb[0][:, f, bass.ts(dh, 512)],
                            start=(f == 0), stop=(f == FT - 1))
                mv = s.inc(mm, "mm", 1)
                s.wait(nc.vector, "mm", mv)
                s.wait(nc.vector, f"moe_st{pi}", ysb_free[pi])
                if e == 0:
                    for dh in range(2):
                        nc.vector.tensor_scalar(ysb[pi][:, bass.ts(dh, 512)], pair[dh][:],
                                                comb[:, blk, e:e + 1], None, OP.mult)
                else:
                    s.wait(nc.vector, f"moe_yld{pi}", yin_rdy)
                    for dh in range(2):
                        nc.vector.scalar_tensor_tensor(ysb[pi][:, bass.ts(dh, 512)], pair[dh][:],
                                                       comb[:, blk, e:e + 1],
                                                       yin[pi][:, bass.ts(dh, 512)],
                                                       OP.mult, OP.add)
                ypair_free[pi] = s.inc(nc.vector.drain(), "moe_v", 1)
                s.wait(nc.gpsimd, "moe_v", ypair_free[pi])
                sd = nc.gpsimd.dma_start(t["yacc"][bass.ts(blk, 128), :], ysb[pi][:])
                ysb_free[pi] = s.inc(sd, f"moe_st{pi}", 16)
        if e + 1 < n_experts:
            w1sb_free = s.val("mm")
            w2sb_free = s.val("mm")
            w1_ready, w2_ready = emit_w_load(e + 1)
    for pi in range(2):
        s.wait(nc.gpsimd, f"moe_st{pi}", s.val(f"moe_st{pi}"))
    k.pop_scope()



def emit_cast_pass(k: Kb, jobs, eng=None, tag="cst"):
    k.push_scope()
    nc, s = k.nc, k.s
    eng = eng or nc.vector
    NBUF = 3
    fin = [k.sb(f"{tag}_fin{i}", [128, 2048], F32) for i in range(NBUF)]
    bout = [k.sb(f"{tag}_bout{i}", [128, 2048], BF16) for i in range(NBUF)]
    i = 0
    cast_done = [0] * NBUF
    out_done = [0] * NBUF
    for srcap, dst in jobs:
        R, C = srcap.shape
        assert R % 128 == 0 and C <= 2048
        for r in range(R // 128):
            j = i % NBUF
            s.wait(nc.sync, tag + "_c", cast_done[j])
            ld = nc.sync.dma_start(fin[j][:, 0:C], srcap[bass.ts(r, 128), :])
            lv = s.inc(ld, f"{tag}_l{j}", 16)
            s.wait(eng, f"{tag}_l{j}", lv)
            s.wait(eng, f"{tag}_o{j}", out_done[j])
            eng.tensor_copy(bout[j][:, 0:C], fin[j][:, 0:C])
            cast_done[j] = s.inc(eng.drain(), tag + "_c", 1)
            s.wait(nc.gpsimd, tag + "_c", cast_done[j])
            st = nc.gpsimd.dma_start(dst[bass.ts(r, 128), :], bout[j][:, 0:C])
            out_done[j] = s.inc(st, f"{tag}_o{j}", 16)
            i += 1
    for j in range(NBUF):
        s.wait(nc.gpsimd, f"{tag}_o{j}", s.val(f"{tag}_o{j}"))
    k.pop_scope()


def emit_cast_stream(k: Kb, jobs, tag="cs"):
    k.push_scope()
    nc, s = k.nc, k.s
    NBUF = 3
    CW = 1024
    fin = [k.sb(f"{tag}_fin{i}", [128, CW], F32) for i in range(NBUF)]
    bout = [k.sb(f"{tag}_bout{i}", [128, CW], BF16) for i in range(NBUF)]
    conv_done = [0] * NBUF
    st_done = [0] * NBUF
    i = 0
    for srcap, dst in jobs:
        R, C = srcap.shape
        assert R % 128 == 0
        for r in range(R // 128):
            for c0 in range(0, C, CW):
                cw = min(CW, C - c0)
                j = i % NBUF
                s.wait(nc.sync, f"{tag}_c{j}", conv_done[j])
                ld = nc.sync.dma_start(fin[j][:, 0:cw],
                                       srcap[bass.ts(r, 128), c0:c0 + cw])
                lv = s.inc(ld, f"{tag}_l{j}", 16)
                s.wait(nc.gpsimd, f"{tag}_l{j}", lv)
                s.wait(nc.gpsimd, f"{tag}_o{j}", st_done[j])
                nc.gpsimd.tensor_copy(bout[j][:, 0:cw], fin[j][:, 0:cw])
                conv_done[j] = s.inc(nc.gpsimd.drain(), f"{tag}_c{j}", 1)
                st = nc.gpsimd.dma_start(dst[bass.ts(r, 128), c0:c0 + cw],
                                         bout[j][:, 0:cw])
                st_done[j] = s.inc(st, f"{tag}_o{j}", 16)
                i += 1


def _ln_plain(k, eng_small, xin, xout, tmp, eps=1e-5, w_rep=None, b_rep=None,
              out2=None):
    nc, s = k.nc, k.s
    bn, mv, t1, t2 = tmp["bn"], tmp["mv"], tmp["t1"], tmp["t2"]
    nc.vector.bn_stats(bn[:, 0:6], xin[:, 0:512])
    v = s.inc(nc.vector.bn_stats(bn[:, 6:12], xin[:, 512:1024]), "ln", 1)
    s.wait(nc.vector, "ln", v)
    v = s.inc(nc.vector.bn_aggr(mv[:], bn[:]), "ln", 1)
    s.wait(nc.vector, "ln", v)
    v = s.inc(nc.vector.tensor_scalar(t1[:], mv[:, 1:2], eps, None, OP.add), "ln", 1)
    s.wait(nc.scalar, "ln", v)
    v = s.inc(nc.scalar.activation(t1[:], t1[:], AF.Sqrt), "ln", 1)
    s.wait(nc.vector, "ln", v)
    v = s.inc(nc.vector.reciprocal(t2[:], t1[:]), "ln", 1)
    s.wait(nc.vector, "ln", v)
    nc.vector.tensor_scalar(xout[:], xin[:], mv[:, 0:1], t2[:], OP.subtract, OP.mult)
    if w_rep is not None:
        nc.vector.tensor_tensor(xout[:], xout[:], w_rep[:], OP.mult)
    if b_rep is not None:
        nc.vector.tensor_tensor(out2 if out2 is not None else xout[:],
                                xout[:], b_rep[:], OP.add)


def emit_l1f_qkv(k: Kb, t):
    k.push_scope()
    nc, s = k.nc, k.s
    wq = k.sb("p4_wq", [128, 8, 2, 512], BF16)
    wk = k.sb("p4_wk", [128, 8, 2, 512], BF16)
    wv = k.sb("p4_wv", [128, 8, 2, 512], BF16)
    bias_bf = k.sb("p4_biasbf", [1, 4096], BF16)
    bias_f = k.sb("p4_biasf", [1, 4096], F32)
    onesr = k.sb("p4_ones", [1, 128], BF16)
    ident = k.sb("p4_ident", [128, 128], BF16)
    xs = k.sb("p4_xs", [128, D], F32)
    xn = k.sb("p4_xn", [128, D], F32)
    xnb = k.sb("p4_xnb", [128, D], BF16)
    xnT = k.sb("p4_xnT", [128, 8, 512], BF16)
    ctile = k.sb("p4_cos", [128, D], F32)
    stile = k.sb("p4_sin", [128, D], F32)
    qsb = k.sb("p4_qsb", [128, D], F32)
    qrot = k.sb("p4_qrot", [128, D], F32)
    qe = k.sb("p4_qe", [128, D], F32)
    qb2 = k.sb("p4_qb2", [128, D], BF16)
    qtev = k.sb("p4_qtev", [128, D], BF16)
    vsb = k.sb("p4_vsb", [128, D], BF16)
    tmp = {"bn": k.sb("p4_bn", [128, 12], F32), "mv": k.sb("p4_mv", [128, 2], F32),
           "t1": k.sb("p4_t1", [128, 1], F32), "t2": k.sb("p4_t2", [128, 1], F32)}

    for nm, w in (("wq_bf", wq), ("wk_bf", wk), ("wv_bf", wv)):
        s.inc(nc.sync.dma_start(w[:], t[nm].rearrange("c h p n -> p c h n")), "p4w", 16)
    s.inc(nc.sync.dma_start(bias_f[:], t["biases"]), "p4w", 16)
    s.inc(nc.sync.dma_start(ident[:], t["ident"]), "p4w", 16)
    s.wait(nc.vector, "p4w", s.val("p4w"))
    nc.vector.memset(onesr[:], 1.0)
    nc.vector.tensor_copy(bias_bf[:], bias_f[:])
    wrdy = s.inc(nc.vector.drain(), "p4v", 1)
    s.wait(nc.tensor, "p4v", wrdy)

    pb = k.pb
    xs_free = 0
    rope_free = 0

    def selfwait(eng=nc.vector, name="p4sw"):
        v = s.inc(eng.nop(), name, 1)
        s.wait(eng, name, v)

    for g in range(NGRP):
        for bi in range(4):
            blk = g * 4 + bi
            s.wait(nc.scalar, "p4v", xs_free)
            lv = s.inc(nc.scalar.dma_start(xs[:], t["s3"][bass.ts(blk, 128), :]), "p4xs", 16)
            s.wait(nc.vector, "p4xs", lv)
            s.wait(nc.vector, "p4t", s.val("p4t"))
            s.wait(nc.vector, "p4xo", s.val("p4xo"))
            _ln_plain(k, None, xs, xn, tmp)
            selfwait()
            nc.vector.tensor_copy(xnb[:], xn[:])
            xv = xs_free = s.inc(nc.vector.drain(), "p4v", 1)
            s.wait(nc.scalar, "p4v", xv)
            s.inc(nc.scalar.dma_start(t["xn"][bass.ts(blk, 128), :], xn[:]), "p4xo", 16)
            s.wait(nc.tensor, "p4v", xv)
            s.wait(nc.tensor, "p4tv", s.val("p4tv"))
            for c in range(8):
                mm = nc.tensor.transpose(k.pt[:, bass.ts(c, 128)],
                                         xnb[:, bass.ts(c, 128)], ident[:])
            tv = s.inc(mm, "p4t", 1)
            tv = s.inc(nc.tensor.nop(), "p4t", 1)
            s.wait(nc.vector, "p4t", tv)
            if bi == 0:
                s.wait(nc.vector, "mm", s.val("mm"))
            nc.vector.tensor_copy(xnT[:, :, bi * 128:(bi + 1) * 128],
                                  k.pt[:].rearrange("p (c n) -> p c n", c=8))
            s.inc(nc.vector.drain(), "p4tv", 1)
        xnT_rdy = s.val("p4tv")
        s.wait(nc.tensor, "p4tv", xnT_rdy)

        for bi in range(4):
            blk = g * 4 + bi
            s.wait(nc.tensor, "p4e", s.val("p4e"))
            for half in range(2):
                hp = pb[half][:]
                for c in range(8):
                    nc.tensor.matmul(hp, xnT[:, c, bass.ts(bi, 128)],
                                     wv[:, c, half, :], start=(c == 0), stop=False)
                mm = nc.tensor.matmul(hp, onesr[:],
                                      bias_bf[0:1, 2048 + half * 512:2048 + half * 512 + 512],
                                      start=False, stop=True)
            mv_ = s.inc(mm, "mm", 1)
            s.wait(nc.vector, "mm", mv_)
            s.wait(nc.vector, "p4vo", s.val("p4vo"))
            nc.vector.tensor_copy(vsb[:, 0:512], pb[0][:])
            nc.vector.tensor_copy(vsb[:, 512:1024], pb[1][:])
            ev = s.inc(nc.vector.drain(), "p4e", 1)
            s.wait(nc.gpsimd, "p4e", ev)
            sd = nc.gpsimd.dma_start(
                t["vh"][:, blk].rearrange("h p c -> p h c"),
                vsb.rearrange("p (h c) -> p h c", h=H))
            s.inc(sd, "p4vo", 16)

        for w, boff, dstname in ((wq, 0, "qT"), (wk, 1024, "kT")):
            for bi in range(4):
                blk = g * 4 + bi
                s.wait(nc.scalar, "p4v", rope_free)
                s.inc(nc.scalar.dma_start(ctile[:], t["cos_rep"][bass.ts(blk, 128), :]),
                      "p4xc", 16)
                lv2 = s.inc(nc.scalar.dma_start(stile[:], t["sinS_rep"][bass.ts(blk, 128), :]),
                            "p4xd", 16)
                s.wait(nc.tensor, "p4e", s.val("p4e"))
                for half in range(2):
                    hp = pb[half][:]
                    for c in range(8):
                        nc.tensor.matmul(hp, xnT[:, c, bass.ts(bi, 128)],
                                         w[:, c, half, :], start=(c == 0), stop=False)
                    mm = nc.tensor.matmul(hp, onesr[:],
                                          bias_bf[0:1, boff + half * 512:boff + half * 512 + 512],
                                          start=False, stop=True)
                mv_ = s.inc(mm, "mm", 1)
                s.wait(nc.vector, "mm", mv_)
                s.wait(nc.vector, "p4xc", s.val("p4xc"))
                s.wait(nc.vector, "p4xd", lv2)
                nc.vector.tensor_copy(qsb[:, 0:512], pb[0][:])
                nc.vector.tensor_copy(qsb[:, 512:1024], pb[1][:])
                ev = s.inc(nc.vector.drain(), "p4e", 1)
                selfwait()
                qv = qsb.rearrange("p (h s c) -> p h s c", h=H, s=2)
                rv = qrot.rearrange("p (h s c) -> p h s c", h=H, s=2)
                nc.vector.tensor_copy(rv[:, :, 0, :], qv[:, :, 1, :])
                selfwait()
                nc.vector.tensor_copy(rv[:, :, 1, :], qv[:, :, 0, :])
                selfwait()
                nc.vector.tensor_tensor(qe[:], qsb[:], ctile[:], OP.mult)
                selfwait()
                nc.vector.tensor_tensor(qrot[:], qrot[:], stile[:], OP.mult)
                selfwait()
                nc.vector.tensor_tensor(qb2[:], qe[:], qrot[:], OP.add)
                rpv = rope_free = s.inc(nc.vector.drain(), "p4v", 1)
                s.wait(nc.tensor, "p4v", rpv)
                s.wait(nc.tensor, "p4tv", s.val("p4tv"))
                for c in range(8):
                    mm = nc.tensor.transpose(k.pt[:, bass.ts(c, 128)],
                                             qb2[:, bass.ts(c, 128)], ident[:])
                tv = s.inc(mm, "p4t", 1)
                tv = s.inc(nc.tensor.nop(), "p4t", 1)
                s.wait(nc.vector, "p4t", tv)
                s.wait(nc.vector, "p4qo", s.val("p4qo"))
                nc.vector.tensor_copy(qtev[:], k.pt[:])
                qtv = s.inc(nc.vector.drain(), "p4tv", 1)
                s.wait(nc.gpsimd, "p4tv", qtv)
                sd = nc.gpsimd.dma_start(
                    t[dstname].rearrange("(c p) n -> p c n", p=128)[:, :, bass.ts(blk, 128)],
                    qtev.rearrange("p (c n) -> p c n", c=8))
                s.inc(sd, "p4qo", 16)
    s.wait(nc.gpsimd, "p4qo", s.val("p4qo"))
    s.wait(nc.gpsimd, "p4vo", s.val("p4vo"))
    s.wait(nc.scalar, "p4xo", s.val("p4xo"))
    k.pop_scope()


def emit_attention(k: Kb, t, mode, scale, tag):
    k.push_scope()
    nc, s = k.nc, k.s
    qt = k.sb(tag + "_qt", [64, 2, NT], BF16)
    kt = k.sb(tag + "_kt", [64, 2, NT], BF16)
    vaug = [k.sb(f"{tag}_va{i}", [128, NBLK, 65], BF16) for i in range(2)]
    eb = [k.sb(f"{tag}_eb{i}", [128, 1024], BF16) for i in range(3)]
    osb = k.sb(tag + "_osb", [128, 1024], BF16)
    oout = k.sb(tag + "_oout", [128, 4, 64], BF16)
    rcol = k.sb(tag + "_rcol", [128, 1], F32)
    rcan = k.sb(tag + "_rcan", [128, 8], F32)
    ident = k.sb(tag + "_ident", [128, 128], BF16)
    qkld = k.sb(tag + "_qkld", [128, NBLK, 64], BF16)

    s.inc(nc.sync.dma_start(ident[:], t["ident"]), tag + "w", 16)
    s.wait(nc.vector, tag + "w", s.val(tag + "w"))
    nc.vector.memset(vaug[0][:, :, 64:65], 1.0)
    nc.vector.memset(vaug[1][:, :, 64:65], 1.0)
    ones_rdy = s.inc(nc.vector.drain(), tag + "v", 1)
    s.wait(nc.tensor, tag + "v", ones_rdy)

    if mode == "l1a":
        qflat = t["q"].rearrange("a (u c) -> (a u) c", c=64)
        kflat = t["k"].rearrange("a (u c) -> (a u) c", c=64)
        vflat = t["v"].rearrange("a (u c) -> (a u) c", c=64)
        aoflat = t["ao"].rearrange("a (u c) -> (a u) c", c=64)

    pair_lastmm = 0
    va_free = [0, 0]
    sco_free = [0, 0]
    oev = 0
    odr_cnt = 0

    for hp_i in range(8):
        if mode == "l1f":
            s.wait(nc.sync, "mm", pair_lastmm)
            ld = nc.sync.dma_start(qt[:], t["qT"][bass.ts(hp_i, 128), :]
                                   .rearrange("(g p) n -> p g n", g=2))
            s.inc(ld, tag + "q", 16)
            ld = nc.sync.dma_start(kt[:], t["kT"][bass.ts(hp_i, 128), :]
                                   .rearrange("(g p) n -> p g n", g=2))
            k_ld = s.inc(ld, tag + "q", 16)
            s.wait(nc.tensor, tag + "q", k_ld)
        else:
            s.wait(nc.tensor, tag + "pv", s.val(tag + "pv"))
            for gi in range(2):
                grp = 2 * hp_i + gi
                for which, flat, dst in (("q", qflat, qt), ("k", kflat, kt)):
                    s.wait(nc.sync, tag + "tv", s.val(tag + "tv"))
                    if gi == 0 and which == "q":
                        s.wait(nc.sync, "mm", pair_lastmm)
                    src = flat[grp * 4096:(grp + 1) * 4096, :]
                    ld = nc.sync.dma_start(qkld[:], src.rearrange("(b p) c -> p b c", p=128))
                    lv = s.inc(ld, tag + "q", 16)
                    s.wait(nc.tensor, tag + "q", lv)
                    for bt in range(4):
                        s.wait(nc.tensor, tag + "tv", s.val(tag + "tv"))
                        for ti in range(8):
                            pt_i = bt * 8 + ti
                            mm = nc.tensor.transpose(k.pt[0:64, bass.ts(ti, 128)],
                                                     qkld[:, pt_i, :], ident[:])
                        tv = s.inc(mm, tag + "t", 1)
                        tv = s.inc(nc.tensor.nop(), tag + "t", 1)
                        s.wait(nc.vector, tag + "t", tv)
                        if bt == 0 and which == "q" and gi == 0:
                            s.wait(nc.vector, "mm", pair_lastmm)
                        nc.vector.tensor_copy(
                            dst[0:64, gi, bt * 1024:(bt + 1) * 1024]
                               .rearrange("p (a n) -> p a n", a=8),
                            k.pt[0:64, :].rearrange("p (a n) -> p a n", a=8))
                        s.inc(nc.vector.drain(), tag + "tv", 1)
            s.wait(nc.tensor, tag + "tv", s.val(tag + "tv"))
        for gi in range(2):
            s.wait(nc.scalar, "mm", va_free[gi])
            if mode == "l1f":
                h = 2 * hp_i + gi
                ld = nc.scalar.dma_start(vaug[gi][:, :, 0:64],
                                         t["vh"][h].rearrange("b p c -> p b c"))
            else:
                grp = 2 * hp_i + gi
                src = vflat[grp * 4096:(grp + 1) * 4096, :]
                ld = nc.scalar.dma_start(vaug[gi][:, :, 0:64],
                                         src.rearrange("(b p) c -> p b c", p=128))
            va_free[gi] = s.inc(ld, tag + "va", 16)
        s.wait(nc.tensor, tag + "va", max(va_free))

        for qb in range(8):
            sco_free = [0, 0]
            exp_cnt = {}
            smv = {}

            def esc(kb):
                sp = kb % 2
                S = (k.pb[1], k.pb[0]) if sp == 0 else (k.pb[3], k.pb[2])
                s.wait(nc.tensor, tag + "e", sco_free[sp])
                nc.tensor.matmul(S[0][:], kt[0:64, 0, bass.ts(kb, 128)],
                                 qt[0:64, 0, bass.ts(qb, 512)], start=True, stop=True)
                mm = nc.tensor.matmul(S[1][:], kt[0:64, 1, bass.ts(kb, 128)],
                                      qt[0:64, 1, bass.ts(qb, 512)], start=True, stop=True)
                smv[kb] = s.inc(mm, "mm", 1)

            def eexp(kb, wv):
                sp = kb % 2
                S = (k.pb[1], k.pb[0]) if sp == 0 else (k.pb[3], k.pb[2])
                E = eb[kb % 3]
                s.wait(nc.scalar, "mm", wv)
                nc.scalar.activation(E[:, 0:512], S[0][:], AF.Exp, scale=scale)
                ex = nc.scalar.activation(E[:, 512:1024], S[1][:], AF.Exp, scale=scale)
                sco_free[sp] = exp_cnt[kb] = s.inc(ex, tag + "e", 1)

            def epv(kb, wv, last=False):
                s.wait(nc.tensor, tag + "e", wv)
                if kb == 0:
                    s.wait(nc.tensor, tag + "v", oev)
                E = eb[kb % 3]
                for gi in range(2):
                    mm = nc.tensor.matmul(k.pb[4 + gi][0:65, :], vaug[gi][:, kb, :],
                                          E[:, bass.ts(gi, 512)],
                                          start=(kb == 0), stop=last)
                return mm

            for kb in range(NBLK):
                esc(kb)
                if kb >= 1:
                    eexp(kb - 1, smv[kb - 1])
                if kb >= 2:
                    epv(kb - 2, exp_cnt[kb - 2])
            eexp(NBLK - 1, smv[NBLK - 1])
            epv(NBLK - 2, exp_cnt[NBLK - 2])
            epv(NBLK - 1, exp_cnt[NBLK - 1], last=True)
            fence = nc.tensor.matmul(k.pg[0:1, 100:101], vaug[0][:, 0, 0:1],
                                     vaug[0][:, 0, 0:1], start=True, stop=True)
            pv_v = s.inc(fence, "mm", 1)
            va_free = [pv_v, pv_v]
            pair_lastmm = pv_v

            s.wait(nc.vector, "mm", pv_v)
            s.wait(nc.vector, tag + "t", s.val(tag + "t"))
            nc.vector.tensor_copy(osb[64:128, 0:512], osb[64:128, 0:512])
            nc.vector.tensor_copy(osb[0:65, 0:512], k.pb[4][0:65, :])
            nc.vector.tensor_copy(osb[0:65, 512:1024], k.pb[5][0:65, :])
            nc.vector.drain()
            nc.vector.tensor_copy(rcan[:], osb[0:128, 1016:1024])
            oev = s.inc(nc.vector.drain(), tag + "v", 1)
            for gi in range(2):
                s.wait(nc.tensor, tag + "v", oev)
                s.wait(nc.tensor, tag + "pv", odr_cnt)
                for ti in range(4):
                    mm = nc.tensor.transpose(k.pt[:, ti * 68:ti * 68 + 65],
                                             osb[0:65, gi * 512 + ti * 128:gi * 512 + (ti + 1) * 128],
                                             ident[0:65, 0:65])
                ptv = s.inc(mm, tag + "t", 1)
                ptv = s.inc(nc.tensor.nop(), tag + "t", 1)
                s.wait(nc.vector, tag + "t", ptv)
                s.wait(nc.vector, tag + "o", s.val(tag + "o"))
                for ti in range(4):
                    tile = k.pt[:, ti * 68:ti * 68 + 65]
                    v1 = s.inc(nc.vector.reciprocal(rcol[:], tile[:, 64:65]), tag + "v", 1)
                    s.wait(nc.vector, tag + "v", v1)
                    nc.vector.tensor_scalar(oout[:, ti, :], tile[:, 0:64], rcol[:],
                                            None, OP.mult)
                odr_cnt = s.inc(nc.vector.drain(), tag + "pv", 1)
                s.wait(nc.gpsimd, tag + "pv", odr_cnt)
                if mode == "l1f":
                    h = 2 * hp_i + gi
                    dst = t["ao"][qb * 512:(qb + 1) * 512, 64 * h:64 * h + 64]
                else:
                    grp = 2 * hp_i + gi
                    dst = aoflat[grp * 4096 + qb * 512: grp * 4096 + (qb + 1) * 512, :]
                sd = nc.gpsimd.dma_start(dst.rearrange("(ti p) c -> p ti c", p=128), oout[:])
                s.inc(sd, tag + "o", 16)
    s.wait(nc.gpsimd, tag + "o", s.val(tag + "o"))
    k.pop_scope()


def emit_l1f_x1gate(k: Kb, t):
    k.push_scope()
    nc, s = k.nc, k.s
    wo = k.sb("p5_wo", [128, 8, 2, 512], BF16)
    wg = k.sb("p5_wg", [128, 8, 8], BF16)
    bias_f = k.sb("p5_biasf", [1, 4096], F32)
    bias_bf = k.sb("p5_biasbf", [1, 4096], BF16)
    bg_f = k.sb("p5_bgf", [1, 8], F32)
    bg_bf = k.sb("p5_bgbf", [1, 8], BF16)
    onesr = k.sb("p5_ones", [1, 128], BF16)
    ident = k.sb("p5_ident", [128, 128], BF16)
    g1r = k.sb("p5_g1r", [128, D], F32)
    be1r = k.sb("p5_be1r", [128, D], F32)
    aob = k.sb("p5_aob", [128, D], BF16)
    aoT = k.sb("p5_aoT", [128, 8, 128], BF16)
    xnb = k.sb("p5_xnb", [128, D], F32)
    y1 = k.sb("p5_y1", [128, D], F32)
    x1f = k.sb("p5_x1f", [128, D], F32)
    x1b = k.sb("p5_x1b", [128, D], BF16)
    x1T = k.sb("p5_x1T", [128, 8, 128], BF16)
    pe_ = k.sb("p5_pe", [128, 8], F32)
    m1 = k.sb("p5_m1", [128, 1], F32)
    nm1 = k.sb("p5_nm1", [128, 1], F32)
    pm = k.sb("p5_pm", [128, 8], F32)
    m2 = k.sb("p5_m2", [128, 1], F32)
    s12 = k.sb("p5_s12", [128, 1], F32)
    rs = k.sb("p5_rs", [128, 1], F32)
    mask1 = k.sb("p5_mask1", [128, 8], F32)
    mask2 = k.sb("p5_mask2", [128, 8], F32)
    combv = k.sb("p5_comb", [128, 8], F32)
    tmp = {"bn": k.sb("p5_bn", [128, 12], F32), "mv": k.sb("p5_mv", [128, 2], F32),
           "t1": k.sb("p5_t1", [128, 1], F32), "t2": k.sb("p5_t2", [128, 1], F32)}

    s.inc(nc.sync.dma_start(wo[:], t["wo_bf"].rearrange("c h p n -> p c h n")), "p5w", 16)
    s.inc(nc.sync.dma_start(wg[:], t["wg_bf"].rearrange("c p n -> p c n")), "p5w", 16)
    s.inc(nc.sync.dma_start(bias_f[:], t["biases"]), "p5w", 16)
    s.inc(nc.sync.dma_start(bg_f[:], t["bg"]), "p5w", 16)
    s.inc(nc.sync.dma_start(ident[:], t["ident"]), "p5w", 16)
    s.inc(nc.sync.dma_start(g1r[:], t["g1_rep"]), "p5w", 16)
    s.inc(nc.sync.dma_start(be1r[:], t["be1_rep"]), "p5w", 16)
    s.wait(nc.vector, "p5w", s.val("p5w"))
    nc.vector.memset(onesr[:], 1.0)
    nc.vector.tensor_copy(bias_bf[:], bias_f[:])
    nc.vector.tensor_copy(bg_bf[:], bg_f[:])
    wrdy = s.inc(nc.vector.drain(), "p5v", 1)
    s.wait(nc.tensor, "p5v", wrdy)

    def selfwait(eng=nc.vector, name="p5sw"):
        v = s.inc(eng.nop(), name, 1)
        s.wait(eng, name, v)

    aob_free = 0
    xnb_free = 0
    for blk in range(NBLK):
        g, bi = blk // 4, blk % 4
        s.wait(nc.scalar, "p5t", aob_free)
        lv = s.inc(nc.scalar.dma_start(aob[:], t["ao"][bass.ts(blk, 128), :]), "p5xa", 16)
        s.wait(nc.scalar, "p5e", xnb_free)
        lvx = s.inc(nc.scalar.dma_start(xnb[:], t["xn"][bass.ts(blk, 128), :]), "p5xb", 16)
        s.wait(nc.tensor, "p5xa", lv)
        s.wait(nc.tensor, "p5tv", s.val("p5tv"))
        for c in range(8):
            mm = nc.tensor.transpose(k.pt[:, bass.ts(c, 128)], aob[:, bass.ts(c, 128)],
                                     ident[:])
        tv = aob_free = s.inc(mm, "p5t", 1)
        tv = s.inc(nc.tensor.nop(), "p5t", 1)
        s.wait(nc.vector, "p5t", tv)
        s.wait(nc.vector, "mm", s.val("mm"))
        nc.vector.tensor_copy(aoT[:], k.pt[:].rearrange("p (c n) -> p c n", c=8))
        aotv = s.inc(nc.vector.drain(), "p5tv", 1)
        s.wait(nc.tensor, "p5tv", aotv)
        s.wait(nc.tensor, "p5e", s.val("p5e"))
        for half in range(2):
            hp = k.pb[half][:]
            for c in range(8):
                nc.tensor.matmul(hp, aoT[:, c, :], wo[:, c, half, :],
                                 start=(c == 0), stop=False)
            mm = nc.tensor.matmul(hp, onesr[:],
                                  bias_bf[0:1, 3072 + half * 512:3072 + half * 512 + 512],
                                  start=False, stop=True)
        mv_ = s.inc(mm, "mm", 1)
        s.wait(nc.vector, "mm", mv_)
        s.wait(nc.vector, "p5xb", lvx)
        nc.vector.tensor_tensor(y1[:, 0:512], k.pb[0][:], xnb[:, 0:512], OP.add)
        nc.vector.tensor_tensor(y1[:, 512:1024], k.pb[1][:], xnb[:, 512:1024], OP.add)
        ev = xnb_free = s.inc(nc.vector.drain(), "p5e", 1)
        selfwait()
        s.wait(nc.vector, "p5xo", s.val("p5xo"))
        _ln_plain(k, None, y1, x1f, tmp, w_rep=g1r, b_rep=be1r)
        selfwait()
        nc.vector.tensor_copy(x1b[:], x1f[:])
        x1v = s.inc(nc.vector.drain(), "p5v", 1)
        s.wait(nc.scalar, "p5v", x1v)
        s.inc(nc.scalar.dma_start(t["x1"][bass.ts(blk, 128), :], x1f[:]), "p5xo", 16)
        s.wait(nc.tensor, "p5v", x1v)
        s.wait(nc.tensor, "p5tv", s.val("p5tv"))
        for c in range(8):
            mm = nc.tensor.transpose(k.pt[:, bass.ts(c, 128)], x1b[:, bass.ts(c, 128)],
                                     ident[:])
        tv = s.inc(mm, "p5t", 1)
        tv = s.inc(nc.tensor.nop(), "p5t", 1)
        s.wait(nc.vector, "p5t", tv)
        s.wait(nc.vector, "p5to", s.val("p5to"))
        nc.vector.tensor_copy(x1T[:], k.pt[:].rearrange("p (c n) -> p c n", c=8))
        x1tv = s.inc(nc.vector.drain(), "p5tv", 1)
        s.wait(nc.gpsimd, "p5tv", x1tv)
        sd = nc.gpsimd.dma_start(t["x1t"][g][:, :, bass.ts(bi, 128)].rearrange("c p n -> p c n"),
                                 x1T[:])
        s.inc(sd, "p5to", 16)
        s.wait(nc.tensor, "p5tv", x1tv)
        s.wait(nc.tensor, "p5g", s.val("p5g"))
        for c in range(8):
            nc.tensor.matmul(k.pg[:, 0:8], x1T[:, c, :], wg[:, c, :],
                             start=(c == 0), stop=False)
        mm = nc.tensor.matmul(k.pg[:, 0:8], onesr[:], bg_bf[:], start=False, stop=True)
        gv = s.inc(mm, "mm", 1)
        s.wait(nc.vector, "mm", gv)
        v1 = s.inc(nc.vector.tensor_reduce(m1[:], k.pg[:, 0:8], AX.X, OP.max), "p5g", 1)
        s.wait(nc.vector, "p5g", v1)
        v1 = s.inc(nc.vector.tensor_scalar(nm1[:], m1[:], -1.0, None, OP.mult), "p5g", 1)
        s.wait(nc.scalar, "p5g", v1)
        ex = nc.scalar.activation(pe_[:], k.pg[:, 0:8], AF.Exp, bias=nm1[:], scale=1.0)
        sv = s.inc(nc.scalar.drain(), "p5g", 1)
        s.wait(nc.vector, "p5g", sv)
        v1 = s.inc(nc.vector.tensor_reduce(m1[:], pe_[:], AX.X, OP.max), "p5g", 1)
        s.wait(nc.vector, "p5g", v1)
        v1 = s.inc(nc.vector.tensor_scalar(mask1[:], pe_[:], m1[:], None, OP.is_ge), "p5g", 1)
        s.wait(nc.vector, "p5g", v1)
        v1 = s.inc(nc.vector.scalar_tensor_tensor(pm[:], mask1[:], -1e9, pe_[:],
                                                  OP.mult, OP.add), "p5g", 1)
        s.wait(nc.vector, "p5g", v1)
        v1 = s.inc(nc.vector.tensor_reduce(m2[:], pm[:], AX.X, OP.max), "p5g", 1)
        s.wait(nc.vector, "p5g", v1)
        v1 = s.inc(nc.vector.tensor_tensor(s12[:], m1[:], m2[:], OP.add), "p5g", 1)
        s.wait(nc.vector, "p5g", v1)
        v1 = s.inc(nc.vector.reciprocal(rs[:], s12[:]), "p5g", 1)
        s.wait(nc.vector, "p5g", v1)
        v1 = s.inc(nc.vector.tensor_scalar(mask2[:], pe_[:], m2[:], None, OP.is_ge), "p5g", 1)
        s.wait(nc.vector, "p5g", v1)
        s.wait(nc.vector, "p5go", s.val("p5go"))
        v1 = s.inc(nc.vector.scalar_tensor_tensor(combv[:], pe_[:], rs[:], mask2[:],
                                                  OP.mult, OP.mult), "p5g", 1)
        cdr = s.inc(nc.vector.drain(), "p5gd", 1)
        s.wait(nc.gpsimd, "p5gd", cdr)
        sd = nc.gpsimd.dma_start(t["combt"][:, blk, :], combv[:])
        s.inc(sd, "p5go", 16)
    s.wait(nc.gpsimd, "p5go", s.val("p5go"))
    s.wait(nc.gpsimd, "p5to", s.val("p5to"))
    s.wait(nc.scalar, "p5xo", s.val("p5xo"))
    k.pop_scope()


def emit_moe_final(k: Kb, t, out_key="out"):
    k.push_scope()
    nc, s = k.nc, k.s
    ident = k.sb("p6_ident", [128, 128], BF16)
    g2r = k.sb("p6_g2r", [128, D], F32)
    be2r = k.sb("p6_be2r", [128, D], F32)
    b2f = k.sb("p6_b2f", [128, D], F32)
    b2b = k.sb("p6_b2b", [128, D], BF16)
    comb = k.sb("p6_comb", [128, NBLK, 8], F32)
    cbb = [k.sb(f"p6_cbb{j}", [128, 8], BF16) for j in range(2)]
    cbT = [k.sb(f"p6_cbT{j}", [128, 128], BF16) for j in range(2)]
    ya = [k.sb(f"p6_ya{j}", [128, D], F32) for j in range(2)]
    x1f = [k.sb(f"p6_x1f{j}", [128, D], F32) for j in range(2)]
    s3b = [k.sb(f"p6_s3b{j}", [128, D], F32) for j in range(2)]
    yt = [k.sb(f"p6_yt{j}", [128, D], F32) for j in range(2)]
    f2 = k.sb("p6_f2", [128, D], F32)
    ot = [k.sb(f"p6_ot{j}", [128, D], F32) for j in range(2)]
    tmp = {"bn": k.sb("p6_bn", [128, 12], F32), "mv": k.sb("p6_mv", [128, 2], F32),
           "t1": k.sb("p6_t1", [128, 1], F32), "t2": k.sb("p6_t2", [128, 1], F32)}

    s.inc(nc.sync.dma_start(ident[:], t["ident"]), "p6w", 16)
    s.inc(nc.sync.dma_start(g2r[:], t["g2_rep"]), "p6w", 16)
    s.inc(nc.sync.dma_start(be2r[:], t["be2_rep"]), "p6w", 16)
    s.inc(nc.sync.dma_start(b2f[0:8, :], t["b2"]), "p6w", 16)
    s.inc(nc.sync.dma_start(comb[:], t["combt"]), "p6w", 16)
    s.wait(nc.vector, "p6w", s.val("p6w"))
    s.wait(nc.gpsimd, "p6w", s.val("p6w"))
    nc.vector.tensor_copy(b2b[:], b2f[:])
    wrdy = s.inc(nc.vector.drain(), "p6v", 1)
    s.wait(nc.tensor, "p6v", wrdy)

    def selfwait():
        v = s.inc(nc.vector.nop(), "p6sw", 1)
        s.wait(nc.vector, "p6sw", v)

    in_free = [0, 0]
    cb_free = [0, 0]
    mm_rdy = [0, 0]
    pb_free = [0, 0]
    ld_rdy = [0, 0]
    ot_free = [0, 0]
    pt_free = 0
    for blk in range(NBLK + 1):
        j = blk % 2
        if blk < NBLK:
            s.wait(nc.scalar, "p6v" + str(j), in_free[j])
            lv = s.inc(nc.scalar.dma_start(ya[j][:], t["yacc"][bass.ts(blk, 128), :]),
                       "p6x" + str(j), 16)
            lv = s.inc(nc.scalar.dma_start(x1f[j][:], t["x1"][bass.ts(blk, 128), :]),
                       "p6x" + str(j), 16)
            lv = s.inc(nc.scalar.dma_start(s3b[j][:], t["s3"][bass.ts(blk, 128), :]),
                       "p6x" + str(j), 16)
            ld_rdy[j] = lv
            s.wait(nc.gpsimd, "mm", cb_free[j])
            nc.gpsimd.tensor_copy(cbb[j][:], comb[:, blk, :])
            cv = s.inc(nc.gpsimd.drain(), "p6cb", 1)
            s.wait(nc.tensor, "p6cb", cv)
            s.wait(nc.tensor, "p6pt", pt_free)
            mm = nc.tensor.transpose(k.pt[0:8, 0:128], cbb[j][:], ident[:])
            cb_free[j] = s.inc(mm, "mm", 1)
            tv = s.inc(nc.tensor.nop(), "p6t", 1)
            s.wait(nc.scalar, "p6t", tv)
            s.wait(nc.scalar, "p6x" + str(j), ld_rdy[j])
            nc.scalar.copy(cbT[j][0:8, :], k.pt[0:8, 0:128])
            pt_free = s.inc(nc.scalar.drain(), "p6pt", 1)
            s.wait(nc.tensor, "p6pt", pt_free)
            s.wait(nc.tensor, "p6e" + str(j), pb_free[j])
            pp = (k.pb[0], k.pb[1]) if j == 0 else (k.pb[2], k.pb[3])
            for half in range(2):
                mm = nc.tensor.matmul(pp[half][:], cbT[j][0:8, :],
                                      b2b[0:8, bass.ts(half, 512)],
                                      start=True, stop=True)
            mm_rdy[j] = s.inc(mm, "mm", 1)
        if blk >= 1:
            pj = (blk - 1) % 2
            pp = (k.pb[0], k.pb[1]) if pj == 0 else (k.pb[2], k.pb[3])
            s.wait(nc.vector, "mm", mm_rdy[pj])
            s.wait(nc.vector, "p6x" + str(pj), ld_rdy[pj])
            nc.vector.tensor_tensor(yt[pj][:, 0:512], pp[0][:], ya[pj][:, 0:512], OP.add)
            nc.vector.tensor_tensor(yt[pj][:, 512:1024], pp[1][:], ya[pj][:, 512:1024],
                                    OP.add)
            pb_free[pj] = s.inc(nc.vector.drain(), "p6e" + str(pj), 1)
            selfwait()
            nc.vector.tensor_tensor(yt[pj][:], yt[pj][:], x1f[pj][:], OP.add)
            selfwait()
            _ln_plain(k, None, yt[pj], f2, tmp, w_rep=g2r, b_rep=be2r)
            selfwait()
            _ln_plain(k, None, f2, f2, tmp)
            selfwait()
            s.wait(nc.vector, "p6o" + str(pj), ot_free[pj])
            nc.vector.tensor_tensor(ot[pj][:], f2[:], s3b[pj][:], OP.add)
            ov = in_free[pj] = s.inc(nc.vector.drain(), "p6v" + str(pj), 1)
            s.wait(nc.gpsimd, "p6v" + str(pj), ov)
            sd = nc.gpsimd.dma_start(t[out_key][bass.ts(blk - 1, 128), :], ot[pj][:])
            ot_free[pj] = s.inc(sd, "p6o" + str(pj), 16)
    for j in range(2):
        s.wait(nc.gpsimd, "p6o" + str(j), s.val("p6o" + str(j)))
    k.pop_scope()


def emit_l1a_qkv(k: Kb, t):
    k.push_scope()
    nc, s = k.nc, k.s
    wq = k.sb("p3_wq", [128, 8, 2, 512], BF16)
    wk = k.sb("p3_wk", [128, 8, 2, 512], BF16)
    wv = k.sb("p3_wv", [128, 8, 2, 512], BF16)
    ident = k.sb("p3_ident", [128, 128], BF16)
    xs = k.sb("p3_xs", [128, D], F32)
    xn = k.sb("p3_xn", [128, D], F32)
    xnb = k.sb("p3_xnb", [128, D], BF16)
    xnT = k.sb("p3_xnT", [128, 8, 512], BF16)
    osb = [k.sb(f"p3_o{i}", [128, D], BF16) for i in range(3)]
    tmp = {"bn": k.sb("p3_bn", [128, 12], F32), "mv": k.sb("p3_mv", [128, 2], F32),
           "t1": k.sb("p3_t1", [128, 1], F32), "t2": k.sb("p3_t2", [128, 1], F32)}

    for nm, w in (("wqT_bf", wq), ("wkT_bf", wk), ("wvT_bf", wv)):
        s.inc(nc.sync.dma_start(w[:], t[nm].rearrange("c h p n -> p c h n")), "p3w", 16)
    s.inc(nc.sync.dma_start(ident[:], t["ident"]), "p3w", 16)
    s.wait(nc.tensor, "p3w", s.val("p3w"))

    xs_free = 0
    o_free = [0, 0, 0]
    for g in range(NGRP):
        for bi in range(4):
            blk = g * 4 + bi
            s.wait(nc.scalar, "p3v", xs_free)
            lv = s.inc(nc.scalar.dma_start(xs[:], t["s2"][bass.ts(blk, 128), :]), "p3x", 16)
            s.wait(nc.vector, "p3x", lv)
            s.wait(nc.vector, "p3t", s.val("p3t"))
            s.wait(nc.vector, "p3xo", s.val("p3xo"))
            _ln_plain(k, None, xs, xn, tmp)
            v = s.inc(nc.vector.nop(), "p3v", 1)
            s.wait(nc.vector, "p3v", v)
            nc.vector.tensor_copy(xnb[:], xn[:])
            xv = xs_free = s.inc(nc.vector.drain(), "p3v", 1)
            s.wait(nc.scalar, "p3v", xv)
            s.inc(nc.scalar.dma_start(t["xn2"][bass.ts(blk, 128), :], xn[:]), "p3xo", 16)
            s.wait(nc.tensor, "p3v", xv)
            s.wait(nc.tensor, "p3tv", s.val("p3tv"))
            for c in range(8):
                mm = nc.tensor.transpose(k.pt[:, bass.ts(c, 128)],
                                         xnb[:, bass.ts(c, 128)], ident[:])
            tv = s.inc(mm, "p3t", 1)
            tv = s.inc(nc.tensor.nop(), "p3t", 1)
            s.wait(nc.vector, "p3t", tv)
            if bi == 0:
                s.wait(nc.vector, "mm", s.val("mm"))
            nc.vector.tensor_copy(xnT[:, :, bi * 128:(bi + 1) * 128],
                                  k.pt[:].rearrange("p (c n) -> p c n", c=8))
            s.inc(nc.vector.drain(), "p3tv", 1)
        s.wait(nc.tensor, "p3tv", s.val("p3tv"))
        for wi, (w, dstname) in enumerate(((wq, "q"), (wk, "k"), (wv, "v"))):
            for bi in range(4):
                blk = g * 4 + bi
                oi = (wi * 4 + bi) % 3
                s.wait(nc.tensor, "p3e", s.val("p3e"))
                for half in range(2):
                    hp = k.pb[half][:]
                    for c in range(8):
                        mm = nc.tensor.matmul(hp, xnT[:, c, bass.ts(bi, 128)],
                                              w[:, c, half, :],
                                              start=(c == 0), stop=(c == 7))
                mv_ = s.inc(mm, "mm", 1)
                s.wait(nc.vector, "mm", mv_)
                s.wait(nc.vector, f"p3o{oi}", o_free[oi])
                nc.vector.tensor_copy(osb[oi][:, 0:512], k.pb[0][:])
                nc.vector.tensor_copy(osb[oi][:, 512:1024], k.pb[1][:])
                ev = s.inc(nc.vector.drain(), "p3e", 1)
                s.wait(nc.gpsimd, "p3e", ev)
                sd = nc.gpsimd.dma_start(t[dstname][bass.ts(blk, 128), :], osb[oi][:])
                o_free[oi] = s.inc(sd, f"p3o{oi}", 16)
    for oi in range(3):
        s.wait(nc.gpsimd, f"p3o{oi}", s.val(f"p3o{oi}"))
    s.wait(nc.scalar, "p3xo", s.val("p3xo"))
    k.pop_scope()


def emit_l1a_norm(k: Kb, t):
    k.push_scope()
    nc, s = k.nc, k.s
    aob = [k.sb(f"p3n_ao{i}", [128, D], BF16) for i in range(2)]
    s2b = [k.sb(f"p3n_s2{i}", [128, D], F32) for i in range(2)]
    yb = [k.sb(f"p3n_y{i}", [128, D], F32) for i in range(2)]
    part = k.sb("p3n_part", [128, 2], F32)
    tpart = k.sb("p3n_t", [128, 2], F32)
    sq = k.sb("p3n_sq", [128, D], F32)
    onesc = k.sb("p3n_ones", [128, 1], F32)
    musd = k.sb("p3n_musd", [128, 2], F32)
    sc = k.sb("p3n_sc", [1, 4], F32)
    prow = k.sb("p3n_prow", [1, 256], F32)
    onesr = k.sb("p3n_onesr", [1, 128], F32)

    def vsw():
        v = s.inc(nc.vector.nop(), "p3n_v", 1)
        s.wait(nc.vector, "p3n_v", v)

    nc.vector.memset(part[:], 0.0)
    nc.vector.memset(onesc[:], 1.0)
    nc.vector.memset(onesr[:], 1.0)
    vsw()

    ld_free = [0, 0]
    for blk in range(NBLK):
        j = blk % 2
        s.wait(nc.scalar, "p3n_c", ld_free[j])
        s.inc(nc.scalar.dma_start(aob[j][:], t["ao"][bass.ts(blk, 128), :]), f"p3n_l{j}", 16)
        lv = s.inc(nc.scalar.dma_start(s2b[j][:], t["xn2"][bass.ts(blk, 128), :]), f"p3n_l{j}", 16)
        s.wait(nc.vector, f"p3n_l{j}", lv)
        s.wait(nc.vector, f"p3n_s{j}", s.val(f"p3n_s{j}"))
        nc.vector.tensor_tensor(yb[j][:], aob[j][:], s2b[j][:], OP.add)
        yv = s.inc(nc.vector.drain(), "p3n_y", 1)
        v = s.inc(nc.vector.reduce_sum(tpart[:, 0:1], yb[j][:], AX.X), "p3n_v", 1)
        s.wait(nc.scalar, "p3n_y", yv)
        sv = s.inc(nc.scalar.activation(sq[:], yb[j][:], AF.Square,
                                        accum_out=tpart[:, 1:2]), "p3n_sc", 1)
        s.wait(nc.vector, "p3n_sc", sv)
        s.wait(nc.vector, "p3n_v", v)
        v = s.inc(nc.vector.tensor_tensor(part[:], part[:], tpart[:], OP.add), "p3n_v", 1)
        s.wait(nc.vector, "p3n_v", v)
        ld_free[j] = s.inc(nc.vector.nop(), "p3n_c", 1)
        s.wait(nc.gpsimd, "p3n_c", ld_free[j])
        sd = nc.gpsimd.dma_start(t["yg"][bass.ts(blk, 128), :], yb[j][:])
        s.inc(sd, f"p3n_s{j}", 16)

    cv = s.inc(nc.vector.drain(), "p3n_v", 1)
    s.wait(nc.gpsimd, "p3n_v", cv)
    s.inc(nc.gpsimd.dma_start(t["gscr"][0:1, 0:128].rearrange("a b -> b a"),
                              part[:, 0:1]), "p3n_g1", 16)
    sd = s.inc(nc.gpsimd.dma_start(t["gscr"][0:1, 128:256].rearrange("a b -> b a"),
                                   part[:, 1:2]), "p3n_g1", 16)
    s.wait(nc.sync, "p3n_g1", sd)
    sr = s.inc(nc.sync.dma_start(prow[:], t["gscr"][0:1, 0:256]), "p3n_g2", 16)
    s.wait(nc.vector, "p3n_g2", sr)
    NTD = float(NT * D)
    v = s.inc(nc.vector.reduce_sum(sc[0:1, 0:1], prow[0:1, 0:128], AX.X), "p3n_v", 1)
    s.wait(nc.vector, "p3n_v", v)
    v = s.inc(nc.vector.reduce_sum(sc[0:1, 1:2], prow[0:1, 128:256], AX.X), "p3n_v", 1)
    s.wait(nc.vector, "p3n_v", v)
    v = s.inc(nc.vector.tensor_scalar(sc[0:1, 0:2], sc[0:1, 0:2], 1.0 / NTD, None,
                                      OP.mult), "p3n_v", 1)
    s.wait(nc.vector, "p3n_v", v)
    v = s.inc(nc.vector.tensor_tensor(sc[0:1, 2:3], sc[0:1, 0:1], sc[0:1, 0:1],
                                      OP.mult), "p3n_v", 1)
    s.wait(nc.vector, "p3n_v", v)
    v = s.inc(nc.vector.tensor_tensor(sc[0:1, 3:4], sc[0:1, 1:2], sc[0:1, 2:3],
                                      OP.subtract), "p3n_v", 1)
    s.wait(nc.vector, "p3n_v", v)
    v = s.inc(nc.vector.tensor_scalar(sc[0:1, 3:4], sc[0:1, 3:4], 1e-5, None, OP.add),
              "p3n_v", 1)
    s.wait(nc.scalar, "p3n_v", v)
    sv = s.inc(nc.scalar.activation(sc[0:1, 3:4], sc[0:1, 3:4], AF.Sqrt), "p3n_sc", 1)
    s.wait(nc.vector, "p3n_sc", sv)
    v = s.inc(nc.vector.reciprocal(sc[0:1, 1:2], sc[0:1, 3:4]), "p3n_v", 1)
    s.wait(nc.vector, "p3n_v", v)
    bv = s.inc(nc.vector.drain(), "p3n_v", 1)
    s.wait(nc.gpsimd, "p3n_v", bv)
    sd = s.inc(nc.gpsimd.dma_start(t["gscr"][0:1, 256:258], sc[0:1, 0:2]), "p3n_g1", 16)
    s.wait(nc.sync, "p3n_g1", sd)
    bsrc = bass.AP(t["gscr"].tensor, 256, [[0, 128], [1, 2]])
    sr = s.inc(nc.sync.dma_start(musd[:], bsrc), "p3n_g2", 16)
    s.wait(nc.vector, "p3n_g2", sr)

    st_free = [0, 0]
    tmp2 = {"bn": k.sb("p3n_bn", [128, 12], F32), "mv": k.sb("p3n_mv", [128, 2], F32),
            "t1": k.sb("p3n_t1", [128, 1], F32), "t2": k.sb("p3n_t2", [128, 1], F32)}
    for blk in range(NBLK):
        j = blk % 2
        s.wait(nc.scalar, "p3n_v", st_free[j])
        lv = s.inc(nc.scalar.dma_start(yb[j][:], t["yg"][bass.ts(blk, 128), :]),
                   f"p3n_l2{j}", 16)
        lv = s.inc(nc.scalar.dma_start(aob[j].bitcast(F32) if False else s2b[j][:],
                                       t["s2"][bass.ts(blk, 128), :]), f"p3n_l2{j}", 16)
        s.wait(nc.vector, f"p3n_l2{j}", lv)
        s.wait(nc.vector, f"p3n_o{j}", s.val(f"p3n_o{j}"))
        nc.vector.tensor_scalar(yb[j][:], yb[j][:], musd[:, 0:1], musd[:, 1:2],
                                OP.subtract, OP.mult)
        gv = s.inc(nc.vector.drain(), "p3n_v", 1)
        s.wait(nc.vector, "p3n_v", gv)
        _ln_plain(k, None, yb[j], sq, tmp2)
        gv = s.inc(nc.vector.nop(), "p3n_v", 1)
        s.wait(nc.vector, "p3n_v", gv)
        nc.vector.tensor_tensor(sq[:], sq[:], s2b[j][:], OP.add)
        ov = s.inc(nc.vector.drain(), "p3n_v", 1)
        st_free[j] = ov
        s.wait(nc.gpsimd, "p3n_v", ov)
        sd = nc.gpsimd.dma_start(t["s3"][bass.ts(blk, 128), :], sq[:])
        s.inc(sd, f"p3n_o{j}", 16)
    for j in range(2):
        s.wait(nc.gpsimd, f"p3n_o{j}", s.val(f"p3n_o{j}"))
        s.wait(nc.gpsimd, f"p3n_s{j}", s.val(f"p3n_s{j}"))
    k.pop_scope()


def emit_l0_quant(k: Kb, t, wkey, dstkey, R, C, tag="q", scr=None):
    k.push_scope()
    nc, s = k.nc, k.s
    NB = R // 128
    stg = [k.sb(f"{tag}_stg{i}", [128, C], F32) for i in range(2)]
    aw = k.sb(tag + "_aw", [128, C], F32)
    sg = k.sb(tag + "_sg", [128, C], F32)
    msk = k.sb(tag + "_msk", [128, C], F32)
    qb = [k.sb(f"{tag}_qb{i}", [128, C], BF16) for i in range(2)]
    acc = k.sb(tag + "_acc", [128, 2], F32)
    tpart = k.sb(tag + "_tp", [128, 2], F32)
    onesc = k.sb(tag + "_ones", [128, 1], F32)
    thr = k.sb(tag + "_thr", [128, 1], F32)
    sc1 = k.sb(tag + "_sc1", [1, 2], F32)
    sc1r = k.sb(tag + "_sc1r", [1, 128], F32)

    def vsw():
        v = s.inc(nc.vector.nop(), "q_v", 1)
        s.wait(nc.vector, "q_v", v)

    nc.vector.memset(acc[:], 0.0)
    nc.vector.memset(tpart[:], 0.0)
    nc.vector.memset(onesc[:], 1.0)
    vsw()
    ld_free = [0, 0]
    for b in range(NB):
        j = b % 2
        s.wait(nc.sync, "q_v", ld_free[j])
        lv = s.inc(nc.sync.dma_start(stg[j][:], t[wkey][bass.ts(b, 128), :]), f"q_l{j}", 16)
        s.wait(nc.vector, f"q_l{j}", lv)
        v = s.inc(nc.vector.tensor_reduce(tpart[:, 0:1], stg[j][:], AX.X, OP.add,
                                          apply_absolute_value=True), "q_v", 1)
        s.wait(nc.vector, "q_v", v)
        ld_free[j] = s.inc(nc.vector.tensor_tensor(acc[:], acc[:], tpart[:], OP.add),
                           "q_v", 1)
        s.wait(nc.vector, "q_v", ld_free[j])
    cv = s.inc(nc.vector.drain(), "q_v", 1)
    s.wait(nc.gpsimd, "q_v", cv)
    sd = s.inc(nc.gpsimd.dma_start(scr[0:1, 0:128].rearrange("a b -> b a"),
                                   acc[:, 0:1]), "q_g1", 16)
    s.wait(nc.sync, "q_g1", sd)
    sr = s.inc(nc.sync.dma_start(sc1r[:], scr[0:1, 0:128]), "q_g2", 16)
    s.wait(nc.vector, "q_g2", sr)
    v = s.inc(nc.vector.reduce_sum(sc1[0:1, 0:1], sc1r[:], AX.X), "q_v", 1)
    s.wait(nc.vector, "q_v", v)
    v = s.inc(nc.vector.tensor_scalar(sc1[0:1, 1:2], sc1[0:1, 0:1], 1.0 / (R * C), 1e-6,
                                      OP.mult, OP.add), "q_v", 1)
    s.wait(nc.vector, "q_v", v)
    v = s.inc(nc.vector.tensor_scalar(sc1[0:1, 1:2], sc1[0:1, 1:2], 0.5, None, OP.mult),
              "q_v", 1)
    bv = s.inc(nc.vector.drain(), "q_v", 1)
    s.wait(nc.gpsimd, "q_v", bv)
    sd = s.inc(nc.gpsimd.dma_start(scr[0:1, 128:129], sc1[0:1, 1:2]), "q_g1", 16)
    s.wait(nc.sync, "q_g1", sd)
    bsrc = bass.AP(scr.tensor, 128, [[0, 128], [1, 1]])
    sr = s.inc(nc.sync.dma_start(thr[:], bsrc), "q_g2", 16)
    s.wait(nc.vector, "q_g2", sr)
    ld_free = [0, 0]
    out_done = [0, 0]
    prev_qv = 0
    for b in range(NB):
        j = b % 2
        s.wait(nc.sync, "q_v", ld_free[j])
        lv = s.inc(nc.sync.dma_start(stg[j][:], t[wkey][bass.ts(b, 128), :]), f"q_l{j}", 16)
        s.wait(nc.scalar, f"q_l{j}", lv)
        s.wait(nc.scalar, "q_v", prev_qv)
        sv = s.inc(nc.scalar.activation(aw[:], stg[j][:], AF.Abs), "q_s", 1)
        sv = s.inc(nc.scalar.activation(sg[:], stg[j][:], AF.Sign), "q_s", 1)
        s.wait(nc.vector, "q_s", sv)
        v = s.inc(nc.vector.tensor_scalar(msk[:], aw[:], thr[:], None, OP.is_ge), "q_v", 1)
        ld_free[j] = v
        s.wait(nc.vector, "q_v", v)
        s.wait(nc.vector, f"q_o{j}", out_done[j])
        nc.vector.tensor_tensor(qb[j][:], msk[:], sg[:], OP.mult)
        qv = prev_qv = s.inc(nc.vector.drain(), "q_v", 1)
        s.wait(nc.gpsimd, "q_v", qv)
        sd = nc.gpsimd.dma_start(t[dstkey][b], qb[j][:])
        out_done[j] = s.inc(sd, f"q_o{j}", 16)
    for j in range(2):
        s.wait(nc.gpsimd, f"q_o{j}", s.val(f"q_o{j}"))
    k.pop_scope()


def emit_l0_ffn(k: Kb, t):
    k.push_scope()
    nc, s = k.nc, k.s
    wfc = k.sb("f0_wfc", [128, 8, FF], BF16)
    wpj = k.sb("f0_wpj", [128, FT, D], BF16)
    lnw = k.sb("f0_lnw", [128, FF], F32)
    lnb = k.sb("f0_lnb", [128, FF], F32)
    ident = k.sb("f0_ident", [128, 128], BF16)
    xs = k.sb("f0_xs", [128, D], F32)
    xnb = k.sb("f0_xnb", [128, D], BF16)
    xnT = k.sb("f0_xnT", [128, 8, 128], BF16)
    hsb = k.sb("f0_h", [128, FF], F32)
    hnb = k.sb("f0_hnb", [128, FF], BF16)
    hT = k.sb("f0_hT", [128, FT, 128], BF16)
    ot_ = k.sb("f0_ot", [128, D], F32)
    ot = [ot_, ot_]
    tmp = {"bn": k.sb("f0_bn", [128, 48], F32), "mv": k.sb("f0_mv", [128, 2], F32),
           "t1": k.sb("f0_t1", [128, 1], F32), "t2": k.sb("f0_t2", [128, 1], F32)}

    s.inc(nc.sync.dma_start(wfc[:], t["wfcq"].rearrange("c p f -> p c f")), "f0_w", 16)
    s.inc(nc.sync.dma_start(wpj[:], t["wprojq"].rearrange("f p d -> p f d")), "f0_w", 16)
    s.inc(nc.sync.dma_start(lnw[:], t["lnw_rep"]), "f0_w", 16)
    s.inc(nc.sync.dma_start(lnb[:], t["lnb_rep"]), "f0_w", 16)
    wl = s.inc(nc.sync.dma_start(ident[:], t["ident"]), "f0_w", 16)
    s.wait(nc.tensor, "f0_w", wl)
    s.wait(nc.vector, "f0_w", wl)

    xs_free = 0
    st_free = [0, 0]
    def lnfull(xin, xout, n, w_rep=None, b_rep=None):
        for i in range(n):
            mm = nc.vector.bn_stats(tmp["bn"][:, 6 * i:6 * i + 6],
                                    xin[:, 512 * i:512 * (i + 1)])
        v = s.inc(mm, "ln", 1)
        s.wait(nc.vector, "ln", v)
        v = s.inc(nc.vector.bn_aggr(tmp["mv"][:], tmp["bn"][:, 0:6 * n]), "ln", 1)
        s.wait(nc.vector, "ln", v)
        v = s.inc(nc.vector.tensor_scalar(tmp["t1"][:], tmp["mv"][:, 1:2], 1e-5, None,
                                          OP.add), "ln", 1)
        s.wait(nc.scalar, "ln", v)
        v = s.inc(nc.scalar.activation(tmp["t1"][:], tmp["t1"][:], AF.Sqrt), "ln", 1)
        s.wait(nc.vector, "ln", v)
        v = s.inc(nc.vector.reciprocal(tmp["t2"][:], tmp["t1"][:]), "ln", 1)
        s.wait(nc.vector, "ln", v)
        nc.vector.tensor_scalar(xout[:], xin[:], tmp["mv"][:, 0:1], tmp["t2"][:],
                                OP.subtract, OP.mult)
        if w_rep is not None:
            v = s.inc(nc.vector.nop(), "ln", 1)
            s.wait(nc.vector, "ln", v)
            nc.vector.tensor_tensor(xout[:], xout[:], w_rep[:], OP.mult)
            v = s.inc(nc.vector.nop(), "ln", 1)
            s.wait(nc.vector, "ln", v)
            nc.vector.tensor_tensor(xout[:], xout[:], b_rep[:], OP.add)

    for blk in range(NBLK):
        j = blk % 2
        s.wait(nc.scalar, "f0_v", xs_free)
        lv = s.inc(nc.scalar.dma_start(xs[:], t["s1"][bass.ts(blk, 128), :]), "f0_x", 16)
        s.wait(nc.vector, "f0_x", lv)
        s.wait(nc.vector, "f0_t", s.val("f0_t"))
        lnfull(xs, xnb, 2)
        xv = s.inc(nc.vector.drain(), "f0_v", 1)
        s.wait(nc.tensor, "f0_v", xv)
        s.wait(nc.tensor, "f0_tv", s.val("f0_tv"))
        for c in range(8):
            mm = nc.tensor.transpose(k.pt[:, bass.ts(c, 128)], xnb[:, bass.ts(c, 128)],
                                     ident[:])
        tv = s.inc(mm, "f0_t", 1)
        tv = s.inc(nc.tensor.nop(), "f0_t", 1)
        s.wait(nc.vector, "f0_t", tv)
        s.wait(nc.vector, "mm", s.val("mm"))
        nc.vector.tensor_copy(xnT[:], k.pt[:].rearrange("p (c n) -> p c n", c=8))
        xtv = s.inc(nc.vector.drain(), "f0_tv", 1)
        s.wait(nc.tensor, "f0_tv", xtv)
        for f8 in range(8):
            hp = k.pb[f8 % 2][:]
            s.wait(nc.tensor, "f0_e", s.val("f0_e") - 1 if f8 >= 2 or blk > 0 else 0)
            for c in range(8):
                mm = nc.tensor.matmul(hp, xnT[:, c, :], wfc[:, c, bass.ts(f8, 512)],
                                      start=(c == 0), stop=(c == 7))
            hv = s.inc(mm, "mm", 1)
            s.wait(nc.scalar, "mm", hv)
            ge = nc.scalar.activation(hsb[:, bass.ts(f8, 512)], hp, AF.Gelu)
            s.inc(ge, "f0_e", 1)
        gl = s.inc(nc.scalar.drain(), "f0_e", 1)
        s.wait(nc.vector, "f0_e", gl)
        lnfull(hsb, hsb, 8, w_rep=lnw, b_rep=lnb)
        v = s.inc(nc.vector.nop(), "f0_v", 1)
        s.wait(nc.vector, "f0_v", v)
        nc.vector.tensor_copy(hnb[:], hsb[:])
        hv2 = s.inc(nc.vector.drain(), "f0_v", 1)
        for bt in range(4):
            s.wait(nc.tensor, "f0_v", hv2)
            s.wait(nc.tensor, "f0_tv", s.val("f0_tv"))
            for ti in range(8):
                c = bt * 8 + ti
                mm = nc.tensor.transpose(k.pt[:, bass.ts(ti, 128)],
                                         hnb[:, bass.ts(c, 128)], ident[:])
            tv = s.inc(mm, "f0_t", 1)
            tv = s.inc(nc.tensor.nop(), "f0_t", 1)
            s.wait(nc.vector, "f0_t", tv)
            nc.vector.tensor_copy(hT[:, bass.ts(bt, 8), :],
                                  k.pt[:].rearrange("p (c n) -> p c n", c=8))
            s.inc(nc.vector.drain(), "f0_tv", 1)
        htv = s.val("f0_tv")
        s.wait(nc.tensor, "f0_tv", htv)
        for half in range(2):
            yp = k.pb[4 + half][:]
            for f in range(FT):
                mm = nc.tensor.matmul(yp, hT[:, f, :], wpj[:, f, bass.ts(half, 512)],
                                      start=(f == 0), stop=(f == FT - 1))
        fence = nc.tensor.matmul(k.pg[0:1, 100:101], ident[0:128, 0:1], ident[0:128, 0:1],
                                 start=True, stop=True)
        yv = s.inc(fence, "mm", 1)
        yv = s.inc(nc.tensor.nop(), "mm", 1)
        s.wait(nc.vector, "mm", yv)
        s.wait(nc.vector, "f0_o0", st_free[0])
        s.wait(nc.vector, "f0_o1", st_free[1])
        nc.vector.tensor_copy(ot[j][64:128, 0:512], ot[j][64:128, 0:512])
        nc.vector.tensor_copy(ot[j][:, 0:512], k.pb[4][:])
        nc.vector.tensor_copy(ot[j][:, 512:1024], k.pb[5][:])
        v = s.inc(nc.vector.drain(), "f0_v", 1)
        s.wait(nc.vector, "f0_v", v)
        lnfull(ot[j], ot[j], 2)
        v = s.inc(nc.vector.nop(), "f0_v", 1)
        s.wait(nc.vector, "f0_v", v)
        nc.vector.tensor_tensor(ot[j][:], ot[j][:], xs[:], OP.add)
        ov = xs_free = s.inc(nc.vector.drain(), "f0_v", 1)
        s.wait(nc.gpsimd, "f0_v", ov)
        sd = nc.gpsimd.dma_start(t["s2"][bass.ts(blk, 128), :], ot[j][:])
        st_free[j] = s.inc(sd, f"f0_o{j}", 16)
    for j in range(2):
        s.wait(nc.gpsimd, f"f0_o{j}", s.val(f"f0_o{j}"))
    k.pop_scope()


def emit_l0_attn(k: Kb, t, nchunks=NBLK):
    import os as _o
    CUT = int(_o.environ.get("CUT", "9"))
    """s1 = x0 + LN_plain(linattn(LN_plain(x0))), chunked causal-inclusive scan.

    All matmul operands at partition base 0 (base-64 operands crash the PE).
    PSUM: pb0/1 = Qp then A-slots; pb2/3 = S state (persistent, [0:64, 8, 64]
    even/odd heads); pb4/5 = Vp then O (time-shared); pt transposes; pg free.
    """
    k.push_scope()
    nc, s = k.nc, k.s
    wq = k.sb("l0_wq", [128, 8, 2, 512], BF16)
    wv = k.sb("l0_wv", [128, 8, 2, 512], BF16)
    qnw = k.sb("l0_qnw", [128, D], F32)
    vnw = k.sb("l0_vnw", [128, D], F32)
    tri = k.sb("l0_tri", [128, 128], F32)
    ident = k.sb("l0_ident", [128, 128], BF16)
    xs = k.sb("l0_xs", [128, D], F32)
    xn = k.sb("l0_xn", [128, D], F32)
    xnb = k.sb("l0_xnb", [128, D], BF16)
    xnT = k.sb("l0_xnT", [128, 8, 128], BF16)
    xnK = k.sb("l0_xnK", [64, 16, 128], BF16)
    sq = k.sb("l0_sq", [128, D], F32)
    rr = k.sb("l0_r", [128, 16], F32)
    qsc = k.sb("l0_qsc", [128, D], F32)
    qbf = k.sb("l0_qbf", [128, D], BF16)
    vbf = k.sb("l0_vbf", [128, D], BF16)
    qT = k.sb("l0_qT", [64, 16, 128], BF16)
    Ssb = k.sb("l0_Ssb", [64, 16, 64], BF16)
    Asb = k.sb("l0_Asb", [128, 16, 128], BF16)
    osb = k.sb("l0_osb", [128, D], F32)
    ot = [k.sb(f"l0_ot{i}", [128, D], F32) for i in range(2)]
    can = k.sb("l0_can", [128, 8], F32)
    tmp = {"bn": k.sb("l0_bn", [128, 12], F32), "mv": k.sb("l0_mv", [128, 2], F32),
           "t1": k.sb("l0_t1", [128, 1], F32), "t2": k.sb("l0_t2", [128, 1], F32)}

    s.inc(nc.scalar.dma_start(wq[:], t["wqT_bf"].rearrange("c h p n -> p c h n")), "l0_w", 16)
    s.inc(nc.scalar.dma_start(wv[:], t["wvT_bf"].rearrange("c h p n -> p c h n")), "l0_w", 16)
    s.inc(nc.scalar.dma_start(qnw[:], t["qnw_rep"]), "l0_w", 16)
    s.inc(nc.scalar.dma_start(vnw[:], t["vnw_rep"]), "l0_w", 16)
    s.inc(nc.scalar.dma_start(tri[:], t["triT"]), "l0_w", 16)
    wl = s.inc(nc.scalar.dma_start(ident[:], t["ident"]), "l0_w", 16)
    s.wait(nc.tensor, "l0_w", wl)
    s.wait(nc.vector, "l0_w", wl)

    def vsw():
        v = s.inc(nc.vector.nop(), "l0_v", 1)
        s.wait(nc.vector, "l0_v", v)

    def rms_chain(banks, wrep, outbf, gate_mm):
        s.wait(nc.scalar, "mm", gate_mm)
        s.wait(nc.scalar, "l0_v", s.val("l0_v"))
        nc.scalar.activation(sq[:, 0:512], banks[0][:], AF.Square)
        sv = s.inc(nc.scalar.activation(sq[:, 512:1024], banks[1][:], AF.Square), "l0_s", 1)
        s.wait(nc.vector, "l0_s", sv)
        v = s.inc(nc.vector.tensor_reduce(rr[:], sq.rearrange("p (h c) -> p h c", h=16),
                                          AX.X, OP.add), "l0_v", 1)
        s.wait(nc.vector, "l0_v", v)
        v = s.inc(nc.vector.tensor_scalar(rr[:], rr[:], 1.0 / 64, 1e-8, OP.mult, OP.add),
                  "l0_v", 1)
        s.wait(nc.scalar, "l0_v", v)
        sv = s.inc(nc.scalar.activation(rr[:], rr[:], AF.Sqrt), "l0_s", 1)
        s.wait(nc.vector, "l0_s", sv)
        v = s.inc(nc.vector.reciprocal(rr[:], rr[:]), "l0_v", 1)
        s.wait(nc.vector, "l0_v", v)
        for h in range(16):
            nc.vector.tensor_scalar(qsc[:, 64 * h:64 * h + 64],
                                    banks[h // 8][:, (h % 8) * 64:(h % 8) * 64 + 64],
                                    rr[:, h:h + 1], None, OP.mult)
        vsw()
        nc.vector.tensor_tensor(outbf[:], qsc[:], wrep[:], OP.mult)
        return s.inc(nc.vector.drain(), "l0_v", 1)

    xs_free = 0
    st_free = [0, 0]
    qdone = vdone = 0
    odone = 0
    fence_prev = 0
    for c in range(nchunks):
        j = c % 2
        s.wait(nc.scalar, "l0_v", xs_free)
        lv = s.inc(nc.scalar.dma_start(xs[:], t["x0"][bass.ts(c, 128), :]), "l0_x", 16)
        s.wait(nc.vector, "l0_x", lv)
        s.wait(nc.vector, "l0_t", s.val("l0_t"))
        _ln_plain(k, None, xs, xn, tmp)
        vsw()
        nc.vector.tensor_copy(xnb[:], xn[:])
        xv = s.inc(nc.vector.drain(), "l0_v", 1)
        s.wait(nc.tensor, "l0_v", xv)
        s.wait(nc.tensor, "l0_tv", s.val("l0_tv"))
        for cc in range(8):
            mm = nc.tensor.transpose(k.pt[:, bass.ts(cc, 128)], xnb[:, bass.ts(cc, 128)],
                                     ident[:])
        tv = s.inc(mm, "l0_t", 1)
        tv = s.inc(nc.tensor.nop(), "l0_t", 1)
        s.wait(nc.vector, "l0_t", tv)
        s.wait(nc.vector, "mm", s.val("mm"))
        nc.vector.tensor_copy(xnT[:], k.pt[:].rearrange("p (c n) -> p c n", c=8))
        xtv = s.inc(nc.vector.drain(), "l0_tv", 1)
        for bt in range(2):
            s.wait(nc.tensor, "l0_tv", s.val("l0_tv"))
            for ti in range(8):
                h = bt * 8 + ti
                mm = nc.tensor.transpose(k.pt[0:64, bass.ts(ti, 128)],
                                         xnb[:, 64 * h:64 * h + 64], ident[:])
            tv = s.inc(mm, "l0_t", 1)
            tv = s.inc(nc.tensor.nop(), "l0_t", 1)
            s.wait(nc.vector, "l0_t", tv)
            nc.vector.tensor_copy(xnK[0:64, bass.ts(bt, 8), :],
                                  k.pt[0:64, :].rearrange("p (a n) -> p a n", a=8))
            s.inc(nc.vector.drain(), "l0_tv", 1)
        s.wait(nc.tensor, "l0_tv", s.val("l0_tv"))
        if CUT <= 0:
            xs_free = xv
            continue
        s.wait(nc.tensor, "l0_v", qdone)
        for half in range(2):
            for cc in range(8):
                mm = nc.tensor.matmul(k.pb[half][:], xnT[:, cc, :],
                                      wq[:, cc, half, :], start=(cc == 0), stop=(cc == 7))
        qmv = s.inc(mm, "mm", 1)
        s.wait(nc.tensor, "l0_v", odone)
        for half in range(2):
            for cc in range(8):
                mm = nc.tensor.matmul(k.pb[4 + half][:], xnT[:, cc, :],
                                      wv[:, cc, half, :], start=(cc == 0), stop=(cc == 7))
        vmv = s.inc(mm, "mm", 1)
        qdone_ = rms_chain((k.pb[0], k.pb[1]), qnw, qbf, qmv)
        vdone = rms_chain((k.pb[4], k.pb[5]), vnw, vbf, vmv)
        if CUT <= 1:
            qdone = qdone_
            odone = vdone
            xs_free = vdone
            continue
        s.wait(nc.tensor, "l0_v", qdone_)
        for bt in range(2):
            s.wait(nc.tensor, "l0_tv", s.val("l0_tv"))
            for ti in range(8):
                h = bt * 8 + ti
                mm = nc.tensor.transpose(k.pt[0:64, bass.ts(ti, 128)],
                                         qbf[:, 64 * h:64 * h + 64], ident[:])
            tv = s.inc(mm, "l0_t", 1)
            tv = s.inc(nc.tensor.nop(), "l0_t", 1)
            s.wait(nc.vector, "l0_t", tv)
            nc.vector.tensor_copy(qT[0:64, bass.ts(bt, 8), :],
                                  k.pt[0:64, :].rearrange("p (a n) -> p a n", a=8))
            s.inc(nc.vector.drain(), "l0_tv", 1)
        s.wait(nc.tensor, "l0_tv", s.val("l0_tv"))
        if CUT <= 2:
            qdone = qdone_
            odone = vdone
            xs_free = vdone
            continue
        s.wait(nc.vector, "mm", fence_prev)
        if c == 0:
            nc.vector.memset(Ssb[:], 0.0)
        else:
            sv2 = Ssb.rearrange("p (a b) c -> p a b c", b=2)
            nc.vector.tensor_copy(sv2[0:64, :, 0, :],
                                  k.pb[2][0:64, :].rearrange("p (a c) -> p a c", a=8))
            nc.vector.tensor_copy(sv2[0:64, :, 1, :],
                                  k.pb[3][0:64, :].rearrange("p (a c) -> p a c", a=8))
        nc.vector.drain()
        nc.vector.tensor_copy(can[0:64, 0:8], Ssb[0:64, 0, :][:, 0:8])
        sdone = s.inc(nc.vector.drain(), "l0_v", 1)
        mk = 0
        for wv_i in range(2):
            s.wait(nc.tensor, "l0_v", qdone_ if wv_i == 0 else mk)
            for ti in range(8):
                h = wv_i * 8 + ti
                mm = nc.tensor.matmul(k.pb[ti // 4][:, (ti % 4) * 128:(ti % 4) * 128 + 128],
                                      xnK[0:64, h, :], qT[0:64, h, :],
                                      start=True, stop=True)
            amv = s.inc(mm, "mm", 1)
            amv = s.inc(nc.tensor.nop(), "mm", 1)
            s.wait(nc.vector, "mm", amv)
            for ti in range(8):
                h = wv_i * 8 + ti
                nc.vector.tensor_tensor(Asb[:, h, :],
                                        k.pb[ti // 4][:, (ti % 4) * 128:(ti % 4) * 128 + 128],
                                        tri[:], OP.mult)
            nc.vector.drain()
            nc.vector.tensor_copy(can[:], Asb[:, wv_i * 8, 0:8])
            mk = s.inc(nc.vector.drain(), "l0_v", 1)
            s.wait(nc.tensor, "l0_v", mk)
            s.wait(nc.tensor, "l0_v", sdone)
            s.wait(nc.tensor, "l0_v", vdone)
            for ti in range(8):
                h = wv_i * 8 + ti
                oslot = k.pb[4 + h // 8][:, (h % 8) * 64:(h % 8) * 64 + 64]
                nc.tensor.matmul(oslot, Asb[:, h, :], vbf[:, 64 * h:64 * h + 64],
                                 start=(h % 8 == 0), stop=False)
                nc.tensor.matmul(oslot, qT[0:64, h, :], Ssb[0:64, h, :],
                                 start=False, stop=True)
                mm = nc.tensor.matmul(k.pb[2 + h % 2][0:64, (h // 2) * 64:(h // 2) * 64 + 64],
                                      xnb[:, 64 * h:64 * h + 64],
                                      vbf[:, 64 * h:64 * h + 64],
                                      start=(c == 0 and h <= 1),
                                      stop=(c == nchunks - 1))
            s.inc(mm, "mm", 1)
        if CUT <= 3:
            qdone = qdone_
            odone = vdone
            xs_free = vdone
            continue
        fence = nc.tensor.matmul(k.pg[0:1, 100:101], ident[0:128, 0:1], ident[0:128, 0:1],
                                 start=True, stop=True)
        fence_prev = s.inc(fence, "mm", 1)
        fence_prev = s.inc(nc.tensor.nop(), "mm", 1)
        s.wait(nc.vector, "mm", fence_prev)
        s.wait(nc.vector, f"l0_o{j}", st_free[j])
        nc.vector.tensor_copy(osb[64:128, 0:512], osb[64:128, 0:512])
        nc.vector.tensor_copy(osb[:, 0:512], k.pb[4][:])
        nc.vector.tensor_copy(osb[:, 512:1024], k.pb[5][:])
        odone = s.inc(nc.vector.drain(), "l0_v", 1)
        qdone = qdone_
        vsw()
        _ln_plain(k, None, osb, ot[j], tmp)
        vsw()
        nc.vector.tensor_tensor(ot[j][:], ot[j][:], xs[:], OP.add)
        ov = xs_free = s.inc(nc.vector.drain(), "l0_v", 1)
        s.wait(nc.scalar, "l0_v", ov)
        sd = nc.scalar.dma_start(t["s1"][bass.ts(c, 128), :], ot[j][:])
        st_free[j] = s.inc(sd, f"l0_o{j}", 16)
    for j in range(2):
        s.wait(nc.scalar, f"l0_o{j}", s.val(f"l0_o{j}"))
    k.pop_scope()


def _barrier(k: Kb, recycle=False):
    s = k.s
    for eng in (k.nc.sync, k.nc.tensor, k.nc.vector, k.nc.scalar, k.nc.gpsimd):
        for name in list(s.h):
            v = s.val(name)
            if v > 0:
                eng.wait_ge(s.get(name), v)
    if recycle:
        keep = {"mm", "ln"}
        drop = [n for n in list(s.h) if n not in keep]
        for n in drop:
            s.pool.append((s.h[n], s.v[n]))
            del s.h[n]
            del s.v[n]


def build_fused():
    nc = bass.Bass("TRN2", target_bir_lowering=False)
    t = {}
    EI = lambda n, sh, dt=F32: nc.dram_tensor(n, sh, dt, kind="ExternalInput").ap()
    IN = lambda n, sh, dt=F32: nc.dram_tensor(n, sh, dt, kind="Internal").ap()
    t["ident"] = EI("ident", [128, 128], BF16)
    t["x0"] = EI("x0", [NT, D])
    t["l0_wqT"] = EI("l0_wqT", [D, D]); t["l0_wvT"] = EI("l0_wvT", [D, D])
    t["qnw_rep"] = EI("qnw_rep", [128, D]); t["vnw_rep"] = EI("vnw_rep", [128, D])
    t["triT"] = EI("triT", [128, 128])
    t["wfcT"] = EI("wfcT", [D, FF]); t["wprojT"] = EI("wprojT", [FF, D])
    t["lnw_rep"] = EI("lnw_rep", [128, FF]); t["lnb_rep"] = EI("lnb_rep", [128, FF])
    t["l1a_wqT"] = EI("l1a_wqT", [D, D]); t["l1a_wkT"] = EI("l1a_wkT", [D, D])
    t["l1a_wvT"] = EI("l1a_wvT", [D, D])
    for nm in ("wq", "wk", "wv", "wo"):
        t["l1f_" + nm] = EI("l1f_" + nm, [D, D])
    t["biases"] = EI("biases", [1, 4096])
    t["wgw"] = EI("wgw", [D, 8]); t["bg"] = EI("bg", [1, 8])
    t["w1"] = EI("w1", [E, D, FF]); t["w2"] = EI("w2", [E, FF, D])
    t["b1t"] = EI("b1t", [128, FT, E]); t["b2"] = EI("b2", [8, D])
    t["cos_rep"] = EI("cos_rep", [NT, D]); t["sinS_rep"] = EI("sinS_rep", [NT, D])
    for nm in ("g1_rep", "be1_rep", "g2_rep", "be2_rep"):
        t[nm] = EI(nm, [128, D])
    for nm in ("l0_wqT_bf", "l0_wvT_bf", "l1a_wqT_bf", "l1a_wkT_bf", "l1a_wvT_bf",
               "l1f_wq_bf", "l1f_wk_bf", "l1f_wv_bf", "l1f_wo_bf"):
        t[nm] = IN(nm + "_s", [8, 2, 128, 512], BF16)
    t["wg_bf"] = IN("wg_bf_s", [8, 128, 8], BF16)
    t["w1bf"] = IN("w1bf_s", [E, 8, 128, FF], BF16)
    t["w2bf"] = IN("w2bf_s", [E, FT, 128, D], BF16)
    t["wfcq"] = IN("wfcq_s", [8, 128, FF], BF16)
    t["wprojq"] = IN("wprojq_s", [FT, 128, D], BF16)
    t["qscr_a"] = IN("qscr_a", [1, 256]); t["qscr_b"] = IN("qscr_b", [1, 256])
    t["gscr"] = IN("gscr", [1, 512])
    t["s1"] = IN("s1_s", [NT, D]); t["s2"] = IN("s2_s", [NT, D])
    t["s3"] = IN("s3_s", [NT, D])
    t["q"] = IN("q_s", [NT, D], BF16); t["k"] = IN("k_s", [NT, D], BF16)
    t["v"] = IN("v_s", [NT, D], BF16)
    t["ao1"] = IN("ao1_s", [NT, D], BF16)
    t["yg"] = IN("yg_s", [NT, D]); t["xn2"] = IN("xn2_s", [NT, D])
    t["xn"] = IN("xn_s", [NT, D])
    t["qT"] = IN("qT_s", [D, NT], BF16); t["kT"] = IN("kT_s", [D, NT], BF16)
    t["vh"] = IN("vh_s", [H, NBLK, 128, 64], BF16)
    t["ao2"] = IN("ao2_s", [NT, D], BF16)
    t["x1"] = IN("x1_s", [NT, D])
    t["x1t"] = IN("x1t_s", [NGRP, 8, 128, GRP], BF16)
    t["combt"] = IN("combt_s", [128, NBLK, 8])
    t["yacc"] = IN("yacc_s", [NT, D])
    t["out"] = nc.dram_tensor("out", [NT, D], F32, kind="ExternalOutput").ap()

    es = ExitStack()
    k = Kb(nc, es)
    jobs_l0 = []
    for srcnm, dstnm in (("l0_wqT", "l0_wqT_bf"), ("l0_wvT", "l0_wvT_bf")):
        for c in range(8):
            for hh in range(2):
                jobs_l0.append((t[srcnm][c * 128:(c + 1) * 128, hh * 512:(hh + 1) * 512],
                                t[dstnm][c, hh]))
    jobs = []
    for srcnm, dstnm in (("l1a_wqT", "l1a_wqT_bf"), ("l1a_wkT", "l1a_wkT_bf"),
                         ("l1a_wvT", "l1a_wvT_bf"), ("l1f_wq", "l1f_wq_bf"),
                         ("l1f_wk", "l1f_wk_bf"), ("l1f_wv", "l1f_wv_bf"),
                         ("l1f_wo", "l1f_wo_bf")):
        for c in range(8):
            for hh in range(2):
                jobs.append((t[srcnm][c * 128:(c + 1) * 128, hh * 512:(hh + 1) * 512],
                             t[dstnm][c, hh]))
    for c in range(8):
        jobs.append((t["wgw"][c * 128:(c + 1) * 128, :], t["wg_bf"][c]))
    for e in range(E):
        for c in range(8):
            for hh in range(2):
                jobs.append((t["w1"][e, c * 128:(c + 1) * 128, hh * 2048:(hh + 1) * 2048],
                             t["w1bf"][e, c][:, hh * 2048:(hh + 1) * 2048]))
        for f in range(FT):
            jobs.append((t["w2"][e, f * 128:(f + 1) * 128, :], t["w2bf"][e, f]))
    emit_cast_pass(k, jobs_l0)
    _barrier(k, recycle=True)
    emit_l0_quant(k, t, "wfcT", "wfcq", D, FF, tag="qa", scr=t["qscr_a"])
    _barrier(k)
    emit_l0_quant(k, t, "wprojT", "wprojq", FF, D, tag="qb", scr=t["qscr_b"])
    _barrier(k, recycle=True)
    emit_cast_stream(k, jobs)
    emit_l0_attn(k, dict(t, wqT_bf=t["l0_wqT_bf"], wvT_bf=t["l0_wvT_bf"]))
    _barrier(k, recycle=True)
    k.pop_scope()
    emit_l0_ffn(k, t)
    _barrier(k, recycle=True)
    emit_l1a_qkv(k, dict(t, wqT_bf=t["l1a_wqT_bf"], wkT_bf=t["l1a_wkT_bf"],
                         wvT_bf=t["l1a_wvT_bf"]))
    _barrier(k, recycle=True)
    emit_attention(k, dict(t, ao=t["ao1"]), "l1a", 1.0, "a3")
    _barrier(k, recycle=True)
    emit_l1a_norm(k, dict(t, ao=t["ao1"]))
    _barrier(k, recycle=True)
    emit_l1f_qkv(k, dict(t, wq_bf=t["l1f_wq_bf"], wk_bf=t["l1f_wk_bf"],
                         wv_bf=t["l1f_wv_bf"]))
    _barrier(k, recycle=True)
    emit_attention(k, dict(t, ao=t["ao2"]), "l1f", 0.125, "a4")
    _barrier(k, recycle=True)
    emit_l1f_x1gate(k, dict(t, ao=t["ao2"], wo_bf=t["l1f_wo_bf"]))
    _barrier(k, recycle=True)
    emit_moe(k, t)
    _barrier(k, recycle=True)
    emit_moe_final(k, t)
    es.close()
    return nc


def _declare_common(nc, t):
    EI = lambda n, sh, dt=F32: nc.dram_tensor(n, sh, dt, kind="ExternalInput").ap()
    t["ident"] = EI("ident", [128, 128], BF16)
    return EI


def build_p1():
    nc = bass.Bass("TRN2", target_bir_lowering=False)
    t = {}
    EI = _declare_common(nc, t)
    IN = lambda n, sh, dt=F32: nc.dram_tensor(n, sh, dt, kind="Internal").ap()
    t["x0"] = EI("x0", [NT, D])
    t["l0_wqT"] = EI("l0_wqT", [D, D]); t["l0_wvT"] = EI("l0_wvT", [D, D])
    t["qnw_rep"] = EI("qnw_rep", [128, D]); t["vnw_rep"] = EI("vnw_rep", [128, D])
    t["triT"] = EI("triT", [128, 128])
    t["wfcT"] = EI("wfcT", [D, FF]); t["wprojT"] = EI("wprojT", [FF, D])
    t["lnw_rep"] = EI("lnw_rep", [128, FF]); t["lnb_rep"] = EI("lnb_rep", [128, FF])
    t["wqT_bf"] = IN("wqT_bf_s", [8, 2, 128, 512], BF16)
    t["wvT_bf"] = IN("wvT_bf_s", [8, 2, 128, 512], BF16)
    t["wfcq"] = IN("wfcq_s", [8, 128, FF], BF16)
    t["wprojq"] = IN("wprojq_s", [FT, 128, D], BF16)
    t["qscr_a"] = IN("qscr_a", [1, 256]); t["qscr_b"] = IN("qscr_b", [1, 256])
    t["s1"] = IN("s1_s", [NT, D])
    t["s2"] = nc.dram_tensor("s2", [NT, D], F32, kind="ExternalOutput").ap()
    es = ExitStack()
    k = Kb(nc, es)
    jobs = []
    for nm in ("wqT", "wvT"):
        for c in range(8):
            for hh in range(2):
                jobs.append((t["l0_" + nm][c * 128:(c + 1) * 128, hh * 512:(hh + 1) * 512],
                             t[nm + "_bf"][c, hh]))
    emit_cast_pass(k, jobs)
    _barrier(k)
    emit_l0_quant(k, t, "wfcT", "wfcq", D, FF, tag="qa", scr=t["qscr_a"])
    _barrier(k)
    emit_l0_quant(k, t, "wprojT", "wprojq", FF, D, tag="qb", scr=t["qscr_b"])
    _barrier(k)
    emit_l0_attn(k, t)
    _barrier(k)
    emit_l0_ffn(k, t)
    es.close()
    return nc


def build_p2():
    nc = bass.Bass("TRN2", target_bir_lowering=False)
    t = {}
    EI = _declare_common(nc, t)
    IN = lambda n, sh, dt=F32: nc.dram_tensor(n, sh, dt, kind="Internal").ap()
    t["s2"] = EI("s2", [NT, D])
    t["l1a_wqT"] = EI("l1a_wqT", [D, D]); t["l1a_wkT"] = EI("l1a_wkT", [D, D])
    t["l1a_wvT"] = EI("l1a_wvT", [D, D])
    for nm in ("wqT", "wkT", "wvT"):
        t[nm + "_bf"] = IN(nm + "_bf_s", [8, 2, 128, 512], BF16)
    t["q"] = IN("q_s", [NT, D], BF16); t["k"] = IN("k_s", [NT, D], BF16)
    t["v"] = IN("v_s", [NT, D], BF16)
    t["ao"] = IN("ao_s", [NT, D], BF16)
    t["yg"] = IN("yg_s", [NT, D])
    t["xn2"] = IN("xn2_s", [NT, D])
    t["gscr"] = IN("gscr", [1, 512])
    t["s3"] = nc.dram_tensor("s3", [NT, D], F32, kind="ExternalOutput").ap()
    es = ExitStack()
    k = Kb(nc, es)
    jobs = []
    for nm in ("wqT", "wkT", "wvT"):
        for c in range(8):
            for hh in range(2):
                jobs.append((t["l1a_" + nm][c * 128:(c + 1) * 128, hh * 512:(hh + 1) * 512],
                             t[nm + "_bf"][c, hh]))
    emit_cast_pass(k, jobs)
    _barrier(k)
    emit_l1a_qkv(k, t)
    _barrier(k)
    emit_attention(k, t, "l1a", 1.0, "a3")
    _barrier(k)
    emit_l1a_norm(k, t)
    es.close()
    return nc


def build_p3():
    nc = bass.Bass("TRN2", target_bir_lowering=False)
    t = {}
    EI = _declare_common(nc, t)
    IN = lambda n, sh, dt=F32: nc.dram_tensor(n, sh, dt, kind="Internal").ap()
    t["s3"] = EI("s3", [NT, D])
    for nm in ("wq", "wk", "wv", "wo"):
        t["l1f_" + nm] = EI("l1f_" + nm, [D, D])
    t["biases"] = EI("biases", [1, 4096])
    t["wgw"] = EI("wgw", [D, 8]); t["bg"] = EI("bg", [1, 8])
    t["w1"] = EI("w1", [E, D, FF]); t["w2"] = EI("w2", [E, FF, D])
    t["b1t"] = EI("b1t", [128, FT, E]); t["b2"] = EI("b2", [8, D])
    t["cos_rep"] = EI("cos_rep", [NT, D]); t["sinS_rep"] = EI("sinS_rep", [NT, D])
    for nm in ("g1_rep", "be1_rep", "g2_rep", "be2_rep"):
        t[nm] = EI(nm, [128, D])
    for nm in ("wq", "wk", "wv", "wo"):
        t[nm + "_bf"] = IN(nm + "_bf_s", [8, 2, 128, 512], BF16)
    t["wg_bf"] = IN("wg_bf_s", [8, 128, 8], BF16)
    t["w1bf"] = IN("w1bf_s", [E, 8, 128, FF], BF16)
    t["w2bf"] = IN("w2bf_s", [E, FT, 128, D], BF16)
    t["xn"] = IN("xn_s", [NT, D])
    t["qT"] = IN("qT_s", [D, NT], BF16); t["kT"] = IN("kT_s", [D, NT], BF16)
    t["vh"] = IN("vh_s", [H, NBLK, 128, 64], BF16)
    t["ao"] = IN("ao_s", [NT, D], BF16)
    t["x1"] = IN("x1_s", [NT, D])
    t["x1t"] = IN("x1t_s", [NGRP, 8, 128, GRP], BF16)
    t["combt"] = IN("combt_s", [128, NBLK, 8])
    t["yacc"] = IN("yacc_s", [NT, D])
    t["wo_bf"] = t["wo_bf"]
    t["out"] = nc.dram_tensor("out", [NT, D], F32, kind="ExternalOutput").ap()
    es = ExitStack()
    k = Kb(nc, es)
    jobs = []
    for nm in ("wq", "wk", "wv", "wo"):
        for c in range(8):
            for hh in range(2):
                jobs.append((t["l1f_" + nm][c * 128:(c + 1) * 128, hh * 512:(hh + 1) * 512],
                             t[nm + "_bf"][c, hh]))
    for c in range(8):
        jobs.append((t["wgw"][c * 128:(c + 1) * 128, :], t["wg_bf"][c]))
    for e in range(E):
        for c in range(8):
            for hh in range(2):
                jobs.append((t["w1"][e, c * 128:(c + 1) * 128, hh * 2048:(hh + 1) * 2048],
                             t["w1bf"][e, c][:, hh * 2048:(hh + 1) * 2048]))
        for f in range(FT):
            jobs.append((t["w2"][e, f * 128:(f + 1) * 128, :], t["w2bf"][e, f]))
    emit_cast_pass(k, jobs)
    _barrier(k)
    emit_l1f_qkv(k, t)
    _barrier(k)
    emit_attention(k, t, "l1f", 0.125, "a4")
    _barrier(k)
    emit_l1f_x1gate(k, t)
    _barrier(k)
    emit_moe(k, t)
    _barrier(k)
    emit_moe_final(k, t)
    es.close()
    return nc


def _host_prep(inputs):
    import ml_dtypes
    P = inputs
    base = {}
    base["l0_wqT"] = np.ascontiguousarray(P["L0_attn_Wq"].T)
    base["l0_wvT"] = np.ascontiguousarray(P["L0_attn_Wv"].T)
    base["qnw_rep"] = np.ascontiguousarray(
        np.broadcast_to(P["L0_attn_q_norm_w"].reshape(-1), (128, D)))
    base["vnw_rep"] = np.ascontiguousarray(
        np.broadcast_to(P["L0_attn_v_norm_w"].reshape(-1), (128, D)))
    base["triT"] = np.triu(np.ones((128, 128), np.float32))
    base["ident"] = np.eye(128, dtype=ml_dtypes.bfloat16)
    base["wfcT"] = np.ascontiguousarray(P["L0_ffn_W_fc"].T)
    base["wprojT"] = np.ascontiguousarray(P["L0_ffn_W_proj"].T)
    base["lnw_rep"] = np.ascontiguousarray(np.broadcast_to(P["L0_ffn_ln_w"], (128, FF)))
    base["lnb_rep"] = np.ascontiguousarray(np.broadcast_to(P["L0_ffn_ln_b"], (128, FF)))
    base["l1a_wqT"] = np.ascontiguousarray(P["L1_attn_Wq"].T)
    base["l1a_wkT"] = np.ascontiguousarray(P["L1_attn_Wk"].T)
    base["l1a_wvT"] = np.ascontiguousarray(P["L1_attn_Wv"].T)
    base["l1f_wq"] = np.ascontiguousarray(P["L1_ffn_Wq"])
    base["l1f_wk"] = np.ascontiguousarray(P["L1_ffn_Wk"])
    base["l1f_wv"] = np.ascontiguousarray(P["L1_ffn_Wv"])
    base["l1f_wo"] = np.ascontiguousarray(P["L1_ffn_Wo"])
    base["biases"] = np.concatenate([P["L1_ffn_bq"], P["L1_ffn_bk"], P["L1_ffn_bv"],
                                     P["L1_ffn_bo"]]).astype(np.float32)[None, :]
    base["wgw"] = np.ascontiguousarray(P["L1_ffn_Wg"])
    base["bg"] = P["L1_ffn_bg"].astype(np.float32)[None, :]
    base["w1"] = np.ascontiguousarray(P["L1_ffn_W1"])
    base["w2"] = np.ascontiguousarray(P["L1_ffn_W2"])
    base["b1t"] = np.ascontiguousarray(
        P["L1_ffn_b1"].reshape(E, FT, 128).transpose(2, 1, 0))
    base["b2"] = np.ascontiguousarray(P["L1_ffn_b2"])
    hd = 64
    inv = (1.0 / 10000.0 ** (np.arange(0, hd, 2, dtype=np.float32) / hd)).astype(np.float32)
    tt_ = np.arange(NT, dtype=np.float32)[:, None] * inv[None, :]
    emb = np.concatenate([tt_, tt_], -1).astype(np.float32)
    cos = np.cos(emb).astype(np.float32)
    sin = np.sin(emb).astype(np.float32)
    sinS = sin.copy(); sinS[:, :32] *= -1
    base["cos_rep"] = np.ascontiguousarray(np.tile(cos, (1, H)))
    base["sinS_rep"] = np.ascontiguousarray(np.tile(sinS, (1, H)))
    base["g1_rep"] = np.ascontiguousarray(np.broadcast_to(P["L1_ffn_g1"], (128, D)))
    base["be1_rep"] = np.ascontiguousarray(np.broadcast_to(P["L1_ffn_be1"], (128, D)))
    base["g2_rep"] = np.ascontiguousarray(np.broadcast_to(P["L1_ffn_g2"], (128, D)))
    base["be2_rep"] = np.ascontiguousarray(np.broadcast_to(P["L1_ffn_be2"], (128, D)))
    return base


FUSED = True


def kernel(**inputs):
    base = _host_prep(inputs)
    x = np.ascontiguousarray(inputs["L0_attn_X"])
    cores = list(range(8))
    if FUSED:
        nc = build_fused()
        keys = ("ident", "l0_wqT", "l0_wvT", "qnw_rep", "vnw_rep", "triT",
                "wfcT", "wprojT", "lnw_rep", "lnb_rep", "l1a_wqT", "l1a_wkT",
                "l1a_wvT", "l1f_wq", "l1f_wk", "l1f_wv", "l1f_wo", "biases",
                "wgw", "bg", "w1", "w2", "b1t", "b2", "cos_rep", "sinS_rep",
                "g1_rep", "be1_rep", "g2_rep", "be2_rep")
        maps = [dict({kk: base[kk] for kk in keys}, x0=np.ascontiguousarray(x[c]))
                for c in cores]
        res = run_bass_kernel_spmd(nc, maps, core_ids=cores)
        out = np.stack([np.asarray(r["out"]) for r in res.results], axis=0)
        return out.astype(np.float32)
    keys1 = ("ident", "l0_wqT", "l0_wvT", "qnw_rep", "vnw_rep", "triT",
             "wfcT", "wprojT", "lnw_rep", "lnb_rep")
    nc1 = build_p1()
    maps = [dict({kk: base[kk] for kk in keys1}, x0=np.ascontiguousarray(x[c]))
            for c in cores]
    r1 = run_bass_kernel_spmd(nc1, maps, core_ids=cores)
    s2 = [np.asarray(r["s2"]) for r in r1.results]
    keys2 = ("ident", "l1a_wqT", "l1a_wkT", "l1a_wvT")
    nc2 = build_p2()
    maps = [dict({kk: base[kk] for kk in keys2}, s2=s2[c]) for c in cores]
    r2 = run_bass_kernel_spmd(nc2, maps, core_ids=cores)
    s3 = [np.asarray(r["s3"]) for r in r2.results]
    keys3 = ("ident", "l1f_wq", "l1f_wk", "l1f_wv", "l1f_wo", "biases", "wgw",
             "bg", "w1", "w2", "b1t", "b2", "cos_rep", "sinS_rep",
             "g1_rep", "be1_rep", "g2_rep", "be2_rep")
    nc3 = build_p3()
    maps = [dict({kk: base[kk] for kk in keys3}, s3=s3[c]) for c in cores]
    r3 = run_bass_kernel_spmd(nc3, maps, core_ids=cores)
    out = np.stack([np.asarray(r["out"]) for r in r3.results], axis=0)
    return out.astype(np.float32)
